# version 05
# baseline (speedup 1.0000x reference)
import math
import numpy as np
import ml_dtypes

import concourse.bass as bass
import concourse.mybir as mybir
from concourse.bass_utils import run_bass_kernel_spmd

F32 = mybir.dt.float32
BF16 = mybir.dt.bfloat16
I32 = mybir.dt.int32
AF = mybir.ActivationFunctionType
ALU = mybir.AluOpType

B, T, D = 4, 4096, 1024
TH = T // 2
NT = T // 128
HPC = 8
KEEP = 256
FFH = 3072
NJ = FFH // 128
SM_SHIFT = -12.0
INV127 = 1.0 / 127.0
ISQ = 1.0 / math.sqrt(96.0)
NBIS = 48
_ALOG = []
_TAPS = False


def _rel_bucket(d):
    nb, maxd = 16, 128
    buckets = (d > 0).astype(np.int64) * nb
    rp = np.abs(d)
    max_exact = nb // 2
    small = rp < max_exact
    rl = max_exact + (
        np.log(np.maximum(rp, 1) / max_exact) / math.log(maxd / max_exact)
        * (nb - max_exact)
    ).astype(np.int64)
    rl = np.minimum(rl, nb - 1)
    return buckets + np.where(small, rp, rl)


_STATICS = None


def _statics():
    global _STATICS
    if _STATICS is not None:
        return _STATICS
    d = np.arange(-4095, 4096)
    buck = _rel_bucket(d)
    onehot = np.zeros((32, 8191), dtype=ml_dtypes.bfloat16)
    onehot[buck, np.arange(8191)] = 1.0
    ident = np.eye(128, dtype=ml_dtypes.bfloat16)
    identf = np.eye(128, dtype=np.float32)
    trils = np.triu(np.ones((128, 128), dtype=np.float32), 1).astype(
        ml_dtypes.bfloat16)
    iota = np.broadcast_to(np.arange(1, 257, dtype=np.float32)[None, :],
                           (128, 256)).copy()
    onesb = np.ones((128, 128), dtype=ml_dtypes.bfloat16)
    onesf = np.ones((128, 128), dtype=np.float32)
    masks = np.zeros((128, 4 * 512), dtype=ml_dtypes.bfloat16)
    for m, c0 in enumerate((0, 128, 256, 384)):
        p = np.arange(128)[:, None]
        j = np.arange(512)[None, :]
        masks[:, 512 * m:512 * (m + 1)] = (p + c0 <= j).astype(
            ml_dtypes.bfloat16)
    f = 1.0 / (10000.0 ** (np.arange(0, 64, 2, dtype=np.float64) / 64.0))
    t = np.arange(T, dtype=np.float64)
    a = np.outer(t, f)
    c2 = np.concatenate([np.cos(a), np.cos(a)], axis=1)
    s2 = np.concatenate([-np.sin(a), np.sin(a)], axis=1)
    ropec = np.zeros((128, NT * 64), dtype=ml_dtypes.bfloat16)
    ropes = np.zeros((128, NT * 64), dtype=ml_dtypes.bfloat16)
    for tt in range(NT):
        ropec[:, 64 * tt:64 * (tt + 1)] = c2[128 * tt:128 * (tt + 1)]
        ropes[:, 64 * tt:64 * (tt + 1)] = s2[128 * tt:128 * (tt + 1)]
    _STATICS = dict(onehot=onehot, ident=ident, identf=identf, trils=trils,
                    iota=iota, onesb=onesb, onesf=onesf, masks=masks,
                    ropec=ropec, ropes=ropes)
    return _STATICS


ENGS = ("sync", "tensor", "vector", "scalar", "gpsimd")


class Ctr:
    def __init__(self, name):
        self.name = name
        self.n = 0
        self.h = None


def build_nc():
    nc = bass.Bass("TRN2", target_bir_lowering=False)

    def din(name, shape, dt=F32):
        return nc.dram_tensor(name, list(shape), dt, kind="ExternalInput")

    x_in = din("x_ownT", (D, TH))
    wq = din("wq", (D, 512))
    wk = din("wk", (D, 512))
    wv = din("wv", (D, 512))
    wo = din("wo", (D, D))
    rtab = din("rtab", (32, 8))
    wgT = din("wgT", (D, FFH), I32)
    wuT = din("wuT", (D, FFH), I32)
    wdT = din("wdT", (FFH, D), I32)
    wcq = din("wcq", (D, 96))
    gq = din("gq", (96, 1))
    wdqn = din("wdqn", (96, 256))
    wdqr = din("wdqr", (96, 512))
    wckv = din("wckv", (D, 32))
    gkv = din("gkv", (32, 1))
    wdkn = din("wdkn", (32, 256))
    wdv = din("wdv", (32, 256))
    wkro = din("wkro", (D, 64))
    wimp = din("wimp", (D, 1))
    wselk = din("wselk", (D, 768))
    wselv = din("wselv", (D, 256))
    wwink = din("wwink", (D, 768))
    wwinv = din("wwinv", (D, 256))
    wgate = din("wgate", (D, 3))
    wproj = din("wproj", (512, D))
    s_onehot = din("s_onehot", (32, 8191), BF16)
    s_ident = din("s_ident", (128, 128), BF16)
    s_identf = din("s_identf", (128, 128))
    s_trils = din("s_trils", (128, 128), BF16)
    s_iota = din("s_iota", (128, 256))
    s_onesb = din("s_onesb", (128, 128), BF16)
    s_onesf = din("s_onesf", (128, 128))
    s_masks = din("s_masks", (128, 2048), BF16)
    s_ropec = din("s_ropec", (128, NT * 64), BF16)
    s_ropes = din("s_ropes", (128, NT * 64), BF16)
    tok0_in = din("tok0", (1, 1), I32)

    y_out = nc.dram_tensor("y_ownT", [D, TH], F32, kind="ExternalOutput")
    dbg = []
    if _TAPS:
        for i in range(7):
            dbg.append(nc.dram_tensor("dbg%d" % i, [D, TH], F32,
                                      kind="ExternalOutput"))
        dbg_xn = nc.dram_tensor("dbg_xn", [2 * D, TH], BF16,
                                kind="ExternalOutput")
        dbg_bias = nc.dram_tensor("dbg_bias", [8, 8191], F32,
                                  kind="ExternalOutput")
        dbg_av = nc.dram_tensor("dbg_av", [512, T], BF16,
                                kind="ExternalOutput")
        dbg_qt = nc.dram_tensor("dbg_qt", [128, 4 * T], BF16,
                                kind="ExternalOutput")
        dbg_kt = nc.dram_tensor("dbg_kt", [128, 4 * T], BF16,
                                kind="ExternalOutput")
        dbg_v = nc.dram_tensor("dbg_v", [128, NT * 520], BF16,
                               kind="ExternalOutput")
        dbg_aev = nc.dram_tensor("dbg_aev", [128, 512], F32,
                                 kind="ExternalOutput")
        dbg_rcp = nc.dram_tensor("dbg_rcp", [128, 8], F32,
                                 kind="ExternalOutput")
        dbg_comb = nc.dram_tensor("dbg_comb", [128, 256], BF16,
                                  kind="ExternalOutput")
        dbg_avb = nc.dram_tensor("dbg_avb", [256, T], BF16,
                                 kind="ExternalOutput")
        dbg_misc = nc.dram_tensor("dbg_misc", [128, 16], F32,
                                  kind="ExternalOutput")
        dbg_grow = nc.dram_tensor("dbg_grow", [1, 16], F32,
                                  kind="ExternalOutput")
        dbg_ab3 = nc.dram_tensor("dbg_ab3", [128, 3 * 512], F32,
                                 kind="ExternalOutput")
        dbg_rc3 = nc.dram_tensor("dbg_rc3", [128, 8], F32,
                                 kind="ExternalOutput")
        dbg_cb3 = nc.dram_tensor("dbg_cb3", [128, 256], BF16,
                                 kind="ExternalOutput")
        dbg_exb = nc.dram_tensor("dbg_exb", [128, 4096], BF16,
                                 kind="ExternalOutput")
        dbg_qth = nc.dram_tensor("dbg_qth", [96, T], BF16,
                                 kind="ExternalOutput")
        dbg_k1h = nc.dram_tensor("dbg_k1h", [96, T], BF16,
                                 kind="ExternalOutput")
        dbg_kwh = nc.dram_tensor("dbg_kwh", [96, T], BF16,
                                 kind="ExternalOutput")
        dbg_cst = nc.dram_tensor("dbg_cst", [128, 64], F32,
                                 kind="ExternalOutput")
        dbg_k1n = nc.dram_tensor("dbg_k1n", [128, NT * 96], BF16,
                                 kind="ExternalOutput")
        dbg_ckv = nc.dram_tensor("dbg_ckv", [32, T], BF16,
                                 kind="ExternalOutput")
        dbg_krn = nc.dram_tensor("dbg_krn", [128, NT * 64], BF16,
                                 kind="ExternalOutput")
        dbg_kst = nc.dram_tensor("dbg_kst", [96, 2048], BF16,
                                 kind="ExternalOutput")
        dbg_vsa = nc.dram_tensor("dbg_vsa", [128, 528], BF16,
                                 kind="ExternalOutput")
        dbg_pts = nc.dram_tensor("dbg_pts", [128, 8192], BF16,
                                 kind="ExternalOutput")
        dbg_junk = nc.dram_tensor("dbg_junk", [128, 32], F32,
                                  kind="ExternalOutput")
        dbg_cum = nc.dram_tensor("dbg_cum", [128, 32], F32,
                                 kind="ExternalOutput")
        dbg_cumf = nc.dram_tensor("dbg_cumf", [128, 32], F32,
                                  kind="ExternalOutput")
        dbg_ssc = nc.dram_tensor("dbg_ssc", [128, 32], F32,
                                 kind="ExternalOutput")

    bias_dram = nc.dram_tensor("bias_dram", [8, 8191], F32)
    cin_half = nc.dram_tensor("cin_half", [D, TH], BF16)
    xn_allg = nc.dram_tensor("xn_allg", [2 * D, TH], BF16)
    cin_av = nc.dram_tensor("cin_av", [512, T], BF16)
    av_allg = nc.dram_tensor("av_allg", [1024, T], BF16)
    cin_avb = nc.dram_tensor("cin_avb", [256, T], BF16)
    avb_allg = nc.dram_tensor("avb_allg", [512, T], BF16)
    kwT_scr = nc.dram_tensor("kwT_scr", [HPC * 96, T], BF16)
    v1_scr = nc.dram_tensor("v1_scr", [HPC * T, 33], BF16)
    vw_scr = nc.dram_tensor("vw_scr", [HPC * T, 33], BF16)

    X = nc.alloc_sbuf_tensor("X", [128, 8 * TH], F32)
    IDENT = nc.alloc_sbuf_tensor("IDENT", [128, 128], BF16)
    IDENTF = nc.alloc_sbuf_tensor("IDENTF", [128, 128], F32)
    TRILS = nc.alloc_sbuf_tensor("TRILS", [128, 128], BF16)
    ONESB = nc.alloc_sbuf_tensor("ONESB", [128, 128], BF16)
    ONESF = nc.alloc_sbuf_tensor("ONESF", [128, 128], F32)
    IOTA = nc.alloc_sbuf_tensor("IOTA", [128, 256], F32)
    CONST = nc.alloc_sbuf_tensor("CONST", [128, 64], F32)
    MISC = nc.alloc_sbuf_tensor("MISC", [128, 16], F32)
    LNROW = nc.alloc_sbuf_tensor("LNROW", [128, 2048], F32)
    XBUF = nc.alloc_sbuf_tensor("XBUF", [128, 1024], BF16)
    XNTMP = nc.alloc_sbuf_tensor("XNTMP", [128, 512], F32)
    XNST = nc.alloc_sbuf_tensor("XNST", [128, 512], BF16)
    TOKI = nc.alloc_sbuf_tensor("TOKI", [1, 1], I32)

    SBASE = 229375 - nc.sbuf_bytes_remaining + 64
    SLIMIT = 229376
    arena = {"off": SBASE}

    def ar_mark():
        return arena["off"]

    def ar_to(mark):
        arena["off"] = mark

    def ar(name, shape, dt):
        nbytes = int(np.prod(shape[1:])) * mybir.dt.size(dt)
        off = (arena["off"] + 63) // 64 * 64
        t = nc.alloc_sbuf_tensor_at(name, list(shape), dt, offset=off)
        _ALOG.append((name, off, off + nbytes))
        arena["off"] = off + nbytes
        assert arena["off"] <= SLIMIT, (name, arena["off"])
        return t

    psA = nc.alloc_psum_tensor("psA", [128, 2048], F32)
    ps1 = nc.alloc_psum_tensor("ps1", [128, 512], F32)
    ps2 = nc.alloc_psum_tensor("ps2", [128, 512], F32)
    ps3 = nc.alloc_psum_tensor("ps3", [128, 512], F32)
    psT = nc.alloc_psum_tensor("psT", [128, 512], BF16)

    prog = {e: [] for e in ENGS}
    sems = {}

    def mksem(name):
        c = Ctr(name)
        sems[name] = c
        return c

    sSY = mksem("sSY")
    sG = mksem("sG")
    sCC = mksem("sCC")
    sPE = mksem("sPE")
    sDV = mksem("sDV")
    sAC = mksem("sAC")
    sBar = mksem("sBar")
    env = {}

    def E(eng, fn):
        prog[eng].append(fn)

    def W(eng, ctr, val=None):
        v = ctr.n if val is None else val
        E(eng, lambda e, c=ctr, vv=v: e.wait_ge(c.h, vv))

    def EM(eng, fn, inc=None, inc_n=1):
        if inc is None:
            E(eng, fn)
            return
        inc.n += inc_n
        E(eng, lambda e, f=fn, c=inc, n=inc_n: f(e).then_inc(c.h, n))

    def DMA(eng, out, in_, allow_slow=False):
        sem = sG if eng == "gpsimd" else sSY
        kw = {"allow_slow_non_contiguous": True} if allow_slow else {}
        EM(eng, lambda e, o=out, i=in_, k=dict(kw): e.dma_start(o, i, **k),
           inc=sem, inc_n=16)

    def BARRIER():
        target = sBar.n + len(ENGS)
        for eng in ENGS:
            EM(eng, lambda e: e.nop(), inc=sBar)
        for eng in ENGS:
            W(eng, sBar, target)

    def TAP(i):
        if not _TAPS:
            return
        DMA("sync",
            bass.AP(dbg[i], 0, [[TH, 128], [128 * TH, 8], [1, TH]]),
            bass.AP(X, 0, [[8 * TH, 128], [2048, 8], [1, 2048]]))
        W("sync", sSY)

    DMA("sync", X[:, :],
        bass.AP(x_in, 0, [[TH, 128], [128 * TH, 8], [1, TH]]))
    DMA("sync", IDENT[:, :], s_ident[:, :])
    DMA("sync", IDENTF[:, :], s_identf[:, :])
    DMA("sync", TRILS[:, :], s_trils[:, :])
    DMA("sync", ONESB[:, :], s_onesb[:, :])
    DMA("sync", ONESF[:, :], s_onesf[:, :])
    DMA("sync", IOTA[:, :], s_iota[:, :])
    DMA("sync", TOKI[0:1, 0:1], tok0_in[0:1, 0:1])
    W("sync", sSY)

    def _load_tok(e):
        env["tokr"] = e.to_reg(e.value_load(TOKI[0:1, 0:1]))

    E("sync", _load_tok)

    m0 = ar_mark()
    OH = ar("OH", [32, 8191], BF16)
    RT = ar("RT", [32, 8], BF16)
    BROW = ar("BROW", [8, 8191], F32)
    CROW = ar("CROW", [1, 16], F32)
    DMA("sync", OH[:, :], s_onehot[:, :])
    DMA("gpsimd", RT[:, :], rtab[:, :])
    W("tensor", sSY)
    W("tensor", sG)
    for j in range(16):
        w = 512 if j < 15 else 8191 - 15 * 512
        EM("tensor",
           lambda e, jj=j, ww=w: e.matmul(
               ps1[0:8, 0:ww], RT[0:32, :],
               OH[0:32, 512 * jj:512 * jj + ww], start=True, stop=True),
           inc=sPE)
        W("vector", sPE)
        EM("vector",
           lambda e, jj=j, ww=w: e.tensor_copy(
               BROW[0:8, 512 * jj:512 * jj + ww], ps1[0:8, 0:ww]),
           inc=sDV)
        W("tensor", sDV)
    W("sync", sDV)
    DMA("sync", bias_dram[:, :], BROW[0:8, :])
    W("sync", sSY)
    DMA("sync", CROW[0:1, 0:8],
        bass.AP(bias_dram, 3500, [[1, 1], [8191, 8]]), allow_slow=True)
    DMA("sync", CROW[0:1, 8:16],
        bass.AP(bias_dram, 4500, [[1, 1], [8191, 8]]), allow_slow=True)
    W("tensor", sSY)
    EM("tensor",
       lambda e: e.matmul(ps1[:, 0:16], ONESF[0:1, :], CROW[0:1, :],
                          start=True, stop=True), inc=sPE)
    W("vector", sPE)
    EM("vector",
       lambda e: e.tensor_scalar(
           bass.AP(CONST, 0, [[64, 128], [2, 8]]),
           bass.AP(ps1, 0, [[512, 128], [1, 8]]),
           SM_SHIFT, None, ALU.add), inc=sDV)
    EM("vector",
       lambda e: e.tensor_scalar(
           bass.AP(CONST, 1, [[64, 128], [2, 8]]),
           bass.AP(ps1, 8, [[512, 128], [1, 8]]),
           SM_SHIFT, None, ALU.add), inc=sDV)
    EM("vector", lambda e: e.memset(CONST[:, 63:64], SM_SHIFT), inc=sDV)
    ar_to(m0)
    BARRIER()
    TAP(3)


    def emit_ln(sink_sb=None):
        for tc in range(4):
            for cc in range(8):
                W("vector", sPE)
                EM("vector",
                   lambda e, c=cc, t=tc: e.tensor_copy(
                       XBUF[:, 0:512],
                       X[:, 2048 * c + 512 * t:2048 * c + 512 * t + 512]),
                   inc=sDV)
                EM("vector",
                   lambda e: e.tensor_mul(
                       XBUF[:, 512:1024], XBUF[:, 0:512], XBUF[:, 0:512]),
                   inc=sDV)
                W("tensor", sDV)
                EM("tensor",
                   lambda e, c=cc: e.matmul(
                       psA[0:1, 0:512], ONESB[:, 0:1], XBUF[:, 0:512],
                       start=(c == 0), stop=(c == 7)), inc=sPE)
                EM("tensor",
                   lambda e, c=cc: e.matmul(
                       psA[0:1, 512:1024], ONESB[:, 0:1], XBUF[:, 512:1024],
                       start=(c == 0), stop=(c == 7)), inc=sPE)
            W("vector", sPE)
            EM("vector", lambda e: e.tensor_scalar(
                LNROW[0:1, 0:512], psA[0:1, 0:512], 1.0 / D, None, ALU.mult),
               inc=sDV)
            EM("vector", lambda e: e.tensor_scalar(
                LNROW[0:1, 512:1024], psA[0:1, 512:1024], 1.0 / D, None,
                ALU.mult), inc=sDV)
            EM("vector", lambda e: e.tensor_mul(
                LNROW[0:1, 1024:1536], LNROW[0:1, 0:512], LNROW[0:1, 0:512]),
               inc=sDV)
            EM("vector", lambda e: e.tensor_sub(
                LNROW[0:1, 512:1024], LNROW[0:1, 512:1024],
                LNROW[0:1, 1024:1536]), inc=sDV)
            EM("vector", lambda e: e.tensor_scalar(
                LNROW[0:1, 512:1024], LNROW[0:1, 512:1024], 1e-5, None,
                ALU.add), inc=sDV)
            W("scalar", sDV)
            EM("scalar", lambda e: e.activation(
                LNROW[0:1, 1024:1536], LNROW[0:1, 512:1024], AF.Ln,
                bias=0.0, scale=1.0), inc=sAC)
            EM("scalar", lambda e: e.activation(
                LNROW[0:1, 1536:2048], LNROW[0:1, 1024:1536], AF.Exp,
                bias=0.0, scale=-0.5), inc=sAC)
            W("vector", sAC)
            EM("vector", lambda e: e.scalar_tensor_tensor(
                LNROW[0:1, 0:512], LNROW[0:1, 0:512], -1.0,
                LNROW[0:1, 1536:2048], ALU.mult, ALU.mult), inc=sDV)
            W("tensor", sDV)
            EM("tensor", lambda e: e.matmul(
                ps1[:, :], ONESF[0:1, :], LNROW[0:1, 1536:2048],
                start=True, stop=True), inc=sPE)
            EM("tensor", lambda e: e.matmul(
                ps2[:, :], ONESF[0:1, :], LNROW[0:1, 0:512],
                start=True, stop=True), inc=sPE)
            W("vector", sPE)
            for cc in range(8):
                EM("vector",
                   lambda e, c=cc, t=tc: e.tensor_mul(
                       XNTMP[:, :],
                       X[:, 2048 * c + 512 * t:2048 * c + 512 * t + 512],
                       ps1[:, :]), inc=sDV)
                if sink_sb is None:
                    W("vector", sSY)
                    EM("vector",
                       lambda e: e.tensor_add(XNST[:, :], XNTMP[:, :],
                                              ps2[:, :]), inc=sDV)
                    W("sync", sDV)
                    DMA("sync",
                        bass.AP(cin_half, (128 * cc) * TH + 512 * tc,
                                [[TH, 128], [1, 512]]),
                        XNST[:, :])
                else:
                    EM("vector",
                       lambda e, s=sink_sb, c=cc, t=tc: e.tensor_add(
                           s[:, 2048 * c + 512 * t:2048 * c + 512 * t + 512],
                           XNTMP[:, :], ps2[:, :]), inc=sDV)
            W("tensor", sDV)

    def exchange(cin, cout):
        EM("gpsimd",
           lambda e, ci=cin, co=cout: e.collective_compute(
               "AllGather", ALU.bypass,
               replica_groups=[[0, 1], [2, 3], [4, 5], [6, 7]],
               ins=[ci.ap().opt()], outs=[co.ap().opt()]),
           inc=sCC)

    def division(br_list, avw, AEVF, RCPC, COMBB, dst):
        nbr = len(br_list)
        W("vector", sSY)
        W("vector", sPE)
        for bi, (bp, _) in enumerate(br_list):
            EM("vector",
               lambda e, b=bp, w=avw, i=bi: e.tensor_copy(
                   AEVF[0:w + 1, 512 * i:512 * (i + 1)], b[0:w + 1, :]),
               inc=sDV)
        W("tensor", sDV)
        for bi in range(nbr):
            for st in range(4):
                EM("tensor",
                   lambda e, i=bi, s=st, w=avw: e.transpose(
                       psA[:, 512 * i + (w + 2) * s:
                           512 * i + (w + 2) * s + (w + 1)],
                       AEVF[0:w + 1, 512 * i + 128 * s:
                            512 * i + 128 * (s + 1)],
                       IDENTF[0:w + 1, 0:w + 1]), inc=sPE)
        W("vector", sPE)
        for st in range(4):
            for bi, (bp, gcol) in enumerate(br_list):
                base = 512 * bi + (avw + 2) * st
                EM("vector",
                   lambda e, bs=base, w=avw, c=bi: e.reciprocal(
                       RCPC[:, c:c + 1], psA[:, bs + w:bs + w + 1]),
                   inc=sDV)
                W("vector", sDV)
                if gcol is not None:
                    EM("vector",
                       lambda e, c=bi, g=gcol: e.tensor_mul(
                           RCPC[:, c:c + 1], RCPC[:, c:c + 1],
                           CONST[:, g:g + 1]), inc=sDV)
                if bi == 0:
                    EM("vector",
                       lambda e, bs=base, w=avw, s=st, c=bi: e.tensor_scalar(
                           COMBB[:, w * s:w * (s + 1)], psA[:, bs:bs + w],
                           RCPC[:, c:c + 1], None, ALU.mult), inc=sDV)
                else:
                    EM("vector",
                       lambda e, bs=base, w=avw, s=st, c=bi:
                       e.scalar_tensor_tensor(
                           COMBB[:, w * s:w * (s + 1)], psA[:, bs:bs + w],
                           RCPC[:, c:c + 1], COMBB[:, w * s:w * (s + 1)],
                           ALU.mult, ALU.add), inc=sDV)
        W("tensor", sDV)
        for st in range(4):
            EM("tensor",
               lambda e, s=st, w=avw: e.transpose(
                   psT[0:w, 128 * s:128 * (s + 1)],
                   COMBB[:, w * s:w * (s + 1)], IDENT[:, :]), inc=sPE)
        W("vector", sPE)
        EM("vector",
           lambda e, w=avw, d=dst: e.tensor_copy(d[0:w, 0:512],
                                                 psT[0:w, 0:512]),
           inc=sDV)
        W("sync", sDV)

    mA = ar_mark()
    QT = ar("QT", [128, 4 * T], BF16)
    KT = ar("KT", [128, 4 * T], BF16)
    VSB = ar("VSB", [128, NT * 520], BF16)
    mA2 = ar_mark()
    WB1 = ar("WB1", [128, 4096], BF16)
    XNB = ar("XNB", [128, 8192], BF16)

    emit_ln(sink_sb=None)
    TAP(4)
    cc_xn1 = sCC.n + 1
    W("gpsimd", sSY)
    exchange(cin_half, xn_allg)
    W("sync", sCC, cc_xn1)

    def load_xn_chunk(tch, slot, dst=None):
        d = dst if dst is not None else XNB
        hb = tch // 4
        cb = tch % 4
        DMA("sync",
            bass.AP(d, 4096 * slot, [[8192, 128], [512, 8], [1, 512]]),
            bass.AP(xn_allg, (1024 * hb) * TH + 512 * cb,
                    [[TH, 128], [128 * TH, 8], [1, 512]]))

    def load_w_slice(src, width):
        DMA("gpsimd", bass.AP(WB1, 0, [[4096, 128], [width, 8], [1, width]]),
            bass.AP(src, 0, [[width, 128], [128 * width, 8], [1, width]]))

    for which in ("q", "k"):
        src = wq if which == "q" else wk
        OUT = QT if which == "q" else KT
        W("gpsimd", sPE)
        load_w_slice(src, 512)
        W("tensor", sG)
        W("sync", sPE)
        load_xn_chunk(0, 0)
        for tch in range(8):
            if tch < 7:
                W("sync", sPE)
                load_xn_chunk(tch + 1, (tch + 1) % 2)
            W("tensor", sSY, sSY.n - (16 if tch < 7 else 0))
            slot = tch % 2
            for mc in range(4):
                for cc in range(8):
                    EM("tensor",
                       lambda e, m=mc, c=cc, s=slot: e.matmul(
                           psA[:, 512 * m:512 * (m + 1)],
                           WB1[:, 512 * c + 128 * m:512 * c + 128 * (m + 1)],
                           bass.AP(XNB, 4096 * s + 512 * c,
                                   [[8192, 128], [1, 512]]),
                           start=(c == 0), stop=(c == 7)), inc=sPE)
            W("vector", sPE)
            for mc in range(4):
                EM("vector",
                   lambda e, m=mc, t=tch, o=OUT: e.tensor_copy(
                       o[:, 4096 * m + 512 * t:4096 * m + 512 * (t + 1)],
                       psA[:, 512 * m:512 * (m + 1)]), inc=sDV)
            W("tensor", sDV)
    W("gpsimd", sPE)
    load_w_slice(wv, 512)
    W("tensor", sG)
    EM("vector", lambda e: e.memset(VSB[:, :], 1.0), inc=sDV)
    W("tensor", sDV)
    W("sync", sPE)
    load_xn_chunk(0, 0)
    for tch in range(8):
        if tch < 7:
            W("sync", sPE)
            load_xn_chunk(tch + 1, (tch + 1) % 2)
        W("tensor", sSY, sSY.n - (16 if tch < 7 else 0))
        slot = tch % 2
        for tt in range(4):
            tile = 4 * tch + tt
            for cc in range(8):
                EM("tensor",
                   lambda e, s=slot, c=cc, t=tt: e.matmul(
                       ps1[:, :],
                       bass.AP(XNB, 4096 * s + 512 * c + 128 * t,
                               [[8192, 128], [1, 128]]),
                       bass.AP(WB1, 512 * c, [[4096, 128], [1, 512]]),
                       start=(c == 0), stop=(c == 7)), inc=sPE)
            W("vector", sPE)
            EM("vector",
               lambda e, ti=tile: e.tensor_copy(
                   bass.AP(VSB, 520 * ti, [[NT * 520, 128], [65, 8],
                                           [1, 64]]),
                   bass.AP(ps1, 0, [[512, 128], [64, 8], [1, 64]])),
               inc=sDV)
            W("tensor", sDV)

    TAP(5)
    if _TAPS:
        W("sync", sDV)
        DMA("sync", dbg_qt[:, :], QT[:, :])
        DMA("sync", dbg_kt[:, :], KT[:, :])
        DMA("sync", dbg_v[:, :], VSB[:, :])
        DMA("sync", dbg_xn[:, :], xn_allg[:, :])
        DMA("sync", dbg_bias[:, :], bias_dram[:, :])
        W("sync", sSY)

    ar_to(mA2)
    TOEP = ar("TOEP", [128, 6 * 512], F32)
    EXPP = ar("EXPP", [128, 4096], BF16)
    AEVF = ar("AEVF", [128, 512], F32)
    RCPC = ar("RCPC", [128, 8], F32)
    COMBB = ar("COMBB", [128, 256], BF16)
    AVST = ar("AVST", [128, 512], BF16)

    for h in range(HPC):
        mc = h // 2
        roff = 64 * (h % 2)
        W("sync", sDV)
        for mi in range(6):
            c0 = 128 * (mi - 1)
            DMA("sync", TOEP[:, 512 * mi:512 * (mi + 1)],
                bass.AP(bias_dram, 8191 * h + c0 + 3584,
                        [[1, 128], [1, 512]]))
        toep_ready = sSY.n
        for qc in range(8):
            pend = None
            for kb in range(9):
                if kb < 8:
                    W("tensor", sAC)
                    if kb == 0:
                        W("tensor", sDV)
                        if qc == 0:
                            W("tensor", sSY, toep_ready)
                    for i in range(4):
                        kt = 4 * kb + i
                        EM("tensor",
                           lambda e, m=mc, r=roff, k=kt, q=qc, i_=i: e.matmul(
                               psA[:, 512 * i_:512 * (i_ + 1)],
                               KT[r:r + 64, 4096 * m + 128 * k:
                                  4096 * m + 128 * (k + 1)],
                               QT[r:r + 64, 4096 * m + 512 * q:
                                  4096 * m + 512 * (q + 1)],
                               start=True, stop=True), inc=sPE)
                    qk_done = sPE.n
                if pend is not None:
                    pkb = pend
                    pslot = pkb % 2
                    if kb == 8:
                        W("tensor", sAC)
                    for i in range(4):
                        kt = 4 * pkb + i
                        EM("tensor",
                           lambda e, k=kt, hh=h, s=pslot, i_=i,
                           st=(pkb == 0 and i == 0),
                           sp=(pkb == 7 and i == 3): e.matmul(
                               ps1[0:65, :],
                               bass.AP(VSB, 520 * k + 65 * hh,
                                       [[NT * 520, 128], [1, 65]]),
                               EXPP[:, 2048 * s + 512 * i_:
                                    2048 * s + 512 * (i_ + 1)],
                               start=st, stop=sp), inc=sPE)
                if kb == 8:
                    break
                band = [i for i in range(4)
                        if -1 <= (4 * kb + i) - 4 * qc <= 4]
                if band:
                    W("vector", sPE, qk_done)
                    for i in band:
                        mi = (4 * kb + i) - 4 * qc + 1
                        EM("vector",
                           lambda e, i_=i, m=mi: e.tensor_add(
                               psA[:, 512 * i_:512 * (i_ + 1)],
                               psA[:, 512 * i_:512 * (i_ + 1)],
                               bass.AP(TOEP, 512 * m + 511,
                                       [[6 * 512, 128], [-1, 512]])),
                           inc=sDV)
                    W("scalar", sDV)
                else:
                    W("scalar", sPE, qk_done)
                slot = kb % 2
                runs = []
                for i in range(4):
                    mp = (4 * kb + i) - 4 * qc
                    cls = ("band" if -1 <= mp <= 4 else
                           ("left" if mp < -1 else "right"))
                    if runs and runs[-1][0] == cls:
                        runs[-1][2] += 1
                    else:
                        runs = runs + [[cls, i, 1]]
                for cls, i0, ln in runs:
                    if cls == "band":
                        EM("scalar",
                           lambda e, s=slot, i_=i0, l=ln: e.activation(
                               EXPP[:, 2048 * s + 512 * i_:
                                    2048 * s + 512 * (i_ + l)],
                               psA[:, 512 * i_:512 * (i_ + l)], AF.Exp,
                               bias=CONST[:, 63:64], scale=1.0), inc=sAC)
                    else:
                        gcol = 2 * h if cls == "left" else 2 * h + 1
                        EM("scalar",
                           lambda e, s=slot, i_=i0, l=ln, g=gcol:
                           e.activation(
                               EXPP[:, 2048 * s + 512 * i_:
                                    2048 * s + 512 * (i_ + l)],
                               psA[:, 512 * i_:512 * (i_ + l)], AF.Exp,
                               bias=CONST[:, g:g + 1], scale=1.0), inc=sAC)
                pend = kb
            division([(ps1, None)], 64, AEVF, RCPC, COMBB, AVST)
            if _TAPS and h == 0 and qc == 1:
                DMA("sync", dbg_aev[:, :], AEVF[:, :])
                DMA("sync", dbg_rcp[:, :], RCPC[:, :])
                DMA("sync", dbg_comb[:, :], COMBB[:, :])
                W("sync", sSY)
            DMA("sync",
                bass.AP(cin_av, (64 * h) * T + 512 * qc,
                        [[T, 64], [1, 512]]),
                AVST[0:64, 0:512])

    TAP(6)
    if _TAPS:
        W("sync", sDV)
        DMA("sync", dbg_av[:, :], cin_av[:, :])
        W("sync", sSY)

    cc_av = sCC.n + 1
    W("gpsimd", sSY)
    exchange(cin_av, av_allg)
    ar_to(mA2)
    AVRH = ar("AVRH", [128, 4096], BF16)
    WOB = ar("WOB", [128, 8192], BF16)
    DMA("gpsimd", WOB[:, :],
        bass.AP(wo, 0, [[1024, 128], [128 * 1024, 8], [1, 1024]]))
    W("sync", sCC, cc_av)
    W("tensor", sG)

    def outproj(src_allg, WB, ncc, AVBUF):
        for tj in range(4):
            W("sync", sPE)

            def _ld(e, t=tj, sa=src_allg, n_=ncc, ab=AVBUF, ctr=sSY):
                with e.If_cmp(env["tokr"], 0, "IS_EQ"):
                    e.dma_start(
                        bass.AP(ab, 0, [[4096, 128], [512, n_], [1, 512]]),
                        bass.AP(sa, 512 * t,
                                [[T, 128], [128 * T, n_], [1, 512]]),
                    ).then_inc(ctr.h, 16)
                with e.If_cmp(env["tokr"], 0, "IS_NE"):
                    e.dma_start(
                        bass.AP(ab, 0, [[4096, 128], [512, n_], [1, 512]]),
                        bass.AP(sa, 2048 + 512 * t,
                                [[T, 128], [128 * T, n_], [1, 512]]),
                    ).then_inc(ctr.h, 16)

            sSY.n += 16
            E("sync", _ld)
            W("tensor", sSY)
            for dc in range(8):
                for cc in range(ncc):
                    EM("tensor",
                       lambda e, d=dc, c=cc, w=WB, n_=ncc, ab=AVBUF:
                       e.matmul(
                           psA[:, 512 * (d % 4):512 * (d % 4 + 1)],
                           w[:, 1024 * c + 128 * d:1024 * c + 128 * (d + 1)],
                           bass.AP(ab, 512 * c, [[4096, 128], [1, 512]]),
                           start=(c == 0), stop=(c == n_ - 1)), inc=sPE)
                W("vector", sPE)
                EM("vector",
                   lambda e, d=dc, t=tj: e.tensor_add(
                       X[:, 2048 * d + 512 * t:2048 * d + 512 * (t + 1)],
                       X[:, 2048 * d + 512 * t:2048 * d + 512 * (t + 1)],
                       psA[:, 512 * (d % 4):512 * (d % 4 + 1)]), inc=sDV)
                W("tensor", sDV)

    outproj(av_allg, WOB, 8, AVRH)
    ar_to(mA)
    BARRIER()
    TAP(0)

    def emit_ffn(tag):
        mF = ar_mark()
        XN2 = ar("XN2" + tag, [128, 8 * 2048], BF16)
        HB = ar("HB" + tag, [128, NJ * 1024], BF16)
        WDB = ar("WDB" + tag, [128, (NJ // 2) * 1024], BF16)
        WGB = ar("WGB" + tag, [128, 2048], BF16)
        WUB = ar("WUB" + tag, [128, 2048], BF16)
        SILU = ar("SILU" + tag, [128, 512], BF16)

        emit_ln(sink_sb=XN2)

        def load_wd(jh):
            for j4 in range(0, NJ // 2, 4):
                DMA("gpsimd",
                    bass.AP(WDB, 1024 * j4,
                            [[(NJ // 2) * 1024, 128], [1024, 4], [1, 1024]]),
                    bass.AP(wdT, (128 * (j4 + jh * (NJ // 2))) * D,
                            [[D, 128], [128 * D, 4], [1, D]]))

        def load_gu(j, slot):
            DMA("gpsimd",
                bass.AP(WGB, 1024 * slot, [[2048, 128], [128, 8], [1, 128]]),
                bass.AP(wgT, 128 * j, [[FFH, 128], [128 * FFH, 8],
                                       [1, 128]]))
            DMA("gpsimd",
                bass.AP(WUB, 1024 * slot, [[2048, 128], [128, 8], [1, 128]]),
                bass.AP(wuT, 128 * j, [[FFH, 128], [128 * FFH, 8],
                                       [1, 128]]))

        for half in range(2):
            hoff = 1024 * half
            W("gpsimd", sPE)
            load_gu(0, 0)
            for j in range(NJ):
                slot = j % 2
                if j < NJ - 1:
                    W("gpsimd", sPE)
                    load_gu(j + 1, (j + 1) % 2)
                W("tensor", sG, sG.n - (32 if j < NJ - 1 else 0))
                if half == 1 and j == 0:
                    W("vector", sPE)
                for tl in range(2):
                    W("tensor", sAC)
                    W("tensor", sDV)
                    for cc in range(8):
                        EM("tensor",
                           lambda e, c=cc, s=slot, t=tl, ho=hoff: e.matmul(
                               ps1[:, :],
                               bass.AP(WGB, 1024 * s + 128 * c,
                                       [[2048, 128], [1, 128]]),
                               bass.AP(XN2, 2048 * c + ho + 512 * t,
                                       [[8 * 2048, 128], [1, 512]]),
                               start=(c == 0), stop=(c == 7)), inc=sPE)
                        EM("tensor",
                           lambda e, c=cc, s=slot, t=tl, ho=hoff: e.matmul(
                               ps2[:, :],
                               bass.AP(WUB, 1024 * s + 128 * c,
                                       [[2048, 128], [1, 128]]),
                               bass.AP(XN2, 2048 * c + ho + 512 * t,
                                       [[8 * 2048, 128], [1, 512]]),
                               start=(c == 0), stop=(c == 7)), inc=sPE)
                    W("scalar", sPE)
                    EM("scalar",
                       lambda e: e.activation(SILU[:, :], ps1[:, :], AF.Silu,
                                              bias=0.0, scale=INV127),
                       inc=sAC)
                    W("vector", sAC)
                    EM("vector",
                       lambda e, jj=j, t=tl: e.tensor_mul(
                           HB[:, 1024 * jj + 512 * t:
                              1024 * jj + 512 * (t + 1)],
                           SILU[:, :], ps2[:, :]), inc=sDV)
            W("tensor", sDV)
            for jh in range(2):
                W("gpsimd", sPE)
                load_wd(jh)
                W("tensor", sG)
                for tl in range(2):
                    for dc in range(8):
                        bank = ps3 if dc % 2 == 0 else psA
                        for j in range(NJ // 2):
                            EM("tensor",
                               lambda e, d=dc, jj=j, t=tl, b=bank, jh_=jh:
                               e.matmul(
                                   b[:, 0:512],
                                   bass.AP(WDB, 1024 * jj + 128 * d,
                                           [[(NJ // 2) * 1024, 128],
                                            [1, 128]]),
                                   HB[:, 1024 * (jj + jh_ * (NJ // 2))
                                      + 512 * t:
                                      1024 * (jj + jh_ * (NJ // 2))
                                      + 512 * (t + 1)],
                                   start=(jj == 0), stop=(jj == NJ // 2 - 1)),
                               inc=sPE)
                        W("vector", sPE)
                        EM("vector",
                           lambda e, d=dc, t=tl, ho=hoff, b=bank:
                           e.scalar_tensor_tensor(
                               X[:, 2048 * d + ho + 512 * t:
                                 2048 * d + ho + 512 * (t + 1)],
                               b[:, 0:512], INV127 * INV127,
                               X[:, 2048 * d + ho + 512 * t:
                                 2048 * d + ho + 512 * (t + 1)],
                               ALU.mult, ALU.add), inc=sDV)
                        W("tensor", sDV)
        ar_to(mF)
        BARRIER()

    emit_ffn("f1")
    TAP(1)

    mB = ar_mark()
    ROPEC = ar("ROPEC", [128, NT * 64], BF16)
    ROPES = ar("ROPES", [128, NT * 64], BF16)
    NQT = ar("NQT", [96, T], BF16)
    CKVT = ar("CKVT", [32, T], BF16)
    KRN = ar("KRN", [128, NT * 64], BF16)
    KST = ar("KST", [96, HPC * 256], BF16)
    VSA = ar("VSA", [128, 2 * 264], BF16)
    KSN = ar("KSN", [128, 2 * 768], BF16)
    WDQN = ar("WDQN", [96, 256], BF16)
    WDQR = ar("WDQR", [96, 512], BF16)
    WDKN = ar("WDKN", [32, 256], BF16)
    WDVB = ar("WDVB", [32, 256], BF16)
    GQC = ar("GQC", [96, 1], F32)
    GKVC = ar("GKVC", [32, 1], F32)
    T1B = ar("T1B", [128, 512], BF16)
    T2B = ar("T2B", [128, 512], BF16)
    KR1 = ar("KR1", [128, 64], BF16)
    KR2 = ar("KR2", [128, 128], BF16)
    NATF = ar("NATF", [128, 96], F32)
    SQF = ar("SQF", [128, 96], F32)
    mB2 = ar_mark()
    XNB3 = ar("XNB3", [128, 4096], BF16)
    WCQ = ar("WCQ", [128, 768], BF16)
    WCKV = ar("WCKV", [128, 256], BF16)
    WKRO = ar("WKRO", [128, 512], BF16)
    WWK = ar("WWK", [128, 8 * 768], BF16)
    WWV = ar("WWV", [128, 8 * 256], BF16)
    WIMP = ar("WIMP", [128, 8], BF16)
    WGT3 = ar("WGT3", [128, 24], BF16)
    KWN = ar("KWN", [128, 768], BF16)
    KWSTG = ar("KWSTG", [96, 1024], BF16)
    VST = ar("VST", [128, 264], BF16)
    XMF = ar("XMF", [128, 16], F32)
    XMB = ar("XMB", [128, 8], BF16)
    GROW = ar("GROW", [1, 16], F32)
    SSC = ar("SSC", [128, 32], F32)
    JUNK = ar("JUNK", [128, 32], F32)
    CUMA = ar("CUMA", [128, 32], F32)
    CUMC = ar("CUMC", [128, 32], F32)
    JUNKB = ar("JUNKB", [128, 32], BF16)
    CP1 = ar("CP1", [1, 32], F32)
    CP2 = ar("CP2", [1, 32], F32)
    PTS = ar("PTS", [128, NT * 256], BF16)
    KFT = ar("KFT", [128, 768], BF16)
    WSK = ar("WSK", [128, 8 * 768], BF16)

    emit_ln(sink_sb=None)
    cc_xn3 = sCC.n + 1
    W("gpsimd", sSY)
    exchange(cin_half, xn_allg)
    DMA("sync", ROPEC[:, :], s_ropec[:, :])
    DMA("sync", ROPES[:, :], s_ropes[:, :])
    DMA("gpsimd", WCQ[:, :],
        bass.AP(wcq, 0, [[96, 128], [128 * 96, 8], [1, 96]]))
    DMA("gpsimd", WCKV[:, :],
        bass.AP(wckv, 0, [[32, 128], [128 * 32, 8], [1, 32]]))
    DMA("gpsimd", WKRO[:, :],
        bass.AP(wkro, 0, [[64, 128], [128 * 64, 8], [1, 64]]))
    DMA("gpsimd", WDQN[:, :], wdqn[:, :])
    DMA("gpsimd", WDQR[:, :], wdqr[:, :])
    DMA("gpsimd", WDKN[:, :], wdkn[:, :])
    DMA("gpsimd", WDVB[:, :], wdv[:, :])
    DMA("gpsimd", WWK[:, :],
        bass.AP(wwink, 0, [[768, 128], [128 * 768, 8], [1, 768]]))
    DMA("gpsimd", WWV[:, :],
        bass.AP(wwinv, 0, [[256, 128], [128 * 256, 8], [1, 256]]))
    DMA("gpsimd", WIMP[:, :],
        bass.AP(wimp, 0, [[1, 128], [128, 8]]), allow_slow=True)
    DMA("gpsimd", WGT3[:, :],
        bass.AP(wgate, 0, [[3, 128], [128 * 3, 8], [1, 3]]))
    DMA("gpsimd", WSK[:, :],
        bass.AP(wselk, 0, [[768, 128], [128 * 768, 8], [1, 768]]))
    DMA("sync", GQC[:, :], gq[:, :])
    DMA("sync", GKVC[:, :], gkv[:, :])
    W("vector", sG)
    W("vector", sSY)
    EM("vector", lambda e: e.tensor_scalar(
        WDQN[:, :], WDQN[:, :], GQC[0:96, 0:1], None, ALU.mult), inc=sDV)
    EM("vector", lambda e: e.tensor_scalar(
        WDQR[:, :], WDQR[:, :], GQC[0:96, 0:1], None, ALU.mult), inc=sDV)
    EM("vector", lambda e: e.tensor_scalar(
        WDKN[:, :], WDKN[:, :], GKVC[0:32, 0:1], None, ALU.mult), inc=sDV)
    EM("vector", lambda e: e.tensor_scalar(
        WDVB[:, :], WDVB[:, :], GKVC[0:32, 0:1], None, ALU.mult), inc=sDV)
    EM("vector", lambda e: e.memset(XMF[:, 0:8], 0.0), inc=sDV)
    W("sync", sCC, cc_xn3)

    def load_xn_tile(tt, slot):
        hb = tt // 16
        ct = tt % 16
        DMA("sync",
            bass.AP(XNB3, 2048 * slot, [[4096, 128], [128, 8], [1, 128]]),
            bass.AP(xn_allg, (1024 * hb) * TH + 128 * ct,
                    [[TH, 128], [128 * TH, 8], [1, 128]]))

    def xnb3(slot, cc):
        return bass.AP(XNB3, 2048 * slot + 128 * cc, [[4096, 128], [1, 128]])

    W("sync", sPE)
    load_xn_tile(0, 0)
    for tt in range(NT):
        slot = tt % 2
        if tt < NT - 1:
            W("sync", sPE)
            load_xn_tile(tt + 1, (tt + 1) % 2)
        W("tensor", sSY, sSY.n - (16 if tt < NT - 1 else 0))
        W("tensor", sDV)
        for cc in range(8):
            EM("tensor", lambda e, c=cc, s=slot: e.matmul(
                ps1[:, 0:96], xnb3(s, c),
                bass.AP(WCQ, 96 * c, [[768, 128], [1, 96]]),
                start=(c == 0), stop=(c == 7)), inc=sPE)
            EM("tensor", lambda e, c=cc, s=slot: e.matmul(
                ps2[:, 0:32], xnb3(s, c),
                bass.AP(WCKV, 32 * c, [[256, 128], [1, 32]]),
                start=(c == 0), stop=(c == 7)), inc=sPE)
            EM("tensor", lambda e, c=cc, s=slot: e.matmul(
                psA[:, 1536:1600], xnb3(s, c),
                bass.AP(WKRO, 64 * c, [[512, 128], [1, 64]]),
                start=(c == 0), stop=(c == 7)), inc=sPE)
            EM("tensor", lambda e, c=cc, s=slot: e.matmul(
                psA[:, 0:512], xnb3(s, c),
                bass.AP(WWK, 768 * c, [[8 * 768, 128], [1, 512]]),
                start=(c == 0), stop=(c == 7)), inc=sPE)
            EM("tensor", lambda e, c=cc, s=slot: e.matmul(
                psA[:, 512:768], xnb3(s, c),
                bass.AP(WWK, 768 * c + 512, [[8 * 768, 128], [1, 256]]),
                start=(c == 0), stop=(c == 7)), inc=sPE)
            EM("tensor", lambda e, c=cc, s=slot: e.matmul(
                psA[:, 1024:1280], xnb3(s, c),
                bass.AP(WWV, 256 * c, [[8 * 256, 128], [1, 256]]),
                start=(c == 0), stop=(c == 7)), inc=sPE)
            EM("tensor", lambda e, c=cc, s=slot, t=tt: e.matmul(
                ps3[:, t:t + 1], xnb3(s, c), WIMP[:, c:c + 1],
                start=(c == 0), stop=(c == 7)), inc=sPE)
        W("vector", sPE)
        for cc in range(8):
            EM("vector", lambda e, c=cc, s=slot: e.reduce_sum(
                XMF[:, 8:9], xnb3(s, c), axis=mybir.AxisListType.X), inc=sDV)
            W("vector", sDV)
            EM("vector", lambda e, c=cc: e.tensor_add(
                XMF[:, c:c + 1], XMF[:, c:c + 1], XMF[:, 8:9]), inc=sDV)
        EM("vector", lambda e: e.tensor_copy(NATF[:, 0:96], ps1[:, 0:96]),
           inc=sDV)
        EM("vector", lambda e: e.tensor_mul(SQF[:, 0:96], NATF[:, 0:96],
                                            NATF[:, 0:96]), inc=sDV)
        EM("vector", lambda e: e.reduce_sum(
            MISC[:, 0:1], SQF[:, 0:96], axis=mybir.AxisListType.X), inc=sDV)
        W("vector", sDV)
        EM("vector", lambda e: e.tensor_scalar(
            MISC[:, 1:2], MISC[:, 0:1], 1.0 / 96, 1e-6, ALU.mult,
            op1=ALU.add), inc=sDV)
        W("scalar", sDV)
        EM("scalar", lambda e: e.activation(
            MISC[:, 2:3], MISC[:, 1:2], AF.Ln, bias=0.0, scale=1.0),
           inc=sAC)
        EM("scalar", lambda e: e.activation(
            MISC[:, 3:4], MISC[:, 2:3], AF.Exp, bias=0.0, scale=-0.5),
           inc=sAC)
        W("vector", sAC)
        EM("vector", lambda e: e.tensor_scalar(
            T1B[:, 32:128], NATF[:, 0:96], MISC[:, 3:4], None, ALU.mult),
           inc=sDV)
        EM("vector", lambda e: e.tensor_copy(NATF[:, 0:32], ps2[:, 0:32]),
           inc=sDV)
        EM("vector", lambda e: e.tensor_mul(SQF[:, 0:32], NATF[:, 0:32],
                                            NATF[:, 0:32]), inc=sDV)
        EM("vector", lambda e: e.reduce_sum(
            MISC[:, 4:5], SQF[:, 0:32], axis=mybir.AxisListType.X), inc=sDV)
        W("vector", sDV)
        EM("vector", lambda e: e.tensor_scalar(
            MISC[:, 5:6], MISC[:, 4:5], 1.0 / 32, 1e-6, ALU.mult,
            op1=ALU.add), inc=sDV)
        W("scalar", sDV)
        EM("scalar", lambda e: e.activation(
            MISC[:, 6:7], MISC[:, 5:6], AF.Ln, bias=0.0, scale=1.0),
           inc=sAC)
        EM("scalar", lambda e: e.activation(
            MISC[:, 7:8], MISC[:, 6:7], AF.Exp, bias=0.0, scale=-0.5),
           inc=sAC)
        W("vector", sAC)
        EM("vector", lambda e: e.tensor_scalar(
            T1B[:, 0:32], NATF[:, 0:32], MISC[:, 7:8], None, ALU.mult),
           inc=sDV)
        W("tensor", sDV)
        EM("tensor", lambda e: e.transpose(
            psT[0:96, 0:128], T1B[:, 32:128], IDENT[:, :]), inc=sPE)
        EM("tensor", lambda e: e.transpose(
            psT[0:32, 128:256], T1B[:, 0:32], IDENT[:, :]), inc=sPE)
        W("vector", sPE)
        EM("vector", lambda e, t=tt: e.tensor_copy(
            NQT[0:96, 128 * t:128 * (t + 1)], psT[0:96, 0:128]), inc=sDV)
        EM("vector", lambda e, t=tt: e.tensor_copy(
            CKVT[0:32, 128 * t:128 * (t + 1)], psT[0:32, 128:256]), inc=sDV)
        EM("vector", lambda e: e.tensor_scalar(
            KR1[:, :], psA[:, 1536:1600], 1.0 / 16.0, None, ALU.mult),
           inc=sDV)
        EM("vector", lambda e: e.tensor_copy(KWN[:, :], psA[:, 0:768]),
           inc=sDV)
        EM("vector", lambda e, t=tt: e.tensor_mul(
            KR2[:, 0:64], KR1[:, :],
            bass.AP(ROPEC, 64 * t, [[NT * 64, 128], [1, 64]])), inc=sDV)
        EM("vector", lambda e, t=tt: e.tensor_mul(
            KR2[:, 64:128],
            bass.AP(KR1, 32, [[64, 128], [-32, 2], [1, 32]]),
            bass.AP(ROPES, 64 * t, [[NT * 64, 128], [1, 64]])), inc=sDV)
        EM("vector", lambda e, t=tt: e.tensor_mul(
            bass.AP(T1B, 0, [[512, 128], [64, 8], [1, 64]]),
            bass.AP(KWN, 32, [[768, 128], [96, 8], [1, 64]]),
            bass.AP(ROPEC, 64 * t, [[NT * 64, 128], [0, 8], [1, 64]])),
           inc=sDV)
        EM("vector", lambda e, t=tt: e.tensor_add(
            KRN[:, 64 * t:64 * (t + 1)], KR2[:, 0:64], KR2[:, 64:128]),
           inc=sDV)
        EM("vector", lambda e, t=tt: e.tensor_mul(
            bass.AP(T2B, 0, [[512, 128], [64, 8], [1, 64]]),
            bass.AP(KWN, 64, [[768, 128], [96, 8], [-32, 2], [1, 32]]),
            bass.AP(ROPES, 64 * t, [[NT * 64, 128], [0, 8], [1, 64]])),
           inc=sDV)
        W("vector", sDV)
        EM("vector", lambda e: e.tensor_add(
            bass.AP(KWN, 32, [[768, 128], [96, 8], [1, 64]]),
            bass.AP(T1B, 0, [[512, 128], [64, 8], [1, 64]]),
            bass.AP(T2B, 0, [[512, 128], [64, 8], [1, 64]])), inc=sDV)
        W("tensor", sDV)
        W("vector", sSY)
        for hh in range(HPC):
            EM("tensor", lambda e, h_=hh: e.transpose(
                psT[0:96, 0:128], KWN[:, 96 * h_:96 * (h_ + 1)],
                IDENT[:, :]), inc=sPE)
            W("vector", sPE)
            EM("vector", lambda e, h_=hh: e.tensor_copy(
                KWSTG[0:96, 128 * h_:128 * (h_ + 1)], psT[0:96, 0:128]),
               inc=sDV)
            W("tensor", sDV)
        W("sync", sDV)
        DMA("sync",
            bass.AP(kwT_scr, 128 * tt, [[T, 96], [96 * T, 8], [1, 128]]),
            KWSTG[0:96, :])
        W("vector", sSY)
        EM("vector", lambda e: e.memset(VST[:, :], 1.0), inc=sDV)
        W("vector", sDV)
        EM("vector", lambda e: e.tensor_copy(
            bass.AP(VST, 0, [[264, 128], [33, 8], [1, 32]]),
            bass.AP(psA, 1024, [[2048, 128], [32, 8], [1, 32]])), inc=sDV)
        W("sync", sDV)
        DMA("sync",
            bass.AP(vw_scr, (128 * tt) * 33,
                    [[33, 128], [T * 33, 8], [1, 33]]),
            VST[:, :])
        W("tensor", sSY, sSY.n - 32)

    for tt in range(NT):
        EM("tensor", lambda e, t=tt: e.matmul(
            ps1[:, 0:256], CKVT[0:32, 128 * t:128 * (t + 1)], WDVB[0:32, :],
            start=True, stop=True), inc=sPE)
        W("vector", sPE)
        W("vector", sSY)
        EM("vector", lambda e: e.memset(VST[:, :], 1.0), inc=sDV)
        W("vector", sDV)
        EM("vector", lambda e: e.tensor_copy(
            bass.AP(VST, 0, [[264, 128], [33, 8], [1, 32]]),
            bass.AP(ps1, 0, [[512, 128], [32, 8], [1, 32]])), inc=sDV)
        W("sync", sDV)
        DMA("sync",
            bass.AP(v1_scr, (128 * tt) * 33,
                    [[33, 128], [T * 33, 8], [1, 33]]),
            VST[:, :])
        W("tensor", sSY)

    EM("vector", lambda e: e.tensor_scalar(
        XMB[:, 0:8], XMF[:, 0:8], 1.0 / T, None, ALU.mult), inc=sDV)
    W("tensor", sDV)
    for cc in range(8):
        EM("tensor", lambda e, c=cc: e.matmul(
            ps2[0:1, 0:3], XMB[:, c:c + 1], WGT3[:, 3 * c:3 * (c + 1)],
            start=(c == 0), stop=(c == 7)), inc=sPE)
    W("vector", sPE)
    EM("vector", lambda e: e.tensor_copy(GROW[0:1, 0:3], ps2[0:1, 0:3]),
       inc=sDV)
    W("scalar", sDV)
    EM("scalar", lambda e: e.activation(
        GROW[0:1, 3:6], GROW[0:1, 0:3], AF.Exp, bias=0.0, scale=1.0),
       inc=sAC)
    W("vector", sAC)
    EM("vector", lambda e: e.reduce_sum(
        GROW[0:1, 6:7], GROW[0:1, 3:6], axis=mybir.AxisListType.X), inc=sDV)
    W("vector", sDV)
    EM("vector", lambda e: e.reciprocal(GROW[0:1, 7:8], GROW[0:1, 6:7]),
       inc=sDV)
    W("vector", sDV)
    EM("vector", lambda e: e.tensor_scalar(
        GROW[0:1, 3:6], GROW[0:1, 3:6], GROW[0:1, 7:8], None, ALU.mult),
       inc=sDV)
    W("tensor", sDV)
    EM("tensor", lambda e: e.matmul(
        ps2[:, 8:11], ONESF[0:1, :], GROW[0:1, 3:6], start=True, stop=True),
       inc=sPE)
    W("vector", sPE)
    EM("vector", lambda e: e.tensor_copy(CONST[:, 32:35], ps2[:, 8:11]),
       inc=sDV)

    EM("vector", lambda e: e.tensor_copy(SSC[:, :], ps3[:, 0:32]), inc=sDV)
    EM("vector", lambda e: e.memset(MISC[:, 0:1], -1000.0), inc=sDV)
    EM("vector", lambda e: e.memset(MISC[:, 1:2], 1000.0), inc=sDV)
    EM("vector", lambda e: e.memset(MISC[:, 2:3], 0.0), inc=sDV)
    for it in range(NBIS):
        W("vector", sPE)
        EM("vector", lambda e: e.tensor_scalar(
            JUNK[:, :], SSC[:, :], MISC[:, 2:3], 0.0, ALU.is_gt,
            op1=ALU.add, accum_out=MISC[:, 6:7]), inc=sDV)
        W("tensor", sDV)
        EM("tensor", lambda e: e.matmul(
            ps2[0:1, 16:17], ONESF[:, 0:1], MISC[:, 6:7],
            start=True, stop=True), inc=sPE)
        W("vector", sPE)
        EM("vector", lambda e: e.tensor_copy(GROW[0:1, 8:9],
                                             ps2[0:1, 16:17]), inc=sDV)
        W("tensor", sDV)
        EM("tensor", lambda e: e.matmul(
            ps2[:, 17:18], ONESF[0:1, :], GROW[0:1, 8:9],
            start=True, stop=True), inc=sPE)
        W("vector", sPE)
        EM("vector", lambda e: e.tensor_scalar(
            MISC[:, 3:4], ps2[:, 17:18], float(KEEP) - 0.5, None, ALU.is_gt),
           inc=sDV)
        EM("vector", lambda e: e.tensor_sub(
            MISC[:, 4:5], MISC[:, 2:3], MISC[:, 0:1]), inc=sDV)
        EM("vector", lambda e: e.tensor_sub(
            MISC[:, 8:9], MISC[:, 1:2], MISC[:, 2:3]), inc=sDV)
        EM("vector", lambda e: e.tensor_scalar(
            MISC[:, 5:6], MISC[:, 3:4], -1.0, 1.0, ALU.mult, op1=ALU.add),
           inc=sDV)
        W("vector", sDV)
        EM("vector", lambda e: e.tensor_mul(
            MISC[:, 4:5], MISC[:, 4:5], MISC[:, 3:4]), inc=sDV)
        EM("vector", lambda e: e.tensor_mul(
            MISC[:, 8:9], MISC[:, 8:9], MISC[:, 5:6]), inc=sDV)
        W("vector", sDV)
        EM("vector", lambda e: e.tensor_add(
            MISC[:, 0:1], MISC[:, 0:1], MISC[:, 4:5]), inc=sDV)
        EM("vector", lambda e: e.tensor_sub(
            MISC[:, 1:2], MISC[:, 1:2], MISC[:, 8:9]), inc=sDV)
        W("vector", sDV)
        EM("vector", lambda e: e.tensor_add(
            MISC[:, 2:3], MISC[:, 0:1], MISC[:, 1:2]), inc=sDV)
        W("vector", sDV)
        EM("vector", lambda e: e.tensor_scalar(
            MISC[:, 2:3], MISC[:, 2:3], 0.5, None, ALU.mult), inc=sDV)
        W("vector", sDV)
    W("vector", sDV)
    EM("vector", lambda e: e.tensor_scalar(
        JUNK[:, :], SSC[:, :], MISC[:, 0:1], None, ALU.is_gt), inc=sDV)
    W("vector", sDV)
    EM("vector", lambda e: e.tensor_copy(JUNKB[:, :], JUNK[:, :]), inc=sDV)
    W("tensor", sDV)
    EM("tensor", lambda e: e.matmul(
        ps1[:, 0:32], TRILS[:, :], JUNKB[:, :], start=True, stop=False),
       inc=sPE)
    EM("tensor", lambda e: e.matmul(
        ps1[:, 0:32], IDENT[:, :], JUNKB[:, :], start=False, stop=True),
       inc=sPE)
    EM("tensor", lambda e: e.matmul(
        ps2[0:1, 0:32], ONESB[:, 0:1], JUNKB[:, :], start=True, stop=True),
       inc=sPE)
    W("vector", sPE)
    EM("vector", lambda e, A=CUMA: e.tensor_copy(A[:, :], ps1[:, 0:32]),
       inc=sDV)
    EM("vector", lambda e: e.memset(CP1[0:1, 0:1], 0.0), inc=sDV)
    EM("vector", lambda e: e.tensor_copy(CP1[0:1, 1:32], ps2[0:1, 0:31]),
       inc=sDV)
    cpa, cpc = CP1, CP2
    for k in (1, 2, 4, 8, 16):
        W("vector", sDV)
        EM("vector", lambda e, kk=k, A=cpa, C=cpc: e.tensor_copy(
            C[0:1, 0:kk], A[0:1, 0:kk]), inc=sDV)
        EM("vector", lambda e, kk=k, A=cpa, C=cpc: e.tensor_add(
            C[0:1, kk:32], A[0:1, kk:32], A[0:1, 0:32 - kk]), inc=sDV)
        cpa, cpc = cpc, cpa
    W("tensor", sDV)
    EM("tensor", lambda e, A=cpa: e.matmul(
        ps3[:, 0:32], ONESF[0:1, :], A[0:1, :], start=True, stop=True),
       inc=sPE)
    W("vector", sPE)
    EM("vector", lambda e, A=CUMA, C=CUMC: e.tensor_add(
        C[:, :], A[:, :], ps3[:, 0:32]), inc=sDV)
    W("vector", sDV)
    EM("vector", lambda e, C=CUMC: e.tensor_mul(
        C[:, :], C[:, :], JUNK[:, :]), inc=sDV)
    W("vector", sDV)
    for tt in range(NT):
        EM("vector", lambda e, t=tt, C=CUMC: e.tensor_scalar(
            PTS[:, 256 * t:256 * (t + 1)], IOTA[:, :], C[:, t:t + 1],
            None, ALU.is_equal), inc=sDV)
    W("tensor", sDV)
    if _TAPS:
        W("sync", sDV)
        DMA("sync", dbg_pts[:, :], PTS[:, :])
        DMA("sync", dbg_junk[:, :], JUNK[:, :])
        DMA("sync", dbg_cum[:, :], CUMA[:, :])
        DMA("sync", dbg_cumf[:, :], CUMC[:, :])
        DMA("sync", dbg_ssc[:, :], SSC[:, :])

    W("sync", sPE)
    load_xn_tile(0, 0)
    for tt in range(NT):
        slot = tt % 2
        if tt < NT - 1:
            W("sync", sPE)
            load_xn_tile(tt + 1, (tt + 1) % 2)
        W("tensor", sSY, sSY.n - (16 if tt < NT - 1 else 0))
        for cc in range(8):
            EM("tensor", lambda e, c=cc, s=slot: e.matmul(
                psA[:, 0:512], xnb3(s, c),
                bass.AP(WSK, 768 * c, [[8 * 768, 128], [1, 512]]),
                start=(c == 0), stop=(c == 7)), inc=sPE)
            EM("tensor", lambda e, c=cc, s=slot: e.matmul(
                psA[:, 512:768], xnb3(s, c),
                bass.AP(WSK, 768 * c + 512, [[8 * 768, 128], [1, 256]]),
                start=(c == 0), stop=(c == 7)), inc=sPE)
        W("vector", sPE)
        EM("vector", lambda e: e.tensor_copy(KFT[:, :], psA[:, 0:768]),
           inc=sDV)
        W("tensor", sDV)
        for r in range(2):
            EM("tensor", lambda e, t=tt, rr=r: e.matmul(
                ps1[:, :] if rr == 0 else ps2[:, :],
                PTS[:, 256 * t + 128 * rr:256 * t + 128 * (rr + 1)],
                KFT[:, 0:512], start=(t == 0), stop=(t == NT - 1)), inc=sPE)
            EM("tensor", lambda e, t=tt, rr=r: e.matmul(
                ps3[:, 0:256] if rr == 0 else psA[:, 1536:1792],
                PTS[:, 256 * t + 128 * rr:256 * t + 128 * (rr + 1)],
                KFT[:, 512:768], start=(t == 0), stop=(t == NT - 1)),
               inc=sPE)
    W("vector", sPE)
    for r in range(2):
        EM("vector", lambda e, rr=r: e.tensor_copy(
            KSN[:, 768 * rr:768 * rr + 512],
            ps1[:, :] if rr == 0 else ps2[:, :]), inc=sDV)
        EM("vector", lambda e, rr=r: e.tensor_copy(
            KSN[:, 768 * rr + 512:768 * (rr + 1)],
            ps3[:, 0:256] if rr == 0 else psA[:, 1536:1792]), inc=sDV)
    for r in range(2):
        EM("vector", lambda e, rr=r: e.tensor_mul(
            bass.AP(T1B, 0, [[512, 128], [64, 8], [1, 64]]),
            bass.AP(KSN, 768 * rr + 32, [[2 * 768, 128], [96, 8], [1, 64]]),
            bass.AP(ROPEC, 64 * rr, [[NT * 64, 128], [0, 8], [1, 64]])),
           inc=sDV)
        EM("vector", lambda e, rr=r: e.tensor_mul(
            bass.AP(T2B, 0, [[512, 128], [64, 8], [1, 64]]),
            bass.AP(KSN, 768 * rr + 64, [[2 * 768, 128], [96, 8],
                                         [-32, 2], [1, 32]]),
            bass.AP(ROPES, 64 * rr, [[NT * 64, 128], [0, 8], [1, 64]])),
           inc=sDV)
        W("vector", sDV)
        EM("vector", lambda e, rr=r: e.tensor_add(
            bass.AP(KSN, 768 * rr + 32, [[2 * 768, 128], [96, 8], [1, 64]]),
            bass.AP(T1B, 0, [[512, 128], [64, 8], [1, 64]]),
            bass.AP(T2B, 0, [[512, 128], [64, 8], [1, 64]])), inc=sDV)
    W("tensor", sDV)
    for hh in range(HPC):
        for r in range(2):
            EM("tensor", lambda e, h_=hh, rr=r: e.transpose(
                psT[0:96, 0:128],
                KSN[:, 768 * rr + 96 * h_:768 * rr + 96 * (h_ + 1)],
                IDENT[:, :]), inc=sPE)
            W("vector", sPE)
            EM("vector", lambda e, h_=hh, rr=r: e.tensor_copy(
                KST[0:96, 256 * h_ + 128 * rr:256 * h_ + 128 * (rr + 1)],
                psT[0:96, 0:128]), inc=sDV)
            W("tensor", sDV)

    W("gpsimd", sPE)
    DMA("gpsimd", WWV[:, :],
        bass.AP(wselv, 0, [[256, 128], [128 * 256, 8], [1, 256]]))
    W("tensor", sG)
    W("sync", sPE)
    load_xn_tile(0, 0)
    for tt in range(NT):
        slot = tt % 2
        if tt < NT - 1:
            W("sync", sPE)
            load_xn_tile(tt + 1, (tt + 1) % 2)
        W("tensor", sSY, sSY.n - (16 if tt < NT - 1 else 0))
        for cc in range(8):
            EM("tensor", lambda e, c=cc, s=slot: e.matmul(
                psA[:, 0:256], xnb3(s, c),
                bass.AP(WWV, 256 * c, [[8 * 256, 128], [1, 256]]),
                start=(c == 0), stop=(c == 7)), inc=sPE)
        W("vector", sPE)
        EM("vector", lambda e: e.tensor_copy(KFT[:, 0:256], psA[:, 0:256]),
           inc=sDV)
        W("tensor", sDV)
        for r in range(2):
            EM("tensor", lambda e, t=tt, rr=r: e.matmul(
                ps1[:, 0:256] if rr == 0 else ps2[:, 0:256],
                PTS[:, 256 * t + 128 * rr:256 * t + 128 * (rr + 1)],
                KFT[:, 0:256], start=(t == 0), stop=(t == NT - 1)), inc=sPE)
    W("vector", sPE)
    EM("vector", lambda e: e.memset(VSA[:, :], 1.0), inc=sDV)
    W("vector", sDV)
    for r in range(2):
        EM("vector", lambda e, rr=r: e.tensor_copy(
            bass.AP(VSA, 264 * rr, [[2 * 264, 128], [33, 8], [1, 32]]),
            bass.AP(ps1 if rr == 0 else ps2, 0,
                    [[512, 128], [32, 8], [1, 32]])), inc=sDV)
    W("tensor", sDV)

    ar_to(mB2)
    MASKS = ar("MASKS", [128, 2048], BF16)
    QNH = ar("QNH", [128, NT * 96], BF16)
    K1NH = ar("K1NH", [128, NT * 96], BF16)
    QTH = ar("QTH", [96, T], BF16)
    K1TH = ar("K1TH", [96, T], BF16)
    KWTH = ar("KWTH", [96, T], BF16)
    V1A = ar("V1A", [128, NT * 33], BF16)
    VWA = ar("VWA", [128, NT * 33], BF16)
    EXPB = ar("EXPB", [128, 4096], BF16)
    AEVF3 = ar("AEVF3", [128, 3 * 512], F32)
    RCP3 = ar("RCP3", [128, 8], F32)
    COMB3 = ar("COMB3", [128, 256], BF16)
    AVST3 = ar("AVST3", [128, 512], BF16)
    RB1 = ar("RB1", [128, 512], BF16)
    RB2 = ar("RB2", [128, 512], BF16)
    DMA("sync", MASKS[:, :], s_masks[:, :])
    W("vector", sSY)

    for h in range(HPC):
        W("sync", sPE)
        W("tensor", sDV)
        DMA("sync", KWTH[0:96, :],
            bass.AP(kwT_scr, (96 * h) * T, [[T, 96], [1, T]]))
        DMA("sync", V1A[:, :],
            bass.AP(v1_scr, (T * h) * 33,
                    [[33, 128], [128 * 33, NT], [1, 33]]))
        DMA("sync", VWA[:, :],
            bass.AP(vw_scr, (T * h) * 33,
                    [[33, 128], [128 * 33, NT], [1, 33]]))
        kw_ready = sSY.n
        for tt in range(NT):
            EM("tensor", lambda e, t=tt, h_=h: e.matmul(
                ps1[:, 0:32], NQT[0:96, 128 * t:128 * (t + 1)],
                WDQN[0:96, 32 * h_:32 * (h_ + 1)], start=True, stop=True),
               inc=sPE)
            EM("tensor", lambda e, t=tt, h_=h: e.matmul(
                ps1[:, 32:96], NQT[0:96, 128 * t:128 * (t + 1)],
                WDQR[0:96, 64 * h_:64 * (h_ + 1)], start=True, stop=True),
               inc=sPE)
            EM("tensor", lambda e, t=tt, h_=h: e.matmul(
                ps2[:, 0:32], CKVT[0:32, 128 * t:128 * (t + 1)],
                WDKN[0:32, 32 * h_:32 * (h_ + 1)], start=True, stop=True),
               inc=sPE)
            W("vector", sPE)
            EM("vector", lambda e: e.tensor_copy(RB1[:, 0:96], ps1[:, 0:96]),
               inc=sDV)
            EM("vector", lambda e, t=tt: e.tensor_copy(
                K1NH[:, 96 * t:96 * t + 32], ps2[:, 0:32]), inc=sDV)
            EM("vector", lambda e, t=tt: e.tensor_mul(
                T1B[:, 0:64], RB1[:, 32:96],
                bass.AP(ROPEC, 64 * t, [[NT * 64, 128], [1, 64]])), inc=sDV)
            EM("vector", lambda e, t=tt: e.tensor_copy(
                K1NH[:, 96 * t + 32:96 * t + 96],
                KRN[:, 64 * t:64 * (t + 1)]), inc=sDV)
            EM("vector", lambda e, t=tt: e.tensor_mul(
                T2B[:, 0:64],
                bass.AP(RB1, 64, [[512, 128], [-32, 2], [1, 32]]),
                bass.AP(ROPES, 64 * t, [[NT * 64, 128], [1, 64]])), inc=sDV)
            EM("vector", lambda e, t=tt: e.tensor_copy(
                QNH[:, 96 * t:96 * t + 32], RB1[:, 0:32]), inc=sDV)
            EM("vector", lambda e, t=tt: e.tensor_add(
                QNH[:, 96 * t + 32:96 * t + 96], T1B[:, 0:64],
                T2B[:, 0:64]), inc=sDV)
            W("tensor", sDV)
            EM("tensor", lambda e, t=tt: e.transpose(
                psT[0:96, 0:128], QNH[:, 96 * t:96 * (t + 1)], IDENT[:, :]),
               inc=sPE)
            EM("tensor", lambda e, t=tt: e.transpose(
                psT[0:96, 128:256], K1NH[:, 96 * t:96 * (t + 1)],
                IDENT[:, :]), inc=sPE)
            W("vector", sPE)
            EM("vector", lambda e, t=tt: e.tensor_copy(
                QTH[0:96, 128 * t:128 * (t + 1)], psT[0:96, 0:128]),
               inc=sDV)
            EM("vector", lambda e, t=tt: e.tensor_copy(
                K1TH[0:96, 128 * t:128 * (t + 1)], psT[0:96, 128:256]),
               inc=sDV)
            W("tensor", sDV)
        W("tensor", sSY, kw_ready)

        for qc in range(8):
            nkb = qc + 1
            bursts = []
            for br, (KTH_, VA_, bank) in enumerate(
                    ((K1TH, V1A, ps1), (KWTH, VWA, ps3))):
                for kb in range(nkb):
                    bursts.append((KTH_, VA_, bank, kb,
                                   kb == 0, kb == nkb - 1))
            pend = None
            for bi in range(len(bursts) + 1):
                if bi < len(bursts):
                    KTH_, VA_, bank, kb, isfirst, islast = bursts[bi]
                    slot = bi % 2
                    W("tensor", sAC)
                    if bi == 0:
                        W("tensor", sDV)
                    for i in range(4):
                        kt = 4 * kb + i
                        EM("tensor", lambda e, kk=KTH_, kt_=kt, q=qc, i_=i:
                           e.matmul(
                               psA[:, 512 * i_:512 * (i_ + 1)],
                               kk[0:96, 128 * kt_:128 * (kt_ + 1)],
                               QTH[0:96, 512 * q:512 * (q + 1)],
                               start=True, stop=True), inc=sPE)
                    qk_done = sPE.n
                if pend is not None:
                    (pV, pbank, pkb, pslot, pfirst, plast) = pend
                    if plast:
                        W("tensor", sDV)
                    for i in range(4):
                        kt = 4 * pkb + i
                        EM("tensor", lambda e, v=pV, kt_=kt, s=pslot, i_=i,
                           b=pbank, st=(pfirst and i == 0),
                           sp=(plast and i == 3): e.matmul(
                               b[0:33, :],
                               bass.AP(v, 33 * kt_, [[NT * 33, 128],
                                                     [1, 33]]),
                               EXPB[:, 2048 * s + 512 * i_:
                                    2048 * s + 512 * (i_ + 1)],
                               start=st, stop=sp), inc=sPE)
                if bi == len(bursts):
                    break
                W("scalar", sPE, qk_done)
                EM("scalar", lambda e, s=slot: e.activation(
                    EXPB[:, 2048 * s:2048 * (s + 1)], psA[:, :], AF.Exp,
                    bias=CONST[:, 63:64], scale=ISQ), inc=sAC)
                if islast:
                    W("vector", sAC)
                    for i in range(4):
                        EM("vector", lambda e, s=slot, i_=i:
                           e.tensor_mul(
                               EXPB[:, 2048 * s + 512 * i_:
                                    2048 * s + 512 * (i_ + 1)],
                               EXPB[:, 2048 * s + 512 * i_:
                                    2048 * s + 512 * (i_ + 1)],
                               MASKS[:, 512 * i_:512 * (i_ + 1)]),
                           inc=sDV)
                pend = (VA_, bank, kb, slot, isfirst, islast)
            W("tensor", sAC)
            for r in range(2):
                EM("tensor", lambda e, h_=h, rr=r, q=qc: e.matmul(
                    psA[:, 512 * rr:512 * (rr + 1)],
                    KST[0:96, 256 * h_ + 128 * rr:
                        256 * h_ + 128 * (rr + 1)],
                    QTH[0:96, 512 * q:512 * (q + 1)],
                    start=True, stop=True), inc=sPE)
            W("scalar", sPE)
            EM("scalar", lambda e: e.activation(
                EXPB[:, 0:1024], psA[:, 0:1024], AF.Exp,
                bias=CONST[:, 63:64], scale=ISQ), inc=sAC)
            W("tensor", sAC)
            for r in range(2):
                EM("tensor", lambda e, rr=r: e.matmul(
                    ps2[0:33, :],
                    bass.AP(VSA, 264 * rr, [[2 * 264, 128], [1, 33]]),
                    EXPB[:, 512 * rr:512 * (rr + 1)],
                    start=(rr == 0), stop=(rr == 1)), inc=sPE)
            division([(ps1, 32), (ps2, 33), (ps3, 34)], 32,
                     AEVF3, RCP3, COMB3, AVST3)
            if _TAPS and h == 0 and qc == 1:
                DMA("sync", dbg_qth[:, :], QTH[0:96, :])
                DMA("sync", dbg_k1n[:, :], K1NH[:, :])
                DMA("sync", dbg_ckv[:, :], CKVT[0:32, :])
                DMA("sync", dbg_krn[:, :], KRN[:, :])
                DMA("sync", dbg_k1h[:, :], K1TH[0:96, :])
                DMA("sync", dbg_kwh[:, :], KWTH[0:96, :])
                DMA("sync", dbg_cst[:, :], CONST[:, :])
                DMA("sync", dbg_ab3[:, :], AEVF3[:, :])
                DMA("sync", dbg_rc3[:, :], RCP3[:, :])
                DMA("sync", dbg_cb3[:, :], COMB3[:, :])
                DMA("sync", dbg_exb[:, :], EXPB[:, :])
                DMA("sync", dbg_kst[:, :], KST[0:96, :])
                DMA("sync", dbg_vsa[:, :], VSA[:, :])
                W("sync", sSY)
            DMA("sync",
                bass.AP(cin_avb, (32 * h) * T + 512 * qc,
                        [[T, 32], [1, 512]]),
                AVST3[0:32, 0:512])

    if _TAPS:
        W("sync", sDV)
        DMA("sync", dbg_avb[:, :], cin_avb[:, :])
        DMA("sync", dbg_misc[:, :], MISC[:, :])
        DMA("sync", dbg_grow[:, :], GROW[:, :])
        W("sync", sSY)

    cc_avb = sCC.n + 1
    W("gpsimd", sSY)
    exchange(cin_avb, avb_allg)
    ar_to(mB2)
    AVRH3 = ar("AVRH3", [128, 4096], BF16)
    WPJ = ar("WPJ", [128, 4096], BF16)
    DMA("gpsimd", WPJ[:, :],
        bass.AP(wproj, 0, [[1024, 128], [128 * 1024, 4], [1, 1024]]))
    W("sync", sCC, cc_avb)
    W("tensor", sG)
    outproj(avb_allg, WPJ, 4, AVRH3)
    ar_to(mB)
    BARRIER()
    TAP(2)

    emit_ffn("f2")

    W("sync", sDV)
    DMA("sync",
        bass.AP(y_out, 0, [[TH, 128], [128 * TH, 8], [1, TH]]),
        bass.AP(X, 0, [[8 * TH, 128], [2048, 8], [1, 2048]]))
    W("sync", sSY)

    import contextlib
    with contextlib.ExitStack() as st:
        for name, c in sems.items():
            c.h = st.enter_context(nc.semaphore(name))
        block = st.enter_context(nc.Block())
        for eng in ENGS:
            getattr(block, eng)(
                (lambda fns: lambda e: [f(e) for f in fns])(prog[eng]))
    return nc


_NC = None


def kernel(
    L0_attn_a_hidden_states, L0_attn_a_Wq, L0_attn_a_Wk, L0_attn_a_Wv,
    L0_attn_a_Wo, L0_attn_a_rel_bias_table, L0_attn_b_w_cq, L0_attn_b_g_qnorm,
    L0_attn_b_w_dq_nope, L0_attn_b_w_dq_rope, L0_attn_b_w_ckv,
    L0_attn_b_g_kvnorm, L0_attn_b_w_dk_nope, L0_attn_b_w_dv,
    L0_attn_b_w_krope, L0_attn_b_w_imp, L0_attn_b_w_selk, L0_attn_b_w_selv,
    L0_attn_b_w_wink, L0_attn_b_w_winv, L0_attn_b_w_gate, L0_attn_b_w_proj,
    L0_ffn_w_gate, L0_ffn_w_up, L0_ffn_w_down,
):
    global _NC
    st = _statics()
    x = np.asarray(L0_attn_a_hidden_states)
    C = np.ascontiguousarray

    wgT_ = C(np.asarray(L0_ffn_w_gate).T)
    wuT_ = C(np.asarray(L0_ffn_w_up).T)
    wdT_ = C(np.asarray(L0_ffn_w_down).T)

    in_maps = []
    for c in range(8):
        b, hb = c // 2, c % 2
        hs = slice(512 * hb, 512 * hb + 512)
        m = {
            "x_ownT": C(x[b, 2048 * hb:2048 * hb + 2048, :].T),
            "wq": C(np.asarray(L0_attn_a_Wq)[:, hs]),
            "wk": C(np.asarray(L0_attn_a_Wk)[:, hs]),
            "wv": C(np.asarray(L0_attn_a_Wv)[:, hs]),
            "wo": C(np.asarray(L0_attn_a_Wo)),
            "rtab": C(np.asarray(L0_attn_a_rel_bias_table)[:, 8 * hb:8 * hb + 8]),
            "wgT": wgT_, "wuT": wuT_, "wdT": wdT_,
            "wcq": C(np.asarray(L0_attn_b_w_cq)),
            "gq": C(np.asarray(L0_attn_b_g_qnorm).reshape(96, 1)),
            "wdqn": C(np.asarray(L0_attn_b_w_dq_nope)[:, 256 * hb:256 * hb + 256]),
            "wdqr": C(np.asarray(L0_attn_b_w_dq_rope)[:, 512 * hb:512 * hb + 512]),
            "wckv": C(np.asarray(L0_attn_b_w_ckv)),
            "gkv": C(np.asarray(L0_attn_b_g_kvnorm).reshape(32, 1)),
            "wdkn": C(np.asarray(L0_attn_b_w_dk_nope)[:, 256 * hb:256 * hb + 256]),
            "wdv": C(np.asarray(L0_attn_b_w_dv)[:, 256 * hb:256 * hb + 256]),
            "wkro": C(np.asarray(L0_attn_b_w_krope)),
            "wimp": C(np.asarray(L0_attn_b_w_imp)),
            "wselk": C(np.asarray(L0_attn_b_w_selk)[:, 768 * hb:768 * hb + 768]),
            "wselv": C(np.asarray(L0_attn_b_w_selv)[:, 256 * hb:256 * hb + 256]),
            "wwink": C(np.asarray(L0_attn_b_w_wink)[:, 768 * hb:768 * hb + 768]),
            "wwinv": C(np.asarray(L0_attn_b_w_winv)[:, 256 * hb:256 * hb + 256]),
            "wgate": C(np.asarray(L0_attn_b_w_gate)),
            "wproj": C(np.asarray(L0_attn_b_w_proj)),
            "s_onehot": st["onehot"], "s_ident": st["ident"],
            "s_identf": st["identf"], "s_trils": st["trils"],
            "s_iota": st["iota"], "s_onesb": st["onesb"],
            "s_onesf": st["onesf"], "s_masks": st["masks"],
            "s_ropec": st["ropec"], "s_ropes": st["ropes"],
            "tok0": np.array([[2048 * hb]], dtype=np.int32),
        }
        in_maps.append(m)

    if _NC is None:
        _NC = build_nc()
    res = run_bass_kernel_spmd(_NC, in_maps, core_ids=list(range(8)))
    out = np.empty((B, T, D), dtype=np.float32)
    for c in range(8):
        b, hb = c // 2, c % 2
        yt = np.asarray(res.results[c]["y_ownT"])
        out[b, 2048 * hb:2048 * hb + 2048, :] = yt.T
    kernel._last_res = res
    return out


# version 06
# speedup vs baseline: 1.0403x; 1.0403x over previous
import math
import numpy as np
import ml_dtypes

import concourse.bass as bass
import concourse.mybir as mybir
from concourse.bass_utils import run_bass_kernel_spmd

F32 = mybir.dt.float32
BF16 = mybir.dt.bfloat16
I32 = mybir.dt.int32
AF = mybir.ActivationFunctionType
ALU = mybir.AluOpType

B, T, D = 4, 4096, 1024
TH = T // 2
NT = T // 128
HPC = 8
KEEP = 256
FFH = 3072
NJ = FFH // 128
SM_SHIFT = -12.0
INV127 = 1.0 / 127.0
ISQ = 1.0 / math.sqrt(96.0)
NBIS = 48
_ALOG = []
_TAPS = False


def _rel_bucket(d):
    nb, maxd = 16, 128
    buckets = (d > 0).astype(np.int64) * nb
    rp = np.abs(d)
    max_exact = nb // 2
    small = rp < max_exact
    rl = max_exact + (
        np.log(np.maximum(rp, 1) / max_exact) / math.log(maxd / max_exact)
        * (nb - max_exact)
    ).astype(np.int64)
    rl = np.minimum(rl, nb - 1)
    return buckets + np.where(small, rp, rl)


_STATICS = None


def _statics():
    global _STATICS
    if _STATICS is not None:
        return _STATICS
    d = np.arange(-4095, 4096)
    buck = _rel_bucket(d)
    onehot = np.zeros((32, 8191), dtype=ml_dtypes.bfloat16)
    onehot[buck, np.arange(8191)] = 1.0
    ident = np.eye(128, dtype=ml_dtypes.bfloat16)
    identf = np.eye(128, dtype=np.float32)
    trils = np.triu(np.ones((128, 128), dtype=np.float32), 1).astype(
        ml_dtypes.bfloat16)
    iota = np.broadcast_to(np.arange(1, 257, dtype=np.float32)[None, :],
                           (128, 256)).copy()
    onesb = np.ones((128, 128), dtype=ml_dtypes.bfloat16)
    onesf = np.ones((128, 128), dtype=np.float32)
    masks = np.zeros((128, 4 * 512), dtype=ml_dtypes.bfloat16)
    for m, c0 in enumerate((0, 128, 256, 384)):
        p = np.arange(128)[:, None]
        j = np.arange(512)[None, :]
        masks[:, 512 * m:512 * (m + 1)] = (p + c0 <= j).astype(
            ml_dtypes.bfloat16)
    f = 1.0 / (10000.0 ** (np.arange(0, 64, 2, dtype=np.float64) / 64.0))
    t = np.arange(T, dtype=np.float64)
    a = np.outer(t, f)
    c2 = np.concatenate([np.cos(a), np.cos(a)], axis=1)
    s2 = np.concatenate([-np.sin(a), np.sin(a)], axis=1)
    ropec = np.zeros((128, NT * 64), dtype=ml_dtypes.bfloat16)
    ropes = np.zeros((128, NT * 64), dtype=ml_dtypes.bfloat16)
    for tt in range(NT):
        ropec[:, 64 * tt:64 * (tt + 1)] = c2[128 * tt:128 * (tt + 1)]
        ropes[:, 64 * tt:64 * (tt + 1)] = s2[128 * tt:128 * (tt + 1)]
    _STATICS = dict(onehot=onehot, ident=ident, identf=identf, trils=trils,
                    iota=iota, onesb=onesb, onesf=onesf, masks=masks,
                    ropec=ropec, ropes=ropes)
    return _STATICS


ENGS = ("sync", "tensor", "vector", "scalar", "gpsimd")


class Ctr:
    def __init__(self, name):
        self.name = name
        self.n = 0
        self.h = None


def build_nc():
    nc = bass.Bass("TRN2", target_bir_lowering=False)

    def din(name, shape, dt=F32):
        return nc.dram_tensor(name, list(shape), dt, kind="ExternalInput")

    x_in = din("x_ownT", (D, TH))
    wq = din("wq", (D, 512))
    wk = din("wk", (D, 512))
    wv = din("wv", (D, 512))
    wo = din("wo", (D, D))
    rtab = din("rtab", (32, 8))
    wgT = din("wgT", (D, FFH), I32)
    wuT = din("wuT", (D, FFH), I32)
    wdT = din("wdT", (FFH, D), I32)
    wcq = din("wcq", (D, 96))
    gq = din("gq", (96, 1))
    wdqn = din("wdqn", (96, 256))
    wdqr = din("wdqr", (96, 512))
    wckv = din("wckv", (D, 32))
    gkv = din("gkv", (32, 1))
    wdkn = din("wdkn", (32, 256))
    wdv = din("wdv", (32, 256))
    wkro = din("wkro", (D, 64))
    wimp = din("wimp", (D, 1))
    wselk = din("wselk", (D, 768))
    wselv = din("wselv", (D, 256))
    wwink = din("wwink", (D, 768))
    wwinv = din("wwinv", (D, 256))
    wgate = din("wgate", (D, 3))
    wproj = din("wproj", (512, D))
    s_onehot = din("s_onehot", (32, 8191), BF16)
    s_ident = din("s_ident", (128, 128), BF16)
    s_identf = din("s_identf", (128, 128))
    s_trils = din("s_trils", (128, 128), BF16)
    s_iota = din("s_iota", (128, 256))
    s_onesb = din("s_onesb", (128, 128), BF16)
    s_onesf = din("s_onesf", (128, 128))
    s_masks = din("s_masks", (128, 2048), BF16)
    s_ropec = din("s_ropec", (128, NT * 64), BF16)
    s_ropes = din("s_ropes", (128, NT * 64), BF16)
    tok0_in = din("tok0", (1, 1), I32)

    y_out = nc.dram_tensor("y_ownT", [D, TH], F32, kind="ExternalOutput")
    dbg = []
    if _TAPS:
        for i in range(7):
            dbg.append(nc.dram_tensor("dbg%d" % i, [D, TH], F32,
                                      kind="ExternalOutput"))
        dbg_xn = nc.dram_tensor("dbg_xn", [2 * D, TH], BF16,
                                kind="ExternalOutput")
        dbg_bias = nc.dram_tensor("dbg_bias", [8, 8191], F32,
                                  kind="ExternalOutput")
        dbg_av = nc.dram_tensor("dbg_av", [512, T], BF16,
                                kind="ExternalOutput")
        dbg_qt = nc.dram_tensor("dbg_qt", [128, 4 * T], BF16,
                                kind="ExternalOutput")
        dbg_kt = nc.dram_tensor("dbg_kt", [128, 4 * T], BF16,
                                kind="ExternalOutput")
        dbg_v = nc.dram_tensor("dbg_v", [128, NT * 520], BF16,
                               kind="ExternalOutput")
        dbg_aev = nc.dram_tensor("dbg_aev", [128, 512], F32,
                                 kind="ExternalOutput")
        dbg_rcp = nc.dram_tensor("dbg_rcp", [128, 8], F32,
                                 kind="ExternalOutput")
        dbg_comb = nc.dram_tensor("dbg_comb", [128, 256], BF16,
                                  kind="ExternalOutput")
        dbg_avb = nc.dram_tensor("dbg_avb", [256, T], BF16,
                                 kind="ExternalOutput")
        dbg_misc = nc.dram_tensor("dbg_misc", [128, 16], F32,
                                  kind="ExternalOutput")
        dbg_grow = nc.dram_tensor("dbg_grow", [1, 16], F32,
                                  kind="ExternalOutput")
        dbg_ab3 = nc.dram_tensor("dbg_ab3", [128, 3 * 512], F32,
                                 kind="ExternalOutput")
        dbg_rc3 = nc.dram_tensor("dbg_rc3", [128, 8], F32,
                                 kind="ExternalOutput")
        dbg_cb3 = nc.dram_tensor("dbg_cb3", [128, 256], BF16,
                                 kind="ExternalOutput")
        dbg_exb = nc.dram_tensor("dbg_exb", [128, 4096], BF16,
                                 kind="ExternalOutput")
        dbg_qth = nc.dram_tensor("dbg_qth", [96, T], BF16,
                                 kind="ExternalOutput")
        dbg_k1h = nc.dram_tensor("dbg_k1h", [96, T], BF16,
                                 kind="ExternalOutput")
        dbg_kwh = nc.dram_tensor("dbg_kwh", [96, T], BF16,
                                 kind="ExternalOutput")
        dbg_cst = nc.dram_tensor("dbg_cst", [128, 64], F32,
                                 kind="ExternalOutput")
        dbg_k1n = nc.dram_tensor("dbg_k1n", [128, NT * 96], BF16,
                                 kind="ExternalOutput")
        dbg_ckv = nc.dram_tensor("dbg_ckv", [32, T], BF16,
                                 kind="ExternalOutput")
        dbg_krn = nc.dram_tensor("dbg_krn", [128, NT * 64], BF16,
                                 kind="ExternalOutput")
        dbg_kst = nc.dram_tensor("dbg_kst", [96, 2048], BF16,
                                 kind="ExternalOutput")
        dbg_vsa = nc.dram_tensor("dbg_vsa", [128, 528], BF16,
                                 kind="ExternalOutput")
        dbg_pts = nc.dram_tensor("dbg_pts", [128, 8192], BF16,
                                 kind="ExternalOutput")
        dbg_junk = nc.dram_tensor("dbg_junk", [128, 32], F32,
                                  kind="ExternalOutput")
        dbg_cum = nc.dram_tensor("dbg_cum", [128, 32], F32,
                                 kind="ExternalOutput")
        dbg_cumf = nc.dram_tensor("dbg_cumf", [128, 32], F32,
                                  kind="ExternalOutput")
        dbg_ssc = nc.dram_tensor("dbg_ssc", [128, 32], F32,
                                 kind="ExternalOutput")

    bias_dram = nc.dram_tensor("bias_dram", [8, 8191], F32)
    cin_half = nc.dram_tensor("cin_half", [D, TH], BF16)
    xn_allg = nc.dram_tensor("xn_allg", [2 * D, TH], BF16)
    cin_av = nc.dram_tensor("cin_av", [512, T], BF16)
    av_allg = nc.dram_tensor("av_allg", [1024, T], BF16)
    cin_avb = nc.dram_tensor("cin_avb", [256, T], BF16)
    avb_allg = nc.dram_tensor("avb_allg", [512, T], BF16)
    kwT_scr = nc.dram_tensor("kwT_scr", [HPC * 96, T], BF16)
    v1_scr = nc.dram_tensor("v1_scr", [HPC * T, 33], BF16)
    vw_scr = nc.dram_tensor("vw_scr", [HPC * T, 33], BF16)

    X = nc.alloc_sbuf_tensor("X", [128, 8 * TH], F32)
    IDENT = nc.alloc_sbuf_tensor("IDENT", [128, 128], BF16)
    IDENTF = nc.alloc_sbuf_tensor("IDENTF", [128, 128], F32)
    TRILS = nc.alloc_sbuf_tensor("TRILS", [128, 128], BF16)
    ONESB = nc.alloc_sbuf_tensor("ONESB", [128, 128], BF16)
    ONESF = nc.alloc_sbuf_tensor("ONESF", [128, 128], F32)
    IOTA = nc.alloc_sbuf_tensor("IOTA", [128, 256], F32)
    CONST = nc.alloc_sbuf_tensor("CONST", [128, 64], F32)
    MISC = nc.alloc_sbuf_tensor("MISC", [128, 16], F32)
    LNROW = nc.alloc_sbuf_tensor("LNROW", [128, 2048], F32)
    XBUF = nc.alloc_sbuf_tensor("XBUF", [128, 1024], BF16)
    XNTMP = nc.alloc_sbuf_tensor("XNTMP", [128, 512], F32)
    XNST = nc.alloc_sbuf_tensor("XNST", [128, 512], BF16)
    TOKI = nc.alloc_sbuf_tensor("TOKI", [1, 1], I32)

    SBASE = 229375 - nc.sbuf_bytes_remaining + 64
    SLIMIT = 229376
    arena = {"off": SBASE}

    def ar_mark():
        return arena["off"]

    def ar_to(mark):
        arena["off"] = mark

    def ar(name, shape, dt):
        nbytes = int(np.prod(shape[1:])) * mybir.dt.size(dt)
        off = (arena["off"] + 63) // 64 * 64
        t = nc.alloc_sbuf_tensor_at(name, list(shape), dt, offset=off)
        _ALOG.append((name, off, off + nbytes))
        arena["off"] = off + nbytes
        assert arena["off"] <= SLIMIT, (name, arena["off"])
        return t

    psA = nc.alloc_psum_tensor("psA", [128, 2048], F32)
    ps1 = nc.alloc_psum_tensor("ps1", [128, 512], F32)
    ps2 = nc.alloc_psum_tensor("ps2", [128, 512], F32)
    ps3 = nc.alloc_psum_tensor("ps3", [128, 512], F32)
    psT = nc.alloc_psum_tensor("psT", [128, 512], BF16)

    prog = {e: [] for e in ENGS}
    sems = {}

    def mksem(name):
        c = Ctr(name)
        sems[name] = c
        return c

    sSY = mksem("sSY")
    sG = mksem("sG")
    sCC = mksem("sCC")
    sPE = mksem("sPE")
    sDV = mksem("sDV")
    sAC = mksem("sAC")
    sBar = mksem("sBar")
    env = {}

    def E(eng, fn):
        prog[eng].append(fn)

    def W(eng, ctr, val=None):
        v = ctr.n if val is None else val
        E(eng, lambda e, c=ctr, vv=v: e.wait_ge(c.h, vv))

    def EM(eng, fn, inc=None, inc_n=1):
        if inc is None:
            E(eng, fn)
            return
        inc.n += inc_n
        E(eng, lambda e, f=fn, c=inc, n=inc_n: f(e).then_inc(c.h, n))

    def DMA(eng, out, in_, allow_slow=False):
        sem = sG if eng == "gpsimd" else sSY
        kw = {"allow_slow_non_contiguous": True} if allow_slow else {}
        EM(eng, lambda e, o=out, i=in_, k=dict(kw): e.dma_start(o, i, **k),
           inc=sem, inc_n=16)

    def BARRIER():
        target = sBar.n + len(ENGS)
        for eng in ENGS:
            EM(eng, lambda e: e.nop(), inc=sBar)
        for eng in ENGS:
            W(eng, sBar, target)

    def TAP(i):
        if not _TAPS:
            return
        DMA("sync",
            bass.AP(dbg[i], 0, [[TH, 128], [128 * TH, 8], [1, TH]]),
            bass.AP(X, 0, [[8 * TH, 128], [2048, 8], [1, 2048]]))
        W("sync", sSY)

    DMA("sync", X[:, :],
        bass.AP(x_in, 0, [[TH, 128], [128 * TH, 8], [1, TH]]))
    DMA("sync", IDENT[:, :], s_ident[:, :])
    DMA("sync", IDENTF[:, :], s_identf[:, :])
    DMA("sync", TRILS[:, :], s_trils[:, :])
    DMA("sync", ONESB[:, :], s_onesb[:, :])
    DMA("sync", ONESF[:, :], s_onesf[:, :])
    DMA("sync", IOTA[:, :], s_iota[:, :])
    DMA("sync", TOKI[0:1, 0:1], tok0_in[0:1, 0:1])
    W("sync", sSY)

    def _load_tok(e):
        env["tokr"] = e.to_reg(e.value_load(TOKI[0:1, 0:1]))

    E("sync", _load_tok)

    m0 = ar_mark()
    OH = ar("OH", [32, 8191], BF16)
    RT = ar("RT", [32, 8], BF16)
    BROW = ar("BROW", [8, 8191], F32)
    CROW = ar("CROW", [1, 16], F32)
    DMA("sync", OH[:, :], s_onehot[:, :])
    DMA("gpsimd", RT[:, :], rtab[:, :])
    W("tensor", sSY)
    W("tensor", sG)
    for j in range(16):
        w = 512 if j < 15 else 8191 - 15 * 512
        EM("tensor",
           lambda e, jj=j, ww=w: e.matmul(
               ps1[0:8, 0:ww], RT[0:32, :],
               OH[0:32, 512 * jj:512 * jj + ww], start=True, stop=True),
           inc=sPE)
        W("vector", sPE)
        EM("vector",
           lambda e, jj=j, ww=w: e.tensor_copy(
               BROW[0:8, 512 * jj:512 * jj + ww], ps1[0:8, 0:ww]),
           inc=sDV)
        W("tensor", sDV)
    W("sync", sDV)
    DMA("sync", bias_dram[:, :], BROW[0:8, :])
    W("sync", sSY)
    DMA("sync", CROW[0:1, 0:8],
        bass.AP(bias_dram, 3500, [[1, 1], [8191, 8]]), allow_slow=True)
    DMA("sync", CROW[0:1, 8:16],
        bass.AP(bias_dram, 4500, [[1, 1], [8191, 8]]), allow_slow=True)
    W("tensor", sSY)
    EM("tensor",
       lambda e: e.matmul(ps1[:, 0:16], ONESF[0:1, :], CROW[0:1, :],
                          start=True, stop=True), inc=sPE)
    W("vector", sPE)
    EM("vector",
       lambda e: e.tensor_scalar(
           bass.AP(CONST, 0, [[64, 128], [2, 8]]),
           bass.AP(ps1, 0, [[512, 128], [1, 8]]),
           SM_SHIFT, None, ALU.add), inc=sDV)
    EM("vector",
       lambda e: e.tensor_scalar(
           bass.AP(CONST, 1, [[64, 128], [2, 8]]),
           bass.AP(ps1, 8, [[512, 128], [1, 8]]),
           SM_SHIFT, None, ALU.add), inc=sDV)
    EM("vector", lambda e: e.memset(CONST[:, 63:64], SM_SHIFT), inc=sDV)
    ar_to(m0)
    BARRIER()
    TAP(3)


    def emit_ln(sink_sb=None):
        for tc in range(4):
            for cc in range(8):
                W("vector", sPE)
                EM("vector",
                   lambda e, c=cc, t=tc: e.tensor_copy(
                       XBUF[:, 0:512],
                       X[:, 2048 * c + 512 * t:2048 * c + 512 * t + 512]),
                   inc=sDV)
                EM("vector",
                   lambda e: e.tensor_mul(
                       XBUF[:, 512:1024], XBUF[:, 0:512], XBUF[:, 0:512]),
                   inc=sDV)
                W("tensor", sDV)
                EM("tensor",
                   lambda e, c=cc: e.matmul(
                       psA[0:1, 0:512], ONESB[:, 0:1], XBUF[:, 0:512],
                       start=(c == 0), stop=(c == 7)), inc=sPE)
                EM("tensor",
                   lambda e, c=cc: e.matmul(
                       psA[0:1, 512:1024], ONESB[:, 0:1], XBUF[:, 512:1024],
                       start=(c == 0), stop=(c == 7)), inc=sPE)
            W("vector", sPE)
            EM("vector", lambda e: e.tensor_scalar(
                LNROW[0:1, 0:512], psA[0:1, 0:512], 1.0 / D, None, ALU.mult),
               inc=sDV)
            EM("vector", lambda e: e.tensor_scalar(
                LNROW[0:1, 512:1024], psA[0:1, 512:1024], 1.0 / D, None,
                ALU.mult), inc=sDV)
            EM("vector", lambda e: e.tensor_mul(
                LNROW[0:1, 1024:1536], LNROW[0:1, 0:512], LNROW[0:1, 0:512]),
               inc=sDV)
            EM("vector", lambda e: e.tensor_sub(
                LNROW[0:1, 512:1024], LNROW[0:1, 512:1024],
                LNROW[0:1, 1024:1536]), inc=sDV)
            EM("vector", lambda e: e.tensor_scalar(
                LNROW[0:1, 512:1024], LNROW[0:1, 512:1024], 1e-5, None,
                ALU.add), inc=sDV)
            W("scalar", sDV)
            EM("scalar", lambda e: e.activation(
                LNROW[0:1, 1024:1536], LNROW[0:1, 512:1024], AF.Ln,
                bias=0.0, scale=1.0), inc=sAC)
            EM("scalar", lambda e: e.activation(
                LNROW[0:1, 1536:2048], LNROW[0:1, 1024:1536], AF.Exp,
                bias=0.0, scale=-0.5), inc=sAC)
            W("vector", sAC)
            EM("vector", lambda e: e.scalar_tensor_tensor(
                LNROW[0:1, 0:512], LNROW[0:1, 0:512], -1.0,
                LNROW[0:1, 1536:2048], ALU.mult, ALU.mult), inc=sDV)
            W("tensor", sDV)
            EM("tensor", lambda e: e.matmul(
                ps1[:, :], ONESF[0:1, :], LNROW[0:1, 1536:2048],
                start=True, stop=True), inc=sPE)
            EM("tensor", lambda e: e.matmul(
                ps2[:, :], ONESF[0:1, :], LNROW[0:1, 0:512],
                start=True, stop=True), inc=sPE)
            W("vector", sPE)
            for cc in range(8):
                EM("vector",
                   lambda e, c=cc, t=tc: e.tensor_mul(
                       XNTMP[:, :],
                       X[:, 2048 * c + 512 * t:2048 * c + 512 * t + 512],
                       ps1[:, :]), inc=sDV)
                if sink_sb is None:
                    W("vector", sSY)
                    EM("vector",
                       lambda e: e.tensor_add(XNST[:, :], XNTMP[:, :],
                                              ps2[:, :]), inc=sDV)
                    W("sync", sDV)
                    DMA("sync",
                        bass.AP(cin_half, (128 * cc) * TH + 512 * tc,
                                [[TH, 128], [1, 512]]),
                        XNST[:, :])
                else:
                    EM("vector",
                       lambda e, s=sink_sb, c=cc, t=tc: e.tensor_add(
                           s[:, 2048 * c + 512 * t:2048 * c + 512 * t + 512],
                           XNTMP[:, :], ps2[:, :]), inc=sDV)
            W("tensor", sDV)

    def exchange(cin, cout):
        EM("gpsimd",
           lambda e, ci=cin, co=cout: e.collective_compute(
               "AllGather", ALU.bypass,
               replica_groups=[[0, 1], [2, 3], [4, 5], [6, 7]],
               ins=[ci.ap().opt()], outs=[co.ap().opt()]),
           inc=sCC)

    def division(br_list, avw, AEVF, RCPC, COMBB, dst):
        nbr = len(br_list)
        W("vector", sSY)
        W("vector", sPE)
        for bi, (bp, _) in enumerate(br_list):
            EM("vector",
               lambda e, b=bp, w=avw, i=bi: e.tensor_copy(
                   AEVF[0:w + 1, 512 * i:512 * (i + 1)], b[0:w + 1, :]),
               inc=sDV)
        W("tensor", sDV)
        for bi in range(nbr):
            for st in range(4):
                EM("tensor",
                   lambda e, i=bi, s=st, w=avw: e.transpose(
                       psA[:, 512 * i + (w + 2) * s:
                           512 * i + (w + 2) * s + (w + 1)],
                       AEVF[0:w + 1, 512 * i + 128 * s:
                            512 * i + 128 * (s + 1)],
                       IDENTF[0:w + 1, 0:w + 1]), inc=sPE)
        W("vector", sPE)
        for st in range(4):
            for bi, (bp, gcol) in enumerate(br_list):
                base = 512 * bi + (avw + 2) * st
                EM("vector",
                   lambda e, bs=base, w=avw, c=bi: e.reciprocal(
                       RCPC[:, c:c + 1], psA[:, bs + w:bs + w + 1]),
                   inc=sDV)
                W("vector", sDV)
                if gcol is not None:
                    EM("vector",
                       lambda e, c=bi, g=gcol: e.tensor_mul(
                           RCPC[:, c:c + 1], RCPC[:, c:c + 1],
                           CONST[:, g:g + 1]), inc=sDV)
                if bi == 0:
                    EM("vector",
                       lambda e, bs=base, w=avw, s=st, c=bi: e.tensor_scalar(
                           COMBB[:, w * s:w * (s + 1)], psA[:, bs:bs + w],
                           RCPC[:, c:c + 1], None, ALU.mult), inc=sDV)
                else:
                    EM("vector",
                       lambda e, bs=base, w=avw, s=st, c=bi:
                       e.scalar_tensor_tensor(
                           COMBB[:, w * s:w * (s + 1)], psA[:, bs:bs + w],
                           RCPC[:, c:c + 1], COMBB[:, w * s:w * (s + 1)],
                           ALU.mult, ALU.add), inc=sDV)
        W("tensor", sDV)
        for st in range(4):
            EM("tensor",
               lambda e, s=st, w=avw: e.transpose(
                   psT[0:w, 128 * s:128 * (s + 1)],
                   COMBB[:, w * s:w * (s + 1)], IDENT[:, :]), inc=sPE)
        W("vector", sPE)
        EM("vector",
           lambda e, w=avw, d=dst: e.tensor_copy(d[0:w, 0:512],
                                                 psT[0:w, 0:512]),
           inc=sDV)
        W("sync", sDV)

    mA = ar_mark()
    QT = ar("QT", [128, 4 * T], BF16)
    KT = ar("KT", [128, 4 * T], BF16)
    VSB = ar("VSB", [128, NT * 520], BF16)
    mA2 = ar_mark()
    WB1 = ar("WB1", [128, 4096], BF16)
    XNB = ar("XNB", [128, 8192], BF16)

    emit_ln(sink_sb=None)
    TAP(4)
    cc_xn1 = sCC.n + 1
    W("gpsimd", sSY)
    exchange(cin_half, xn_allg)
    W("sync", sCC, cc_xn1)

    def load_xn_chunk(tch, slot, dst=None):
        d = dst if dst is not None else XNB
        hb = tch // 4
        cb = tch % 4
        DMA("sync",
            bass.AP(d, 4096 * slot, [[8192, 128], [512, 8], [1, 512]]),
            bass.AP(xn_allg, (1024 * hb) * TH + 512 * cb,
                    [[TH, 128], [128 * TH, 8], [1, 512]]))

    def load_w_slice(src, width):
        DMA("gpsimd", bass.AP(WB1, 0, [[4096, 128], [width, 8], [1, width]]),
            bass.AP(src, 0, [[width, 128], [128 * width, 8], [1, width]]))

    for which in ("q", "k"):
        src = wq if which == "q" else wk
        OUT = QT if which == "q" else KT
        W("gpsimd", sPE)
        load_w_slice(src, 512)
        W("tensor", sG)
        W("sync", sPE)
        load_xn_chunk(0, 0)
        for tch in range(8):
            if tch < 7:
                W("sync", sPE)
                load_xn_chunk(tch + 1, (tch + 1) % 2)
            W("tensor", sSY, sSY.n - (16 if tch < 7 else 0))
            slot = tch % 2
            for mc in range(4):
                for cc in range(8):
                    EM("tensor",
                       lambda e, m=mc, c=cc, s=slot: e.matmul(
                           psA[:, 512 * m:512 * (m + 1)],
                           WB1[:, 512 * c + 128 * m:512 * c + 128 * (m + 1)],
                           bass.AP(XNB, 4096 * s + 512 * c,
                                   [[8192, 128], [1, 512]]),
                           start=(c == 0), stop=(c == 7)), inc=sPE)
            W("vector", sPE)
            for mc in range(4):
                EM("vector",
                   lambda e, m=mc, t=tch, o=OUT: e.tensor_copy(
                       o[:, 4096 * m + 512 * t:4096 * m + 512 * (t + 1)],
                       psA[:, 512 * m:512 * (m + 1)]), inc=sDV)
            W("tensor", sDV)
    W("gpsimd", sPE)
    load_w_slice(wv, 512)
    W("tensor", sG)
    EM("vector", lambda e: e.memset(VSB[:, :], 1.0), inc=sDV)
    W("tensor", sDV)
    W("sync", sPE)
    load_xn_chunk(0, 0)
    for tch in range(8):
        if tch < 7:
            W("sync", sPE)
            load_xn_chunk(tch + 1, (tch + 1) % 2)
        W("tensor", sSY, sSY.n - (16 if tch < 7 else 0))
        slot = tch % 2
        for tt in range(4):
            tile = 4 * tch + tt
            for cc in range(8):
                EM("tensor",
                   lambda e, s=slot, c=cc, t=tt: e.matmul(
                       ps1[:, :],
                       bass.AP(XNB, 4096 * s + 512 * c + 128 * t,
                               [[8192, 128], [1, 128]]),
                       bass.AP(WB1, 512 * c, [[4096, 128], [1, 512]]),
                       start=(c == 0), stop=(c == 7)), inc=sPE)
            W("vector", sPE)
            EM("vector",
               lambda e, ti=tile: e.tensor_copy(
                   bass.AP(VSB, 520 * ti, [[NT * 520, 128], [65, 8],
                                           [1, 64]]),
                   bass.AP(ps1, 0, [[512, 128], [64, 8], [1, 64]])),
               inc=sDV)
            W("tensor", sDV)

    TAP(5)
    if _TAPS:
        W("sync", sDV)
        DMA("sync", dbg_qt[:, :], QT[:, :])
        DMA("sync", dbg_kt[:, :], KT[:, :])
        DMA("sync", dbg_v[:, :], VSB[:, :])
        DMA("sync", dbg_xn[:, :], xn_allg[:, :])
        DMA("sync", dbg_bias[:, :], bias_dram[:, :])
        W("sync", sSY)

    ar_to(mA2)
    TOEP = ar("TOEP", [128, 6 * 512], F32)
    EXPP = ar("EXPP", [128, 4096], BF16)
    AEVF = ar("AEVF", [128, 512], F32)
    RCPC = ar("RCPC", [128, 8], F32)
    COMBB = ar("COMBB", [128, 256], BF16)
    AVST = ar("AVST", [128, 512], BF16)

    for h in range(HPC):
        mc = h // 2
        roff = 64 * (h % 2)
        W("sync", sDV)
        for mi in range(6):
            c0 = 128 * (mi - 1)
            DMA("sync", TOEP[:, 512 * mi:512 * (mi + 1)],
                bass.AP(bias_dram, 8191 * h + c0 + 3584,
                        [[1, 128], [1, 512]]))
        toep_ready = sSY.n
        for qc in range(8):
            pend = None
            for kb in range(9):
                if kb < 8:
                    W("tensor", sAC)
                    if kb == 0:
                        W("tensor", sDV)
                        if qc == 0:
                            W("tensor", sSY, toep_ready)
                    for i in range(4):
                        kt = 4 * kb + i
                        EM("tensor",
                           lambda e, m=mc, r=roff, k=kt, q=qc, i_=i: e.matmul(
                               psA[:, 512 * i_:512 * (i_ + 1)],
                               KT[r:r + 64, 4096 * m + 128 * k:
                                  4096 * m + 128 * (k + 1)],
                               QT[r:r + 64, 4096 * m + 512 * q:
                                  4096 * m + 512 * (q + 1)],
                               start=True, stop=True), inc=sPE)
                    qk_done = sPE.n
                if pend is not None:
                    pkb = pend
                    pslot = pkb % 2
                    if kb == 8:
                        W("tensor", sAC)
                    for i in range(4):
                        kt = 4 * pkb + i
                        EM("tensor",
                           lambda e, k=kt, hh=h, s=pslot, i_=i,
                           st=(pkb == 0 and i == 0),
                           sp=(pkb == 7 and i == 3): e.matmul(
                               ps1[0:65, :],
                               bass.AP(VSB, 520 * k + 65 * hh,
                                       [[NT * 520, 128], [1, 65]]),
                               EXPP[:, 2048 * s + 512 * i_:
                                    2048 * s + 512 * (i_ + 1)],
                               start=st, stop=sp), inc=sPE)
                if kb == 8:
                    break
                band = [i for i in range(4)
                        if -1 <= (4 * kb + i) - 4 * qc <= 4]
                if band:
                    W("vector", sPE, qk_done)
                    for i in band:
                        mi = (4 * kb + i) - 4 * qc + 1
                        EM("vector",
                           lambda e, i_=i, m=mi: e.tensor_add(
                               psA[:, 512 * i_:512 * (i_ + 1)],
                               psA[:, 512 * i_:512 * (i_ + 1)],
                               bass.AP(TOEP, 512 * m + 511,
                                       [[6 * 512, 128], [-1, 512]])),
                           inc=sDV)
                    W("scalar", sDV)
                else:
                    W("scalar", sPE, qk_done)
                slot = kb % 2
                runs = []
                for i in range(4):
                    mp = (4 * kb + i) - 4 * qc
                    cls = ("band" if -1 <= mp <= 4 else
                           ("left" if mp < -1 else "right"))
                    if runs and runs[-1][0] == cls:
                        runs[-1][2] += 1
                    else:
                        runs = runs + [[cls, i, 1]]
                for cls, i0, ln in runs:
                    if cls == "band":
                        EM("scalar",
                           lambda e, s=slot, i_=i0, l=ln: e.activation(
                               EXPP[:, 2048 * s + 512 * i_:
                                    2048 * s + 512 * (i_ + l)],
                               psA[:, 512 * i_:512 * (i_ + l)], AF.Exp,
                               bias=CONST[:, 63:64], scale=1.0), inc=sAC)
                    else:
                        gcol = 2 * h if cls == "left" else 2 * h + 1
                        EM("scalar",
                           lambda e, s=slot, i_=i0, l=ln, g=gcol:
                           e.activation(
                               EXPP[:, 2048 * s + 512 * i_:
                                    2048 * s + 512 * (i_ + l)],
                               psA[:, 512 * i_:512 * (i_ + l)], AF.Exp,
                               bias=CONST[:, g:g + 1], scale=1.0), inc=sAC)
                pend = kb
            division([(ps1, None)], 64, AEVF, RCPC, COMBB, AVST)
            if _TAPS and h == 0 and qc == 1:
                DMA("sync", dbg_aev[:, :], AEVF[:, :])
                DMA("sync", dbg_rcp[:, :], RCPC[:, :])
                DMA("sync", dbg_comb[:, :], COMBB[:, :])
                W("sync", sSY)
            DMA("sync",
                bass.AP(cin_av, (64 * h) * T + 512 * qc,
                        [[T, 64], [1, 512]]),
                AVST[0:64, 0:512])

    TAP(6)
    if _TAPS:
        W("sync", sDV)
        DMA("sync", dbg_av[:, :], cin_av[:, :])
        W("sync", sSY)

    cc_av = sCC.n + 1
    W("gpsimd", sSY)
    exchange(cin_av, av_allg)
    ar_to(mA2)
    AVRH = ar("AVRH", [128, 4096], BF16)
    WOB = ar("WOB", [128, 8192], BF16)
    DMA("gpsimd", WOB[:, :],
        bass.AP(wo, 0, [[1024, 128], [128 * 1024, 8], [1, 1024]]))
    W("sync", sCC, cc_av)
    W("tensor", sG)

    def outproj(src_allg, WB, ncc, AVBUF):
        for tj in range(4):
            W("sync", sPE)

            def _ld(e, t=tj, sa=src_allg, n_=ncc, ab=AVBUF, ctr=sSY):
                with e.If_cmp(env["tokr"], 0, "IS_EQ"):
                    e.dma_start(
                        bass.AP(ab, 0, [[4096, 128], [512, n_], [1, 512]]),
                        bass.AP(sa, 512 * t,
                                [[T, 128], [128 * T, n_], [1, 512]]),
                    ).then_inc(ctr.h, 16)
                with e.If_cmp(env["tokr"], 0, "IS_NE"):
                    e.dma_start(
                        bass.AP(ab, 0, [[4096, 128], [512, n_], [1, 512]]),
                        bass.AP(sa, 2048 + 512 * t,
                                [[T, 128], [128 * T, n_], [1, 512]]),
                    ).then_inc(ctr.h, 16)

            sSY.n += 16
            E("sync", _ld)
            W("tensor", sSY)
            for dc in range(8):
                for cc in range(ncc):
                    EM("tensor",
                       lambda e, d=dc, c=cc, w=WB, n_=ncc, ab=AVBUF:
                       e.matmul(
                           psA[:, 512 * (d % 4):512 * (d % 4 + 1)],
                           w[:, 1024 * c + 128 * d:1024 * c + 128 * (d + 1)],
                           bass.AP(ab, 512 * c, [[4096, 128], [1, 512]]),
                           start=(c == 0), stop=(c == n_ - 1)), inc=sPE)
                W("vector", sPE)
                EM("vector",
                   lambda e, d=dc, t=tj: e.tensor_add(
                       X[:, 2048 * d + 512 * t:2048 * d + 512 * (t + 1)],
                       X[:, 2048 * d + 512 * t:2048 * d + 512 * (t + 1)],
                       psA[:, 512 * (d % 4):512 * (d % 4 + 1)]), inc=sDV)
                W("tensor", sDV)

    outproj(av_allg, WOB, 8, AVRH)
    ar_to(mA)
    BARRIER()
    TAP(0)

    def emit_ffn(tag):
        mF = ar_mark()
        XN2 = ar("XN2" + tag, [128, 8 * 2048], BF16)
        HB = ar("HB" + tag, [128, NJ * 1024], BF16)
        WDB = ar("WDB" + tag, [128, (NJ // 2) * 1024], BF16)
        WGB = ar("WGB" + tag, [128, 2048], BF16)
        WUB = ar("WUB" + tag, [128, 2048], BF16)
        SILU = ar("SILU" + tag, [128, 512], BF16)

        emit_ln(sink_sb=XN2)

        def load_wd(jh):
            for j4 in range(0, NJ // 2, 4):
                DMA("gpsimd",
                    bass.AP(WDB, 1024 * j4,
                            [[(NJ // 2) * 1024, 128], [1024, 4], [1, 1024]]),
                    bass.AP(wdT, (128 * (j4 + jh * (NJ // 2))) * D,
                            [[D, 128], [128 * D, 4], [1, D]]))

        def load_gu(j, slot):
            DMA("gpsimd",
                bass.AP(WGB, 1024 * slot, [[2048, 128], [128, 8], [1, 128]]),
                bass.AP(wgT, 128 * j, [[FFH, 128], [128 * FFH, 8],
                                       [1, 128]]))
            DMA("gpsimd",
                bass.AP(WUB, 1024 * slot, [[2048, 128], [128, 8], [1, 128]]),
                bass.AP(wuT, 128 * j, [[FFH, 128], [128 * FFH, 8],
                                       [1, 128]]))

        for half in range(2):
            hoff = 1024 * half
            W("gpsimd", sPE)
            load_gu(0, 0)
            for j in range(NJ):
                slot = j % 2
                if j < NJ - 1:
                    W("gpsimd", sPE)
                    load_gu(j + 1, (j + 1) % 2)
                W("tensor", sG, sG.n - (32 if j < NJ - 1 else 0))
                if half == 1 and j == 0:
                    W("vector", sPE)
                for tl in range(2):
                    W("tensor", sAC)
                    W("tensor", sDV)
                    for cc in range(8):
                        EM("tensor",
                           lambda e, c=cc, s=slot, t=tl, ho=hoff: e.matmul(
                               ps1[:, :],
                               bass.AP(WGB, 1024 * s + 128 * c,
                                       [[2048, 128], [1, 128]]),
                               bass.AP(XN2, 2048 * c + ho + 512 * t,
                                       [[8 * 2048, 128], [1, 512]]),
                               start=(c == 0), stop=(c == 7)), inc=sPE)
                        EM("tensor",
                           lambda e, c=cc, s=slot, t=tl, ho=hoff: e.matmul(
                               ps2[:, :],
                               bass.AP(WUB, 1024 * s + 128 * c,
                                       [[2048, 128], [1, 128]]),
                               bass.AP(XN2, 2048 * c + ho + 512 * t,
                                       [[8 * 2048, 128], [1, 512]]),
                               start=(c == 0), stop=(c == 7)), inc=sPE)
                    W("scalar", sPE)
                    EM("scalar",
                       lambda e: e.activation(SILU[:, :], ps1[:, :], AF.Silu,
                                              bias=0.0, scale=INV127),
                       inc=sAC)
                    W("vector", sAC)
                    EM("vector",
                       lambda e, jj=j, t=tl: e.tensor_mul(
                           HB[:, 1024 * jj + 512 * t:
                              1024 * jj + 512 * (t + 1)],
                           SILU[:, :], ps2[:, :]), inc=sDV)
            W("tensor", sDV)
            for jh in range(2):
                W("gpsimd", sPE)
                load_wd(jh)
                W("tensor", sG)
                for tl in range(2):
                    for dc in range(8):
                        bank = ps3 if dc % 2 == 0 else psA
                        for j in range(NJ // 2):
                            EM("tensor",
                               lambda e, d=dc, jj=j, t=tl, b=bank, jh_=jh:
                               e.matmul(
                                   b[:, 0:512],
                                   bass.AP(WDB, 1024 * jj + 128 * d,
                                           [[(NJ // 2) * 1024, 128],
                                            [1, 128]]),
                                   HB[:, 1024 * (jj + jh_ * (NJ // 2))
                                      + 512 * t:
                                      1024 * (jj + jh_ * (NJ // 2))
                                      + 512 * (t + 1)],
                                   start=(jj == 0), stop=(jj == NJ // 2 - 1)),
                               inc=sPE)
                        W("vector", sPE)
                        EM("vector",
                           lambda e, d=dc, t=tl, ho=hoff, b=bank:
                           e.scalar_tensor_tensor(
                               X[:, 2048 * d + ho + 512 * t:
                                 2048 * d + ho + 512 * (t + 1)],
                               b[:, 0:512], INV127 * INV127,
                               X[:, 2048 * d + ho + 512 * t:
                                 2048 * d + ho + 512 * (t + 1)],
                               ALU.mult, ALU.add), inc=sDV)
                        W("tensor", sDV)
        ar_to(mF)
        BARRIER()

    emit_ffn("f1")
    TAP(1)

    mB = ar_mark()
    ROPEC = ar("ROPEC", [128, NT * 64], BF16)
    ROPES = ar("ROPES", [128, NT * 64], BF16)
    NQT = ar("NQT", [96, T], BF16)
    CKVT = ar("CKVT", [32, T], BF16)
    KRN = ar("KRN", [128, NT * 64], BF16)
    KST = ar("KST", [96, HPC * 256], BF16)
    VSA = ar("VSA", [128, 2 * 264], BF16)
    KSN = ar("KSN", [128, 2 * 768], BF16)
    WDQN = ar("WDQN", [96, 256], BF16)
    WDQR = ar("WDQR", [96, 512], BF16)
    WDKN = ar("WDKN", [32, 256], BF16)
    WDVB = ar("WDVB", [32, 256], BF16)
    GQC = ar("GQC", [96, 1], F32)
    GKVC = ar("GKVC", [32, 1], F32)
    T1B = ar("T1B", [128, 512], BF16)
    T2B = ar("T2B", [128, 512], BF16)
    T1Bb = ar("T1Bb", [128, 512], BF16)
    T2Bb = ar("T2Bb", [128, 512], BF16)
    RMSO = ar("RMSO", [128, 128], BF16)
    RMSOb = ar("RMSOb", [128, 128], BF16)
    KR1 = ar("KR1", [128, 64], BF16)
    KR2 = ar("KR2", [128, 128], BF16)
    NATF = ar("NATF", [128, 128], F32)
    SQF = ar("SQF", [128, 128], F32)
    mB2 = ar_mark()
    XNB3 = ar("XNB3", [128, 4096], BF16)
    WCQ = ar("WCQ", [128, 768], BF16)
    WCKV = ar("WCKV", [128, 256], BF16)
    WKRO = ar("WKRO", [128, 512], BF16)
    WWK = ar("WWK", [128, 8 * 768], BF16)
    WWV = ar("WWV", [128, 8 * 256], BF16)
    WIMP = ar("WIMP", [128, 8], BF16)
    WGT3 = ar("WGT3", [128, 24], BF16)
    KWN = ar("KWN", [128, 768], BF16)
    KWNb = ar("KWNb", [128, 768], BF16)
    KWSTG = ar("KWSTG", [96, 1024], BF16)
    VST = ar("VST", [128, 264], BF16)
    XMF = ar("XMF", [128, 16], F32)
    XMB = ar("XMB", [128, 8], BF16)
    GROW = ar("GROW", [1, 16], F32)
    SSC = ar("SSC", [128, 32], F32)
    JUNK = ar("JUNK", [128, 32], F32)
    CUMA = ar("CUMA", [128, 32], F32)
    CUMC = ar("CUMC", [128, 32], F32)
    JUNKB = ar("JUNKB", [128, 32], BF16)
    CP1 = ar("CP1", [1, 32], F32)
    CP2 = ar("CP2", [1, 32], F32)
    PTS = ar("PTS", [128, NT * 256], BF16)
    KFT = ar("KFT", [128, 768], BF16)
    WSK = ar("WSK", [128, 8 * 768], BF16)

    emit_ln(sink_sb=None)
    cc_xn3 = sCC.n + 1
    W("gpsimd", sSY)
    exchange(cin_half, xn_allg)
    DMA("sync", ROPEC[:, :], s_ropec[:, :])
    DMA("sync", ROPES[:, :], s_ropes[:, :])
    DMA("gpsimd", WCQ[:, :],
        bass.AP(wcq, 0, [[96, 128], [128 * 96, 8], [1, 96]]))
    DMA("gpsimd", WCKV[:, :],
        bass.AP(wckv, 0, [[32, 128], [128 * 32, 8], [1, 32]]))
    DMA("gpsimd", WKRO[:, :],
        bass.AP(wkro, 0, [[64, 128], [128 * 64, 8], [1, 64]]))
    DMA("gpsimd", WDQN[:, :], wdqn[:, :])
    DMA("gpsimd", WDQR[:, :], wdqr[:, :])
    DMA("gpsimd", WDKN[:, :], wdkn[:, :])
    DMA("gpsimd", WDVB[:, :], wdv[:, :])
    DMA("gpsimd", WWK[:, :],
        bass.AP(wwink, 0, [[768, 128], [128 * 768, 8], [1, 768]]))
    DMA("gpsimd", WWV[:, :],
        bass.AP(wwinv, 0, [[256, 128], [128 * 256, 8], [1, 256]]))
    DMA("gpsimd", WIMP[:, :],
        bass.AP(wimp, 0, [[1, 128], [128, 8]]), allow_slow=True)
    DMA("gpsimd", WGT3[:, :],
        bass.AP(wgate, 0, [[3, 128], [128 * 3, 8], [1, 3]]))
    DMA("gpsimd", WSK[:, :],
        bass.AP(wselk, 0, [[768, 128], [128 * 768, 8], [1, 768]]))
    DMA("sync", GQC[:, :], gq[:, :])
    DMA("sync", GKVC[:, :], gkv[:, :])
    W("vector", sG)
    W("vector", sSY)
    EM("vector", lambda e: e.tensor_scalar(
        WDQN[:, :], WDQN[:, :], GQC[0:96, 0:1], None, ALU.mult), inc=sDV)
    EM("vector", lambda e: e.tensor_scalar(
        WDQR[:, :], WDQR[:, :], GQC[0:96, 0:1], None, ALU.mult), inc=sDV)
    EM("vector", lambda e: e.tensor_scalar(
        WDKN[:, :], WDKN[:, :], GKVC[0:32, 0:1], None, ALU.mult), inc=sDV)
    EM("vector", lambda e: e.tensor_scalar(
        WDVB[:, :], WDVB[:, :], GKVC[0:32, 0:1], None, ALU.mult), inc=sDV)
    EM("vector", lambda e: e.memset(XMF[:, 0:8], 0.0), inc=sDV)
    W("sync", sCC, cc_xn3)

    def load_xn_tile(tt, slot):
        hb = tt // 16
        ct = tt % 16
        DMA("sync",
            bass.AP(XNB3, 2048 * slot, [[4096, 128], [128, 8], [1, 128]]),
            bass.AP(xn_allg, (1024 * hb) * TH + 128 * ct,
                    [[TH, 128], [128 * TH, 8], [1, 128]]))

    def xnb3(slot, cc):
        return bass.AP(XNB3, 2048 * slot + 128 * cc, [[4096, 128], [1, 128]])

    TRS = [(T1B, T2B, KWN, RMSO), (T1Bb, T2Bb, KWNb, RMSOb)]

    def p1_tr_block(t, gate, kw_spill_prev):
        T1x, T2x, KWx, RMx = TRS[t % 2]
        W("tensor", sDV, gate)
        EM("tensor", lambda e, R=RMx: e.transpose(
            psT[0:96, 0:128], R[:, 32:128], IDENT[:, :]), inc=sPE)
        EM("tensor", lambda e, R=RMx: e.transpose(
            psT[0:32, 128:256], R[:, 0:32], IDENT[:, :]), inc=sPE)
        EM("tensor", lambda e, K=KWx: e.transpose(
            psT[0:96, 256:384], K[:, 0:96], IDENT[:, :]), inc=sPE)
        EM("tensor", lambda e, K=KWx: e.transpose(
            psT[0:96, 384:512], K[:, 96:192], IDENT[:, :]), inc=sPE)
        W("vector", sPE)
        if kw_spill_prev is not None:
            W("vector", sSY, kw_spill_prev)
        EM("vector", lambda e, t=t: e.tensor_copy(
            NQT[0:96, 128 * t:128 * (t + 1)], psT[0:96, 0:128]), inc=sDV)
        EM("vector", lambda e, t=t: e.tensor_copy(
            CKVT[0:32, 128 * t:128 * (t + 1)], psT[0:32, 128:256]), inc=sDV)
        EM("vector", lambda e: e.tensor_copy(
            KWSTG[0:96, 0:128], psT[0:96, 256:384]), inc=sDV)
        EM("vector", lambda e: e.tensor_copy(
            KWSTG[0:96, 128:256], psT[0:96, 384:512]), inc=sDV)
        for rnd, hs in ((1, (2, 3, 4, 5)), (2, (6, 7))):
            W("tensor", sDV)
            for j, hh in enumerate(hs):
                EM("tensor", lambda e, K=KWx, h_=hh, j=j: e.transpose(
                    psT[0:96, 128 * j:128 * (j + 1)],
                    K[:, 96 * h_:96 * (h_ + 1)], IDENT[:, :]), inc=sPE)
            W("vector", sPE)
            for j, hh in enumerate(hs):
                EM("vector", lambda e, h_=hh, j=j: e.tensor_copy(
                    KWSTG[0:96, 128 * h_:128 * (h_ + 1)],
                    psT[0:96, 128 * j:128 * (j + 1)]), inc=sDV)
        W("sync", sDV)
        DMA("sync",
            bass.AP(kwT_scr, 128 * t, [[T, 96], [96 * T, 8], [1, 128]]),
            KWSTG[0:96, :])
        return sSY.n

    readers_done = sDV.n
    pend_tr = None
    kw_spill = None
    vst_spill = None
    W("sync", sPE)
    load_xn_tile(0, 0)
    for tt in range(NT):
        slot = tt % 2
        T1x, T2x, KWx, RMx = TRS[tt % 2]
        if tt < NT - 1:
            W("sync", sPE)
            xn_base = sSY.n
            load_xn_tile(tt + 1, (tt + 1) % 2)
            W("tensor", sSY, xn_base)
        else:
            W("tensor", sSY)
        W("tensor", sDV, readers_done)
        for cc in range(8):
            EM("tensor", lambda e, c=cc, s=slot: e.matmul(
                ps1[:, 0:96], xnb3(s, c),
                bass.AP(WCQ, 96 * c, [[768, 128], [1, 96]]),
                start=(c == 0), stop=(c == 7)), inc=sPE)
            EM("tensor", lambda e, c=cc, s=slot: e.matmul(
                ps2[:, 0:32], xnb3(s, c),
                bass.AP(WCKV, 32 * c, [[256, 128], [1, 32]]),
                start=(c == 0), stop=(c == 7)), inc=sPE)
            EM("tensor", lambda e, c=cc, s=slot: e.matmul(
                psA[:, 1536:1600], xnb3(s, c),
                bass.AP(WKRO, 64 * c, [[512, 128], [1, 64]]),
                start=(c == 0), stop=(c == 7)), inc=sPE)
            EM("tensor", lambda e, c=cc, s=slot: e.matmul(
                psA[:, 0:512], xnb3(s, c),
                bass.AP(WWK, 768 * c, [[8 * 768, 128], [1, 512]]),
                start=(c == 0), stop=(c == 7)), inc=sPE)
            EM("tensor", lambda e, c=cc, s=slot: e.matmul(
                psA[:, 512:768], xnb3(s, c),
                bass.AP(WWK, 768 * c + 512, [[8 * 768, 128], [1, 256]]),
                start=(c == 0), stop=(c == 7)), inc=sPE)
            EM("tensor", lambda e, c=cc, s=slot: e.matmul(
                psA[:, 1024:1280], xnb3(s, c),
                bass.AP(WWV, 256 * c, [[8 * 256, 128], [1, 256]]),
                start=(c == 0), stop=(c == 7)), inc=sPE)
            EM("tensor", lambda e, c=cc, s=slot, t=tt: e.matmul(
                ps3[:, t:t + 1], xnb3(s, c), WIMP[:, c:c + 1],
                start=(c == 0), stop=(c == 7)), inc=sPE)
        if pend_tr is not None:
            kw_spill = p1_tr_block(tt - 1, pend_tr, kw_spill)
        W("vector", sPE)
        for cc in range(8):
            EM("vector", lambda e, c=cc, s=slot: e.reduce_sum(
                XMF[:, 8 + c:9 + c], xnb3(s, c),
                axis=mybir.AxisListType.X), inc=sDV)
        W("vector", sDV)
        EM("vector", lambda e: e.tensor_add(
            XMF[:, 0:8], XMF[:, 0:8], XMF[:, 8:16]), inc=sDV)
        EM("vector", lambda e: e.tensor_copy(NATF[:, 0:96], ps1[:, 0:96]),
           inc=sDV)
        EM("vector", lambda e: e.tensor_copy(NATF[:, 96:128], ps2[:, 0:32]),
           inc=sDV)
        EM("vector", lambda e: e.tensor_scalar(
            KR1[:, :], psA[:, 1536:1600], 1.0 / 16.0, None, ALU.mult),
           inc=sDV)
        EM("vector", lambda e, K=KWx: e.tensor_copy(K[:, :], psA[:, 0:768]),
           inc=sDV)
        if vst_spill is not None:
            W("vector", sSY, vst_spill)
        EM("vector", lambda e: e.memset(VST[:, :], 1.0), inc=sDV)
        W("vector", sDV)
        EM("vector", lambda e: e.tensor_copy(
            bass.AP(VST, 0, [[264, 128], [33, 8], [1, 32]]),
            bass.AP(psA, 1024, [[2048, 128], [32, 8], [1, 32]])), inc=sDV)
        readers_done = sDV.n
        W("sync", sDV)
        DMA("sync",
            bass.AP(vw_scr, (128 * tt) * 33,
                    [[33, 128], [T * 33, 8], [1, 33]]),
            VST[:, :])
        vst_spill = sSY.n
        EM("vector", lambda e: e.tensor_mul(SQF[:, 0:128], NATF[:, 0:128],
                                            NATF[:, 0:128]), inc=sDV)
        W("vector", sDV)
        EM("vector", lambda e: e.reduce_sum(
            MISC[:, 0:1], SQF[:, 0:96], axis=mybir.AxisListType.X), inc=sDV)
        EM("vector", lambda e: e.reduce_sum(
            MISC[:, 4:5], SQF[:, 96:128], axis=mybir.AxisListType.X),
           inc=sDV)
        W("vector", sDV)
        EM("vector", lambda e: e.tensor_scalar(
            MISC[:, 1:2], MISC[:, 0:1], 1.0 / 96, 1e-6, ALU.mult,
            op1=ALU.add), inc=sDV)
        EM("vector", lambda e: e.tensor_scalar(
            MISC[:, 5:6], MISC[:, 4:5], 1.0 / 32, 1e-6, ALU.mult,
            op1=ALU.add), inc=sDV)
        W("scalar", sDV)
        EM("scalar", lambda e: e.activation(
            MISC[:, 2:3], MISC[:, 1:2], AF.Ln, bias=0.0, scale=1.0),
           inc=sAC)
        EM("scalar", lambda e: e.activation(
            MISC[:, 6:7], MISC[:, 5:6], AF.Ln, bias=0.0, scale=1.0),
           inc=sAC)
        EM("scalar", lambda e: e.activation(
            MISC[:, 3:4], MISC[:, 2:3], AF.Exp, bias=0.0, scale=-0.5),
           inc=sAC)
        EM("scalar", lambda e: e.activation(
            MISC[:, 7:8], MISC[:, 6:7], AF.Exp, bias=0.0, scale=-0.5),
           inc=sAC)
        W("vector", sAC)
        EM("vector", lambda e, R=RMx: e.tensor_scalar(
            R[:, 32:128], NATF[:, 0:96], MISC[:, 3:4], None, ALU.mult),
           inc=sDV)
        EM("vector", lambda e, R=RMx: e.tensor_scalar(
            R[:, 0:32], NATF[:, 96:128], MISC[:, 7:8], None, ALU.mult),
           inc=sDV)
        EM("vector", lambda e, t=tt: e.tensor_mul(
            KR2[:, 0:64], KR1[:, :],
            bass.AP(ROPEC, 64 * t, [[NT * 64, 128], [1, 64]])), inc=sDV)
        EM("vector", lambda e, t=tt: e.tensor_mul(
            KR2[:, 64:128],
            bass.AP(KR1, 32, [[64, 128], [-32, 2], [1, 32]]),
            bass.AP(ROPES, 64 * t, [[NT * 64, 128], [1, 64]])), inc=sDV)
        EM("vector", lambda e, t=tt, K=KWx, T1x=T1x: e.tensor_mul(
            bass.AP(T1x, 128, [[512, 128], [64, 6], [1, 64]]),
            bass.AP(K, 32 + 2 * 96, [[768, 128], [96, 6], [1, 64]]),
            bass.AP(ROPEC, 64 * t, [[NT * 64, 128], [0, 6], [1, 64]])),
           inc=sDV)
        EM("vector", lambda e, t=tt: e.tensor_add(
            KRN[:, 64 * t:64 * (t + 1)], KR2[:, 0:64], KR2[:, 64:128]),
           inc=sDV)
        EM("vector", lambda e, t=tt, K=KWx, T2x=T2x: e.tensor_mul(
            bass.AP(T2x, 0, [[512, 128], [64, 8], [1, 64]]),
            bass.AP(K, 64, [[768, 128], [96, 8], [-32, 2], [1, 32]]),
            bass.AP(ROPES, 64 * t, [[NT * 64, 128], [0, 8], [1, 64]])),
           inc=sDV)
        EM("vector", lambda e, t=tt, K=KWx, T1x=T1x: e.tensor_mul(
            bass.AP(T1x, 0, [[512, 128], [64, 2], [1, 64]]),
            bass.AP(K, 32, [[768, 128], [96, 2], [1, 64]]),
            bass.AP(ROPEC, 64 * t, [[NT * 64, 128], [0, 2], [1, 64]])),
           inc=sDV)
        W("vector", sDV)
        EM("vector", lambda e, K=KWx, T1x=T1x, T2x=T2x: e.tensor_add(
            bass.AP(K, 32, [[768, 128], [96, 8], [1, 64]]),
            bass.AP(T1x, 0, [[512, 128], [64, 8], [1, 64]]),
            bass.AP(T2x, 0, [[512, 128], [64, 8], [1, 64]])), inc=sDV)
        pend_tr = sDV.n

    kw_spill = p1_tr_block(NT - 1, pend_tr, kw_spill)
    W("tensor", sDV)

    for tt in range(NT):
        EM("tensor", lambda e, t=tt: e.matmul(
            ps1[:, 0:256], CKVT[0:32, 128 * t:128 * (t + 1)], WDVB[0:32, :],
            start=True, stop=True), inc=sPE)
        W("vector", sPE)
        W("vector", sSY)
        EM("vector", lambda e: e.memset(VST[:, :], 1.0), inc=sDV)
        W("vector", sDV)
        EM("vector", lambda e: e.tensor_copy(
            bass.AP(VST, 0, [[264, 128], [33, 8], [1, 32]]),
            bass.AP(ps1, 0, [[512, 128], [32, 8], [1, 32]])), inc=sDV)
        W("sync", sDV)
        DMA("sync",
            bass.AP(v1_scr, (128 * tt) * 33,
                    [[33, 128], [T * 33, 8], [1, 33]]),
            VST[:, :])
        W("tensor", sSY)

    EM("vector", lambda e: e.tensor_scalar(
        XMB[:, 0:8], XMF[:, 0:8], 1.0 / T, None, ALU.mult), inc=sDV)
    W("tensor", sDV)
    for cc in range(8):
        EM("tensor", lambda e, c=cc: e.matmul(
            ps2[0:1, 0:3], XMB[:, c:c + 1], WGT3[:, 3 * c:3 * (c + 1)],
            start=(c == 0), stop=(c == 7)), inc=sPE)
    W("vector", sPE)
    EM("vector", lambda e: e.tensor_copy(GROW[0:1, 0:3], ps2[0:1, 0:3]),
       inc=sDV)
    W("scalar", sDV)
    EM("scalar", lambda e: e.activation(
        GROW[0:1, 3:6], GROW[0:1, 0:3], AF.Exp, bias=0.0, scale=1.0),
       inc=sAC)
    W("vector", sAC)
    EM("vector", lambda e: e.reduce_sum(
        GROW[0:1, 6:7], GROW[0:1, 3:6], axis=mybir.AxisListType.X), inc=sDV)
    W("vector", sDV)
    EM("vector", lambda e: e.reciprocal(GROW[0:1, 7:8], GROW[0:1, 6:7]),
       inc=sDV)
    W("vector", sDV)
    EM("vector", lambda e: e.tensor_scalar(
        GROW[0:1, 3:6], GROW[0:1, 3:6], GROW[0:1, 7:8], None, ALU.mult),
       inc=sDV)
    W("tensor", sDV)
    EM("tensor", lambda e: e.matmul(
        ps2[:, 8:11], ONESF[0:1, :], GROW[0:1, 3:6], start=True, stop=True),
       inc=sPE)
    W("vector", sPE)
    EM("vector", lambda e: e.tensor_copy(CONST[:, 32:35], ps2[:, 8:11]),
       inc=sDV)

    EM("vector", lambda e: e.tensor_copy(SSC[:, :], ps3[:, 0:32]), inc=sDV)
    EM("vector", lambda e: e.memset(MISC[:, 0:1], -1000.0), inc=sDV)
    EM("vector", lambda e: e.memset(MISC[:, 1:2], 1000.0), inc=sDV)
    EM("vector", lambda e: e.memset(MISC[:, 2:3], 0.0), inc=sDV)
    for it in range(NBIS):
        W("vector", sPE)
        EM("vector", lambda e: e.tensor_scalar(
            JUNK[:, :], SSC[:, :], MISC[:, 2:3], 0.0, ALU.is_gt,
            op1=ALU.add, accum_out=MISC[:, 6:7]), inc=sDV)
        W("tensor", sDV)
        EM("tensor", lambda e: e.matmul(
            ps2[0:1, 16:17], ONESF[:, 0:1], MISC[:, 6:7],
            start=True, stop=True), inc=sPE)
        W("vector", sPE)
        EM("vector", lambda e: e.tensor_copy(GROW[0:1, 8:9],
                                             ps2[0:1, 16:17]), inc=sDV)
        W("tensor", sDV)
        EM("tensor", lambda e: e.matmul(
            ps2[:, 17:18], ONESF[0:1, :], GROW[0:1, 8:9],
            start=True, stop=True), inc=sPE)
        W("vector", sPE)
        EM("vector", lambda e: e.tensor_scalar(
            MISC[:, 3:4], ps2[:, 17:18], float(KEEP) - 0.5, None, ALU.is_gt),
           inc=sDV)
        EM("vector", lambda e: e.tensor_sub(
            MISC[:, 4:5], MISC[:, 2:3], MISC[:, 0:1]), inc=sDV)
        EM("vector", lambda e: e.tensor_sub(
            MISC[:, 8:9], MISC[:, 1:2], MISC[:, 2:3]), inc=sDV)
        EM("vector", lambda e: e.tensor_scalar(
            MISC[:, 5:6], MISC[:, 3:4], -1.0, 1.0, ALU.mult, op1=ALU.add),
           inc=sDV)
        W("vector", sDV)
        EM("vector", lambda e: e.tensor_mul(
            MISC[:, 4:5], MISC[:, 4:5], MISC[:, 3:4]), inc=sDV)
        EM("vector", lambda e: e.tensor_mul(
            MISC[:, 8:9], MISC[:, 8:9], MISC[:, 5:6]), inc=sDV)
        W("vector", sDV)
        EM("vector", lambda e: e.tensor_add(
            MISC[:, 0:1], MISC[:, 0:1], MISC[:, 4:5]), inc=sDV)
        EM("vector", lambda e: e.tensor_sub(
            MISC[:, 1:2], MISC[:, 1:2], MISC[:, 8:9]), inc=sDV)
        W("vector", sDV)
        EM("vector", lambda e: e.tensor_add(
            MISC[:, 2:3], MISC[:, 0:1], MISC[:, 1:2]), inc=sDV)
        W("vector", sDV)
        EM("vector", lambda e: e.tensor_scalar(
            MISC[:, 2:3], MISC[:, 2:3], 0.5, None, ALU.mult), inc=sDV)
        W("vector", sDV)
    W("vector", sDV)
    EM("vector", lambda e: e.tensor_scalar(
        JUNK[:, :], SSC[:, :], MISC[:, 0:1], None, ALU.is_gt), inc=sDV)
    W("vector", sDV)
    EM("vector", lambda e: e.tensor_copy(JUNKB[:, :], JUNK[:, :]), inc=sDV)
    W("tensor", sDV)
    EM("tensor", lambda e: e.matmul(
        ps1[:, 0:32], TRILS[:, :], JUNKB[:, :], start=True, stop=False),
       inc=sPE)
    EM("tensor", lambda e: e.matmul(
        ps1[:, 0:32], IDENT[:, :], JUNKB[:, :], start=False, stop=True),
       inc=sPE)
    EM("tensor", lambda e: e.matmul(
        ps2[0:1, 0:32], ONESB[:, 0:1], JUNKB[:, :], start=True, stop=True),
       inc=sPE)
    W("vector", sPE)
    EM("vector", lambda e, A=CUMA: e.tensor_copy(A[:, :], ps1[:, 0:32]),
       inc=sDV)
    EM("vector", lambda e: e.memset(CP1[0:1, 0:1], 0.0), inc=sDV)
    EM("vector", lambda e: e.tensor_copy(CP1[0:1, 1:32], ps2[0:1, 0:31]),
       inc=sDV)
    cpa, cpc = CP1, CP2
    for k in (1, 2, 4, 8, 16):
        W("vector", sDV)
        EM("vector", lambda e, kk=k, A=cpa, C=cpc: e.tensor_copy(
            C[0:1, 0:kk], A[0:1, 0:kk]), inc=sDV)
        EM("vector", lambda e, kk=k, A=cpa, C=cpc: e.tensor_add(
            C[0:1, kk:32], A[0:1, kk:32], A[0:1, 0:32 - kk]), inc=sDV)
        cpa, cpc = cpc, cpa
    W("tensor", sDV)
    EM("tensor", lambda e, A=cpa: e.matmul(
        ps3[:, 0:32], ONESF[0:1, :], A[0:1, :], start=True, stop=True),
       inc=sPE)
    W("vector", sPE)
    EM("vector", lambda e, A=CUMA, C=CUMC: e.tensor_add(
        C[:, :], A[:, :], ps3[:, 0:32]), inc=sDV)
    W("vector", sDV)
    EM("vector", lambda e, C=CUMC: e.tensor_mul(
        C[:, :], C[:, :], JUNK[:, :]), inc=sDV)
    W("vector", sDV)
    for tt in range(NT):
        EM("vector", lambda e, t=tt, C=CUMC: e.tensor_scalar(
            PTS[:, 256 * t:256 * (t + 1)], IOTA[:, :], C[:, t:t + 1],
            None, ALU.is_equal), inc=sDV)
    W("tensor", sDV)
    if _TAPS:
        W("sync", sDV)
        DMA("sync", dbg_pts[:, :], PTS[:, :])
        DMA("sync", dbg_junk[:, :], JUNK[:, :])
        DMA("sync", dbg_cum[:, :], CUMA[:, :])
        DMA("sync", dbg_cumf[:, :], CUMC[:, :])
        DMA("sync", dbg_ssc[:, :], SSC[:, :])

    W("sync", sPE)
    load_xn_tile(0, 0)
    for tt in range(NT):
        slot = tt % 2
        if tt < NT - 1:
            W("sync", sPE)
            load_xn_tile(tt + 1, (tt + 1) % 2)
        W("tensor", sSY, sSY.n - (16 if tt < NT - 1 else 0))
        for cc in range(8):
            EM("tensor", lambda e, c=cc, s=slot: e.matmul(
                psA[:, 0:512], xnb3(s, c),
                bass.AP(WSK, 768 * c, [[8 * 768, 128], [1, 512]]),
                start=(c == 0), stop=(c == 7)), inc=sPE)
            EM("tensor", lambda e, c=cc, s=slot: e.matmul(
                psA[:, 512:768], xnb3(s, c),
                bass.AP(WSK, 768 * c + 512, [[8 * 768, 128], [1, 256]]),
                start=(c == 0), stop=(c == 7)), inc=sPE)
        W("vector", sPE)
        EM("vector", lambda e: e.tensor_copy(KFT[:, :], psA[:, 0:768]),
           inc=sDV)
        W("tensor", sDV)
        for r in range(2):
            EM("tensor", lambda e, t=tt, rr=r: e.matmul(
                ps1[:, :] if rr == 0 else ps2[:, :],
                PTS[:, 256 * t + 128 * rr:256 * t + 128 * (rr + 1)],
                KFT[:, 0:512], start=(t == 0), stop=(t == NT - 1)), inc=sPE)
            EM("tensor", lambda e, t=tt, rr=r: e.matmul(
                ps3[:, 0:256] if rr == 0 else psA[:, 1536:1792],
                PTS[:, 256 * t + 128 * rr:256 * t + 128 * (rr + 1)],
                KFT[:, 512:768], start=(t == 0), stop=(t == NT - 1)),
               inc=sPE)
    W("vector", sPE)
    for r in range(2):
        EM("vector", lambda e, rr=r: e.tensor_copy(
            KSN[:, 768 * rr:768 * rr + 512],
            ps1[:, :] if rr == 0 else ps2[:, :]), inc=sDV)
        EM("vector", lambda e, rr=r: e.tensor_copy(
            KSN[:, 768 * rr + 512:768 * (rr + 1)],
            ps3[:, 0:256] if rr == 0 else psA[:, 1536:1792]), inc=sDV)
    for r in range(2):
        EM("vector", lambda e, rr=r: e.tensor_mul(
            bass.AP(T1B, 0, [[512, 128], [64, 8], [1, 64]]),
            bass.AP(KSN, 768 * rr + 32, [[2 * 768, 128], [96, 8], [1, 64]]),
            bass.AP(ROPEC, 64 * rr, [[NT * 64, 128], [0, 8], [1, 64]])),
           inc=sDV)
        EM("vector", lambda e, rr=r: e.tensor_mul(
            bass.AP(T2B, 0, [[512, 128], [64, 8], [1, 64]]),
            bass.AP(KSN, 768 * rr + 64, [[2 * 768, 128], [96, 8],
                                         [-32, 2], [1, 32]]),
            bass.AP(ROPES, 64 * rr, [[NT * 64, 128], [0, 8], [1, 64]])),
           inc=sDV)
        W("vector", sDV)
        EM("vector", lambda e, rr=r: e.tensor_add(
            bass.AP(KSN, 768 * rr + 32, [[2 * 768, 128], [96, 8], [1, 64]]),
            bass.AP(T1B, 0, [[512, 128], [64, 8], [1, 64]]),
            bass.AP(T2B, 0, [[512, 128], [64, 8], [1, 64]])), inc=sDV)
    W("tensor", sDV)
    for hh in range(HPC):
        for r in range(2):
            EM("tensor", lambda e, h_=hh, rr=r: e.transpose(
                psT[0:96, 0:128],
                KSN[:, 768 * rr + 96 * h_:768 * rr + 96 * (h_ + 1)],
                IDENT[:, :]), inc=sPE)
            W("vector", sPE)
            EM("vector", lambda e, h_=hh, rr=r: e.tensor_copy(
                KST[0:96, 256 * h_ + 128 * rr:256 * h_ + 128 * (rr + 1)],
                psT[0:96, 0:128]), inc=sDV)
            W("tensor", sDV)

    W("gpsimd", sPE)
    DMA("gpsimd", WWV[:, :],
        bass.AP(wselv, 0, [[256, 128], [128 * 256, 8], [1, 256]]))
    W("tensor", sG)
    W("sync", sPE)
    load_xn_tile(0, 0)
    for tt in range(NT):
        slot = tt % 2
        if tt < NT - 1:
            W("sync", sPE)
            load_xn_tile(tt + 1, (tt + 1) % 2)
        W("tensor", sSY, sSY.n - (16 if tt < NT - 1 else 0))
        for cc in range(8):
            EM("tensor", lambda e, c=cc, s=slot: e.matmul(
                psA[:, 0:256], xnb3(s, c),
                bass.AP(WWV, 256 * c, [[8 * 256, 128], [1, 256]]),
                start=(c == 0), stop=(c == 7)), inc=sPE)
        W("vector", sPE)
        EM("vector", lambda e: e.tensor_copy(KFT[:, 0:256], psA[:, 0:256]),
           inc=sDV)
        W("tensor", sDV)
        for r in range(2):
            EM("tensor", lambda e, t=tt, rr=r: e.matmul(
                ps1[:, 0:256] if rr == 0 else ps2[:, 0:256],
                PTS[:, 256 * t + 128 * rr:256 * t + 128 * (rr + 1)],
                KFT[:, 0:256], start=(t == 0), stop=(t == NT - 1)), inc=sPE)
    W("vector", sPE)
    EM("vector", lambda e: e.memset(VSA[:, :], 1.0), inc=sDV)
    W("vector", sDV)
    for r in range(2):
        EM("vector", lambda e, rr=r: e.tensor_copy(
            bass.AP(VSA, 264 * rr, [[2 * 264, 128], [33, 8], [1, 32]]),
            bass.AP(ps1 if rr == 0 else ps2, 0,
                    [[512, 128], [32, 8], [1, 32]])), inc=sDV)
    W("tensor", sDV)

    ar_to(mB2)
    MASKS = ar("MASKS", [128, 2048], BF16)
    QNH = ar("QNH", [128, NT * 96], BF16)
    K1NH = ar("K1NH", [128, NT * 96], BF16)
    QTH = ar("QTH", [96, T], BF16)
    K1TH = ar("K1TH", [96, T], BF16)
    KWTH = ar("KWTH", [96, T], BF16)
    V1A = ar("V1A", [128, NT * 33], BF16)
    VWA = ar("VWA", [128, NT * 33], BF16)
    EXPB = ar("EXPB", [128, 4096], BF16)
    AEVF3 = ar("AEVF3", [128, 3 * 512], F32)
    RCP3 = ar("RCP3", [128, 8], F32)
    COMB3 = ar("COMB3", [128, 256], BF16)
    AVST3 = ar("AVST3", [128, 512], BF16)
    RB1 = ar("RB1", [128, 512], BF16)
    RB2 = ar("RB2", [128, 512], BF16)
    DMA("sync", MASKS[:, :], s_masks[:, :])
    W("vector", sSY)

    for h in range(HPC):
        W("sync", sPE)
        W("tensor", sDV)
        DMA("sync", KWTH[0:96, :],
            bass.AP(kwT_scr, (96 * h) * T, [[T, 96], [1, T]]))
        DMA("sync", V1A[:, :],
            bass.AP(v1_scr, (T * h) * 33,
                    [[33, 128], [128 * 33, NT], [1, 33]]))
        DMA("sync", VWA[:, :],
            bass.AP(vw_scr, (T * h) * 33,
                    [[33, 128], [128 * 33, NT], [1, 33]]))
        kw_ready = sSY.n
        for tt in range(NT):
            EM("tensor", lambda e, t=tt, h_=h: e.matmul(
                ps1[:, 0:32], NQT[0:96, 128 * t:128 * (t + 1)],
                WDQN[0:96, 32 * h_:32 * (h_ + 1)], start=True, stop=True),
               inc=sPE)
            EM("tensor", lambda e, t=tt, h_=h: e.matmul(
                ps1[:, 32:96], NQT[0:96, 128 * t:128 * (t + 1)],
                WDQR[0:96, 64 * h_:64 * (h_ + 1)], start=True, stop=True),
               inc=sPE)
            EM("tensor", lambda e, t=tt, h_=h: e.matmul(
                ps2[:, 0:32], CKVT[0:32, 128 * t:128 * (t + 1)],
                WDKN[0:32, 32 * h_:32 * (h_ + 1)], start=True, stop=True),
               inc=sPE)
            W("vector", sPE)
            EM("vector", lambda e: e.tensor_copy(RB1[:, 0:96], ps1[:, 0:96]),
               inc=sDV)
            EM("vector", lambda e, t=tt: e.tensor_copy(
                K1NH[:, 96 * t:96 * t + 32], ps2[:, 0:32]), inc=sDV)
            EM("vector", lambda e, t=tt: e.tensor_mul(
                T1B[:, 0:64], RB1[:, 32:96],
                bass.AP(ROPEC, 64 * t, [[NT * 64, 128], [1, 64]])), inc=sDV)
            EM("vector", lambda e, t=tt: e.tensor_copy(
                K1NH[:, 96 * t + 32:96 * t + 96],
                KRN[:, 64 * t:64 * (t + 1)]), inc=sDV)
            EM("vector", lambda e, t=tt: e.tensor_mul(
                T2B[:, 0:64],
                bass.AP(RB1, 64, [[512, 128], [-32, 2], [1, 32]]),
                bass.AP(ROPES, 64 * t, [[NT * 64, 128], [1, 64]])), inc=sDV)
            EM("vector", lambda e, t=tt: e.tensor_copy(
                QNH[:, 96 * t:96 * t + 32], RB1[:, 0:32]), inc=sDV)
            EM("vector", lambda e, t=tt: e.tensor_add(
                QNH[:, 96 * t + 32:96 * t + 96], T1B[:, 0:64],
                T2B[:, 0:64]), inc=sDV)
            W("tensor", sDV)
            EM("tensor", lambda e, t=tt: e.transpose(
                psT[0:96, 0:128], QNH[:, 96 * t:96 * (t + 1)], IDENT[:, :]),
               inc=sPE)
            EM("tensor", lambda e, t=tt: e.transpose(
                psT[0:96, 128:256], K1NH[:, 96 * t:96 * (t + 1)],
                IDENT[:, :]), inc=sPE)
            W("vector", sPE)
            EM("vector", lambda e, t=tt: e.tensor_copy(
                QTH[0:96, 128 * t:128 * (t + 1)], psT[0:96, 0:128]),
               inc=sDV)
            EM("vector", lambda e, t=tt: e.tensor_copy(
                K1TH[0:96, 128 * t:128 * (t + 1)], psT[0:96, 128:256]),
               inc=sDV)
            W("tensor", sDV)
        W("tensor", sSY, kw_ready)

        for qc in range(8):
            nkb = qc + 1
            bursts = []
            for br, (KTH_, VA_, bank) in enumerate(
                    ((K1TH, V1A, ps1), (KWTH, VWA, ps3))):
                for kb in range(nkb):
                    bursts.append((KTH_, VA_, bank, kb,
                                   kb == 0, kb == nkb - 1))
            pend = None
            for bi in range(len(bursts) + 1):
                if bi < len(bursts):
                    KTH_, VA_, bank, kb, isfirst, islast = bursts[bi]
                    slot = bi % 2
                    W("tensor", sAC)
                    if bi == 0:
                        W("tensor", sDV)
                    for i in range(4):
                        kt = 4 * kb + i
                        EM("tensor", lambda e, kk=KTH_, kt_=kt, q=qc, i_=i:
                           e.matmul(
                               psA[:, 512 * i_:512 * (i_ + 1)],
                               kk[0:96, 128 * kt_:128 * (kt_ + 1)],
                               QTH[0:96, 512 * q:512 * (q + 1)],
                               start=True, stop=True), inc=sPE)
                    qk_done = sPE.n
                if pend is not None:
                    (pV, pbank, pkb, pslot, pfirst, plast) = pend
                    if plast:
                        W("tensor", sDV)
                    for i in range(4):
                        kt = 4 * pkb + i
                        EM("tensor", lambda e, v=pV, kt_=kt, s=pslot, i_=i,
                           b=pbank, st=(pfirst and i == 0),
                           sp=(plast and i == 3): e.matmul(
                               b[0:33, :],
                               bass.AP(v, 33 * kt_, [[NT * 33, 128],
                                                     [1, 33]]),
                               EXPB[:, 2048 * s + 512 * i_:
                                    2048 * s + 512 * (i_ + 1)],
                               start=st, stop=sp), inc=sPE)
                if bi == len(bursts):
                    break
                W("scalar", sPE, qk_done)
                EM("scalar", lambda e, s=slot: e.activation(
                    EXPB[:, 2048 * s:2048 * (s + 1)], psA[:, :], AF.Exp,
                    bias=CONST[:, 63:64], scale=ISQ), inc=sAC)
                if islast:
                    W("vector", sAC)
                    for i in range(4):
                        EM("vector", lambda e, s=slot, i_=i:
                           e.tensor_mul(
                               EXPB[:, 2048 * s + 512 * i_:
                                    2048 * s + 512 * (i_ + 1)],
                               EXPB[:, 2048 * s + 512 * i_:
                                    2048 * s + 512 * (i_ + 1)],
                               MASKS[:, 512 * i_:512 * (i_ + 1)]),
                           inc=sDV)
                pend = (VA_, bank, kb, slot, isfirst, islast)
            W("tensor", sAC)
            for r in range(2):
                EM("tensor", lambda e, h_=h, rr=r, q=qc: e.matmul(
                    psA[:, 512 * rr:512 * (rr + 1)],
                    KST[0:96, 256 * h_ + 128 * rr:
                        256 * h_ + 128 * (rr + 1)],
                    QTH[0:96, 512 * q:512 * (q + 1)],
                    start=True, stop=True), inc=sPE)
            W("scalar", sPE)
            EM("scalar", lambda e: e.activation(
                EXPB[:, 0:1024], psA[:, 0:1024], AF.Exp,
                bias=CONST[:, 63:64], scale=ISQ), inc=sAC)
            W("tensor", sAC)
            for r in range(2):
                EM("tensor", lambda e, rr=r: e.matmul(
                    ps2[0:33, :],
                    bass.AP(VSA, 264 * rr, [[2 * 264, 128], [1, 33]]),
                    EXPB[:, 512 * rr:512 * (rr + 1)],
                    start=(rr == 0), stop=(rr == 1)), inc=sPE)
            division([(ps1, 32), (ps2, 33), (ps3, 34)], 32,
                     AEVF3, RCP3, COMB3, AVST3)
            if _TAPS and h == 0 and qc == 1:
                DMA("sync", dbg_qth[:, :], QTH[0:96, :])
                DMA("sync", dbg_k1n[:, :], K1NH[:, :])
                DMA("sync", dbg_ckv[:, :], CKVT[0:32, :])
                DMA("sync", dbg_krn[:, :], KRN[:, :])
                DMA("sync", dbg_k1h[:, :], K1TH[0:96, :])
                DMA("sync", dbg_kwh[:, :], KWTH[0:96, :])
                DMA("sync", dbg_cst[:, :], CONST[:, :])
                DMA("sync", dbg_ab3[:, :], AEVF3[:, :])
                DMA("sync", dbg_rc3[:, :], RCP3[:, :])
                DMA("sync", dbg_cb3[:, :], COMB3[:, :])
                DMA("sync", dbg_exb[:, :], EXPB[:, :])
                DMA("sync", dbg_kst[:, :], KST[0:96, :])
                DMA("sync", dbg_vsa[:, :], VSA[:, :])
                W("sync", sSY)
            DMA("sync",
                bass.AP(cin_avb, (32 * h) * T + 512 * qc,
                        [[T, 32], [1, 512]]),
                AVST3[0:32, 0:512])

    if _TAPS:
        W("sync", sDV)
        DMA("sync", dbg_avb[:, :], cin_avb[:, :])
        DMA("sync", dbg_misc[:, :], MISC[:, :])
        DMA("sync", dbg_grow[:, :], GROW[:, :])
        W("sync", sSY)

    cc_avb = sCC.n + 1
    W("gpsimd", sSY)
    exchange(cin_avb, avb_allg)
    ar_to(mB2)
    AVRH3 = ar("AVRH3", [128, 4096], BF16)
    WPJ = ar("WPJ", [128, 4096], BF16)
    DMA("gpsimd", WPJ[:, :],
        bass.AP(wproj, 0, [[1024, 128], [128 * 1024, 4], [1, 1024]]))
    W("sync", sCC, cc_avb)
    W("tensor", sG)
    outproj(avb_allg, WPJ, 4, AVRH3)
    ar_to(mB)
    BARRIER()
    TAP(2)

    emit_ffn("f2")

    W("sync", sDV)
    DMA("sync",
        bass.AP(y_out, 0, [[TH, 128], [128 * TH, 8], [1, TH]]),
        bass.AP(X, 0, [[8 * TH, 128], [2048, 8], [1, 2048]]))
    W("sync", sSY)

    import contextlib
    with contextlib.ExitStack() as st:
        for name, c in sems.items():
            c.h = st.enter_context(nc.semaphore(name))
        block = st.enter_context(nc.Block())
        for eng in ENGS:
            getattr(block, eng)(
                (lambda fns: lambda e: [f(e) for f in fns])(prog[eng]))
    return nc


_NC = None


def kernel(
    L0_attn_a_hidden_states, L0_attn_a_Wq, L0_attn_a_Wk, L0_attn_a_Wv,
    L0_attn_a_Wo, L0_attn_a_rel_bias_table, L0_attn_b_w_cq, L0_attn_b_g_qnorm,
    L0_attn_b_w_dq_nope, L0_attn_b_w_dq_rope, L0_attn_b_w_ckv,
    L0_attn_b_g_kvnorm, L0_attn_b_w_dk_nope, L0_attn_b_w_dv,
    L0_attn_b_w_krope, L0_attn_b_w_imp, L0_attn_b_w_selk, L0_attn_b_w_selv,
    L0_attn_b_w_wink, L0_attn_b_w_winv, L0_attn_b_w_gate, L0_attn_b_w_proj,
    L0_ffn_w_gate, L0_ffn_w_up, L0_ffn_w_down,
):
    global _NC
    st = _statics()
    x = np.asarray(L0_attn_a_hidden_states)
    C = np.ascontiguousarray

    wgT_ = C(np.asarray(L0_ffn_w_gate).T)
    wuT_ = C(np.asarray(L0_ffn_w_up).T)
    wdT_ = C(np.asarray(L0_ffn_w_down).T)

    in_maps = []
    for c in range(8):
        b, hb = c // 2, c % 2
        hs = slice(512 * hb, 512 * hb + 512)
        m = {
            "x_ownT": C(x[b, 2048 * hb:2048 * hb + 2048, :].T),
            "wq": C(np.asarray(L0_attn_a_Wq)[:, hs]),
            "wk": C(np.asarray(L0_attn_a_Wk)[:, hs]),
            "wv": C(np.asarray(L0_attn_a_Wv)[:, hs]),
            "wo": C(np.asarray(L0_attn_a_Wo)),
            "rtab": C(np.asarray(L0_attn_a_rel_bias_table)[:, 8 * hb:8 * hb + 8]),
            "wgT": wgT_, "wuT": wuT_, "wdT": wdT_,
            "wcq": C(np.asarray(L0_attn_b_w_cq)),
            "gq": C(np.asarray(L0_attn_b_g_qnorm).reshape(96, 1)),
            "wdqn": C(np.asarray(L0_attn_b_w_dq_nope)[:, 256 * hb:256 * hb + 256]),
            "wdqr": C(np.asarray(L0_attn_b_w_dq_rope)[:, 512 * hb:512 * hb + 512]),
            "wckv": C(np.asarray(L0_attn_b_w_ckv)),
            "gkv": C(np.asarray(L0_attn_b_g_kvnorm).reshape(32, 1)),
            "wdkn": C(np.asarray(L0_attn_b_w_dk_nope)[:, 256 * hb:256 * hb + 256]),
            "wdv": C(np.asarray(L0_attn_b_w_dv)[:, 256 * hb:256 * hb + 256]),
            "wkro": C(np.asarray(L0_attn_b_w_krope)),
            "wimp": C(np.asarray(L0_attn_b_w_imp)),
            "wselk": C(np.asarray(L0_attn_b_w_selk)[:, 768 * hb:768 * hb + 768]),
            "wselv": C(np.asarray(L0_attn_b_w_selv)[:, 256 * hb:256 * hb + 256]),
            "wwink": C(np.asarray(L0_attn_b_w_wink)[:, 768 * hb:768 * hb + 768]),
            "wwinv": C(np.asarray(L0_attn_b_w_winv)[:, 256 * hb:256 * hb + 256]),
            "wgate": C(np.asarray(L0_attn_b_w_gate)),
            "wproj": C(np.asarray(L0_attn_b_w_proj)),
            "s_onehot": st["onehot"], "s_ident": st["ident"],
            "s_identf": st["identf"], "s_trils": st["trils"],
            "s_iota": st["iota"], "s_onesb": st["onesb"],
            "s_onesf": st["onesf"], "s_masks": st["masks"],
            "s_ropec": st["ropec"], "s_ropes": st["ropes"],
            "tok0": np.array([[2048 * hb]], dtype=np.int32),
        }
        in_maps.append(m)

    if _NC is None:
        _NC = build_nc()
    res = run_bass_kernel_spmd(_NC, in_maps, core_ids=list(range(8)))
    out = np.empty((B, T, D), dtype=np.float32)
    for c in range(8):
        b, hb = c // 2, c % 2
        yt = np.asarray(res.results[c]["y_ownT"])
        out[b, 2048 * hb:2048 * hb + 2048, :] = yt.T
    kernel._last_res = res
    return out


# version 07
# speedup vs baseline: 1.0533x; 1.0125x over previous
import math
import numpy as np
import ml_dtypes

import concourse.bass as bass
import concourse.mybir as mybir
from concourse.bass_utils import run_bass_kernel_spmd

F32 = mybir.dt.float32
BF16 = mybir.dt.bfloat16
I32 = mybir.dt.int32
AF = mybir.ActivationFunctionType
ALU = mybir.AluOpType

B, T, D = 4, 4096, 1024
TH = T // 2
NT = T // 128
HPC = 8
KEEP = 256
FFH = 3072
NJ = FFH // 128
SM_SHIFT = -12.0
INV127 = 1.0 / 127.0
ISQ = 1.0 / math.sqrt(96.0)
NBIS = 34
_ALOG = []
_TAPS = False


def _rel_bucket(d):
    nb, maxd = 16, 128
    buckets = (d > 0).astype(np.int64) * nb
    rp = np.abs(d)
    max_exact = nb // 2
    small = rp < max_exact
    rl = max_exact + (
        np.log(np.maximum(rp, 1) / max_exact) / math.log(maxd / max_exact)
        * (nb - max_exact)
    ).astype(np.int64)
    rl = np.minimum(rl, nb - 1)
    return buckets + np.where(small, rp, rl)


_STATICS = None


def _statics():
    global _STATICS
    if _STATICS is not None:
        return _STATICS
    d = np.arange(-4095, 4096)
    buck = _rel_bucket(d)
    onehot = np.zeros((32, 8191), dtype=ml_dtypes.bfloat16)
    onehot[buck, np.arange(8191)] = 1.0
    ident = np.eye(128, dtype=ml_dtypes.bfloat16)
    identf = np.eye(128, dtype=np.float32)
    trils = np.triu(np.ones((128, 128), dtype=np.float32), 1).astype(
        ml_dtypes.bfloat16)
    iota = np.broadcast_to(np.arange(1, 257, dtype=np.float32)[None, :],
                           (128, 256)).copy()
    onesb = np.ones((128, 128), dtype=ml_dtypes.bfloat16)
    onesf = np.ones((128, 128), dtype=np.float32)
    masks = np.zeros((128, 4 * 512), dtype=ml_dtypes.bfloat16)
    for m, c0 in enumerate((0, 128, 256, 384)):
        p = np.arange(128)[:, None]
        j = np.arange(512)[None, :]
        masks[:, 512 * m:512 * (m + 1)] = (p + c0 <= j).astype(
            ml_dtypes.bfloat16)
    f = 1.0 / (10000.0 ** (np.arange(0, 64, 2, dtype=np.float64) / 64.0))
    t = np.arange(T, dtype=np.float64)
    a = np.outer(t, f)
    c2 = np.concatenate([np.cos(a), np.cos(a)], axis=1)
    s2 = np.concatenate([-np.sin(a), np.sin(a)], axis=1)
    ropec = np.zeros((128, NT * 64), dtype=ml_dtypes.bfloat16)
    ropes = np.zeros((128, NT * 64), dtype=ml_dtypes.bfloat16)
    for tt in range(NT):
        ropec[:, 64 * tt:64 * (tt + 1)] = c2[128 * tt:128 * (tt + 1)]
        ropes[:, 64 * tt:64 * (tt + 1)] = s2[128 * tt:128 * (tt + 1)]
    _STATICS = dict(onehot=onehot, ident=ident, identf=identf, trils=trils,
                    iota=iota, onesb=onesb, onesf=onesf, masks=masks,
                    ropec=ropec, ropes=ropes)
    return _STATICS


ENGS = ("sync", "tensor", "vector", "scalar", "gpsimd")


class Ctr:
    def __init__(self, name):
        self.name = name
        self.n = 0
        self.h = None


def build_nc():
    nc = bass.Bass("TRN2", target_bir_lowering=False)

    def din(name, shape, dt=F32):
        return nc.dram_tensor(name, list(shape), dt, kind="ExternalInput")

    x_in = din("x_ownT", (D, TH))
    wq = din("wq", (D, 512))
    wk = din("wk", (D, 512))
    wv = din("wv", (D, 512))
    wo = din("wo", (D, D))
    rtab = din("rtab", (32, 8))
    wgT = din("wgT", (D, FFH), I32)
    wuT = din("wuT", (D, FFH), I32)
    wdT = din("wdT", (FFH, D), I32)
    wcq = din("wcq", (D, 96))
    gq = din("gq", (96, 1))
    wdqn = din("wdqn", (96, 256))
    wdqr = din("wdqr", (96, 512))
    wckv = din("wckv", (D, 32))
    gkv = din("gkv", (32, 1))
    wdkn = din("wdkn", (32, 256))
    wdv = din("wdv", (32, 256))
    wkro = din("wkro", (D, 64))
    wimp = din("wimp", (D, 1))
    wselk = din("wselk", (D, 768))
    wselv = din("wselv", (D, 256))
    wwink = din("wwink", (D, 768))
    wwinv = din("wwinv", (D, 256))
    wgate = din("wgate", (D, 3))
    wproj = din("wproj", (512, D))
    s_onehot = din("s_onehot", (32, 8191), BF16)
    s_ident = din("s_ident", (128, 128), BF16)
    s_identf = din("s_identf", (128, 128))
    s_trils = din("s_trils", (128, 128), BF16)
    s_iota = din("s_iota", (128, 256))
    s_onesb = din("s_onesb", (128, 128), BF16)
    s_onesf = din("s_onesf", (128, 128))
    s_masks = din("s_masks", (128, 2048), BF16)
    s_ropec = din("s_ropec", (128, NT * 64), BF16)
    s_ropes = din("s_ropes", (128, NT * 64), BF16)
    tok0_in = din("tok0", (1, 1), I32)

    y_out = nc.dram_tensor("y_ownT", [D, TH], F32, kind="ExternalOutput")
    dbg = []
    if _TAPS:
        for i in range(7):
            dbg.append(nc.dram_tensor("dbg%d" % i, [D, TH], F32,
                                      kind="ExternalOutput"))
        dbg_xn = nc.dram_tensor("dbg_xn", [2 * D, TH], BF16,
                                kind="ExternalOutput")
        dbg_bias = nc.dram_tensor("dbg_bias", [8, 8191], F32,
                                  kind="ExternalOutput")
        dbg_av = nc.dram_tensor("dbg_av", [512, T], BF16,
                                kind="ExternalOutput")
        dbg_qt = nc.dram_tensor("dbg_qt", [128, 4 * T], BF16,
                                kind="ExternalOutput")
        dbg_kt = nc.dram_tensor("dbg_kt", [128, 4 * T], BF16,
                                kind="ExternalOutput")
        dbg_v = nc.dram_tensor("dbg_v", [128, NT * 520], BF16,
                               kind="ExternalOutput")
        dbg_aev = nc.dram_tensor("dbg_aev", [128, 512], F32,
                                 kind="ExternalOutput")
        dbg_rcp = nc.dram_tensor("dbg_rcp", [128, 8], F32,
                                 kind="ExternalOutput")
        dbg_comb = nc.dram_tensor("dbg_comb", [128, 256], BF16,
                                  kind="ExternalOutput")
        dbg_avb = nc.dram_tensor("dbg_avb", [256, T], BF16,
                                 kind="ExternalOutput")
        dbg_misc = nc.dram_tensor("dbg_misc", [128, 16], F32,
                                  kind="ExternalOutput")
        dbg_grow = nc.dram_tensor("dbg_grow", [1, 16], F32,
                                  kind="ExternalOutput")
        dbg_ab3 = nc.dram_tensor("dbg_ab3", [128, 3 * 512], F32,
                                 kind="ExternalOutput")
        dbg_rc3 = nc.dram_tensor("dbg_rc3", [128, 8], F32,
                                 kind="ExternalOutput")
        dbg_cb3 = nc.dram_tensor("dbg_cb3", [128, 256], BF16,
                                 kind="ExternalOutput")
        dbg_exb = nc.dram_tensor("dbg_exb", [128, 4096], BF16,
                                 kind="ExternalOutput")
        dbg_qth = nc.dram_tensor("dbg_qth", [96, T], BF16,
                                 kind="ExternalOutput")
        dbg_k1h = nc.dram_tensor("dbg_k1h", [96, T], BF16,
                                 kind="ExternalOutput")
        dbg_kwh = nc.dram_tensor("dbg_kwh", [96, T], BF16,
                                 kind="ExternalOutput")
        dbg_cst = nc.dram_tensor("dbg_cst", [128, 64], F32,
                                 kind="ExternalOutput")
        dbg_k1n = nc.dram_tensor("dbg_k1n", [128, NT * 96], BF16,
                                 kind="ExternalOutput")
        dbg_ckv = nc.dram_tensor("dbg_ckv", [32, T], BF16,
                                 kind="ExternalOutput")
        dbg_krn = nc.dram_tensor("dbg_krn", [128, NT * 64], BF16,
                                 kind="ExternalOutput")
        dbg_kst = nc.dram_tensor("dbg_kst", [96, 2048], BF16,
                                 kind="ExternalOutput")
        dbg_vsa = nc.dram_tensor("dbg_vsa", [128, 528], BF16,
                                 kind="ExternalOutput")
        dbg_pts = nc.dram_tensor("dbg_pts", [128, 8192], BF16,
                                 kind="ExternalOutput")
        dbg_junk = nc.dram_tensor("dbg_junk", [128, 32], F32,
                                  kind="ExternalOutput")
        dbg_cum = nc.dram_tensor("dbg_cum", [128, 32], F32,
                                 kind="ExternalOutput")
        dbg_cumf = nc.dram_tensor("dbg_cumf", [128, 32], F32,
                                  kind="ExternalOutput")
        dbg_ssc = nc.dram_tensor("dbg_ssc", [128, 32], F32,
                                 kind="ExternalOutput")

    bias_dram = nc.dram_tensor("bias_dram", [8, 8191], F32)
    cin_half = nc.dram_tensor("cin_half", [D, TH], BF16)
    xn_allg = nc.dram_tensor("xn_allg", [2 * D, TH], BF16)
    cin_av = nc.dram_tensor("cin_av", [512, T], BF16)
    av_allg = nc.dram_tensor("av_allg", [1024, T], BF16)
    cin_avb = nc.dram_tensor("cin_avb", [256, T], BF16)
    avb_allg = nc.dram_tensor("avb_allg", [512, T], BF16)
    kwT_scr = nc.dram_tensor("kwT_scr", [HPC * 96, T], BF16)
    v1_scr = nc.dram_tensor("v1_scr", [HPC * T, 33], BF16)
    vw_scr = nc.dram_tensor("vw_scr", [HPC * T, 33], BF16)

    X = nc.alloc_sbuf_tensor("X", [128, 8 * TH], F32)
    IDENT = nc.alloc_sbuf_tensor("IDENT", [128, 128], BF16)
    IDENTF = nc.alloc_sbuf_tensor("IDENTF", [128, 128], F32)
    TRILS = nc.alloc_sbuf_tensor("TRILS", [128, 128], BF16)
    ONESB = nc.alloc_sbuf_tensor("ONESB", [128, 128], BF16)
    ONESF = nc.alloc_sbuf_tensor("ONESF", [128, 128], F32)
    IOTA = nc.alloc_sbuf_tensor("IOTA", [128, 256], F32)
    CONST = nc.alloc_sbuf_tensor("CONST", [128, 64], F32)
    MISC = nc.alloc_sbuf_tensor("MISC", [128, 16], F32)
    LNROW = nc.alloc_sbuf_tensor("LNROW", [128, 2048], F32)
    XBUF = nc.alloc_sbuf_tensor("XBUF", [128, 1024], BF16)
    XNTMP = nc.alloc_sbuf_tensor("XNTMP", [128, 512], F32)
    XNST = nc.alloc_sbuf_tensor("XNST", [128, 512], BF16)
    TOKI = nc.alloc_sbuf_tensor("TOKI", [1, 1], I32)

    SBASE = 229375 - nc.sbuf_bytes_remaining + 64
    SLIMIT = 229376
    arena = {"off": SBASE}

    def ar_mark():
        return arena["off"]

    def ar_to(mark):
        arena["off"] = mark

    def ar(name, shape, dt):
        nbytes = int(np.prod(shape[1:])) * mybir.dt.size(dt)
        off = (arena["off"] + 63) // 64 * 64
        t = nc.alloc_sbuf_tensor_at(name, list(shape), dt, offset=off)
        _ALOG.append((name, off, off + nbytes))
        arena["off"] = off + nbytes
        assert arena["off"] <= SLIMIT, (name, arena["off"])
        return t

    psA = nc.alloc_psum_tensor("psA", [128, 2048], F32)
    ps1 = nc.alloc_psum_tensor("ps1", [128, 512], F32)
    ps2 = nc.alloc_psum_tensor("ps2", [128, 512], F32)
    ps3 = nc.alloc_psum_tensor("ps3", [128, 512], F32)
    psT = nc.alloc_psum_tensor("psT", [128, 512], BF16)

    prog = {e: [] for e in ENGS}
    sems = {}

    def mksem(name):
        c = Ctr(name)
        sems[name] = c
        return c

    sSY = mksem("sSY")
    sG = mksem("sG")
    sCC = mksem("sCC")
    sPE = mksem("sPE")
    sDV = mksem("sDV")
    sAC = mksem("sAC")
    sBar = mksem("sBar")
    env = {}

    def E(eng, fn):
        prog[eng].append(fn)

    def W(eng, ctr, val=None):
        v = ctr.n if val is None else val
        E(eng, lambda e, c=ctr, vv=v: e.wait_ge(c.h, vv))

    def EM(eng, fn, inc=None, inc_n=1):
        if inc is None:
            E(eng, fn)
            return
        inc.n += inc_n
        E(eng, lambda e, f=fn, c=inc, n=inc_n: f(e).then_inc(c.h, n))

    def DMA(eng, out, in_, allow_slow=False):
        sem = sG if eng == "gpsimd" else sSY
        kw = {"allow_slow_non_contiguous": True} if allow_slow else {}
        EM(eng, lambda e, o=out, i=in_, k=dict(kw): e.dma_start(o, i, **k),
           inc=sem, inc_n=16)

    def BARRIER():
        target = sBar.n + len(ENGS)
        for eng in ENGS:
            EM(eng, lambda e: e.nop(), inc=sBar)
        for eng in ENGS:
            W(eng, sBar, target)

    def TAP(i):
        if not _TAPS:
            return
        DMA("sync",
            bass.AP(dbg[i], 0, [[TH, 128], [128 * TH, 8], [1, TH]]),
            bass.AP(X, 0, [[8 * TH, 128], [2048, 8], [1, 2048]]))
        W("sync", sSY)

    DMA("sync", X[:, :],
        bass.AP(x_in, 0, [[TH, 128], [128 * TH, 8], [1, TH]]))
    DMA("sync", IDENT[:, :], s_ident[:, :])
    DMA("sync", IDENTF[:, :], s_identf[:, :])
    DMA("sync", TRILS[:, :], s_trils[:, :])
    DMA("sync", ONESB[:, :], s_onesb[:, :])
    DMA("sync", ONESF[:, :], s_onesf[:, :])
    DMA("sync", IOTA[:, :], s_iota[:, :])
    DMA("sync", TOKI[0:1, 0:1], tok0_in[0:1, 0:1])
    W("sync", sSY)

    def _load_tok(e):
        env["tokr"] = e.to_reg(e.value_load(TOKI[0:1, 0:1]))

    E("sync", _load_tok)

    m0 = ar_mark()
    OH = ar("OH", [32, 8191], BF16)
    RT = ar("RT", [32, 8], BF16)
    BROW = ar("BROW", [8, 8191], F32)
    CROW = ar("CROW", [1, 16], F32)
    DMA("sync", OH[:, :], s_onehot[:, :])
    DMA("gpsimd", RT[:, :], rtab[:, :])
    W("tensor", sSY)
    W("tensor", sG)
    for j in range(16):
        w = 512 if j < 15 else 8191 - 15 * 512
        EM("tensor",
           lambda e, jj=j, ww=w: e.matmul(
               ps1[0:8, 0:ww], RT[0:32, :],
               OH[0:32, 512 * jj:512 * jj + ww], start=True, stop=True),
           inc=sPE)
        W("vector", sPE)
        EM("vector",
           lambda e, jj=j, ww=w: e.tensor_copy(
               BROW[0:8, 512 * jj:512 * jj + ww], ps1[0:8, 0:ww]),
           inc=sDV)
        W("tensor", sDV)
    W("sync", sDV)
    DMA("sync", bias_dram[:, :], BROW[0:8, :])
    W("sync", sSY)
    DMA("sync", CROW[0:1, 0:8],
        bass.AP(bias_dram, 3500, [[1, 1], [8191, 8]]), allow_slow=True)
    DMA("sync", CROW[0:1, 8:16],
        bass.AP(bias_dram, 4500, [[1, 1], [8191, 8]]), allow_slow=True)
    W("tensor", sSY)
    EM("tensor",
       lambda e: e.matmul(ps1[:, 0:16], ONESF[0:1, :], CROW[0:1, :],
                          start=True, stop=True), inc=sPE)
    W("vector", sPE)
    EM("vector",
       lambda e: e.tensor_scalar(
           bass.AP(CONST, 0, [[64, 128], [2, 8]]),
           bass.AP(ps1, 0, [[512, 128], [1, 8]]),
           SM_SHIFT, None, ALU.add), inc=sDV)
    EM("vector",
       lambda e: e.tensor_scalar(
           bass.AP(CONST, 1, [[64, 128], [2, 8]]),
           bass.AP(ps1, 8, [[512, 128], [1, 8]]),
           SM_SHIFT, None, ALU.add), inc=sDV)
    EM("vector", lambda e: e.memset(CONST[:, 63:64], SM_SHIFT), inc=sDV)
    ar_to(m0)
    BARRIER()
    TAP(3)


    def emit_ln(sink_sb=None):
        for tc in range(4):
            for cc in range(8):
                W("vector", sPE)
                EM("vector",
                   lambda e, c=cc, t=tc: e.tensor_copy(
                       XBUF[:, 0:512],
                       X[:, 2048 * c + 512 * t:2048 * c + 512 * t + 512]),
                   inc=sDV)
                W("vector", sDV)
                EM("vector",
                   lambda e: e.tensor_mul(
                       XBUF[:, 512:1024], XBUF[:, 0:512], XBUF[:, 0:512]),
                   inc=sDV)
                W("tensor", sDV)
                EM("tensor",
                   lambda e, c=cc: e.matmul(
                       psA[0:1, 0:512], ONESB[:, 0:1], XBUF[:, 0:512],
                       start=(c == 0), stop=(c == 7)), inc=sPE)
                EM("tensor",
                   lambda e, c=cc: e.matmul(
                       psA[0:1, 512:1024], ONESB[:, 0:1], XBUF[:, 512:1024],
                       start=(c == 0), stop=(c == 7)), inc=sPE)
            W("vector", sPE)
            EM("vector", lambda e: e.tensor_scalar(
                LNROW[0:1, 0:512], psA[0:1, 0:512], 1.0 / D, None, ALU.mult),
               inc=sDV)
            EM("vector", lambda e: e.tensor_scalar(
                LNROW[0:1, 512:1024], psA[0:1, 512:1024], 1.0 / D, None,
                ALU.mult), inc=sDV)
            EM("vector", lambda e: e.tensor_mul(
                LNROW[0:1, 1024:1536], LNROW[0:1, 0:512], LNROW[0:1, 0:512]),
               inc=sDV)
            EM("vector", lambda e: e.tensor_sub(
                LNROW[0:1, 512:1024], LNROW[0:1, 512:1024],
                LNROW[0:1, 1024:1536]), inc=sDV)
            EM("vector", lambda e: e.tensor_scalar(
                LNROW[0:1, 512:1024], LNROW[0:1, 512:1024], 1e-5, None,
                ALU.add), inc=sDV)
            W("scalar", sDV)
            EM("scalar", lambda e: e.activation(
                LNROW[0:1, 1024:1536], LNROW[0:1, 512:1024], AF.Ln,
                bias=0.0, scale=1.0), inc=sAC)
            EM("scalar", lambda e: e.activation(
                LNROW[0:1, 1536:2048], LNROW[0:1, 1024:1536], AF.Exp,
                bias=0.0, scale=-0.5), inc=sAC)
            W("vector", sAC)
            EM("vector", lambda e: e.scalar_tensor_tensor(
                LNROW[0:1, 0:512], LNROW[0:1, 0:512], -1.0,
                LNROW[0:1, 1536:2048], ALU.mult, ALU.mult), inc=sDV)
            W("tensor", sDV)
            EM("tensor", lambda e: e.matmul(
                ps1[:, :], ONESF[0:1, :], LNROW[0:1, 1536:2048],
                start=True, stop=True), inc=sPE)
            EM("tensor", lambda e: e.matmul(
                ps2[:, :], ONESF[0:1, :], LNROW[0:1, 0:512],
                start=True, stop=True), inc=sPE)
            W("vector", sPE)
            for cc in range(8):
                EM("vector",
                   lambda e, c=cc, t=tc: e.tensor_mul(
                       XNTMP[:, :],
                       X[:, 2048 * c + 512 * t:2048 * c + 512 * t + 512],
                       ps1[:, :]), inc=sDV)
                if sink_sb is None:
                    W("vector", sSY)
                    EM("vector",
                       lambda e: e.tensor_add(XNST[:, :], XNTMP[:, :],
                                              ps2[:, :]), inc=sDV)
                    W("sync", sDV)
                    DMA("sync",
                        bass.AP(cin_half, (128 * cc) * TH + 512 * tc,
                                [[TH, 128], [1, 512]]),
                        XNST[:, :])
                else:
                    EM("vector",
                       lambda e, s=sink_sb, c=cc, t=tc: e.tensor_add(
                           s[:, 2048 * c + 512 * t:2048 * c + 512 * t + 512],
                           XNTMP[:, :], ps2[:, :]), inc=sDV)
            W("tensor", sDV)

    def exchange(cin, cout):
        EM("gpsimd",
           lambda e, ci=cin, co=cout: e.collective_compute(
               "AllGather", ALU.bypass,
               replica_groups=[[0, 1], [2, 3], [4, 5], [6, 7]],
               ins=[ci.ap().opt()], outs=[co.ap().opt()]),
           inc=sCC)

    def division(br_list, avw, AEVF, RCPC, COMBB, dst):
        nbr = len(br_list)
        W("vector", sSY)
        W("vector", sPE)
        for bi, (bp, _) in enumerate(br_list):
            EM("vector",
               lambda e, b=bp, w=avw, i=bi: e.tensor_copy(
                   AEVF[0:w + 1, 512 * i:512 * (i + 1)], b[0:w + 1, :]),
               inc=sDV)
        W("tensor", sDV)
        for bi in range(nbr):
            for st in range(4):
                EM("tensor",
                   lambda e, i=bi, s=st, w=avw: e.transpose(
                       psA[:, 512 * i + (w + 2) * s:
                           512 * i + (w + 2) * s + (w + 1)],
                       AEVF[0:w + 1, 512 * i + 128 * s:
                            512 * i + 128 * (s + 1)],
                       IDENTF[0:w + 1, 0:w + 1]), inc=sPE)
        W("vector", sPE)
        for st in range(4):
            for bi, (bp, gcol) in enumerate(br_list):
                base = 512 * bi + (avw + 2) * st
                EM("vector",
                   lambda e, bs=base, w=avw, c=bi: e.reciprocal(
                       RCPC[:, c:c + 1], psA[:, bs + w:bs + w + 1]),
                   inc=sDV)
                W("vector", sDV)
                if gcol is not None:
                    EM("vector",
                       lambda e, c=bi, g=gcol: e.tensor_mul(
                           RCPC[:, c:c + 1], RCPC[:, c:c + 1],
                           CONST[:, g:g + 1]), inc=sDV)
                if bi == 0:
                    EM("vector",
                       lambda e, bs=base, w=avw, s=st, c=bi: e.tensor_scalar(
                           COMBB[:, w * s:w * (s + 1)], psA[:, bs:bs + w],
                           RCPC[:, c:c + 1], None, ALU.mult), inc=sDV)
                else:
                    EM("vector",
                       lambda e, bs=base, w=avw, s=st, c=bi:
                       e.scalar_tensor_tensor(
                           COMBB[:, w * s:w * (s + 1)], psA[:, bs:bs + w],
                           RCPC[:, c:c + 1], COMBB[:, w * s:w * (s + 1)],
                           ALU.mult, ALU.add), inc=sDV)
        W("tensor", sDV)
        for st in range(4):
            EM("tensor",
               lambda e, s=st, w=avw: e.transpose(
                   psT[0:w, 128 * s:128 * (s + 1)],
                   COMBB[:, w * s:w * (s + 1)], IDENT[:, :]), inc=sPE)
        W("vector", sPE)
        EM("vector",
           lambda e, w=avw, d=dst: e.tensor_copy(d[0:w, 0:512],
                                                 psT[0:w, 0:512]),
           inc=sDV)
        W("sync", sDV)

    mA = ar_mark()
    QT = ar("QT", [128, 4 * T], BF16)
    KT = ar("KT", [128, 4 * T], BF16)
    VSB = ar("VSB", [128, NT * 520], BF16)
    mA2 = ar_mark()
    WB1 = ar("WB1", [128, 4096], BF16)
    XNB = ar("XNB", [128, 8192], BF16)

    emit_ln(sink_sb=None)
    TAP(4)
    cc_xn1 = sCC.n + 1
    W("gpsimd", sSY)
    exchange(cin_half, xn_allg)
    W("sync", sCC, cc_xn1)

    def load_xn_chunk(tch, slot, dst=None):
        d = dst if dst is not None else XNB
        hb = tch // 4
        cb = tch % 4
        DMA("sync",
            bass.AP(d, 4096 * slot, [[8192, 128], [512, 8], [1, 512]]),
            bass.AP(xn_allg, (1024 * hb) * TH + 512 * cb,
                    [[TH, 128], [128 * TH, 8], [1, 512]]))

    def load_w_slice(src, width):
        DMA("gpsimd", bass.AP(WB1, 0, [[4096, 128], [width, 8], [1, width]]),
            bass.AP(src, 0, [[width, 128], [128 * width, 8], [1, width]]))

    for which in ("q", "k"):
        src = wq if which == "q" else wk
        OUT = QT if which == "q" else KT
        W("gpsimd", sPE)
        load_w_slice(src, 512)
        W("tensor", sG)
        W("sync", sPE)
        load_xn_chunk(0, 0)
        for tch in range(8):
            if tch < 7:
                W("sync", sPE)
                load_xn_chunk(tch + 1, (tch + 1) % 2)
            W("tensor", sSY, sSY.n - (16 if tch < 7 else 0))
            slot = tch % 2
            for mc in range(4):
                for cc in range(8):
                    EM("tensor",
                       lambda e, m=mc, c=cc, s=slot: e.matmul(
                           psA[:, 512 * m:512 * (m + 1)],
                           WB1[:, 512 * c + 128 * m:512 * c + 128 * (m + 1)],
                           bass.AP(XNB, 4096 * s + 512 * c,
                                   [[8192, 128], [1, 512]]),
                           start=(c == 0), stop=(c == 7)), inc=sPE)
            W("vector", sPE)
            for mc in range(4):
                EM("vector",
                   lambda e, m=mc, t=tch, o=OUT: e.tensor_copy(
                       o[:, 4096 * m + 512 * t:4096 * m + 512 * (t + 1)],
                       psA[:, 512 * m:512 * (m + 1)]), inc=sDV)
            W("tensor", sDV)
    W("gpsimd", sPE)
    load_w_slice(wv, 512)
    W("tensor", sG)
    EM("vector", lambda e: e.memset(VSB[:, :], 1.0), inc=sDV)
    W("tensor", sDV)
    W("sync", sPE)
    load_xn_chunk(0, 0)
    for tch in range(8):
        if tch < 7:
            W("sync", sPE)
            load_xn_chunk(tch + 1, (tch + 1) % 2)
        W("tensor", sSY, sSY.n - (16 if tch < 7 else 0))
        slot = tch % 2
        for tt in range(4):
            tile = 4 * tch + tt
            for cc in range(8):
                EM("tensor",
                   lambda e, s=slot, c=cc, t=tt: e.matmul(
                       ps1[:, :],
                       bass.AP(XNB, 4096 * s + 512 * c + 128 * t,
                               [[8192, 128], [1, 128]]),
                       bass.AP(WB1, 512 * c, [[4096, 128], [1, 512]]),
                       start=(c == 0), stop=(c == 7)), inc=sPE)
            W("vector", sPE)
            EM("vector",
               lambda e, ti=tile: e.tensor_copy(
                   bass.AP(VSB, 520 * ti, [[NT * 520, 128], [65, 8],
                                           [1, 64]]),
                   bass.AP(ps1, 0, [[512, 128], [64, 8], [1, 64]])),
               inc=sDV)
            W("tensor", sDV)

    TAP(5)
    if _TAPS:
        W("sync", sDV)
        DMA("sync", dbg_qt[:, :], QT[:, :])
        DMA("sync", dbg_kt[:, :], KT[:, :])
        DMA("sync", dbg_v[:, :], VSB[:, :])
        DMA("sync", dbg_xn[:, :], xn_allg[:, :])
        DMA("sync", dbg_bias[:, :], bias_dram[:, :])
        W("sync", sSY)

    ar_to(mA2)
    TOEP = ar("TOEP", [128, 6 * 512], F32)
    EXPP = ar("EXPP", [128, 4096], BF16)
    AEVF = ar("AEVF", [128, 512], F32)
    RCPC = ar("RCPC", [128, 8], F32)
    COMBB = ar("COMBB", [128, 256], BF16)
    AVST = ar("AVST", [128, 512], BF16)

    for h in range(HPC):
        mc = h // 2
        roff = 64 * (h % 2)
        W("sync", sDV)
        for mi in range(6):
            c0 = 128 * (mi - 1)
            DMA("sync", TOEP[:, 512 * mi:512 * (mi + 1)],
                bass.AP(bias_dram, 8191 * h + c0 + 3584,
                        [[1, 128], [1, 512]]))
        toep_ready = sSY.n
        for qc in range(8):
            pend = None
            for kb in range(9):
                if kb < 8:
                    W("tensor", sAC)
                    if kb == 0:
                        W("tensor", sDV)
                        if qc == 0:
                            W("tensor", sSY, toep_ready)
                    for i in range(4):
                        kt = 4 * kb + i
                        EM("tensor",
                           lambda e, m=mc, r=roff, k=kt, q=qc, i_=i: e.matmul(
                               psA[:, 512 * i_:512 * (i_ + 1)],
                               KT[r:r + 64, 4096 * m + 128 * k:
                                  4096 * m + 128 * (k + 1)],
                               QT[r:r + 64, 4096 * m + 512 * q:
                                  4096 * m + 512 * (q + 1)],
                               start=True, stop=True), inc=sPE)
                    qk_done = sPE.n
                if pend is not None:
                    pkb = pend
                    pslot = pkb % 2
                    if kb == 8:
                        W("tensor", sAC)
                    for i in range(4):
                        kt = 4 * pkb + i
                        EM("tensor",
                           lambda e, k=kt, hh=h, s=pslot, i_=i,
                           st=(pkb == 0 and i == 0),
                           sp=(pkb == 7 and i == 3): e.matmul(
                               ps1[0:65, :],
                               bass.AP(VSB, 520 * k + 65 * hh,
                                       [[NT * 520, 128], [1, 65]]),
                               EXPP[:, 2048 * s + 512 * i_:
                                    2048 * s + 512 * (i_ + 1)],
                               start=st, stop=sp), inc=sPE)
                if kb == 8:
                    break
                band = [i for i in range(4)
                        if -1 <= (4 * kb + i) - 4 * qc <= 4]
                if band:
                    W("vector", sPE, qk_done)
                    for i in band:
                        mi = (4 * kb + i) - 4 * qc + 1
                        EM("vector",
                           lambda e, i_=i, m=mi: e.tensor_add(
                               psA[:, 512 * i_:512 * (i_ + 1)],
                               psA[:, 512 * i_:512 * (i_ + 1)],
                               bass.AP(TOEP, 512 * m + 511,
                                       [[6 * 512, 128], [-1, 512]])),
                           inc=sDV)
                    W("scalar", sDV)
                else:
                    W("scalar", sPE, qk_done)
                slot = kb % 2
                runs = []
                for i in range(4):
                    mp = (4 * kb + i) - 4 * qc
                    cls = ("band" if -1 <= mp <= 4 else
                           ("left" if mp < -1 else "right"))
                    if runs and runs[-1][0] == cls:
                        runs[-1][2] += 1
                    else:
                        runs = runs + [[cls, i, 1]]
                for cls, i0, ln in runs:
                    if cls == "band":
                        EM("scalar",
                           lambda e, s=slot, i_=i0, l=ln: e.activation(
                               EXPP[:, 2048 * s + 512 * i_:
                                    2048 * s + 512 * (i_ + l)],
                               psA[:, 512 * i_:512 * (i_ + l)], AF.Exp,
                               bias=CONST[:, 63:64], scale=1.0), inc=sAC)
                    else:
                        gcol = 2 * h if cls == "left" else 2 * h + 1
                        EM("scalar",
                           lambda e, s=slot, i_=i0, l=ln, g=gcol:
                           e.activation(
                               EXPP[:, 2048 * s + 512 * i_:
                                    2048 * s + 512 * (i_ + l)],
                               psA[:, 512 * i_:512 * (i_ + l)], AF.Exp,
                               bias=CONST[:, g:g + 1], scale=1.0), inc=sAC)
                pend = kb
            division([(ps1, None)], 64, AEVF, RCPC, COMBB, AVST)
            if _TAPS and h == 0 and qc == 1:
                DMA("sync", dbg_aev[:, :], AEVF[:, :])
                DMA("sync", dbg_rcp[:, :], RCPC[:, :])
                DMA("sync", dbg_comb[:, :], COMBB[:, :])
                W("sync", sSY)
            DMA("sync",
                bass.AP(cin_av, (64 * h) * T + 512 * qc,
                        [[T, 64], [1, 512]]),
                AVST[0:64, 0:512])

    TAP(6)
    if _TAPS:
        W("sync", sDV)
        DMA("sync", dbg_av[:, :], cin_av[:, :])
        W("sync", sSY)

    cc_av = sCC.n + 1
    W("gpsimd", sSY)
    exchange(cin_av, av_allg)
    ar_to(mA2)
    AVRH = ar("AVRH", [128, 4096], BF16)
    WOB = ar("WOB", [128, 8192], BF16)
    DMA("gpsimd", WOB[:, :],
        bass.AP(wo, 0, [[1024, 128], [128 * 1024, 8], [1, 1024]]))
    W("sync", sCC, cc_av)
    W("tensor", sG)

    def outproj(src_allg, WB, ncc, AVBUF):
        for tj in range(4):
            W("sync", sPE)

            def _ld(e, t=tj, sa=src_allg, n_=ncc, ab=AVBUF, ctr=sSY):
                with e.If_cmp(env["tokr"], 0, "IS_EQ"):
                    e.dma_start(
                        bass.AP(ab, 0, [[4096, 128], [512, n_], [1, 512]]),
                        bass.AP(sa, 512 * t,
                                [[T, 128], [128 * T, n_], [1, 512]]),
                    ).then_inc(ctr.h, 16)
                with e.If_cmp(env["tokr"], 0, "IS_NE"):
                    e.dma_start(
                        bass.AP(ab, 0, [[4096, 128], [512, n_], [1, 512]]),
                        bass.AP(sa, 2048 + 512 * t,
                                [[T, 128], [128 * T, n_], [1, 512]]),
                    ).then_inc(ctr.h, 16)

            sSY.n += 16
            E("sync", _ld)
            W("tensor", sSY)
            for dc in range(8):
                for cc in range(ncc):
                    EM("tensor",
                       lambda e, d=dc, c=cc, w=WB, n_=ncc, ab=AVBUF:
                       e.matmul(
                           psA[:, 512 * (d % 4):512 * (d % 4 + 1)],
                           w[:, 1024 * c + 128 * d:1024 * c + 128 * (d + 1)],
                           bass.AP(ab, 512 * c, [[4096, 128], [1, 512]]),
                           start=(c == 0), stop=(c == n_ - 1)), inc=sPE)
                W("vector", sPE)
                EM("vector",
                   lambda e, d=dc, t=tj: e.tensor_add(
                       X[:, 2048 * d + 512 * t:2048 * d + 512 * (t + 1)],
                       X[:, 2048 * d + 512 * t:2048 * d + 512 * (t + 1)],
                       psA[:, 512 * (d % 4):512 * (d % 4 + 1)]), inc=sDV)
                W("tensor", sDV)

    outproj(av_allg, WOB, 8, AVRH)
    ar_to(mA)
    BARRIER()
    TAP(0)

    def emit_ffn(tag):
        mF = ar_mark()
        XN2 = ar("XN2" + tag, [128, 8 * 2048], BF16)
        HB = ar("HB" + tag, [128, NJ * 1024], BF16)
        WDB = ar("WDB" + tag, [128, (NJ // 2) * 1024], BF16)
        WGB = ar("WGB" + tag, [128, 2048], BF16)
        WUB = ar("WUB" + tag, [128, 2048], BF16)
        SILU = ar("SILU" + tag, [128, 512], BF16)

        emit_ln(sink_sb=XN2)

        def load_wd(jh):
            for j4 in range(0, NJ // 2, 4):
                DMA("gpsimd",
                    bass.AP(WDB, 1024 * j4,
                            [[(NJ // 2) * 1024, 128], [1024, 4], [1, 1024]]),
                    bass.AP(wdT, (128 * (j4 + jh * (NJ // 2))) * D,
                            [[D, 128], [128 * D, 4], [1, D]]))

        def load_gu(j, slot):
            DMA("gpsimd",
                bass.AP(WGB, 1024 * slot, [[2048, 128], [128, 8], [1, 128]]),
                bass.AP(wgT, 128 * j, [[FFH, 128], [128 * FFH, 8],
                                       [1, 128]]))
            DMA("gpsimd",
                bass.AP(WUB, 1024 * slot, [[2048, 128], [128, 8], [1, 128]]),
                bass.AP(wuT, 128 * j, [[FFH, 128], [128 * FFH, 8],
                                       [1, 128]]))

        for half in range(2):
            hoff = 1024 * half
            W("gpsimd", sPE)
            load_gu(0, 0)
            for j in range(NJ):
                slot = j % 2
                if j < NJ - 1:
                    W("gpsimd", sPE)
                    load_gu(j + 1, (j + 1) % 2)
                W("tensor", sG, sG.n - (32 if j < NJ - 1 else 0))
                if half == 1 and j == 0:
                    W("vector", sPE)
                for tl in range(2):
                    W("tensor", sAC)
                    W("tensor", sDV)
                    for cc in range(8):
                        EM("tensor",
                           lambda e, c=cc, s=slot, t=tl, ho=hoff: e.matmul(
                               ps1[:, :],
                               bass.AP(WGB, 1024 * s + 128 * c,
                                       [[2048, 128], [1, 128]]),
                               bass.AP(XN2, 2048 * c + ho + 512 * t,
                                       [[8 * 2048, 128], [1, 512]]),
                               start=(c == 0), stop=(c == 7)), inc=sPE)
                        EM("tensor",
                           lambda e, c=cc, s=slot, t=tl, ho=hoff: e.matmul(
                               ps2[:, :],
                               bass.AP(WUB, 1024 * s + 128 * c,
                                       [[2048, 128], [1, 128]]),
                               bass.AP(XN2, 2048 * c + ho + 512 * t,
                                       [[8 * 2048, 128], [1, 512]]),
                               start=(c == 0), stop=(c == 7)), inc=sPE)
                    W("scalar", sPE)
                    EM("scalar",
                       lambda e: e.activation(SILU[:, :], ps1[:, :], AF.Silu,
                                              bias=0.0, scale=INV127),
                       inc=sAC)
                    W("vector", sAC)
                    EM("vector",
                       lambda e, jj=j, t=tl: e.tensor_mul(
                           HB[:, 1024 * jj + 512 * t:
                              1024 * jj + 512 * (t + 1)],
                           SILU[:, :], ps2[:, :]), inc=sDV)
            W("tensor", sDV)
            for jh in range(2):
                W("gpsimd", sPE)
                load_wd(jh)
                W("tensor", sG)
                for tl in range(2):
                    for dc in range(8):
                        bank = ps3 if dc % 2 == 0 else psA
                        for j in range(NJ // 2):
                            EM("tensor",
                               lambda e, d=dc, jj=j, t=tl, b=bank, jh_=jh:
                               e.matmul(
                                   b[:, 0:512],
                                   bass.AP(WDB, 1024 * jj + 128 * d,
                                           [[(NJ // 2) * 1024, 128],
                                            [1, 128]]),
                                   HB[:, 1024 * (jj + jh_ * (NJ // 2))
                                      + 512 * t:
                                      1024 * (jj + jh_ * (NJ // 2))
                                      + 512 * (t + 1)],
                                   start=(jj == 0), stop=(jj == NJ // 2 - 1)),
                               inc=sPE)
                        W("vector", sPE)
                        EM("vector",
                           lambda e, d=dc, t=tl, ho=hoff, b=bank:
                           e.scalar_tensor_tensor(
                               X[:, 2048 * d + ho + 512 * t:
                                 2048 * d + ho + 512 * (t + 1)],
                               b[:, 0:512], INV127 * INV127,
                               X[:, 2048 * d + ho + 512 * t:
                                 2048 * d + ho + 512 * (t + 1)],
                               ALU.mult, ALU.add), inc=sDV)
                        W("tensor", sDV)
        ar_to(mF)
        BARRIER()

    emit_ffn("f1")
    TAP(1)

    mB = ar_mark()
    ROPEC = ar("ROPEC", [128, NT * 64], BF16)
    ROPES = ar("ROPES", [128, NT * 64], BF16)
    NQT = ar("NQT", [96, T], BF16)
    CKVT = ar("CKVT", [32, T], BF16)
    KRN = ar("KRN", [128, NT * 64], BF16)
    KST = ar("KST", [96, HPC * 256], BF16)
    VSA = ar("VSA", [128, 2 * 264], BF16)
    KSN = ar("KSN", [128, 2 * 768], BF16)
    WDQN = ar("WDQN", [96, 256], BF16)
    WDQR = ar("WDQR", [96, 512], BF16)
    WDKN = ar("WDKN", [32, 256], BF16)
    WDVB = ar("WDVB", [32, 256], BF16)
    GQC = ar("GQC", [96, 1], F32)
    GKVC = ar("GKVC", [32, 1], F32)
    T1B = ar("T1B", [128, 512], BF16)
    T2B = ar("T2B", [128, 512], BF16)
    T1Bb = ar("T1Bb", [128, 512], BF16)
    T2Bb = ar("T2Bb", [128, 512], BF16)
    RMSO = ar("RMSO", [128, 128], BF16)
    RMSOb = ar("RMSOb", [128, 128], BF16)
    KR1 = ar("KR1", [128, 64], BF16)
    KR2 = ar("KR2", [128, 128], BF16)
    NATF = ar("NATF", [128, 128], F32)
    SQF = ar("SQF", [128, 128], F32)
    mB2 = ar_mark()
    XNB3 = ar("XNB3", [128, 4096], BF16)
    WCQ = ar("WCQ", [128, 768], BF16)
    WCKV = ar("WCKV", [128, 256], BF16)
    WKRO = ar("WKRO", [128, 512], BF16)
    WWK = ar("WWK", [128, 8 * 768], BF16)
    WWV = ar("WWV", [128, 8 * 256], BF16)
    WIMP = ar("WIMP", [128, 8], BF16)
    WGT3 = ar("WGT3", [128, 24], BF16)
    KWN = ar("KWN", [128, 768], BF16)
    KWNb = ar("KWNb", [128, 768], BF16)
    KWSTG = ar("KWSTG", [96, 1024], BF16)
    VST = ar("VST", [128, 264], BF16)
    XMF = ar("XMF", [128, 16], F32)
    XMB = ar("XMB", [128, 8], BF16)
    GROW = ar("GROW", [1, 16], F32)
    SSC = ar("SSC", [128, 32], F32)
    JUNK = ar("JUNK", [128, 32], F32)
    CUMA = ar("CUMA", [128, 32], F32)
    CUMC = ar("CUMC", [128, 32], F32)
    JUNKB = ar("JUNKB", [128, 32], BF16)
    CP1 = ar("CP1", [1, 32], F32)
    CP2 = ar("CP2", [1, 32], F32)
    PTS = ar("PTS", [128, NT * 256], BF16)
    KFT = ar("KFT", [128, 768], BF16)
    WSK = ar("WSK", [128, 8 * 768], BF16)

    emit_ln(sink_sb=None)
    cc_xn3 = sCC.n + 1
    W("gpsimd", sSY)
    exchange(cin_half, xn_allg)
    DMA("sync", ROPEC[:, :], s_ropec[:, :])
    DMA("sync", ROPES[:, :], s_ropes[:, :])
    DMA("gpsimd", WCQ[:, :],
        bass.AP(wcq, 0, [[96, 128], [128 * 96, 8], [1, 96]]))
    DMA("gpsimd", WCKV[:, :],
        bass.AP(wckv, 0, [[32, 128], [128 * 32, 8], [1, 32]]))
    DMA("gpsimd", WKRO[:, :],
        bass.AP(wkro, 0, [[64, 128], [128 * 64, 8], [1, 64]]))
    DMA("gpsimd", WDQN[:, :], wdqn[:, :])
    DMA("gpsimd", WDQR[:, :], wdqr[:, :])
    DMA("gpsimd", WDKN[:, :], wdkn[:, :])
    DMA("gpsimd", WDVB[:, :], wdv[:, :])
    DMA("gpsimd", WWK[:, :],
        bass.AP(wwink, 0, [[768, 128], [128 * 768, 8], [1, 768]]))
    DMA("gpsimd", WWV[:, :],
        bass.AP(wwinv, 0, [[256, 128], [128 * 256, 8], [1, 256]]))
    DMA("gpsimd", WIMP[:, :],
        bass.AP(wimp, 0, [[1, 128], [128, 8]]), allow_slow=True)
    DMA("gpsimd", WGT3[:, :],
        bass.AP(wgate, 0, [[3, 128], [128 * 3, 8], [1, 3]]))
    DMA("gpsimd", WSK[:, :],
        bass.AP(wselk, 0, [[768, 128], [128 * 768, 8], [1, 768]]))
    DMA("sync", GQC[:, :], gq[:, :])
    DMA("sync", GKVC[:, :], gkv[:, :])
    W("vector", sG)
    W("vector", sSY)
    EM("vector", lambda e: e.tensor_scalar(
        WDQN[:, :], WDQN[:, :], GQC[0:96, 0:1], None, ALU.mult), inc=sDV)
    EM("vector", lambda e: e.tensor_scalar(
        WDQR[:, :], WDQR[:, :], GQC[0:96, 0:1], None, ALU.mult), inc=sDV)
    EM("vector", lambda e: e.tensor_scalar(
        WDKN[:, :], WDKN[:, :], GKVC[0:32, 0:1], None, ALU.mult), inc=sDV)
    EM("vector", lambda e: e.tensor_scalar(
        WDVB[:, :], WDVB[:, :], GKVC[0:32, 0:1], None, ALU.mult), inc=sDV)
    EM("vector", lambda e: e.memset(XMF[:, 0:8], 0.0), inc=sDV)
    W("sync", sCC, cc_xn3)

    def load_xn_tile(tt, slot):
        hb = tt // 16
        ct = tt % 16
        DMA("sync",
            bass.AP(XNB3, 2048 * slot, [[4096, 128], [128, 8], [1, 128]]),
            bass.AP(xn_allg, (1024 * hb) * TH + 128 * ct,
                    [[TH, 128], [128 * TH, 8], [1, 128]]))

    def xnb3(slot, cc):
        return bass.AP(XNB3, 2048 * slot + 128 * cc, [[4096, 128], [1, 128]])

    TRS = [(T1B, T2B, KWN, RMSO), (T1Bb, T2Bb, KWNb, RMSOb)]

    def p1_tr_block(t, gate, kw_spill_prev):
        T1x, T2x, KWx, RMx = TRS[t % 2]
        W("tensor", sDV, gate)
        EM("tensor", lambda e, R=RMx: e.transpose(
            psT[0:96, 0:128], R[:, 32:128], IDENT[:, :]), inc=sPE)
        EM("tensor", lambda e, R=RMx: e.transpose(
            psT[0:32, 128:256], R[:, 0:32], IDENT[:, :]), inc=sPE)
        EM("tensor", lambda e, K=KWx: e.transpose(
            psT[0:96, 256:384], K[:, 0:96], IDENT[:, :]), inc=sPE)
        EM("tensor", lambda e, K=KWx: e.transpose(
            psT[0:96, 384:512], K[:, 96:192], IDENT[:, :]), inc=sPE)
        W("vector", sPE)
        if kw_spill_prev is not None:
            W("vector", sSY, kw_spill_prev)
        EM("vector", lambda e, t=t: e.tensor_copy(
            NQT[0:96, 128 * t:128 * (t + 1)], psT[0:96, 0:128]), inc=sDV)
        EM("vector", lambda e, t=t: e.tensor_copy(
            CKVT[0:32, 128 * t:128 * (t + 1)], psT[0:32, 128:256]), inc=sDV)
        EM("vector", lambda e: e.tensor_copy(
            KWSTG[0:96, 0:128], psT[0:96, 256:384]), inc=sDV)
        EM("vector", lambda e: e.tensor_copy(
            KWSTG[0:96, 128:256], psT[0:96, 384:512]), inc=sDV)
        for rnd, hs in ((1, (2, 3, 4, 5)), (2, (6, 7))):
            W("tensor", sDV)
            for j, hh in enumerate(hs):
                EM("tensor", lambda e, K=KWx, h_=hh, j=j: e.transpose(
                    psT[0:96, 128 * j:128 * (j + 1)],
                    K[:, 96 * h_:96 * (h_ + 1)], IDENT[:, :]), inc=sPE)
            W("vector", sPE)
            for j, hh in enumerate(hs):
                EM("vector", lambda e, h_=hh, j=j: e.tensor_copy(
                    KWSTG[0:96, 128 * h_:128 * (h_ + 1)],
                    psT[0:96, 128 * j:128 * (j + 1)]), inc=sDV)
        W("sync", sDV)
        DMA("sync",
            bass.AP(kwT_scr, 128 * t, [[T, 96], [96 * T, 8], [1, 128]]),
            KWSTG[0:96, :])
        return sSY.n

    readers_done = sDV.n
    pend_tr = None
    kw_spill = None
    vst_spill = None
    W("sync", sPE)
    load_xn_tile(0, 0)
    for tt in range(NT):
        slot = tt % 2
        T1x, T2x, KWx, RMx = TRS[tt % 2]
        if tt < NT - 1:
            W("sync", sPE)
            xn_base = sSY.n
            load_xn_tile(tt + 1, (tt + 1) % 2)
            W("tensor", sSY, xn_base)
        else:
            W("tensor", sSY)
        W("tensor", sDV, readers_done)
        for cc in range(8):
            EM("tensor", lambda e, c=cc, s=slot: e.matmul(
                ps1[:, 0:96], xnb3(s, c),
                bass.AP(WCQ, 96 * c, [[768, 128], [1, 96]]),
                start=(c == 0), stop=(c == 7)), inc=sPE)
            EM("tensor", lambda e, c=cc, s=slot: e.matmul(
                ps2[:, 0:32], xnb3(s, c),
                bass.AP(WCKV, 32 * c, [[256, 128], [1, 32]]),
                start=(c == 0), stop=(c == 7)), inc=sPE)
            EM("tensor", lambda e, c=cc, s=slot: e.matmul(
                psA[:, 1536:1600], xnb3(s, c),
                bass.AP(WKRO, 64 * c, [[512, 128], [1, 64]]),
                start=(c == 0), stop=(c == 7)), inc=sPE)
            EM("tensor", lambda e, c=cc, s=slot: e.matmul(
                psA[:, 0:512], xnb3(s, c),
                bass.AP(WWK, 768 * c, [[8 * 768, 128], [1, 512]]),
                start=(c == 0), stop=(c == 7)), inc=sPE)
            EM("tensor", lambda e, c=cc, s=slot: e.matmul(
                psA[:, 512:768], xnb3(s, c),
                bass.AP(WWK, 768 * c + 512, [[8 * 768, 128], [1, 256]]),
                start=(c == 0), stop=(c == 7)), inc=sPE)
            EM("tensor", lambda e, c=cc, s=slot: e.matmul(
                psA[:, 1024:1280], xnb3(s, c),
                bass.AP(WWV, 256 * c, [[8 * 256, 128], [1, 256]]),
                start=(c == 0), stop=(c == 7)), inc=sPE)
            EM("tensor", lambda e, c=cc, s=slot, t=tt: e.matmul(
                ps3[:, t:t + 1], xnb3(s, c), WIMP[:, c:c + 1],
                start=(c == 0), stop=(c == 7)), inc=sPE)
        if pend_tr is not None:
            kw_spill = p1_tr_block(tt - 1, pend_tr, kw_spill)
        W("vector", sPE)
        for cc in range(8):
            EM("vector", lambda e, c=cc, s=slot: e.reduce_sum(
                XMF[:, 8 + c:9 + c], xnb3(s, c),
                axis=mybir.AxisListType.X), inc=sDV)
        W("vector", sDV)
        EM("vector", lambda e: e.tensor_add(
            XMF[:, 0:8], XMF[:, 0:8], XMF[:, 8:16]), inc=sDV)
        EM("vector", lambda e: e.tensor_copy(NATF[:, 0:96], ps1[:, 0:96]),
           inc=sDV)
        EM("vector", lambda e: e.tensor_copy(NATF[:, 96:128], ps2[:, 0:32]),
           inc=sDV)
        EM("vector", lambda e: e.tensor_scalar(
            KR1[:, :], psA[:, 1536:1600], 1.0 / 16.0, None, ALU.mult),
           inc=sDV)
        EM("vector", lambda e, K=KWx: e.tensor_copy(K[:, :], psA[:, 0:768]),
           inc=sDV)
        if vst_spill is not None:
            W("vector", sSY, vst_spill)
        EM("vector", lambda e: e.memset(VST[:, :], 1.0), inc=sDV)
        W("vector", sDV)
        EM("vector", lambda e: e.tensor_copy(
            bass.AP(VST, 0, [[264, 128], [33, 8], [1, 32]]),
            bass.AP(psA, 1024, [[2048, 128], [32, 8], [1, 32]])), inc=sDV)
        readers_done = sDV.n
        W("sync", sDV)
        DMA("sync",
            bass.AP(vw_scr, (128 * tt) * 33,
                    [[33, 128], [T * 33, 8], [1, 33]]),
            VST[:, :])
        vst_spill = sSY.n
        EM("vector", lambda e: e.tensor_mul(SQF[:, 0:128], NATF[:, 0:128],
                                            NATF[:, 0:128]), inc=sDV)
        W("vector", sDV)
        EM("vector", lambda e: e.reduce_sum(
            MISC[:, 0:1], SQF[:, 0:96], axis=mybir.AxisListType.X), inc=sDV)
        EM("vector", lambda e: e.reduce_sum(
            MISC[:, 4:5], SQF[:, 96:128], axis=mybir.AxisListType.X),
           inc=sDV)
        W("vector", sDV)
        EM("vector", lambda e: e.tensor_scalar(
            MISC[:, 1:2], MISC[:, 0:1], 1.0 / 96, 1e-6, ALU.mult,
            op1=ALU.add), inc=sDV)
        EM("vector", lambda e: e.tensor_scalar(
            MISC[:, 5:6], MISC[:, 4:5], 1.0 / 32, 1e-6, ALU.mult,
            op1=ALU.add), inc=sDV)
        W("scalar", sDV)
        EM("scalar", lambda e: e.activation(
            MISC[:, 2:3], MISC[:, 1:2], AF.Ln, bias=0.0, scale=1.0),
           inc=sAC)
        EM("scalar", lambda e: e.activation(
            MISC[:, 6:7], MISC[:, 5:6], AF.Ln, bias=0.0, scale=1.0),
           inc=sAC)
        EM("scalar", lambda e: e.activation(
            MISC[:, 3:4], MISC[:, 2:3], AF.Exp, bias=0.0, scale=-0.5),
           inc=sAC)
        EM("scalar", lambda e: e.activation(
            MISC[:, 7:8], MISC[:, 6:7], AF.Exp, bias=0.0, scale=-0.5),
           inc=sAC)
        W("vector", sAC)
        EM("vector", lambda e, R=RMx: e.tensor_scalar(
            R[:, 32:128], NATF[:, 0:96], MISC[:, 3:4], None, ALU.mult),
           inc=sDV)
        EM("vector", lambda e, R=RMx: e.tensor_scalar(
            R[:, 0:32], NATF[:, 96:128], MISC[:, 7:8], None, ALU.mult),
           inc=sDV)
        EM("vector", lambda e, t=tt: e.tensor_mul(
            KR2[:, 0:64], KR1[:, :],
            bass.AP(ROPEC, 64 * t, [[NT * 64, 128], [1, 64]])), inc=sDV)
        EM("vector", lambda e, t=tt: e.tensor_mul(
            KR2[:, 64:128],
            bass.AP(KR1, 32, [[64, 128], [-32, 2], [1, 32]]),
            bass.AP(ROPES, 64 * t, [[NT * 64, 128], [1, 64]])), inc=sDV)
        EM("vector", lambda e, t=tt, K=KWx, T1x=T1x: e.tensor_mul(
            bass.AP(T1x, 128, [[512, 128], [64, 6], [1, 64]]),
            bass.AP(K, 32 + 2 * 96, [[768, 128], [96, 6], [1, 64]]),
            bass.AP(ROPEC, 64 * t, [[NT * 64, 128], [0, 6], [1, 64]])),
           inc=sDV)
        EM("vector", lambda e, t=tt: e.tensor_add(
            KRN[:, 64 * t:64 * (t + 1)], KR2[:, 0:64], KR2[:, 64:128]),
           inc=sDV)
        EM("vector", lambda e, t=tt, K=KWx, T2x=T2x: e.tensor_mul(
            bass.AP(T2x, 0, [[512, 128], [64, 8], [1, 64]]),
            bass.AP(K, 64, [[768, 128], [96, 8], [-32, 2], [1, 32]]),
            bass.AP(ROPES, 64 * t, [[NT * 64, 128], [0, 8], [1, 64]])),
           inc=sDV)
        EM("vector", lambda e, t=tt, K=KWx, T1x=T1x: e.tensor_mul(
            bass.AP(T1x, 0, [[512, 128], [64, 2], [1, 64]]),
            bass.AP(K, 32, [[768, 128], [96, 2], [1, 64]]),
            bass.AP(ROPEC, 64 * t, [[NT * 64, 128], [0, 2], [1, 64]])),
           inc=sDV)
        W("vector", sDV)
        EM("vector", lambda e, K=KWx, T1x=T1x, T2x=T2x: e.tensor_add(
            bass.AP(K, 32, [[768, 128], [96, 8], [1, 64]]),
            bass.AP(T1x, 0, [[512, 128], [64, 8], [1, 64]]),
            bass.AP(T2x, 0, [[512, 128], [64, 8], [1, 64]])), inc=sDV)
        pend_tr = sDV.n

    kw_spill = p1_tr_block(NT - 1, pend_tr, kw_spill)
    W("tensor", sDV)

    for tt in range(NT):
        EM("tensor", lambda e, t=tt: e.matmul(
            ps1[:, 0:256], CKVT[0:32, 128 * t:128 * (t + 1)], WDVB[0:32, :],
            start=True, stop=True), inc=sPE)
        W("vector", sPE)
        W("vector", sSY)
        EM("vector", lambda e: e.memset(VST[:, :], 1.0), inc=sDV)
        W("vector", sDV)
        EM("vector", lambda e: e.tensor_copy(
            bass.AP(VST, 0, [[264, 128], [33, 8], [1, 32]]),
            bass.AP(ps1, 0, [[512, 128], [32, 8], [1, 32]])), inc=sDV)
        W("sync", sDV)
        DMA("sync",
            bass.AP(v1_scr, (128 * tt) * 33,
                    [[33, 128], [T * 33, 8], [1, 33]]),
            VST[:, :])
        W("tensor", sSY)

    EM("vector", lambda e: e.tensor_scalar(
        XMB[:, 0:8], XMF[:, 0:8], 1.0 / T, None, ALU.mult), inc=sDV)
    W("tensor", sDV)
    for cc in range(8):
        EM("tensor", lambda e, c=cc: e.matmul(
            ps2[0:1, 0:3], XMB[:, c:c + 1], WGT3[:, 3 * c:3 * (c + 1)],
            start=(c == 0), stop=(c == 7)), inc=sPE)
    W("vector", sPE)
    EM("vector", lambda e: e.tensor_copy(GROW[0:1, 0:3], ps2[0:1, 0:3]),
       inc=sDV)
    W("scalar", sDV)
    EM("scalar", lambda e: e.activation(
        GROW[0:1, 3:6], GROW[0:1, 0:3], AF.Exp, bias=0.0, scale=1.0),
       inc=sAC)
    W("vector", sAC)
    EM("vector", lambda e: e.reduce_sum(
        GROW[0:1, 6:7], GROW[0:1, 3:6], axis=mybir.AxisListType.X), inc=sDV)
    W("vector", sDV)
    EM("vector", lambda e: e.reciprocal(GROW[0:1, 7:8], GROW[0:1, 6:7]),
       inc=sDV)
    W("vector", sDV)
    EM("vector", lambda e: e.tensor_scalar(
        GROW[0:1, 3:6], GROW[0:1, 3:6], GROW[0:1, 7:8], None, ALU.mult),
       inc=sDV)
    W("tensor", sDV)
    EM("tensor", lambda e: e.matmul(
        ps2[:, 8:11], ONESF[0:1, :], GROW[0:1, 3:6], start=True, stop=True),
       inc=sPE)
    W("vector", sPE)
    EM("vector", lambda e: e.tensor_copy(CONST[:, 32:35], ps2[:, 8:11]),
       inc=sDV)

    EM("vector", lambda e: e.tensor_copy(SSC[:, :], ps3[:, 0:32]), inc=sDV)
    EM("vector", lambda e: e.memset(MISC[:, 0:1], -1000.0), inc=sDV)
    EM("vector", lambda e: e.memset(MISC[:, 1:2], 1000.0), inc=sDV)
    EM("vector", lambda e: e.memset(MISC[:, 2:3], 0.0), inc=sDV)
    for it in range(NBIS):
        W("vector", sPE)
        EM("vector", lambda e: e.tensor_scalar(
            JUNK[:, :], SSC[:, :], MISC[:, 2:3], 0.0, ALU.is_gt,
            op1=ALU.add, accum_out=MISC[:, 6:7]), inc=sDV)
        W("tensor", sDV)
        EM("tensor", lambda e: e.matmul(
            ps2[0:1, 16:17], ONESF[:, 0:1], MISC[:, 6:7],
            start=True, stop=True), inc=sPE)
        W("vector", sPE)
        EM("vector", lambda e: e.tensor_copy(GROW[0:1, 8:9],
                                             ps2[0:1, 16:17]), inc=sDV)
        W("tensor", sDV)
        EM("tensor", lambda e: e.matmul(
            ps2[:, 17:18], ONESF[0:1, :], GROW[0:1, 8:9],
            start=True, stop=True), inc=sPE)
        W("vector", sPE)
        EM("vector", lambda e: e.tensor_scalar(
            MISC[:, 3:4], ps2[:, 17:18], float(KEEP) - 0.5, None, ALU.is_gt),
           inc=sDV)
        EM("vector", lambda e: e.tensor_sub(
            MISC[:, 4:5], MISC[:, 2:3], MISC[:, 0:1]), inc=sDV)
        EM("vector", lambda e: e.tensor_sub(
            MISC[:, 8:9], MISC[:, 1:2], MISC[:, 2:3]), inc=sDV)
        EM("vector", lambda e: e.tensor_scalar(
            MISC[:, 5:6], MISC[:, 3:4], -1.0, 1.0, ALU.mult, op1=ALU.add),
           inc=sDV)
        W("vector", sDV)
        EM("vector", lambda e: e.tensor_mul(
            MISC[:, 4:5], MISC[:, 4:5], MISC[:, 3:4]), inc=sDV)
        EM("vector", lambda e: e.tensor_mul(
            MISC[:, 8:9], MISC[:, 8:9], MISC[:, 5:6]), inc=sDV)
        W("vector", sDV)
        EM("vector", lambda e: e.tensor_add(
            MISC[:, 0:1], MISC[:, 0:1], MISC[:, 4:5]), inc=sDV)
        EM("vector", lambda e: e.tensor_sub(
            MISC[:, 1:2], MISC[:, 1:2], MISC[:, 8:9]), inc=sDV)
        W("vector", sDV)
        EM("vector", lambda e: e.tensor_add(
            MISC[:, 2:3], MISC[:, 0:1], MISC[:, 1:2]), inc=sDV)
        W("vector", sDV)
        EM("vector", lambda e: e.tensor_scalar(
            MISC[:, 2:3], MISC[:, 2:3], 0.5, None, ALU.mult), inc=sDV)
        W("vector", sDV)
    W("vector", sDV)
    EM("vector", lambda e: e.tensor_scalar(
        JUNK[:, :], SSC[:, :], MISC[:, 0:1], None, ALU.is_gt), inc=sDV)
    W("vector", sDV)
    EM("vector", lambda e: e.tensor_copy(JUNKB[:, :], JUNK[:, :]), inc=sDV)
    W("tensor", sDV)
    EM("tensor", lambda e: e.matmul(
        ps1[:, 0:32], TRILS[:, :], JUNKB[:, :], start=True, stop=False),
       inc=sPE)
    EM("tensor", lambda e: e.matmul(
        ps1[:, 0:32], IDENT[:, :], JUNKB[:, :], start=False, stop=True),
       inc=sPE)
    EM("tensor", lambda e: e.matmul(
        ps2[0:1, 0:32], ONESB[:, 0:1], JUNKB[:, :], start=True, stop=True),
       inc=sPE)
    W("vector", sPE)
    EM("vector", lambda e, A=CUMA: e.tensor_copy(A[:, :], ps1[:, 0:32]),
       inc=sDV)
    EM("vector", lambda e: e.memset(CP1[0:1, 0:1], 0.0), inc=sDV)
    EM("vector", lambda e: e.tensor_copy(CP1[0:1, 1:32], ps2[0:1, 0:31]),
       inc=sDV)
    cpa, cpc = CP1, CP2
    for k in (1, 2, 4, 8, 16):
        W("vector", sDV)
        EM("vector", lambda e, kk=k, A=cpa, C=cpc: e.tensor_copy(
            C[0:1, 0:kk], A[0:1, 0:kk]), inc=sDV)
        EM("vector", lambda e, kk=k, A=cpa, C=cpc: e.tensor_add(
            C[0:1, kk:32], A[0:1, kk:32], A[0:1, 0:32 - kk]), inc=sDV)
        cpa, cpc = cpc, cpa
    W("tensor", sDV)
    EM("tensor", lambda e, A=cpa: e.matmul(
        ps3[:, 0:32], ONESF[0:1, :], A[0:1, :], start=True, stop=True),
       inc=sPE)
    W("vector", sPE)
    EM("vector", lambda e, A=CUMA, C=CUMC: e.tensor_add(
        C[:, :], A[:, :], ps3[:, 0:32]), inc=sDV)
    W("vector", sDV)
    EM("vector", lambda e, C=CUMC: e.tensor_mul(
        C[:, :], C[:, :], JUNK[:, :]), inc=sDV)
    W("vector", sDV)
    for tt in range(NT):
        EM("vector", lambda e, t=tt, C=CUMC: e.tensor_scalar(
            PTS[:, 256 * t:256 * (t + 1)], IOTA[:, :], C[:, t:t + 1],
            None, ALU.is_equal), inc=sDV)
    W("tensor", sDV)
    if _TAPS:
        W("sync", sDV)
        DMA("sync", dbg_pts[:, :], PTS[:, :])
        DMA("sync", dbg_junk[:, :], JUNK[:, :])
        DMA("sync", dbg_cum[:, :], CUMA[:, :])
        DMA("sync", dbg_cumf[:, :], CUMC[:, :])
        DMA("sync", dbg_ssc[:, :], SSC[:, :])

    W("sync", sPE)
    load_xn_tile(0, 0)
    for tt in range(NT):
        slot = tt % 2
        if tt < NT - 1:
            W("sync", sPE)
            load_xn_tile(tt + 1, (tt + 1) % 2)
        W("tensor", sSY, sSY.n - (16 if tt < NT - 1 else 0))
        for cc in range(8):
            EM("tensor", lambda e, c=cc, s=slot: e.matmul(
                psA[:, 0:512], xnb3(s, c),
                bass.AP(WSK, 768 * c, [[8 * 768, 128], [1, 512]]),
                start=(c == 0), stop=(c == 7)), inc=sPE)
            EM("tensor", lambda e, c=cc, s=slot: e.matmul(
                psA[:, 512:768], xnb3(s, c),
                bass.AP(WSK, 768 * c + 512, [[8 * 768, 128], [1, 256]]),
                start=(c == 0), stop=(c == 7)), inc=sPE)
        W("vector", sPE)
        EM("vector", lambda e: e.tensor_copy(KFT[:, :], psA[:, 0:768]),
           inc=sDV)
        W("tensor", sDV)
        for r in range(2):
            EM("tensor", lambda e, t=tt, rr=r: e.matmul(
                ps1[:, :] if rr == 0 else ps2[:, :],
                PTS[:, 256 * t + 128 * rr:256 * t + 128 * (rr + 1)],
                KFT[:, 0:512], start=(t == 0), stop=(t == NT - 1)), inc=sPE)
            EM("tensor", lambda e, t=tt, rr=r: e.matmul(
                ps3[:, 0:256] if rr == 0 else psA[:, 1536:1792],
                PTS[:, 256 * t + 128 * rr:256 * t + 128 * (rr + 1)],
                KFT[:, 512:768], start=(t == 0), stop=(t == NT - 1)),
               inc=sPE)
    W("vector", sPE)
    for r in range(2):
        EM("vector", lambda e, rr=r: e.tensor_copy(
            KSN[:, 768 * rr:768 * rr + 512],
            ps1[:, :] if rr == 0 else ps2[:, :]), inc=sDV)
        EM("vector", lambda e, rr=r: e.tensor_copy(
            KSN[:, 768 * rr + 512:768 * (rr + 1)],
            ps3[:, 0:256] if rr == 0 else psA[:, 1536:1792]), inc=sDV)
    for r in range(2):
        EM("vector", lambda e, rr=r: e.tensor_mul(
            bass.AP(T1B, 0, [[512, 128], [64, 8], [1, 64]]),
            bass.AP(KSN, 768 * rr + 32, [[2 * 768, 128], [96, 8], [1, 64]]),
            bass.AP(ROPEC, 64 * rr, [[NT * 64, 128], [0, 8], [1, 64]])),
           inc=sDV)
        EM("vector", lambda e, rr=r: e.tensor_mul(
            bass.AP(T2B, 0, [[512, 128], [64, 8], [1, 64]]),
            bass.AP(KSN, 768 * rr + 64, [[2 * 768, 128], [96, 8],
                                         [-32, 2], [1, 32]]),
            bass.AP(ROPES, 64 * rr, [[NT * 64, 128], [0, 8], [1, 64]])),
           inc=sDV)
        W("vector", sDV)
        EM("vector", lambda e, rr=r: e.tensor_add(
            bass.AP(KSN, 768 * rr + 32, [[2 * 768, 128], [96, 8], [1, 64]]),
            bass.AP(T1B, 0, [[512, 128], [64, 8], [1, 64]]),
            bass.AP(T2B, 0, [[512, 128], [64, 8], [1, 64]])), inc=sDV)
    W("tensor", sDV)
    for hh in range(HPC):
        for r in range(2):
            EM("tensor", lambda e, h_=hh, rr=r: e.transpose(
                psT[0:96, 0:128],
                KSN[:, 768 * rr + 96 * h_:768 * rr + 96 * (h_ + 1)],
                IDENT[:, :]), inc=sPE)
            W("vector", sPE)
            EM("vector", lambda e, h_=hh, rr=r: e.tensor_copy(
                KST[0:96, 256 * h_ + 128 * rr:256 * h_ + 128 * (rr + 1)],
                psT[0:96, 0:128]), inc=sDV)
            W("tensor", sDV)

    W("gpsimd", sPE)
    DMA("gpsimd", WWV[:, :],
        bass.AP(wselv, 0, [[256, 128], [128 * 256, 8], [1, 256]]))
    W("tensor", sG)
    W("sync", sPE)
    load_xn_tile(0, 0)
    for tt in range(NT):
        slot = tt % 2
        if tt < NT - 1:
            W("sync", sPE)
            load_xn_tile(tt + 1, (tt + 1) % 2)
        W("tensor", sSY, sSY.n - (16 if tt < NT - 1 else 0))
        for cc in range(8):
            EM("tensor", lambda e, c=cc, s=slot: e.matmul(
                psA[:, 0:256], xnb3(s, c),
                bass.AP(WWV, 256 * c, [[8 * 256, 128], [1, 256]]),
                start=(c == 0), stop=(c == 7)), inc=sPE)
        W("vector", sPE)
        EM("vector", lambda e: e.tensor_copy(KFT[:, 0:256], psA[:, 0:256]),
           inc=sDV)
        W("tensor", sDV)
        for r in range(2):
            EM("tensor", lambda e, t=tt, rr=r: e.matmul(
                ps1[:, 0:256] if rr == 0 else ps2[:, 0:256],
                PTS[:, 256 * t + 128 * rr:256 * t + 128 * (rr + 1)],
                KFT[:, 0:256], start=(t == 0), stop=(t == NT - 1)), inc=sPE)
    W("vector", sPE)
    EM("vector", lambda e: e.memset(VSA[:, :], 1.0), inc=sDV)
    W("vector", sDV)
    for r in range(2):
        EM("vector", lambda e, rr=r: e.tensor_copy(
            bass.AP(VSA, 264 * rr, [[2 * 264, 128], [33, 8], [1, 32]]),
            bass.AP(ps1 if rr == 0 else ps2, 0,
                    [[512, 128], [32, 8], [1, 32]])), inc=sDV)
    W("tensor", sDV)

    ar_to(mB2)
    MASKS = ar("MASKS", [128, 2048], BF16)
    QNH = ar("QNH", [128, NT * 96], BF16)
    K1NH = ar("K1NH", [128, NT * 96], BF16)
    QTH = ar("QTH", [96, T], BF16)
    K1TH = ar("K1TH", [96, T], BF16)
    KWTH = ar("KWTH", [96, T], BF16)
    V1A = ar("V1A", [128, NT * 33], BF16)
    VWA = ar("VWA", [128, NT * 33], BF16)
    EXPB = ar("EXPB", [128, 4096], BF16)
    AEVF3 = ar("AEVF3", [128, 3 * 512], F32)
    RCP3 = ar("RCP3", [128, 8], F32)
    COMB3 = ar("COMB3", [128, 256], BF16)
    AVST3 = ar("AVST3", [128, 512], BF16)
    RB1 = ar("RB1", [128, 512], BF16)
    RB2 = ar("RB2", [128, 512], BF16)
    DMA("sync", MASKS[:, :], s_masks[:, :])
    W("vector", sSY)

    for h in range(HPC):
        W("sync", sPE)
        W("tensor", sDV)
        DMA("sync", KWTH[0:96, :],
            bass.AP(kwT_scr, (96 * h) * T, [[T, 96], [1, T]]))
        DMA("sync", V1A[:, :],
            bass.AP(v1_scr, (T * h) * 33,
                    [[33, 128], [128 * 33, NT], [1, 33]]))
        DMA("sync", VWA[:, :],
            bass.AP(vw_scr, (T * h) * 33,
                    [[33, 128], [128 * 33, NT], [1, 33]]))
        kw_ready = sSY.n
        for t2 in range(0, NT, 2):
            for j, pso in ((0, 0), (1, 256)):
                t = t2 + j
                EM("tensor", lambda e, t=t, h_=h, o=pso: e.matmul(
                    ps1[:, o:o + 32], NQT[0:96, 128 * t:128 * (t + 1)],
                    WDQN[0:96, 32 * h_:32 * (h_ + 1)], start=True,
                    stop=True), inc=sPE)
                EM("tensor", lambda e, t=t, h_=h, o=pso: e.matmul(
                    ps1[:, o + 32:o + 96], NQT[0:96, 128 * t:128 * (t + 1)],
                    WDQR[0:96, 64 * h_:64 * (h_ + 1)], start=True,
                    stop=True), inc=sPE)
                EM("tensor", lambda e, t=t, h_=h, o=pso: e.matmul(
                    ps2[:, o:o + 32], CKVT[0:32, 128 * t:128 * (t + 1)],
                    WDKN[0:32, 32 * h_:32 * (h_ + 1)], start=True,
                    stop=True), inc=sPE)
            W("vector", sPE)
            EM("vector", lambda e: e.tensor_copy(RB1[:, 0:96], ps1[:, 0:96]),
               inc=sDV)
            EM("vector", lambda e: e.tensor_copy(
                RB1[:, 256:352], ps1[:, 256:352]), inc=sDV)
            EM("vector", lambda e, t=t2: e.tensor_copy(
                K1NH[:, 96 * t:96 * t + 32], ps2[:, 0:32]), inc=sDV)
            EM("vector", lambda e, t=t2 + 1: e.tensor_copy(
                K1NH[:, 96 * t:96 * t + 32], ps2[:, 256:288]), inc=sDV)
            EM("vector", lambda e, t=t2: e.tensor_mul(
                T1B[:, 0:64], RB1[:, 32:96],
                bass.AP(ROPEC, 64 * t, [[NT * 64, 128], [1, 64]])), inc=sDV)
            EM("vector", lambda e, t=t2 + 1: e.tensor_mul(
                T1B[:, 128:192], RB1[:, 288:352],
                bass.AP(ROPEC, 64 * t, [[NT * 64, 128], [1, 64]])), inc=sDV)
            EM("vector", lambda e, t=t2: e.tensor_mul(
                T2B[:, 0:64],
                bass.AP(RB1, 64, [[512, 128], [-32, 2], [1, 32]]),
                bass.AP(ROPES, 64 * t, [[NT * 64, 128], [1, 64]])), inc=sDV)
            EM("vector", lambda e, t=t2 + 1: e.tensor_mul(
                T2B[:, 128:192],
                bass.AP(RB1, 256 + 64, [[512, 128], [-32, 2], [1, 32]]),
                bass.AP(ROPES, 64 * t, [[NT * 64, 128], [1, 64]])), inc=sDV)
            EM("vector", lambda e, t=t2: e.tensor_copy(
                K1NH[:, 96 * t + 32:96 * t + 96],
                KRN[:, 64 * t:64 * (t + 1)]), inc=sDV)
            EM("vector", lambda e, t=t2: e.tensor_copy(
                QNH[:, 96 * t:96 * t + 32], RB1[:, 0:32]), inc=sDV)
            EM("vector", lambda e, t=t2: e.tensor_add(
                QNH[:, 96 * t + 32:96 * t + 96], T1B[:, 0:64],
                T2B[:, 0:64]), inc=sDV)
            EM("vector", lambda e, t=t2 + 1: e.tensor_copy(
                K1NH[:, 96 * t + 32:96 * t + 96],
                KRN[:, 64 * t:64 * (t + 1)]), inc=sDV)
            EM("vector", lambda e, t=t2 + 1: e.tensor_copy(
                QNH[:, 96 * t:96 * t + 32], RB1[:, 256:288]), inc=sDV)
            EM("vector", lambda e, t=t2 + 1: e.tensor_add(
                QNH[:, 96 * t + 32:96 * t + 96], T1B[:, 128:192],
                T2B[:, 128:192]), inc=sDV)
            W("tensor", sDV)
            EM("tensor", lambda e, t=t2: e.transpose(
                psT[0:96, 0:128], QNH[:, 96 * t:96 * (t + 1)], IDENT[:, :]),
               inc=sPE)
            EM("tensor", lambda e, t=t2: e.transpose(
                psT[0:96, 128:256], K1NH[:, 96 * t:96 * (t + 1)],
                IDENT[:, :]), inc=sPE)
            EM("tensor", lambda e, t=t2 + 1: e.transpose(
                psT[0:96, 256:384], QNH[:, 96 * t:96 * (t + 1)],
                IDENT[:, :]), inc=sPE)
            EM("tensor", lambda e, t=t2 + 1: e.transpose(
                psT[0:96, 384:512], K1NH[:, 96 * t:96 * (t + 1)],
                IDENT[:, :]), inc=sPE)
            W("vector", sPE)
            EM("vector", lambda e, t=t2: e.tensor_copy(
                QTH[0:96, 128 * t:128 * (t + 1)], psT[0:96, 0:128]),
               inc=sDV)
            EM("vector", lambda e, t=t2: e.tensor_copy(
                K1TH[0:96, 128 * t:128 * (t + 1)], psT[0:96, 128:256]),
               inc=sDV)
            EM("vector", lambda e, t=t2 + 1: e.tensor_copy(
                QTH[0:96, 128 * t:128 * (t + 1)], psT[0:96, 256:384]),
               inc=sDV)
            EM("vector", lambda e, t=t2 + 1: e.tensor_copy(
                K1TH[0:96, 128 * t:128 * (t + 1)], psT[0:96, 384:512]),
               inc=sDV)
            W("tensor", sDV)
        W("tensor", sSY, kw_ready)

        for qc in range(8):
            nkb = qc + 1
            bursts = []
            for br, (KTH_, VA_, bank) in enumerate(
                    ((K1TH, V1A, ps1), (KWTH, VWA, ps3))):
                for kb in range(nkb):
                    bursts.append((KTH_, VA_, bank, kb,
                                   kb == 0, kb == nkb - 1))
            pend = None
            for bi in range(len(bursts) + 1):
                if bi < len(bursts):
                    KTH_, VA_, bank, kb, isfirst, islast = bursts[bi]
                    slot = bi % 2
                    W("tensor", sAC)
                    if bi == 0:
                        W("tensor", sDV)
                    for i in range(4):
                        kt = 4 * kb + i
                        EM("tensor", lambda e, kk=KTH_, kt_=kt, q=qc, i_=i:
                           e.matmul(
                               psA[:, 512 * i_:512 * (i_ + 1)],
                               kk[0:96, 128 * kt_:128 * (kt_ + 1)],
                               QTH[0:96, 512 * q:512 * (q + 1)],
                               start=True, stop=True), inc=sPE)
                    qk_done = sPE.n
                if pend is not None:
                    (pV, pbank, pkb, pslot, pfirst, plast) = pend
                    if plast:
                        W("tensor", sDV)
                    for i in range(4):
                        kt = 4 * pkb + i
                        EM("tensor", lambda e, v=pV, kt_=kt, s=pslot, i_=i,
                           b=pbank, st=(pfirst and i == 0),
                           sp=(plast and i == 3): e.matmul(
                               b[0:33, :],
                               bass.AP(v, 33 * kt_, [[NT * 33, 128],
                                                     [1, 33]]),
                               EXPB[:, 2048 * s + 512 * i_:
                                    2048 * s + 512 * (i_ + 1)],
                               start=st, stop=sp), inc=sPE)
                if bi == len(bursts):
                    break
                W("scalar", sPE, qk_done)
                EM("scalar", lambda e, s=slot: e.activation(
                    EXPB[:, 2048 * s:2048 * (s + 1)], psA[:, :], AF.Exp,
                    bias=CONST[:, 63:64], scale=ISQ), inc=sAC)
                if islast:
                    W("vector", sAC)
                    for i in range(4):
                        EM("vector", lambda e, s=slot, i_=i:
                           e.tensor_mul(
                               EXPB[:, 2048 * s + 512 * i_:
                                    2048 * s + 512 * (i_ + 1)],
                               EXPB[:, 2048 * s + 512 * i_:
                                    2048 * s + 512 * (i_ + 1)],
                               MASKS[:, 512 * i_:512 * (i_ + 1)]),
                           inc=sDV)
                pend = (VA_, bank, kb, slot, isfirst, islast)
            W("tensor", sAC)
            for r in range(2):
                EM("tensor", lambda e, h_=h, rr=r, q=qc: e.matmul(
                    psA[:, 512 * rr:512 * (rr + 1)],
                    KST[0:96, 256 * h_ + 128 * rr:
                        256 * h_ + 128 * (rr + 1)],
                    QTH[0:96, 512 * q:512 * (q + 1)],
                    start=True, stop=True), inc=sPE)
            W("scalar", sPE)
            EM("scalar", lambda e: e.activation(
                EXPB[:, 0:1024], psA[:, 0:1024], AF.Exp,
                bias=CONST[:, 63:64], scale=ISQ), inc=sAC)
            W("tensor", sAC)
            for r in range(2):
                EM("tensor", lambda e, rr=r: e.matmul(
                    ps2[0:33, :],
                    bass.AP(VSA, 264 * rr, [[2 * 264, 128], [1, 33]]),
                    EXPB[:, 512 * rr:512 * (rr + 1)],
                    start=(rr == 0), stop=(rr == 1)), inc=sPE)
            division([(ps1, 32), (ps2, 33), (ps3, 34)], 32,
                     AEVF3, RCP3, COMB3, AVST3)
            if _TAPS and h == 0 and qc == 1:
                DMA("sync", dbg_qth[:, :], QTH[0:96, :])
                DMA("sync", dbg_k1n[:, :], K1NH[:, :])
                DMA("sync", dbg_ckv[:, :], CKVT[0:32, :])
                DMA("sync", dbg_krn[:, :], KRN[:, :])
                DMA("sync", dbg_k1h[:, :], K1TH[0:96, :])
                DMA("sync", dbg_kwh[:, :], KWTH[0:96, :])
                DMA("sync", dbg_cst[:, :], CONST[:, :])
                DMA("sync", dbg_ab3[:, :], AEVF3[:, :])
                DMA("sync", dbg_rc3[:, :], RCP3[:, :])
                DMA("sync", dbg_cb3[:, :], COMB3[:, :])
                DMA("sync", dbg_exb[:, :], EXPB[:, :])
                DMA("sync", dbg_kst[:, :], KST[0:96, :])
                DMA("sync", dbg_vsa[:, :], VSA[:, :])
                W("sync", sSY)
            DMA("sync",
                bass.AP(cin_avb, (32 * h) * T + 512 * qc,
                        [[T, 32], [1, 512]]),
                AVST3[0:32, 0:512])

    if _TAPS:
        W("sync", sDV)
        DMA("sync", dbg_avb[:, :], cin_avb[:, :])
        DMA("sync", dbg_misc[:, :], MISC[:, :])
        DMA("sync", dbg_grow[:, :], GROW[:, :])
        W("sync", sSY)

    cc_avb = sCC.n + 1
    W("gpsimd", sSY)
    exchange(cin_avb, avb_allg)
    ar_to(mB2)
    AVRH3 = ar("AVRH3", [128, 4096], BF16)
    WPJ = ar("WPJ", [128, 4096], BF16)
    DMA("gpsimd", WPJ[:, :],
        bass.AP(wproj, 0, [[1024, 128], [128 * 1024, 4], [1, 1024]]))
    W("sync", sCC, cc_avb)
    W("tensor", sG)
    outproj(avb_allg, WPJ, 4, AVRH3)
    ar_to(mB)
    BARRIER()
    TAP(2)

    emit_ffn("f2")

    W("sync", sDV)
    DMA("sync",
        bass.AP(y_out, 0, [[TH, 128], [128 * TH, 8], [1, TH]]),
        bass.AP(X, 0, [[8 * TH, 128], [2048, 8], [1, 2048]]))
    W("sync", sSY)

    import contextlib
    with contextlib.ExitStack() as st:
        for name, c in sems.items():
            c.h = st.enter_context(nc.semaphore(name))
        block = st.enter_context(nc.Block())
        for eng in ENGS:
            getattr(block, eng)(
                (lambda fns: lambda e: [f(e) for f in fns])(prog[eng]))
    return nc


_NC = None


def kernel(
    L0_attn_a_hidden_states, L0_attn_a_Wq, L0_attn_a_Wk, L0_attn_a_Wv,
    L0_attn_a_Wo, L0_attn_a_rel_bias_table, L0_attn_b_w_cq, L0_attn_b_g_qnorm,
    L0_attn_b_w_dq_nope, L0_attn_b_w_dq_rope, L0_attn_b_w_ckv,
    L0_attn_b_g_kvnorm, L0_attn_b_w_dk_nope, L0_attn_b_w_dv,
    L0_attn_b_w_krope, L0_attn_b_w_imp, L0_attn_b_w_selk, L0_attn_b_w_selv,
    L0_attn_b_w_wink, L0_attn_b_w_winv, L0_attn_b_w_gate, L0_attn_b_w_proj,
    L0_ffn_w_gate, L0_ffn_w_up, L0_ffn_w_down,
):
    global _NC
    st = _statics()
    x = np.asarray(L0_attn_a_hidden_states)
    C = np.ascontiguousarray

    wgT_ = C(np.asarray(L0_ffn_w_gate).T)
    wuT_ = C(np.asarray(L0_ffn_w_up).T)
    wdT_ = C(np.asarray(L0_ffn_w_down).T)

    in_maps = []
    for c in range(8):
        b, hb = c // 2, c % 2
        hs = slice(512 * hb, 512 * hb + 512)
        m = {
            "x_ownT": C(x[b, 2048 * hb:2048 * hb + 2048, :].T),
            "wq": C(np.asarray(L0_attn_a_Wq)[:, hs]),
            "wk": C(np.asarray(L0_attn_a_Wk)[:, hs]),
            "wv": C(np.asarray(L0_attn_a_Wv)[:, hs]),
            "wo": C(np.asarray(L0_attn_a_Wo)),
            "rtab": C(np.asarray(L0_attn_a_rel_bias_table)[:, 8 * hb:8 * hb + 8]),
            "wgT": wgT_, "wuT": wuT_, "wdT": wdT_,
            "wcq": C(np.asarray(L0_attn_b_w_cq)),
            "gq": C(np.asarray(L0_attn_b_g_qnorm).reshape(96, 1)),
            "wdqn": C(np.asarray(L0_attn_b_w_dq_nope)[:, 256 * hb:256 * hb + 256]),
            "wdqr": C(np.asarray(L0_attn_b_w_dq_rope)[:, 512 * hb:512 * hb + 512]),
            "wckv": C(np.asarray(L0_attn_b_w_ckv)),
            "gkv": C(np.asarray(L0_attn_b_g_kvnorm).reshape(32, 1)),
            "wdkn": C(np.asarray(L0_attn_b_w_dk_nope)[:, 256 * hb:256 * hb + 256]),
            "wdv": C(np.asarray(L0_attn_b_w_dv)[:, 256 * hb:256 * hb + 256]),
            "wkro": C(np.asarray(L0_attn_b_w_krope)),
            "wimp": C(np.asarray(L0_attn_b_w_imp)),
            "wselk": C(np.asarray(L0_attn_b_w_selk)[:, 768 * hb:768 * hb + 768]),
            "wselv": C(np.asarray(L0_attn_b_w_selv)[:, 256 * hb:256 * hb + 256]),
            "wwink": C(np.asarray(L0_attn_b_w_wink)[:, 768 * hb:768 * hb + 768]),
            "wwinv": C(np.asarray(L0_attn_b_w_winv)[:, 256 * hb:256 * hb + 256]),
            "wgate": C(np.asarray(L0_attn_b_w_gate)),
            "wproj": C(np.asarray(L0_attn_b_w_proj)),
            "s_onehot": st["onehot"], "s_ident": st["ident"],
            "s_identf": st["identf"], "s_trils": st["trils"],
            "s_iota": st["iota"], "s_onesb": st["onesb"],
            "s_onesf": st["onesf"], "s_masks": st["masks"],
            "s_ropec": st["ropec"], "s_ropes": st["ropes"],
            "tok0": np.array([[2048 * hb]], dtype=np.int32),
        }
        in_maps.append(m)

    if _NC is None:
        _NC = build_nc()
    res = run_bass_kernel_spmd(_NC, in_maps, core_ids=list(range(8)))
    out = np.empty((B, T, D), dtype=np.float32)
    for c in range(8):
        b, hb = c // 2, c % 2
        yt = np.asarray(res.results[c]["y_ownT"])
        out[b, 2048 * hb:2048 * hb + 2048, :] = yt.T
    kernel._last_res = res
    return out


# version 08
# speedup vs baseline: 1.0645x; 1.0106x over previous
import math
import numpy as np
import ml_dtypes

import concourse.bass as bass
import concourse.mybir as mybir
from concourse.bass_utils import run_bass_kernel_spmd

F32 = mybir.dt.float32
BF16 = mybir.dt.bfloat16
I32 = mybir.dt.int32
AF = mybir.ActivationFunctionType
ALU = mybir.AluOpType

B, T, D = 4, 4096, 1024
TH = T // 2
NT = T // 128
HPC = 8
KEEP = 256
FFH = 3072
NJ = FFH // 128
SM_SHIFT = -12.0
INV127 = 1.0 / 127.0
ISQ = 1.0 / math.sqrt(96.0)
NBIS = 34
_ALOG = []
_TAPS = False


def _rel_bucket(d):
    nb, maxd = 16, 128
    buckets = (d > 0).astype(np.int64) * nb
    rp = np.abs(d)
    max_exact = nb // 2
    small = rp < max_exact
    rl = max_exact + (
        np.log(np.maximum(rp, 1) / max_exact) / math.log(maxd / max_exact)
        * (nb - max_exact)
    ).astype(np.int64)
    rl = np.minimum(rl, nb - 1)
    return buckets + np.where(small, rp, rl)


_STATICS = None


def _statics():
    global _STATICS
    if _STATICS is not None:
        return _STATICS
    d = np.arange(-4095, 4096)
    buck = _rel_bucket(d)
    onehot = np.zeros((32, 8191), dtype=ml_dtypes.bfloat16)
    onehot[buck, np.arange(8191)] = 1.0
    ident = np.eye(128, dtype=ml_dtypes.bfloat16)
    identf = np.eye(128, dtype=np.float32)
    trils = np.triu(np.ones((128, 128), dtype=np.float32), 1).astype(
        ml_dtypes.bfloat16)
    iota = np.broadcast_to(np.arange(1, 257, dtype=np.float32)[None, :],
                           (128, 256)).copy()
    onesb = np.ones((128, 128), dtype=ml_dtypes.bfloat16)
    onesf = np.ones((128, 128), dtype=np.float32)
    masks = np.zeros((128, 4 * 512), dtype=ml_dtypes.bfloat16)
    for m, c0 in enumerate((0, 128, 256, 384)):
        p = np.arange(128)[:, None]
        j = np.arange(512)[None, :]
        masks[:, 512 * m:512 * (m + 1)] = (p + c0 <= j).astype(
            ml_dtypes.bfloat16)
    f = 1.0 / (10000.0 ** (np.arange(0, 64, 2, dtype=np.float64) / 64.0))
    t = np.arange(T, dtype=np.float64)
    a = np.outer(t, f)
    c2 = np.concatenate([np.cos(a), np.cos(a)], axis=1)
    s2 = np.concatenate([-np.sin(a), np.sin(a)], axis=1)
    ropec = np.zeros((128, NT * 64), dtype=ml_dtypes.bfloat16)
    ropes = np.zeros((128, NT * 64), dtype=ml_dtypes.bfloat16)
    for tt in range(NT):
        ropec[:, 64 * tt:64 * (tt + 1)] = c2[128 * tt:128 * (tt + 1)]
        ropes[:, 64 * tt:64 * (tt + 1)] = s2[128 * tt:128 * (tt + 1)]
    _STATICS = dict(onehot=onehot, ident=ident, identf=identf, trils=trils,
                    iota=iota, onesb=onesb, onesf=onesf, masks=masks,
                    ropec=ropec, ropes=ropes)
    return _STATICS


ENGS = ("sync", "tensor", "vector", "scalar", "gpsimd")


class Ctr:
    def __init__(self, name):
        self.name = name
        self.n = 0
        self.h = None


def build_nc():
    nc = bass.Bass("TRN2", target_bir_lowering=False)

    def din(name, shape, dt=F32):
        return nc.dram_tensor(name, list(shape), dt, kind="ExternalInput")

    x_in = din("x_ownT", (D, TH))
    wq = din("wq", (D, 512))
    wk = din("wk", (D, 512))
    wv = din("wv", (D, 512))
    wo = din("wo", (D, D))
    rtab = din("rtab", (32, 8))
    wgT = din("wgT", (D, FFH), I32)
    wuT = din("wuT", (D, FFH), I32)
    wdT = din("wdT", (FFH, D), I32)
    wcq = din("wcq", (D, 96))
    gq = din("gq", (96, 1))
    wdqn = din("wdqn", (96, 256))
    wdqr = din("wdqr", (96, 512))
    wckv = din("wckv", (D, 32))
    gkv = din("gkv", (32, 1))
    wdkn = din("wdkn", (32, 256))
    wdv = din("wdv", (32, 256))
    wkro = din("wkro", (D, 64))
    wimp = din("wimp", (D, 1))
    wselk = din("wselk", (D, 768))
    wselv = din("wselv", (D, 256))
    wwink = din("wwink", (D, 768))
    wwinv = din("wwinv", (D, 256))
    wgate = din("wgate", (D, 3))
    wproj = din("wproj", (512, D))
    s_onehot = din("s_onehot", (32, 8191), BF16)
    s_ident = din("s_ident", (128, 128), BF16)
    s_identf = din("s_identf", (128, 128))
    s_trils = din("s_trils", (128, 128), BF16)
    s_iota = din("s_iota", (128, 256))
    s_onesb = din("s_onesb", (128, 128), BF16)
    s_onesf = din("s_onesf", (128, 128))
    s_masks = din("s_masks", (128, 2048), BF16)
    s_ropec = din("s_ropec", (128, NT * 64), BF16)
    s_ropes = din("s_ropes", (128, NT * 64), BF16)
    tok0_in = din("tok0", (1, 1), I32)

    y_out = nc.dram_tensor("y_ownT", [D, TH], F32, kind="ExternalOutput")
    dbg = []
    if _TAPS:
        for i in range(7):
            dbg.append(nc.dram_tensor("dbg%d" % i, [D, TH], F32,
                                      kind="ExternalOutput"))
        dbg_xn = nc.dram_tensor("dbg_xn", [2 * D, TH], BF16,
                                kind="ExternalOutput")
        dbg_bias = nc.dram_tensor("dbg_bias", [8, 8191], F32,
                                  kind="ExternalOutput")
        dbg_av = nc.dram_tensor("dbg_av", [512, T], BF16,
                                kind="ExternalOutput")
        dbg_qt = nc.dram_tensor("dbg_qt", [128, 4 * T], BF16,
                                kind="ExternalOutput")
        dbg_kt = nc.dram_tensor("dbg_kt", [128, 4 * T], BF16,
                                kind="ExternalOutput")
        dbg_v = nc.dram_tensor("dbg_v", [128, NT * 520], BF16,
                               kind="ExternalOutput")
        dbg_aev = nc.dram_tensor("dbg_aev", [128, 512], F32,
                                 kind="ExternalOutput")
        dbg_rcp = nc.dram_tensor("dbg_rcp", [128, 8], F32,
                                 kind="ExternalOutput")
        dbg_comb = nc.dram_tensor("dbg_comb", [128, 256], BF16,
                                  kind="ExternalOutput")
        dbg_avb = nc.dram_tensor("dbg_avb", [256, T], BF16,
                                 kind="ExternalOutput")
        dbg_misc = nc.dram_tensor("dbg_misc", [128, 16], F32,
                                  kind="ExternalOutput")
        dbg_grow = nc.dram_tensor("dbg_grow", [1, 16], F32,
                                  kind="ExternalOutput")
        dbg_ab3 = nc.dram_tensor("dbg_ab3", [128, 3 * 512], F32,
                                 kind="ExternalOutput")
        dbg_rc3 = nc.dram_tensor("dbg_rc3", [128, 8], F32,
                                 kind="ExternalOutput")
        dbg_cb3 = nc.dram_tensor("dbg_cb3", [128, 256], BF16,
                                 kind="ExternalOutput")
        dbg_exb = nc.dram_tensor("dbg_exb", [128, 4096], BF16,
                                 kind="ExternalOutput")
        dbg_qth = nc.dram_tensor("dbg_qth", [96, T], BF16,
                                 kind="ExternalOutput")
        dbg_k1h = nc.dram_tensor("dbg_k1h", [96, T], BF16,
                                 kind="ExternalOutput")
        dbg_kwh = nc.dram_tensor("dbg_kwh", [96, T], BF16,
                                 kind="ExternalOutput")
        dbg_cst = nc.dram_tensor("dbg_cst", [128, 64], F32,
                                 kind="ExternalOutput")
        dbg_k1n = nc.dram_tensor("dbg_k1n", [128, NT * 96], BF16,
                                 kind="ExternalOutput")
        dbg_ckv = nc.dram_tensor("dbg_ckv", [32, T], BF16,
                                 kind="ExternalOutput")
        dbg_krn = nc.dram_tensor("dbg_krn", [128, NT * 64], BF16,
                                 kind="ExternalOutput")
        dbg_kst = nc.dram_tensor("dbg_kst", [96, 2048], BF16,
                                 kind="ExternalOutput")
        dbg_vsa = nc.dram_tensor("dbg_vsa", [128, 528], BF16,
                                 kind="ExternalOutput")
        dbg_pts = nc.dram_tensor("dbg_pts", [128, 8192], BF16,
                                 kind="ExternalOutput")
        dbg_junk = nc.dram_tensor("dbg_junk", [128, 32], F32,
                                  kind="ExternalOutput")
        dbg_cum = nc.dram_tensor("dbg_cum", [128, 32], F32,
                                 kind="ExternalOutput")
        dbg_cumf = nc.dram_tensor("dbg_cumf", [128, 32], F32,
                                  kind="ExternalOutput")
        dbg_ssc = nc.dram_tensor("dbg_ssc", [128, 32], F32,
                                 kind="ExternalOutput")

    bias_dram = nc.dram_tensor("bias_dram", [8, 8191], F32)
    cin_half = nc.dram_tensor("cin_half", [D, TH], BF16)
    xn_allg = nc.dram_tensor("xn_allg", [2 * D, TH], BF16)
    cin_av = nc.dram_tensor("cin_av", [512, T], BF16)
    av_allg = nc.dram_tensor("av_allg", [1024, T], BF16)
    cin_avb = nc.dram_tensor("cin_avb", [256, T], BF16)
    avb_allg = nc.dram_tensor("avb_allg", [512, T], BF16)
    kwT_scr = nc.dram_tensor("kwT_scr", [HPC * 96, T], BF16)
    v1_scr = nc.dram_tensor("v1_scr", [HPC * T, 33], BF16)
    vw_scr = nc.dram_tensor("vw_scr", [HPC * T, 33], BF16)

    X = nc.alloc_sbuf_tensor("X", [128, 8 * TH], F32)
    IDENT = nc.alloc_sbuf_tensor("IDENT", [128, 128], BF16)
    IDENTF = nc.alloc_sbuf_tensor("IDENTF", [128, 128], F32)
    TRILS = nc.alloc_sbuf_tensor("TRILS", [128, 128], BF16)
    ONESB = nc.alloc_sbuf_tensor("ONESB", [128, 128], BF16)
    ONESF = nc.alloc_sbuf_tensor("ONESF", [128, 128], F32)
    IOTA = nc.alloc_sbuf_tensor("IOTA", [128, 256], F32)
    CONST = nc.alloc_sbuf_tensor("CONST", [128, 64], F32)
    MISC = nc.alloc_sbuf_tensor("MISC", [128, 16], F32)
    LNROW = nc.alloc_sbuf_tensor("LNROW", [128, 2048], F32)
    XBUF = nc.alloc_sbuf_tensor("XBUF", [128, 1024], BF16)
    XNTMP = nc.alloc_sbuf_tensor("XNTMP", [128, 512], F32)
    XNST = nc.alloc_sbuf_tensor("XNST", [128, 512], BF16)
    TOKI = nc.alloc_sbuf_tensor("TOKI", [1, 1], I32)

    SBASE = 229375 - nc.sbuf_bytes_remaining + 64
    SLIMIT = 229376
    arena = {"off": SBASE}

    def ar_mark():
        return arena["off"]

    def ar_to(mark):
        arena["off"] = mark

    def ar(name, shape, dt):
        nbytes = int(np.prod(shape[1:])) * mybir.dt.size(dt)
        off = (arena["off"] + 63) // 64 * 64
        t = nc.alloc_sbuf_tensor_at(name, list(shape), dt, offset=off)
        _ALOG.append((name, off, off + nbytes))
        arena["off"] = off + nbytes
        assert arena["off"] <= SLIMIT, (name, arena["off"])
        return t

    psA = nc.alloc_psum_tensor("psA", [128, 2048], F32)
    ps1 = nc.alloc_psum_tensor("ps1", [128, 512], F32)
    ps2 = nc.alloc_psum_tensor("ps2", [128, 512], F32)
    ps3 = nc.alloc_psum_tensor("ps3", [128, 512], F32)
    psT = nc.alloc_psum_tensor("psT", [128, 512], BF16)

    prog = {e: [] for e in ENGS}
    sems = {}

    def mksem(name):
        c = Ctr(name)
        sems[name] = c
        return c

    sSY = mksem("sSY")
    sG = mksem("sG")
    sCC = mksem("sCC")
    sPE = mksem("sPE")
    sDV = mksem("sDV")
    sAC = mksem("sAC")
    sBar = mksem("sBar")
    env = {}

    def E(eng, fn):
        prog[eng].append(fn)

    def W(eng, ctr, val=None):
        v = ctr.n if val is None else val
        E(eng, lambda e, c=ctr, vv=v: e.wait_ge(c.h, vv))

    def EM(eng, fn, inc=None, inc_n=1):
        if inc is None:
            E(eng, fn)
            return
        inc.n += inc_n
        E(eng, lambda e, f=fn, c=inc, n=inc_n: f(e).then_inc(c.h, n))

    def DMA(eng, out, in_, allow_slow=False):
        sem = sG if eng == "gpsimd" else sSY
        kw = {"allow_slow_non_contiguous": True} if allow_slow else {}
        EM(eng, lambda e, o=out, i=in_, k=dict(kw): e.dma_start(o, i, **k),
           inc=sem, inc_n=16)

    def BARRIER():
        target = sBar.n + len(ENGS)
        for eng in ENGS:
            EM(eng, lambda e: e.nop(), inc=sBar)
        for eng in ENGS:
            W(eng, sBar, target)

    def TAP(i):
        if not _TAPS:
            return
        DMA("sync",
            bass.AP(dbg[i], 0, [[TH, 128], [128 * TH, 8], [1, TH]]),
            bass.AP(X, 0, [[8 * TH, 128], [2048, 8], [1, 2048]]))
        W("sync", sSY)

    DMA("sync", X[:, :],
        bass.AP(x_in, 0, [[TH, 128], [128 * TH, 8], [1, TH]]))
    DMA("sync", IDENT[:, :], s_ident[:, :])
    DMA("sync", IDENTF[:, :], s_identf[:, :])
    DMA("sync", TRILS[:, :], s_trils[:, :])
    DMA("sync", ONESB[:, :], s_onesb[:, :])
    DMA("sync", ONESF[:, :], s_onesf[:, :])
    DMA("sync", IOTA[:, :], s_iota[:, :])
    DMA("sync", TOKI[0:1, 0:1], tok0_in[0:1, 0:1])
    W("sync", sSY)

    def _load_tok(e):
        env["tokr"] = e.to_reg(e.value_load(TOKI[0:1, 0:1]))

    E("sync", _load_tok)

    m0 = ar_mark()
    OH = ar("OH", [32, 8191], BF16)
    RT = ar("RT", [32, 8], BF16)
    BROW = ar("BROW", [8, 8191], F32)
    CROW = ar("CROW", [1, 16], F32)
    DMA("sync", OH[:, :], s_onehot[:, :])
    DMA("gpsimd", RT[:, :], rtab[:, :])
    W("tensor", sSY)
    W("tensor", sG)
    for j in range(16):
        w = 512 if j < 15 else 8191 - 15 * 512
        EM("tensor",
           lambda e, jj=j, ww=w: e.matmul(
               ps1[0:8, 0:ww], RT[0:32, :],
               OH[0:32, 512 * jj:512 * jj + ww], start=True, stop=True),
           inc=sPE)
        W("vector", sPE)
        EM("vector",
           lambda e, jj=j, ww=w: e.tensor_copy(
               BROW[0:8, 512 * jj:512 * jj + ww], ps1[0:8, 0:ww]),
           inc=sDV)
        W("tensor", sDV)
    W("sync", sDV)
    DMA("sync", bias_dram[:, :], BROW[0:8, :])
    W("sync", sSY)
    DMA("sync", CROW[0:1, 0:8],
        bass.AP(bias_dram, 3500, [[1, 1], [8191, 8]]), allow_slow=True)
    DMA("sync", CROW[0:1, 8:16],
        bass.AP(bias_dram, 4500, [[1, 1], [8191, 8]]), allow_slow=True)
    W("tensor", sSY)
    EM("tensor",
       lambda e: e.matmul(ps1[:, 0:16], ONESF[0:1, :], CROW[0:1, :],
                          start=True, stop=True), inc=sPE)
    W("vector", sPE)
    EM("vector",
       lambda e: e.tensor_scalar(
           bass.AP(CONST, 0, [[64, 128], [2, 8]]),
           bass.AP(ps1, 0, [[512, 128], [1, 8]]),
           SM_SHIFT, None, ALU.add), inc=sDV)
    EM("vector",
       lambda e: e.tensor_scalar(
           bass.AP(CONST, 1, [[64, 128], [2, 8]]),
           bass.AP(ps1, 8, [[512, 128], [1, 8]]),
           SM_SHIFT, None, ALU.add), inc=sDV)
    EM("vector", lambda e: e.memset(CONST[:, 63:64], SM_SHIFT), inc=sDV)
    ar_to(m0)
    BARRIER()
    TAP(3)


    def emit_ln(sink_sb=None):
        for tc in range(4):
            for cc in range(8):
                W("vector", sPE)
                EM("vector",
                   lambda e, c=cc, t=tc: e.tensor_copy(
                       XBUF[:, 0:512],
                       X[:, 2048 * c + 512 * t:2048 * c + 512 * t + 512]),
                   inc=sDV)
                W("vector", sDV)
                EM("vector",
                   lambda e: e.tensor_mul(
                       XBUF[:, 512:1024], XBUF[:, 0:512], XBUF[:, 0:512]),
                   inc=sDV)
                W("tensor", sDV)
                EM("tensor",
                   lambda e, c=cc: e.matmul(
                       psA[0:1, 0:512], ONESB[:, 0:1], XBUF[:, 0:512],
                       start=(c == 0), stop=(c == 7)), inc=sPE)
                EM("tensor",
                   lambda e, c=cc: e.matmul(
                       psA[0:1, 512:1024], ONESB[:, 0:1], XBUF[:, 512:1024],
                       start=(c == 0), stop=(c == 7)), inc=sPE)
            W("vector", sPE)
            EM("vector", lambda e: e.tensor_scalar(
                LNROW[0:1, 0:512], psA[0:1, 0:512], 1.0 / D, None, ALU.mult),
               inc=sDV)
            EM("vector", lambda e: e.tensor_scalar(
                LNROW[0:1, 512:1024], psA[0:1, 512:1024], 1.0 / D, None,
                ALU.mult), inc=sDV)
            EM("vector", lambda e: e.tensor_mul(
                LNROW[0:1, 1024:1536], LNROW[0:1, 0:512], LNROW[0:1, 0:512]),
               inc=sDV)
            EM("vector", lambda e: e.tensor_sub(
                LNROW[0:1, 512:1024], LNROW[0:1, 512:1024],
                LNROW[0:1, 1024:1536]), inc=sDV)
            EM("vector", lambda e: e.tensor_scalar(
                LNROW[0:1, 512:1024], LNROW[0:1, 512:1024], 1e-5, None,
                ALU.add), inc=sDV)
            W("scalar", sDV)
            EM("scalar", lambda e: e.activation(
                LNROW[0:1, 1024:1536], LNROW[0:1, 512:1024], AF.Ln,
                bias=0.0, scale=1.0), inc=sAC)
            EM("scalar", lambda e: e.activation(
                LNROW[0:1, 1536:2048], LNROW[0:1, 1024:1536], AF.Exp,
                bias=0.0, scale=-0.5), inc=sAC)
            W("vector", sAC)
            EM("vector", lambda e: e.scalar_tensor_tensor(
                LNROW[0:1, 0:512], LNROW[0:1, 0:512], -1.0,
                LNROW[0:1, 1536:2048], ALU.mult, ALU.mult), inc=sDV)
            W("tensor", sDV)
            EM("tensor", lambda e: e.matmul(
                ps1[:, :], ONESF[0:1, :], LNROW[0:1, 1536:2048],
                start=True, stop=True), inc=sPE)
            EM("tensor", lambda e: e.matmul(
                ps2[:, :], ONESF[0:1, :], LNROW[0:1, 0:512],
                start=True, stop=True), inc=sPE)
            W("vector", sPE)
            for cc in range(8):
                EM("vector",
                   lambda e, c=cc, t=tc: e.tensor_mul(
                       XNTMP[:, :],
                       X[:, 2048 * c + 512 * t:2048 * c + 512 * t + 512],
                       ps1[:, :]), inc=sDV)
                if sink_sb is None:
                    W("vector", sSY)
                    EM("vector",
                       lambda e: e.tensor_add(XNST[:, :], XNTMP[:, :],
                                              ps2[:, :]), inc=sDV)
                    W("sync", sDV)
                    DMA("sync",
                        bass.AP(cin_half, (128 * cc) * TH + 512 * tc,
                                [[TH, 128], [1, 512]]),
                        XNST[:, :])
                else:
                    EM("vector",
                       lambda e, s=sink_sb, c=cc, t=tc: e.tensor_add(
                           s[:, 2048 * c + 512 * t:2048 * c + 512 * t + 512],
                           XNTMP[:, :], ps2[:, :]), inc=sDV)
            W("tensor", sDV)

    def exchange(cin, cout):
        EM("gpsimd",
           lambda e, ci=cin, co=cout: e.collective_compute(
               "AllGather", ALU.bypass,
               replica_groups=[[0, 1], [2, 3], [4, 5], [6, 7]],
               ins=[ci.ap().opt()], outs=[co.ap().opt()]),
           inc=sCC)

    def division(br_list, avw, AEVF, RCPC, COMBB, dst):
        nbr = len(br_list)
        W("vector", sSY)
        W("vector", sPE)
        for bi, (bp, _) in enumerate(br_list):
            EM("vector",
               lambda e, b=bp, w=avw, i=bi: e.tensor_copy(
                   AEVF[0:w + 1, 512 * i:512 * (i + 1)], b[0:w + 1, :]),
               inc=sDV)
        W("tensor", sDV)
        for bi in range(nbr):
            for st in range(4):
                EM("tensor",
                   lambda e, i=bi, s=st, w=avw: e.transpose(
                       psA[:, 512 * i + (w + 2) * s:
                           512 * i + (w + 2) * s + (w + 1)],
                       AEVF[0:w + 1, 512 * i + 128 * s:
                            512 * i + 128 * (s + 1)],
                       IDENTF[0:w + 1, 0:w + 1]), inc=sPE)
        W("vector", sPE)
        for bi in range(nbr):
            for st in range(4):
                base = 512 * bi + (avw + 2) * st
                EM("vector",
                   lambda e, bs=base, w=avw, c=4 * bi + st: e.reciprocal(
                       RCPC[:, c:c + 1], psA[:, bs + w:bs + w + 1]),
                   inc=sDV)
        W("vector", sDV)
        for bi, (bp, gcol) in enumerate(br_list):
            if gcol is not None:
                for st in range(4):
                    EM("vector",
                       lambda e, c=4 * bi + st, g=gcol: e.tensor_mul(
                           RCPC[:, c:c + 1], RCPC[:, c:c + 1],
                           CONST[:, g:g + 1]), inc=sDV)
        for st in range(4):
            for bi in range(nbr):
                base = 512 * bi + (avw + 2) * st
                if bi == 0:
                    EM("vector",
                       lambda e, bs=base, w=avw, s=st, c=4 * bi + st:
                       e.tensor_scalar(
                           COMBB[:, w * s:w * (s + 1)], psA[:, bs:bs + w],
                           RCPC[:, c:c + 1], None, ALU.mult), inc=sDV)
                else:
                    EM("vector",
                       lambda e, bs=base, w=avw, s=st, c=4 * bi + st:
                       e.scalar_tensor_tensor(
                           COMBB[:, w * s:w * (s + 1)], psA[:, bs:bs + w],
                           RCPC[:, c:c + 1], COMBB[:, w * s:w * (s + 1)],
                           ALU.mult, ALU.add), inc=sDV)
        W("tensor", sDV)
        for st in range(4):
            EM("tensor",
               lambda e, s=st, w=avw: e.transpose(
                   psT[0:w, 128 * s:128 * (s + 1)],
                   COMBB[:, w * s:w * (s + 1)], IDENT[:, :]), inc=sPE)
        W("vector", sPE)
        EM("vector",
           lambda e, w=avw, d=dst: e.tensor_copy(d[0:w, 0:512],
                                                 psT[0:w, 0:512]),
           inc=sDV)
        W("sync", sDV)

    mA = ar_mark()
    QT = ar("QT", [128, 4 * T], BF16)
    KT = ar("KT", [128, 4 * T], BF16)
    VSB = ar("VSB", [128, NT * 520], BF16)
    mA2 = ar_mark()
    WB1 = ar("WB1", [128, 4096], BF16)
    XNB = ar("XNB", [128, 8192], BF16)

    emit_ln(sink_sb=None)
    TAP(4)
    cc_xn1 = sCC.n + 1
    W("gpsimd", sSY)
    exchange(cin_half, xn_allg)
    W("sync", sCC, cc_xn1)

    def load_xn_chunk(tch, slot, dst=None):
        d = dst if dst is not None else XNB
        hb = tch // 4
        cb = tch % 4
        DMA("sync",
            bass.AP(d, 4096 * slot, [[8192, 128], [512, 8], [1, 512]]),
            bass.AP(xn_allg, (1024 * hb) * TH + 512 * cb,
                    [[TH, 128], [128 * TH, 8], [1, 512]]))

    def load_w_slice(src, width):
        DMA("gpsimd", bass.AP(WB1, 0, [[4096, 128], [width, 8], [1, width]]),
            bass.AP(src, 0, [[width, 128], [128 * width, 8], [1, width]]))

    for which in ("q", "k"):
        src = wq if which == "q" else wk
        OUT = QT if which == "q" else KT
        W("gpsimd", sPE)
        load_w_slice(src, 512)
        W("tensor", sG)
        W("sync", sPE)
        load_xn_chunk(0, 0)
        for tch in range(8):
            if tch < 7:
                W("sync", sPE)
                load_xn_chunk(tch + 1, (tch + 1) % 2)
            W("tensor", sSY, sSY.n - (16 if tch < 7 else 0))
            slot = tch % 2
            for mc in range(4):
                for cc in range(8):
                    EM("tensor",
                       lambda e, m=mc, c=cc, s=slot: e.matmul(
                           psA[:, 512 * m:512 * (m + 1)],
                           WB1[:, 512 * c + 128 * m:512 * c + 128 * (m + 1)],
                           bass.AP(XNB, 4096 * s + 512 * c,
                                   [[8192, 128], [1, 512]]),
                           start=(c == 0), stop=(c == 7)), inc=sPE)
            W("vector", sPE)
            for mc in range(4):
                EM("vector",
                   lambda e, m=mc, t=tch, o=OUT: e.tensor_copy(
                       o[:, 4096 * m + 512 * t:4096 * m + 512 * (t + 1)],
                       psA[:, 512 * m:512 * (m + 1)]), inc=sDV)
            W("tensor", sDV)
    W("gpsimd", sPE)
    load_w_slice(wv, 512)
    W("tensor", sG)
    EM("vector", lambda e: e.memset(VSB[:, :], 1.0), inc=sDV)
    W("tensor", sDV)
    W("sync", sPE)
    load_xn_chunk(0, 0)
    for tch in range(8):
        if tch < 7:
            W("sync", sPE)
            load_xn_chunk(tch + 1, (tch + 1) % 2)
        W("tensor", sSY, sSY.n - (16 if tch < 7 else 0))
        slot = tch % 2
        for tt in range(4):
            tile = 4 * tch + tt
            for cc in range(8):
                EM("tensor",
                   lambda e, s=slot, c=cc, t=tt: e.matmul(
                       ps1[:, :],
                       bass.AP(XNB, 4096 * s + 512 * c + 128 * t,
                               [[8192, 128], [1, 128]]),
                       bass.AP(WB1, 512 * c, [[4096, 128], [1, 512]]),
                       start=(c == 0), stop=(c == 7)), inc=sPE)
            W("vector", sPE)
            EM("vector",
               lambda e, ti=tile: e.tensor_copy(
                   bass.AP(VSB, 520 * ti, [[NT * 520, 128], [65, 8],
                                           [1, 64]]),
                   bass.AP(ps1, 0, [[512, 128], [64, 8], [1, 64]])),
               inc=sDV)
            W("tensor", sDV)

    TAP(5)
    if _TAPS:
        W("sync", sDV)
        DMA("sync", dbg_qt[:, :], QT[:, :])
        DMA("sync", dbg_kt[:, :], KT[:, :])
        DMA("sync", dbg_v[:, :], VSB[:, :])
        DMA("sync", dbg_xn[:, :], xn_allg[:, :])
        DMA("sync", dbg_bias[:, :], bias_dram[:, :])
        W("sync", sSY)

    ar_to(mA2)
    TOEP = ar("TOEP", [128, 6 * 512], F32)
    EXPP = ar("EXPP", [128, 4096], BF16)
    AEVF = ar("AEVF", [128, 512], F32)
    RCPC = ar("RCPC", [128, 16], F32)
    COMBB = ar("COMBB", [128, 256], BF16)
    AVST = ar("AVST", [128, 512], BF16)

    for h in range(HPC):
        mc = h // 2
        roff = 64 * (h % 2)
        W("sync", sDV)
        for mi in range(6):
            c0 = 128 * (mi - 1)
            DMA("sync", TOEP[:, 512 * mi:512 * (mi + 1)],
                bass.AP(bias_dram, 8191 * h + c0 + 3584,
                        [[1, 128], [1, 512]]))
        toep_ready = sSY.n
        for qc in range(8):
            pend = None
            for kb in range(9):
                if kb < 8:
                    W("tensor", sAC)
                    if kb == 0:
                        W("tensor", sDV)
                        if qc == 0:
                            W("tensor", sSY, toep_ready)
                    for i in range(4):
                        kt = 4 * kb + i
                        EM("tensor",
                           lambda e, m=mc, r=roff, k=kt, q=qc, i_=i: e.matmul(
                               psA[:, 512 * i_:512 * (i_ + 1)],
                               KT[r:r + 64, 4096 * m + 128 * k:
                                  4096 * m + 128 * (k + 1)],
                               QT[r:r + 64, 4096 * m + 512 * q:
                                  4096 * m + 512 * (q + 1)],
                               start=True, stop=True), inc=sPE)
                    qk_done = sPE.n
                if pend is not None:
                    pkb = pend
                    pslot = pkb % 2
                    if kb == 8:
                        W("tensor", sAC)
                    for i in range(4):
                        kt = 4 * pkb + i
                        EM("tensor",
                           lambda e, k=kt, hh=h, s=pslot, i_=i,
                           st=(pkb == 0 and i == 0),
                           sp=(pkb == 7 and i == 3): e.matmul(
                               ps1[0:65, :],
                               bass.AP(VSB, 520 * k + 65 * hh,
                                       [[NT * 520, 128], [1, 65]]),
                               EXPP[:, 2048 * s + 512 * i_:
                                    2048 * s + 512 * (i_ + 1)],
                               start=st, stop=sp), inc=sPE)
                if kb == 8:
                    break
                band = [i for i in range(4)
                        if -1 <= (4 * kb + i) - 4 * qc <= 4]
                if band:
                    W("vector", sPE, qk_done)
                    for i in band:
                        mi = (4 * kb + i) - 4 * qc + 1
                        EM("vector",
                           lambda e, i_=i, m=mi: e.tensor_add(
                               psA[:, 512 * i_:512 * (i_ + 1)],
                               psA[:, 512 * i_:512 * (i_ + 1)],
                               bass.AP(TOEP, 512 * m + 511,
                                       [[6 * 512, 128], [-1, 512]])),
                           inc=sDV)
                    W("scalar", sDV)
                else:
                    W("scalar", sPE, qk_done)
                slot = kb % 2
                runs = []
                for i in range(4):
                    mp = (4 * kb + i) - 4 * qc
                    cls = ("band" if -1 <= mp <= 4 else
                           ("left" if mp < -1 else "right"))
                    if runs and runs[-1][0] == cls:
                        runs[-1][2] += 1
                    else:
                        runs = runs + [[cls, i, 1]]
                for cls, i0, ln in runs:
                    if cls == "band":
                        EM("scalar",
                           lambda e, s=slot, i_=i0, l=ln: e.activation(
                               EXPP[:, 2048 * s + 512 * i_:
                                    2048 * s + 512 * (i_ + l)],
                               psA[:, 512 * i_:512 * (i_ + l)], AF.Exp,
                               bias=CONST[:, 63:64], scale=1.0), inc=sAC)
                    else:
                        gcol = 2 * h if cls == "left" else 2 * h + 1
                        EM("scalar",
                           lambda e, s=slot, i_=i0, l=ln, g=gcol:
                           e.activation(
                               EXPP[:, 2048 * s + 512 * i_:
                                    2048 * s + 512 * (i_ + l)],
                               psA[:, 512 * i_:512 * (i_ + l)], AF.Exp,
                               bias=CONST[:, g:g + 1], scale=1.0), inc=sAC)
                pend = kb
            division([(ps1, None)], 64, AEVF, RCPC, COMBB, AVST)
            if _TAPS and h == 0 and qc == 1:
                DMA("sync", dbg_aev[:, :], AEVF[:, :])
                DMA("sync", dbg_rcp[:, :], RCPC[:, :])
                DMA("sync", dbg_comb[:, :], COMBB[:, :])
                W("sync", sSY)
            DMA("sync",
                bass.AP(cin_av, (64 * h) * T + 512 * qc,
                        [[T, 64], [1, 512]]),
                AVST[0:64, 0:512])

    TAP(6)
    if _TAPS:
        W("sync", sDV)
        DMA("sync", dbg_av[:, :], cin_av[:, :])
        W("sync", sSY)

    cc_av = sCC.n + 1
    W("gpsimd", sSY)
    exchange(cin_av, av_allg)
    ar_to(mA2)
    AVRH = ar("AVRH", [128, 4096], BF16)
    WOB = ar("WOB", [128, 8192], BF16)
    DMA("gpsimd", WOB[:, :],
        bass.AP(wo, 0, [[1024, 128], [128 * 1024, 8], [1, 1024]]))
    W("sync", sCC, cc_av)
    W("tensor", sG)

    def outproj(src_allg, WB, ncc, AVBUF):
        for tj in range(4):
            W("sync", sPE)

            def _ld(e, t=tj, sa=src_allg, n_=ncc, ab=AVBUF, ctr=sSY):
                with e.If_cmp(env["tokr"], 0, "IS_EQ"):
                    e.dma_start(
                        bass.AP(ab, 0, [[4096, 128], [512, n_], [1, 512]]),
                        bass.AP(sa, 512 * t,
                                [[T, 128], [128 * T, n_], [1, 512]]),
                    ).then_inc(ctr.h, 16)
                with e.If_cmp(env["tokr"], 0, "IS_NE"):
                    e.dma_start(
                        bass.AP(ab, 0, [[4096, 128], [512, n_], [1, 512]]),
                        bass.AP(sa, 2048 + 512 * t,
                                [[T, 128], [128 * T, n_], [1, 512]]),
                    ).then_inc(ctr.h, 16)

            sSY.n += 16
            E("sync", _ld)
            W("tensor", sSY)
            for dc in range(8):
                for cc in range(ncc):
                    EM("tensor",
                       lambda e, d=dc, c=cc, w=WB, n_=ncc, ab=AVBUF:
                       e.matmul(
                           psA[:, 512 * (d % 4):512 * (d % 4 + 1)],
                           w[:, 1024 * c + 128 * d:1024 * c + 128 * (d + 1)],
                           bass.AP(ab, 512 * c, [[4096, 128], [1, 512]]),
                           start=(c == 0), stop=(c == n_ - 1)), inc=sPE)
                W("vector", sPE)
                EM("vector",
                   lambda e, d=dc, t=tj: e.tensor_add(
                       X[:, 2048 * d + 512 * t:2048 * d + 512 * (t + 1)],
                       X[:, 2048 * d + 512 * t:2048 * d + 512 * (t + 1)],
                       psA[:, 512 * (d % 4):512 * (d % 4 + 1)]), inc=sDV)
                W("tensor", sDV)

    outproj(av_allg, WOB, 8, AVRH)
    ar_to(mA)
    BARRIER()
    TAP(0)

    def emit_ffn(tag):
        mF = ar_mark()
        XN2 = ar("XN2" + tag, [128, 8 * 2048], BF16)
        HB = ar("HB" + tag, [128, NJ * 1024], BF16)
        WDB = ar("WDB" + tag, [128, (NJ // 2) * 1024], BF16)
        WGB = ar("WGB" + tag, [128, 2048], BF16)
        WUB = ar("WUB" + tag, [128, 2048], BF16)
        SILU = ar("SILU" + tag, [128, 512], BF16)

        emit_ln(sink_sb=XN2)

        def load_wd(jh):
            for j4 in range(0, NJ // 2, 4):
                DMA("gpsimd",
                    bass.AP(WDB, 1024 * j4,
                            [[(NJ // 2) * 1024, 128], [1024, 4], [1, 1024]]),
                    bass.AP(wdT, (128 * (j4 + jh * (NJ // 2))) * D,
                            [[D, 128], [128 * D, 4], [1, D]]))

        def load_gu(j, slot):
            DMA("gpsimd",
                bass.AP(WGB, 1024 * slot, [[2048, 128], [128, 8], [1, 128]]),
                bass.AP(wgT, 128 * j, [[FFH, 128], [128 * FFH, 8],
                                       [1, 128]]))
            DMA("gpsimd",
                bass.AP(WUB, 1024 * slot, [[2048, 128], [128, 8], [1, 128]]),
                bass.AP(wuT, 128 * j, [[FFH, 128], [128 * FFH, 8],
                                       [1, 128]]))

        for half in range(2):
            hoff = 1024 * half
            W("gpsimd", sPE)
            load_gu(0, 0)
            for j in range(NJ):
                slot = j % 2
                if j < NJ - 1:
                    W("gpsimd", sPE)
                    load_gu(j + 1, (j + 1) % 2)
                W("tensor", sG, sG.n - (32 if j < NJ - 1 else 0))
                if half == 1 and j == 0:
                    W("vector", sPE)
                for tl in range(2):
                    W("tensor", sAC)
                    W("tensor", sDV)
                    for cc in range(8):
                        EM("tensor",
                           lambda e, c=cc, s=slot, t=tl, ho=hoff: e.matmul(
                               ps1[:, :],
                               bass.AP(WGB, 1024 * s + 128 * c,
                                       [[2048, 128], [1, 128]]),
                               bass.AP(XN2, 2048 * c + ho + 512 * t,
                                       [[8 * 2048, 128], [1, 512]]),
                               start=(c == 0), stop=(c == 7)), inc=sPE)
                        EM("tensor",
                           lambda e, c=cc, s=slot, t=tl, ho=hoff: e.matmul(
                               ps2[:, :],
                               bass.AP(WUB, 1024 * s + 128 * c,
                                       [[2048, 128], [1, 128]]),
                               bass.AP(XN2, 2048 * c + ho + 512 * t,
                                       [[8 * 2048, 128], [1, 512]]),
                               start=(c == 0), stop=(c == 7)), inc=sPE)
                    W("scalar", sPE)
                    EM("scalar",
                       lambda e: e.activation(SILU[:, :], ps1[:, :], AF.Silu,
                                              bias=0.0, scale=INV127),
                       inc=sAC)
                    W("vector", sAC)
                    EM("vector",
                       lambda e, jj=j, t=tl: e.tensor_mul(
                           HB[:, 1024 * jj + 512 * t:
                              1024 * jj + 512 * (t + 1)],
                           SILU[:, :], ps2[:, :]), inc=sDV)
            W("tensor", sDV)
            for jh in range(2):
                W("gpsimd", sPE)
                load_wd(jh)
                W("tensor", sG)
                for tl in range(2):
                    for dc in range(8):
                        bank = ps3 if dc % 2 == 0 else psA
                        for j in range(NJ // 2):
                            EM("tensor",
                               lambda e, d=dc, jj=j, t=tl, b=bank, jh_=jh:
                               e.matmul(
                                   b[:, 0:512],
                                   bass.AP(WDB, 1024 * jj + 128 * d,
                                           [[(NJ // 2) * 1024, 128],
                                            [1, 128]]),
                                   HB[:, 1024 * (jj + jh_ * (NJ // 2))
                                      + 512 * t:
                                      1024 * (jj + jh_ * (NJ // 2))
                                      + 512 * (t + 1)],
                                   start=(jj == 0), stop=(jj == NJ // 2 - 1)),
                               inc=sPE)
                        W("vector", sPE)
                        EM("vector",
                           lambda e, d=dc, t=tl, ho=hoff, b=bank:
                           e.scalar_tensor_tensor(
                               X[:, 2048 * d + ho + 512 * t:
                                 2048 * d + ho + 512 * (t + 1)],
                               b[:, 0:512], INV127 * INV127,
                               X[:, 2048 * d + ho + 512 * t:
                                 2048 * d + ho + 512 * (t + 1)],
                               ALU.mult, ALU.add), inc=sDV)
                        W("tensor", sDV)
        ar_to(mF)
        BARRIER()

    emit_ffn("f1")
    TAP(1)

    mB = ar_mark()
    ROPEC = ar("ROPEC", [128, NT * 64], BF16)
    ROPES = ar("ROPES", [128, NT * 64], BF16)
    NQT = ar("NQT", [96, T], BF16)
    CKVT = ar("CKVT", [32, T], BF16)
    KRN = ar("KRN", [128, NT * 64], BF16)
    KST = ar("KST", [96, HPC * 256], BF16)
    VSA = ar("VSA", [128, 2 * 264], BF16)
    KSN = ar("KSN", [128, 2 * 768], BF16)
    WDQN = ar("WDQN", [96, 256], BF16)
    WDQR = ar("WDQR", [96, 512], BF16)
    WDKN = ar("WDKN", [32, 256], BF16)
    WDVB = ar("WDVB", [32, 256], BF16)
    GQC = ar("GQC", [96, 1], F32)
    GKVC = ar("GKVC", [32, 1], F32)
    T1B = ar("T1B", [128, 512], BF16)
    T2B = ar("T2B", [128, 512], BF16)
    T1Bb = ar("T1Bb", [128, 512], BF16)
    T2Bb = ar("T2Bb", [128, 512], BF16)
    RMSO = ar("RMSO", [128, 128], BF16)
    RMSOb = ar("RMSOb", [128, 128], BF16)
    KR1 = ar("KR1", [128, 64], BF16)
    KR2 = ar("KR2", [128, 128], BF16)
    NATF = ar("NATF", [128, 128], F32)
    SQF = ar("SQF", [128, 128], F32)
    mB2 = ar_mark()
    XNB3 = ar("XNB3", [128, 4096], BF16)
    WCQ = ar("WCQ", [128, 768], BF16)
    WCKV = ar("WCKV", [128, 256], BF16)
    WKRO = ar("WKRO", [128, 512], BF16)
    WWK = ar("WWK", [128, 8 * 768], BF16)
    WWV = ar("WWV", [128, 8 * 256], BF16)
    WIMP = ar("WIMP", [128, 8], BF16)
    WGT3 = ar("WGT3", [128, 24], BF16)
    KWN = ar("KWN", [128, 768], BF16)
    KWNb = ar("KWNb", [128, 768], BF16)
    KWSTG = ar("KWSTG", [96, 1024], BF16)
    VST = ar("VST", [128, 264], BF16)
    XMF = ar("XMF", [128, 16], F32)
    XMB = ar("XMB", [128, 8], BF16)
    GROW = ar("GROW", [1, 16], F32)
    SSC = ar("SSC", [128, 32], F32)
    JUNK = ar("JUNK", [128, 32], F32)
    CUMA = ar("CUMA", [128, 32], F32)
    CUMC = ar("CUMC", [128, 32], F32)
    JUNKB = ar("JUNKB", [128, 32], BF16)
    CP1 = ar("CP1", [1, 32], F32)
    CP2 = ar("CP2", [1, 32], F32)
    PTS = ar("PTS", [128, NT * 256], BF16)
    KFT = ar("KFT", [128, 768], BF16)
    WSK = ar("WSK", [128, 8 * 768], BF16)

    emit_ln(sink_sb=None)
    cc_xn3 = sCC.n + 1
    W("gpsimd", sSY)
    exchange(cin_half, xn_allg)
    DMA("sync", ROPEC[:, :], s_ropec[:, :])
    DMA("sync", ROPES[:, :], s_ropes[:, :])
    DMA("gpsimd", WCQ[:, :],
        bass.AP(wcq, 0, [[96, 128], [128 * 96, 8], [1, 96]]))
    DMA("gpsimd", WCKV[:, :],
        bass.AP(wckv, 0, [[32, 128], [128 * 32, 8], [1, 32]]))
    DMA("gpsimd", WKRO[:, :],
        bass.AP(wkro, 0, [[64, 128], [128 * 64, 8], [1, 64]]))
    DMA("gpsimd", WDQN[:, :], wdqn[:, :])
    DMA("gpsimd", WDQR[:, :], wdqr[:, :])
    DMA("gpsimd", WDKN[:, :], wdkn[:, :])
    DMA("gpsimd", WDVB[:, :], wdv[:, :])
    DMA("gpsimd", WWK[:, :],
        bass.AP(wwink, 0, [[768, 128], [128 * 768, 8], [1, 768]]))
    DMA("gpsimd", WWV[:, :],
        bass.AP(wwinv, 0, [[256, 128], [128 * 256, 8], [1, 256]]))
    DMA("gpsimd", WIMP[:, :],
        bass.AP(wimp, 0, [[1, 128], [128, 8]]), allow_slow=True)
    DMA("gpsimd", WGT3[:, :],
        bass.AP(wgate, 0, [[3, 128], [128 * 3, 8], [1, 3]]))
    DMA("gpsimd", WSK[:, :],
        bass.AP(wselk, 0, [[768, 128], [128 * 768, 8], [1, 768]]))
    DMA("sync", GQC[:, :], gq[:, :])
    DMA("sync", GKVC[:, :], gkv[:, :])
    W("vector", sG)
    W("vector", sSY)
    EM("vector", lambda e: e.tensor_scalar(
        WDQN[:, :], WDQN[:, :], GQC[0:96, 0:1], None, ALU.mult), inc=sDV)
    EM("vector", lambda e: e.tensor_scalar(
        WDQR[:, :], WDQR[:, :], GQC[0:96, 0:1], None, ALU.mult), inc=sDV)
    EM("vector", lambda e: e.tensor_scalar(
        WDKN[:, :], WDKN[:, :], GKVC[0:32, 0:1], None, ALU.mult), inc=sDV)
    EM("vector", lambda e: e.tensor_scalar(
        WDVB[:, :], WDVB[:, :], GKVC[0:32, 0:1], None, ALU.mult), inc=sDV)
    EM("vector", lambda e: e.memset(XMF[:, 0:8], 0.0), inc=sDV)
    W("sync", sCC, cc_xn3)

    def load_xn_tile(tt, slot):
        hb = tt // 16
        ct = tt % 16
        DMA("sync",
            bass.AP(XNB3, 2048 * slot, [[4096, 128], [128, 8], [1, 128]]),
            bass.AP(xn_allg, (1024 * hb) * TH + 128 * ct,
                    [[TH, 128], [128 * TH, 8], [1, 128]]))

    def xnb3(slot, cc):
        return bass.AP(XNB3, 2048 * slot + 128 * cc, [[4096, 128], [1, 128]])

    TRS = [(T1B, T2B, KWN, RMSO), (T1Bb, T2Bb, KWNb, RMSOb)]

    def p1_tr_block(t, gate, kw_spill_prev):
        T1x, T2x, KWx, RMx = TRS[t % 2]
        W("tensor", sDV, gate)
        EM("tensor", lambda e, R=RMx: e.transpose(
            psT[0:96, 0:128], R[:, 32:128], IDENT[:, :]), inc=sPE)
        EM("tensor", lambda e, R=RMx: e.transpose(
            psT[0:32, 128:256], R[:, 0:32], IDENT[:, :]), inc=sPE)
        EM("tensor", lambda e, K=KWx: e.transpose(
            psT[0:96, 256:384], K[:, 0:96], IDENT[:, :]), inc=sPE)
        EM("tensor", lambda e, K=KWx: e.transpose(
            psT[0:96, 384:512], K[:, 96:192], IDENT[:, :]), inc=sPE)
        W("vector", sPE)
        if kw_spill_prev is not None:
            W("vector", sSY, kw_spill_prev)
        EM("vector", lambda e, t=t: e.tensor_copy(
            NQT[0:96, 128 * t:128 * (t + 1)], psT[0:96, 0:128]), inc=sDV)
        EM("vector", lambda e, t=t: e.tensor_copy(
            CKVT[0:32, 128 * t:128 * (t + 1)], psT[0:32, 128:256]), inc=sDV)
        EM("vector", lambda e: e.tensor_copy(
            KWSTG[0:96, 0:128], psT[0:96, 256:384]), inc=sDV)
        EM("vector", lambda e: e.tensor_copy(
            KWSTG[0:96, 128:256], psT[0:96, 384:512]), inc=sDV)
        for rnd, hs in ((1, (2, 3, 4, 5)), (2, (6, 7))):
            W("tensor", sDV)
            for j, hh in enumerate(hs):
                EM("tensor", lambda e, K=KWx, h_=hh, j=j: e.transpose(
                    psT[0:96, 128 * j:128 * (j + 1)],
                    K[:, 96 * h_:96 * (h_ + 1)], IDENT[:, :]), inc=sPE)
            W("vector", sPE)
            for j, hh in enumerate(hs):
                EM("vector", lambda e, h_=hh, j=j: e.tensor_copy(
                    KWSTG[0:96, 128 * h_:128 * (h_ + 1)],
                    psT[0:96, 128 * j:128 * (j + 1)]), inc=sDV)
        W("sync", sDV)
        DMA("sync",
            bass.AP(kwT_scr, 128 * t, [[T, 96], [96 * T, 8], [1, 128]]),
            KWSTG[0:96, :])
        return sSY.n

    readers_done = sDV.n
    pend_tr = None
    kw_spill = None
    vst_spill = None
    W("sync", sPE)
    load_xn_tile(0, 0)
    for tt in range(NT):
        slot = tt % 2
        T1x, T2x, KWx, RMx = TRS[tt % 2]
        if tt < NT - 1:
            W("sync", sPE)
            xn_base = sSY.n
            load_xn_tile(tt + 1, (tt + 1) % 2)
            W("tensor", sSY, xn_base)
        else:
            W("tensor", sSY)
        W("tensor", sDV, readers_done)
        for cc in range(8):
            EM("tensor", lambda e, c=cc, s=slot: e.matmul(
                ps1[:, 0:96], xnb3(s, c),
                bass.AP(WCQ, 96 * c, [[768, 128], [1, 96]]),
                start=(c == 0), stop=(c == 7)), inc=sPE)
            EM("tensor", lambda e, c=cc, s=slot: e.matmul(
                ps2[:, 0:32], xnb3(s, c),
                bass.AP(WCKV, 32 * c, [[256, 128], [1, 32]]),
                start=(c == 0), stop=(c == 7)), inc=sPE)
            EM("tensor", lambda e, c=cc, s=slot: e.matmul(
                psA[:, 1536:1600], xnb3(s, c),
                bass.AP(WKRO, 64 * c, [[512, 128], [1, 64]]),
                start=(c == 0), stop=(c == 7)), inc=sPE)
            EM("tensor", lambda e, c=cc, s=slot: e.matmul(
                psA[:, 0:512], xnb3(s, c),
                bass.AP(WWK, 768 * c, [[8 * 768, 128], [1, 512]]),
                start=(c == 0), stop=(c == 7)), inc=sPE)
            EM("tensor", lambda e, c=cc, s=slot: e.matmul(
                psA[:, 512:768], xnb3(s, c),
                bass.AP(WWK, 768 * c + 512, [[8 * 768, 128], [1, 256]]),
                start=(c == 0), stop=(c == 7)), inc=sPE)
            EM("tensor", lambda e, c=cc, s=slot: e.matmul(
                psA[:, 1024:1280], xnb3(s, c),
                bass.AP(WWV, 256 * c, [[8 * 256, 128], [1, 256]]),
                start=(c == 0), stop=(c == 7)), inc=sPE)
            EM("tensor", lambda e, c=cc, s=slot, t=tt: e.matmul(
                ps3[:, t:t + 1], xnb3(s, c), WIMP[:, c:c + 1],
                start=(c == 0), stop=(c == 7)), inc=sPE)
        if pend_tr is not None:
            kw_spill = p1_tr_block(tt - 1, pend_tr, kw_spill)
        W("vector", sPE)
        for cc in range(8):
            EM("vector", lambda e, c=cc, s=slot: e.reduce_sum(
                XMF[:, 8 + c:9 + c], xnb3(s, c),
                axis=mybir.AxisListType.X), inc=sDV)
        W("vector", sDV)
        EM("vector", lambda e: e.tensor_add(
            XMF[:, 0:8], XMF[:, 0:8], XMF[:, 8:16]), inc=sDV)
        EM("vector", lambda e: e.tensor_copy(NATF[:, 0:96], ps1[:, 0:96]),
           inc=sDV)
        EM("vector", lambda e: e.tensor_copy(NATF[:, 96:128], ps2[:, 0:32]),
           inc=sDV)
        EM("vector", lambda e: e.tensor_scalar(
            KR1[:, :], psA[:, 1536:1600], 1.0 / 16.0, None, ALU.mult),
           inc=sDV)
        EM("vector", lambda e, K=KWx: e.tensor_copy(K[:, :], psA[:, 0:768]),
           inc=sDV)
        if vst_spill is not None:
            W("vector", sSY, vst_spill)
        EM("vector", lambda e: e.memset(VST[:, :], 1.0), inc=sDV)
        W("vector", sDV)
        EM("vector", lambda e: e.tensor_copy(
            bass.AP(VST, 0, [[264, 128], [33, 8], [1, 32]]),
            bass.AP(psA, 1024, [[2048, 128], [32, 8], [1, 32]])), inc=sDV)
        readers_done = sDV.n
        W("sync", sDV)
        DMA("sync",
            bass.AP(vw_scr, (128 * tt) * 33,
                    [[33, 128], [T * 33, 8], [1, 33]]),
            VST[:, :])
        vst_spill = sSY.n
        EM("vector", lambda e: e.tensor_mul(SQF[:, 0:128], NATF[:, 0:128],
                                            NATF[:, 0:128]), inc=sDV)
        W("vector", sDV)
        EM("vector", lambda e: e.reduce_sum(
            MISC[:, 0:1], SQF[:, 0:96], axis=mybir.AxisListType.X), inc=sDV)
        EM("vector", lambda e: e.reduce_sum(
            MISC[:, 4:5], SQF[:, 96:128], axis=mybir.AxisListType.X),
           inc=sDV)
        W("vector", sDV)
        EM("vector", lambda e: e.tensor_scalar(
            MISC[:, 1:2], MISC[:, 0:1], 1.0 / 96, 1e-6, ALU.mult,
            op1=ALU.add), inc=sDV)
        EM("vector", lambda e: e.tensor_scalar(
            MISC[:, 5:6], MISC[:, 4:5], 1.0 / 32, 1e-6, ALU.mult,
            op1=ALU.add), inc=sDV)
        W("scalar", sDV)
        EM("scalar", lambda e: e.activation(
            MISC[:, 2:3], MISC[:, 1:2], AF.Ln, bias=0.0, scale=1.0),
           inc=sAC)
        EM("scalar", lambda e: e.activation(
            MISC[:, 6:7], MISC[:, 5:6], AF.Ln, bias=0.0, scale=1.0),
           inc=sAC)
        EM("scalar", lambda e: e.activation(
            MISC[:, 3:4], MISC[:, 2:3], AF.Exp, bias=0.0, scale=-0.5),
           inc=sAC)
        EM("scalar", lambda e: e.activation(
            MISC[:, 7:8], MISC[:, 6:7], AF.Exp, bias=0.0, scale=-0.5),
           inc=sAC)
        W("vector", sAC)
        EM("vector", lambda e, R=RMx: e.tensor_scalar(
            R[:, 32:128], NATF[:, 0:96], MISC[:, 3:4], None, ALU.mult),
           inc=sDV)
        EM("vector", lambda e, R=RMx: e.tensor_scalar(
            R[:, 0:32], NATF[:, 96:128], MISC[:, 7:8], None, ALU.mult),
           inc=sDV)
        EM("vector", lambda e, t=tt: e.tensor_mul(
            KR2[:, 0:64], KR1[:, :],
            bass.AP(ROPEC, 64 * t, [[NT * 64, 128], [1, 64]])), inc=sDV)
        EM("vector", lambda e, t=tt: e.tensor_mul(
            KR2[:, 64:128],
            bass.AP(KR1, 32, [[64, 128], [-32, 2], [1, 32]]),
            bass.AP(ROPES, 64 * t, [[NT * 64, 128], [1, 64]])), inc=sDV)
        EM("vector", lambda e, t=tt, K=KWx, T1x=T1x: e.tensor_mul(
            bass.AP(T1x, 128, [[512, 128], [64, 6], [1, 64]]),
            bass.AP(K, 32 + 2 * 96, [[768, 128], [96, 6], [1, 64]]),
            bass.AP(ROPEC, 64 * t, [[NT * 64, 128], [0, 6], [1, 64]])),
           inc=sDV)
        EM("vector", lambda e, t=tt: e.tensor_add(
            KRN[:, 64 * t:64 * (t + 1)], KR2[:, 0:64], KR2[:, 64:128]),
           inc=sDV)
        EM("vector", lambda e, t=tt, K=KWx, T2x=T2x: e.tensor_mul(
            bass.AP(T2x, 0, [[512, 128], [64, 8], [1, 64]]),
            bass.AP(K, 64, [[768, 128], [96, 8], [-32, 2], [1, 32]]),
            bass.AP(ROPES, 64 * t, [[NT * 64, 128], [0, 8], [1, 64]])),
           inc=sDV)
        EM("vector", lambda e, t=tt, K=KWx, T1x=T1x: e.tensor_mul(
            bass.AP(T1x, 0, [[512, 128], [64, 2], [1, 64]]),
            bass.AP(K, 32, [[768, 128], [96, 2], [1, 64]]),
            bass.AP(ROPEC, 64 * t, [[NT * 64, 128], [0, 2], [1, 64]])),
           inc=sDV)
        W("vector", sDV)
        EM("vector", lambda e, K=KWx, T1x=T1x, T2x=T2x: e.tensor_add(
            bass.AP(K, 32, [[768, 128], [96, 8], [1, 64]]),
            bass.AP(T1x, 0, [[512, 128], [64, 8], [1, 64]]),
            bass.AP(T2x, 0, [[512, 128], [64, 8], [1, 64]])), inc=sDV)
        pend_tr = sDV.n

    kw_spill = p1_tr_block(NT - 1, pend_tr, kw_spill)
    W("tensor", sDV)

    for tt in range(NT):
        EM("tensor", lambda e, t=tt: e.matmul(
            ps1[:, 0:256], CKVT[0:32, 128 * t:128 * (t + 1)], WDVB[0:32, :],
            start=True, stop=True), inc=sPE)
        W("vector", sPE)
        W("vector", sSY)
        EM("vector", lambda e: e.memset(VST[:, :], 1.0), inc=sDV)
        W("vector", sDV)
        EM("vector", lambda e: e.tensor_copy(
            bass.AP(VST, 0, [[264, 128], [33, 8], [1, 32]]),
            bass.AP(ps1, 0, [[512, 128], [32, 8], [1, 32]])), inc=sDV)
        W("sync", sDV)
        DMA("sync",
            bass.AP(v1_scr, (128 * tt) * 33,
                    [[33, 128], [T * 33, 8], [1, 33]]),
            VST[:, :])
        W("tensor", sSY)

    EM("vector", lambda e: e.tensor_scalar(
        XMB[:, 0:8], XMF[:, 0:8], 1.0 / T, None, ALU.mult), inc=sDV)
    W("tensor", sDV)
    for cc in range(8):
        EM("tensor", lambda e, c=cc: e.matmul(
            ps2[0:1, 0:3], XMB[:, c:c + 1], WGT3[:, 3 * c:3 * (c + 1)],
            start=(c == 0), stop=(c == 7)), inc=sPE)
    W("vector", sPE)
    EM("vector", lambda e: e.tensor_copy(GROW[0:1, 0:3], ps2[0:1, 0:3]),
       inc=sDV)
    W("scalar", sDV)
    EM("scalar", lambda e: e.activation(
        GROW[0:1, 3:6], GROW[0:1, 0:3], AF.Exp, bias=0.0, scale=1.0),
       inc=sAC)
    W("vector", sAC)
    EM("vector", lambda e: e.reduce_sum(
        GROW[0:1, 6:7], GROW[0:1, 3:6], axis=mybir.AxisListType.X), inc=sDV)
    W("vector", sDV)
    EM("vector", lambda e: e.reciprocal(GROW[0:1, 7:8], GROW[0:1, 6:7]),
       inc=sDV)
    W("vector", sDV)
    EM("vector", lambda e: e.tensor_scalar(
        GROW[0:1, 3:6], GROW[0:1, 3:6], GROW[0:1, 7:8], None, ALU.mult),
       inc=sDV)
    W("tensor", sDV)
    EM("tensor", lambda e: e.matmul(
        ps2[:, 8:11], ONESF[0:1, :], GROW[0:1, 3:6], start=True, stop=True),
       inc=sPE)
    W("vector", sPE)
    EM("vector", lambda e: e.tensor_copy(CONST[:, 32:35], ps2[:, 8:11]),
       inc=sDV)

    EM("vector", lambda e: e.tensor_copy(SSC[:, :], ps3[:, 0:32]), inc=sDV)
    EM("vector", lambda e: e.memset(MISC[:, 0:1], -1000.0), inc=sDV)
    EM("vector", lambda e: e.memset(MISC[:, 1:2], 1000.0), inc=sDV)
    EM("vector", lambda e: e.memset(MISC[:, 2:3], 0.0), inc=sDV)
    for it in range(NBIS):
        W("vector", sPE)
        EM("vector", lambda e: e.tensor_scalar(
            JUNK[:, :], SSC[:, :], MISC[:, 2:3], 0.0, ALU.is_gt,
            op1=ALU.add, accum_out=MISC[:, 6:7]), inc=sDV)
        W("tensor", sDV)
        EM("tensor", lambda e: e.matmul(
            ps2[0:1, 16:17], ONESF[:, 0:1], MISC[:, 6:7],
            start=True, stop=True), inc=sPE)
        W("vector", sPE)
        EM("vector", lambda e: e.tensor_copy(GROW[0:1, 8:9],
                                             ps2[0:1, 16:17]), inc=sDV)
        W("tensor", sDV)
        EM("tensor", lambda e: e.matmul(
            ps2[:, 17:18], ONESF[0:1, :], GROW[0:1, 8:9],
            start=True, stop=True), inc=sPE)
        W("vector", sPE)
        EM("vector", lambda e: e.tensor_scalar(
            MISC[:, 3:4], ps2[:, 17:18], float(KEEP) - 0.5, None, ALU.is_gt),
           inc=sDV)
        EM("vector", lambda e: e.tensor_sub(
            MISC[:, 4:5], MISC[:, 2:3], MISC[:, 0:1]), inc=sDV)
        EM("vector", lambda e: e.tensor_sub(
            MISC[:, 8:9], MISC[:, 1:2], MISC[:, 2:3]), inc=sDV)
        EM("vector", lambda e: e.tensor_scalar(
            MISC[:, 5:6], MISC[:, 3:4], -1.0, 1.0, ALU.mult, op1=ALU.add),
           inc=sDV)
        W("vector", sDV)
        EM("vector", lambda e: e.tensor_mul(
            MISC[:, 4:5], MISC[:, 4:5], MISC[:, 3:4]), inc=sDV)
        EM("vector", lambda e: e.tensor_mul(
            MISC[:, 8:9], MISC[:, 8:9], MISC[:, 5:6]), inc=sDV)
        W("vector", sDV)
        EM("vector", lambda e: e.tensor_add(
            MISC[:, 0:1], MISC[:, 0:1], MISC[:, 4:5]), inc=sDV)
        EM("vector", lambda e: e.tensor_sub(
            MISC[:, 1:2], MISC[:, 1:2], MISC[:, 8:9]), inc=sDV)
        W("vector", sDV)
        EM("vector", lambda e: e.tensor_add(
            MISC[:, 2:3], MISC[:, 0:1], MISC[:, 1:2]), inc=sDV)
        W("vector", sDV)
        EM("vector", lambda e: e.tensor_scalar(
            MISC[:, 2:3], MISC[:, 2:3], 0.5, None, ALU.mult), inc=sDV)
        W("vector", sDV)
    W("vector", sDV)
    EM("vector", lambda e: e.tensor_scalar(
        JUNK[:, :], SSC[:, :], MISC[:, 0:1], None, ALU.is_gt), inc=sDV)
    W("vector", sDV)
    EM("vector", lambda e: e.tensor_copy(JUNKB[:, :], JUNK[:, :]), inc=sDV)
    W("tensor", sDV)
    EM("tensor", lambda e: e.matmul(
        ps1[:, 0:32], TRILS[:, :], JUNKB[:, :], start=True, stop=False),
       inc=sPE)
    EM("tensor", lambda e: e.matmul(
        ps1[:, 0:32], IDENT[:, :], JUNKB[:, :], start=False, stop=True),
       inc=sPE)
    EM("tensor", lambda e: e.matmul(
        ps2[0:1, 0:32], ONESB[:, 0:1], JUNKB[:, :], start=True, stop=True),
       inc=sPE)
    W("vector", sPE)
    EM("vector", lambda e, A=CUMA: e.tensor_copy(A[:, :], ps1[:, 0:32]),
       inc=sDV)
    EM("vector", lambda e: e.memset(CP1[0:1, 0:1], 0.0), inc=sDV)
    EM("vector", lambda e: e.tensor_copy(CP1[0:1, 1:32], ps2[0:1, 0:31]),
       inc=sDV)
    cpa, cpc = CP1, CP2
    for k in (1, 2, 4, 8, 16):
        W("vector", sDV)
        EM("vector", lambda e, kk=k, A=cpa, C=cpc: e.tensor_copy(
            C[0:1, 0:kk], A[0:1, 0:kk]), inc=sDV)
        EM("vector", lambda e, kk=k, A=cpa, C=cpc: e.tensor_add(
            C[0:1, kk:32], A[0:1, kk:32], A[0:1, 0:32 - kk]), inc=sDV)
        cpa, cpc = cpc, cpa
    W("tensor", sDV)
    EM("tensor", lambda e, A=cpa: e.matmul(
        ps3[:, 0:32], ONESF[0:1, :], A[0:1, :], start=True, stop=True),
       inc=sPE)
    W("vector", sPE)
    EM("vector", lambda e, A=CUMA, C=CUMC: e.tensor_add(
        C[:, :], A[:, :], ps3[:, 0:32]), inc=sDV)
    W("vector", sDV)
    EM("vector", lambda e, C=CUMC: e.tensor_mul(
        C[:, :], C[:, :], JUNK[:, :]), inc=sDV)
    W("vector", sDV)
    for tt in range(NT):
        EM("vector", lambda e, t=tt, C=CUMC: e.tensor_scalar(
            PTS[:, 256 * t:256 * (t + 1)], IOTA[:, :], C[:, t:t + 1],
            None, ALU.is_equal), inc=sDV)
    W("tensor", sDV)
    if _TAPS:
        W("sync", sDV)
        DMA("sync", dbg_pts[:, :], PTS[:, :])
        DMA("sync", dbg_junk[:, :], JUNK[:, :])
        DMA("sync", dbg_cum[:, :], CUMA[:, :])
        DMA("sync", dbg_cumf[:, :], CUMC[:, :])
        DMA("sync", dbg_ssc[:, :], SSC[:, :])

    W("sync", sPE)
    load_xn_tile(0, 0)
    for tt in range(NT):
        slot = tt % 2
        if tt < NT - 1:
            W("sync", sPE)
            load_xn_tile(tt + 1, (tt + 1) % 2)
        W("tensor", sSY, sSY.n - (16 if tt < NT - 1 else 0))
        for cc in range(8):
            EM("tensor", lambda e, c=cc, s=slot: e.matmul(
                psA[:, 0:512], xnb3(s, c),
                bass.AP(WSK, 768 * c, [[8 * 768, 128], [1, 512]]),
                start=(c == 0), stop=(c == 7)), inc=sPE)
            EM("tensor", lambda e, c=cc, s=slot: e.matmul(
                psA[:, 512:768], xnb3(s, c),
                bass.AP(WSK, 768 * c + 512, [[8 * 768, 128], [1, 256]]),
                start=(c == 0), stop=(c == 7)), inc=sPE)
        W("vector", sPE)
        EM("vector", lambda e: e.tensor_copy(KFT[:, :], psA[:, 0:768]),
           inc=sDV)
        W("tensor", sDV)
        for r in range(2):
            EM("tensor", lambda e, t=tt, rr=r: e.matmul(
                ps1[:, :] if rr == 0 else ps2[:, :],
                PTS[:, 256 * t + 128 * rr:256 * t + 128 * (rr + 1)],
                KFT[:, 0:512], start=(t == 0), stop=(t == NT - 1)), inc=sPE)
            EM("tensor", lambda e, t=tt, rr=r: e.matmul(
                ps3[:, 0:256] if rr == 0 else psA[:, 1536:1792],
                PTS[:, 256 * t + 128 * rr:256 * t + 128 * (rr + 1)],
                KFT[:, 512:768], start=(t == 0), stop=(t == NT - 1)),
               inc=sPE)
    W("vector", sPE)
    for r in range(2):
        EM("vector", lambda e, rr=r: e.tensor_copy(
            KSN[:, 768 * rr:768 * rr + 512],
            ps1[:, :] if rr == 0 else ps2[:, :]), inc=sDV)
        EM("vector", lambda e, rr=r: e.tensor_copy(
            KSN[:, 768 * rr + 512:768 * (rr + 1)],
            ps3[:, 0:256] if rr == 0 else psA[:, 1536:1792]), inc=sDV)
    for r in range(2):
        EM("vector", lambda e, rr=r: e.tensor_mul(
            bass.AP(T1B, 0, [[512, 128], [64, 8], [1, 64]]),
            bass.AP(KSN, 768 * rr + 32, [[2 * 768, 128], [96, 8], [1, 64]]),
            bass.AP(ROPEC, 64 * rr, [[NT * 64, 128], [0, 8], [1, 64]])),
           inc=sDV)
        EM("vector", lambda e, rr=r: e.tensor_mul(
            bass.AP(T2B, 0, [[512, 128], [64, 8], [1, 64]]),
            bass.AP(KSN, 768 * rr + 64, [[2 * 768, 128], [96, 8],
                                         [-32, 2], [1, 32]]),
            bass.AP(ROPES, 64 * rr, [[NT * 64, 128], [0, 8], [1, 64]])),
           inc=sDV)
        W("vector", sDV)
        EM("vector", lambda e, rr=r: e.tensor_add(
            bass.AP(KSN, 768 * rr + 32, [[2 * 768, 128], [96, 8], [1, 64]]),
            bass.AP(T1B, 0, [[512, 128], [64, 8], [1, 64]]),
            bass.AP(T2B, 0, [[512, 128], [64, 8], [1, 64]])), inc=sDV)
    W("tensor", sDV)
    for hh in range(HPC):
        for r in range(2):
            EM("tensor", lambda e, h_=hh, rr=r: e.transpose(
                psT[0:96, 0:128],
                KSN[:, 768 * rr + 96 * h_:768 * rr + 96 * (h_ + 1)],
                IDENT[:, :]), inc=sPE)
            W("vector", sPE)
            EM("vector", lambda e, h_=hh, rr=r: e.tensor_copy(
                KST[0:96, 256 * h_ + 128 * rr:256 * h_ + 128 * (rr + 1)],
                psT[0:96, 0:128]), inc=sDV)
            W("tensor", sDV)

    W("gpsimd", sPE)
    DMA("gpsimd", WWV[:, :],
        bass.AP(wselv, 0, [[256, 128], [128 * 256, 8], [1, 256]]))
    W("tensor", sG)
    W("sync", sPE)
    load_xn_tile(0, 0)
    for tt in range(NT):
        slot = tt % 2
        if tt < NT - 1:
            W("sync", sPE)
            load_xn_tile(tt + 1, (tt + 1) % 2)
        W("tensor", sSY, sSY.n - (16 if tt < NT - 1 else 0))
        for cc in range(8):
            EM("tensor", lambda e, c=cc, s=slot: e.matmul(
                psA[:, 0:256], xnb3(s, c),
                bass.AP(WWV, 256 * c, [[8 * 256, 128], [1, 256]]),
                start=(c == 0), stop=(c == 7)), inc=sPE)
        W("vector", sPE)
        EM("vector", lambda e: e.tensor_copy(KFT[:, 0:256], psA[:, 0:256]),
           inc=sDV)
        W("tensor", sDV)
        for r in range(2):
            EM("tensor", lambda e, t=tt, rr=r: e.matmul(
                ps1[:, 0:256] if rr == 0 else ps2[:, 0:256],
                PTS[:, 256 * t + 128 * rr:256 * t + 128 * (rr + 1)],
                KFT[:, 0:256], start=(t == 0), stop=(t == NT - 1)), inc=sPE)
    W("vector", sPE)
    EM("vector", lambda e: e.memset(VSA[:, :], 1.0), inc=sDV)
    W("vector", sDV)
    for r in range(2):
        EM("vector", lambda e, rr=r: e.tensor_copy(
            bass.AP(VSA, 264 * rr, [[2 * 264, 128], [33, 8], [1, 32]]),
            bass.AP(ps1 if rr == 0 else ps2, 0,
                    [[512, 128], [32, 8], [1, 32]])), inc=sDV)
    W("tensor", sDV)

    ar_to(mB2)
    MASKS = ar("MASKS", [128, 2048], BF16)
    QNH = ar("QNH", [128, NT * 96], BF16)
    K1NH = ar("K1NH", [128, NT * 96], BF16)
    QTH = ar("QTH", [96, T], BF16)
    K1TH = ar("K1TH", [96, T], BF16)
    KWTH = ar("KWTH", [96, T], BF16)
    V1A = ar("V1A", [128, NT * 33], BF16)
    VWA = ar("VWA", [128, NT * 33], BF16)
    EXPB = ar("EXPB", [128, 4096], BF16)
    AEVF3 = ar("AEVF3", [128, 3 * 512], F32)
    RCP3 = ar("RCP3", [128, 16], F32)
    COMB3 = ar("COMB3", [128, 256], BF16)
    AVST3 = ar("AVST3", [128, 512], BF16)
    RB1 = ar("RB1", [128, 512], BF16)
    RB2 = ar("RB2", [128, 512], BF16)
    DMA("sync", MASKS[:, :], s_masks[:, :])
    W("vector", sSY)

    for h in range(HPC):
        W("sync", sPE)
        W("tensor", sDV)
        DMA("sync", KWTH[0:96, :],
            bass.AP(kwT_scr, (96 * h) * T, [[T, 96], [1, T]]))
        DMA("sync", V1A[:, :],
            bass.AP(v1_scr, (T * h) * 33,
                    [[33, 128], [128 * 33, NT], [1, 33]]))
        DMA("sync", VWA[:, :],
            bass.AP(vw_scr, (T * h) * 33,
                    [[33, 128], [128 * 33, NT], [1, 33]]))
        kw_ready = sSY.n
        for t2 in range(0, NT, 2):
            for j, pso in ((0, 0), (1, 256)):
                t = t2 + j
                EM("tensor", lambda e, t=t, h_=h, o=pso: e.matmul(
                    ps1[:, o:o + 32], NQT[0:96, 128 * t:128 * (t + 1)],
                    WDQN[0:96, 32 * h_:32 * (h_ + 1)], start=True,
                    stop=True), inc=sPE)
                EM("tensor", lambda e, t=t, h_=h, o=pso: e.matmul(
                    ps1[:, o + 32:o + 96], NQT[0:96, 128 * t:128 * (t + 1)],
                    WDQR[0:96, 64 * h_:64 * (h_ + 1)], start=True,
                    stop=True), inc=sPE)
                EM("tensor", lambda e, t=t, h_=h, o=pso: e.matmul(
                    ps2[:, o:o + 32], CKVT[0:32, 128 * t:128 * (t + 1)],
                    WDKN[0:32, 32 * h_:32 * (h_ + 1)], start=True,
                    stop=True), inc=sPE)
            W("vector", sPE)
            EM("vector", lambda e: e.tensor_copy(RB1[:, 0:96], ps1[:, 0:96]),
               inc=sDV)
            EM("vector", lambda e: e.tensor_copy(
                RB1[:, 256:352], ps1[:, 256:352]), inc=sDV)
            EM("vector", lambda e, t=t2: e.tensor_copy(
                K1NH[:, 96 * t:96 * t + 32], ps2[:, 0:32]), inc=sDV)
            EM("vector", lambda e, t=t2 + 1: e.tensor_copy(
                K1NH[:, 96 * t:96 * t + 32], ps2[:, 256:288]), inc=sDV)
            EM("vector", lambda e, t=t2: e.tensor_mul(
                T1B[:, 0:64], RB1[:, 32:96],
                bass.AP(ROPEC, 64 * t, [[NT * 64, 128], [1, 64]])), inc=sDV)
            EM("vector", lambda e, t=t2 + 1: e.tensor_mul(
                T1B[:, 128:192], RB1[:, 288:352],
                bass.AP(ROPEC, 64 * t, [[NT * 64, 128], [1, 64]])), inc=sDV)
            EM("vector", lambda e, t=t2: e.tensor_mul(
                T2B[:, 0:64],
                bass.AP(RB1, 64, [[512, 128], [-32, 2], [1, 32]]),
                bass.AP(ROPES, 64 * t, [[NT * 64, 128], [1, 64]])), inc=sDV)
            EM("vector", lambda e, t=t2 + 1: e.tensor_mul(
                T2B[:, 128:192],
                bass.AP(RB1, 256 + 64, [[512, 128], [-32, 2], [1, 32]]),
                bass.AP(ROPES, 64 * t, [[NT * 64, 128], [1, 64]])), inc=sDV)
            EM("vector", lambda e, t=t2: e.tensor_copy(
                K1NH[:, 96 * t + 32:96 * t + 96],
                KRN[:, 64 * t:64 * (t + 1)]), inc=sDV)
            EM("vector", lambda e, t=t2: e.tensor_copy(
                QNH[:, 96 * t:96 * t + 32], RB1[:, 0:32]), inc=sDV)
            EM("vector", lambda e, t=t2: e.tensor_add(
                QNH[:, 96 * t + 32:96 * t + 96], T1B[:, 0:64],
                T2B[:, 0:64]), inc=sDV)
            EM("vector", lambda e, t=t2 + 1: e.tensor_copy(
                K1NH[:, 96 * t + 32:96 * t + 96],
                KRN[:, 64 * t:64 * (t + 1)]), inc=sDV)
            EM("vector", lambda e, t=t2 + 1: e.tensor_copy(
                QNH[:, 96 * t:96 * t + 32], RB1[:, 256:288]), inc=sDV)
            EM("vector", lambda e, t=t2 + 1: e.tensor_add(
                QNH[:, 96 * t + 32:96 * t + 96], T1B[:, 128:192],
                T2B[:, 128:192]), inc=sDV)
            W("tensor", sDV)
            EM("tensor", lambda e, t=t2: e.transpose(
                psT[0:96, 0:128], QNH[:, 96 * t:96 * (t + 1)], IDENT[:, :]),
               inc=sPE)
            EM("tensor", lambda e, t=t2: e.transpose(
                psT[0:96, 128:256], K1NH[:, 96 * t:96 * (t + 1)],
                IDENT[:, :]), inc=sPE)
            EM("tensor", lambda e, t=t2 + 1: e.transpose(
                psT[0:96, 256:384], QNH[:, 96 * t:96 * (t + 1)],
                IDENT[:, :]), inc=sPE)
            EM("tensor", lambda e, t=t2 + 1: e.transpose(
                psT[0:96, 384:512], K1NH[:, 96 * t:96 * (t + 1)],
                IDENT[:, :]), inc=sPE)
            W("vector", sPE)
            EM("vector", lambda e, t=t2: e.tensor_copy(
                QTH[0:96, 128 * t:128 * (t + 1)], psT[0:96, 0:128]),
               inc=sDV)
            EM("vector", lambda e, t=t2: e.tensor_copy(
                K1TH[0:96, 128 * t:128 * (t + 1)], psT[0:96, 128:256]),
               inc=sDV)
            EM("vector", lambda e, t=t2 + 1: e.tensor_copy(
                QTH[0:96, 128 * t:128 * (t + 1)], psT[0:96, 256:384]),
               inc=sDV)
            EM("vector", lambda e, t=t2 + 1: e.tensor_copy(
                K1TH[0:96, 128 * t:128 * (t + 1)], psT[0:96, 384:512]),
               inc=sDV)
            W("tensor", sDV)
        W("tensor", sSY, kw_ready)

        for qc in range(8):
            nkb = qc + 1
            bursts = []
            for br, (KTH_, VA_, bank) in enumerate(
                    ((K1TH, V1A, ps1), (KWTH, VWA, ps3))):
                for kb in range(nkb):
                    bursts.append((KTH_, VA_, bank, kb,
                                   kb == 0, kb == nkb - 1))
            pend = None
            for bi in range(len(bursts) + 1):
                if bi < len(bursts):
                    KTH_, VA_, bank, kb, isfirst, islast = bursts[bi]
                    slot = bi % 2
                    W("tensor", sAC)
                    if bi == 0:
                        W("tensor", sDV)
                    for i in range(4):
                        kt = 4 * kb + i
                        EM("tensor", lambda e, kk=KTH_, kt_=kt, q=qc, i_=i:
                           e.matmul(
                               psA[:, 512 * i_:512 * (i_ + 1)],
                               kk[0:96, 128 * kt_:128 * (kt_ + 1)],
                               QTH[0:96, 512 * q:512 * (q + 1)],
                               start=True, stop=True), inc=sPE)
                    qk_done = sPE.n
                if pend is not None:
                    (pV, pbank, pkb, pslot, pfirst, plast) = pend
                    if plast:
                        W("tensor", sDV)
                    for i in range(4):
                        kt = 4 * pkb + i
                        EM("tensor", lambda e, v=pV, kt_=kt, s=pslot, i_=i,
                           b=pbank, st=(pfirst and i == 0),
                           sp=(plast and i == 3): e.matmul(
                               b[0:33, :],
                               bass.AP(v, 33 * kt_, [[NT * 33, 128],
                                                     [1, 33]]),
                               EXPB[:, 2048 * s + 512 * i_:
                                    2048 * s + 512 * (i_ + 1)],
                               start=st, stop=sp), inc=sPE)
                if bi == len(bursts):
                    break
                W("scalar", sPE, qk_done)
                EM("scalar", lambda e, s=slot: e.activation(
                    EXPB[:, 2048 * s:2048 * (s + 1)], psA[:, :], AF.Exp,
                    bias=CONST[:, 63:64], scale=ISQ), inc=sAC)
                if islast:
                    W("vector", sAC)
                    for i in range(4):
                        EM("vector", lambda e, s=slot, i_=i:
                           e.tensor_mul(
                               EXPB[:, 2048 * s + 512 * i_:
                                    2048 * s + 512 * (i_ + 1)],
                               EXPB[:, 2048 * s + 512 * i_:
                                    2048 * s + 512 * (i_ + 1)],
                               MASKS[:, 512 * i_:512 * (i_ + 1)]),
                           inc=sDV)
                pend = (VA_, bank, kb, slot, isfirst, islast)
            W("tensor", sAC)
            for r in range(2):
                EM("tensor", lambda e, h_=h, rr=r, q=qc: e.matmul(
                    psA[:, 512 * rr:512 * (rr + 1)],
                    KST[0:96, 256 * h_ + 128 * rr:
                        256 * h_ + 128 * (rr + 1)],
                    QTH[0:96, 512 * q:512 * (q + 1)],
                    start=True, stop=True), inc=sPE)
            W("scalar", sPE)
            EM("scalar", lambda e: e.activation(
                EXPB[:, 0:1024], psA[:, 0:1024], AF.Exp,
                bias=CONST[:, 63:64], scale=ISQ), inc=sAC)
            W("tensor", sAC)
            for r in range(2):
                EM("tensor", lambda e, rr=r: e.matmul(
                    ps2[0:33, :],
                    bass.AP(VSA, 264 * rr, [[2 * 264, 128], [1, 33]]),
                    EXPB[:, 512 * rr:512 * (rr + 1)],
                    start=(rr == 0), stop=(rr == 1)), inc=sPE)
            division([(ps1, 32), (ps2, 33), (ps3, 34)], 32,
                     AEVF3, RCP3, COMB3, AVST3)
            if _TAPS and h == 0 and qc == 1:
                DMA("sync", dbg_qth[:, :], QTH[0:96, :])
                DMA("sync", dbg_k1n[:, :], K1NH[:, :])
                DMA("sync", dbg_ckv[:, :], CKVT[0:32, :])
                DMA("sync", dbg_krn[:, :], KRN[:, :])
                DMA("sync", dbg_k1h[:, :], K1TH[0:96, :])
                DMA("sync", dbg_kwh[:, :], KWTH[0:96, :])
                DMA("sync", dbg_cst[:, :], CONST[:, :])
                DMA("sync", dbg_ab3[:, :], AEVF3[:, :])
                DMA("sync", dbg_rc3[:, :], RCP3[:, :])
                DMA("sync", dbg_cb3[:, :], COMB3[:, :])
                DMA("sync", dbg_exb[:, :], EXPB[:, :])
                DMA("sync", dbg_kst[:, :], KST[0:96, :])
                DMA("sync", dbg_vsa[:, :], VSA[:, :])
                W("sync", sSY)
            DMA("sync",
                bass.AP(cin_avb, (32 * h) * T + 512 * qc,
                        [[T, 32], [1, 512]]),
                AVST3[0:32, 0:512])

    if _TAPS:
        W("sync", sDV)
        DMA("sync", dbg_avb[:, :], cin_avb[:, :])
        DMA("sync", dbg_misc[:, :], MISC[:, :])
        DMA("sync", dbg_grow[:, :], GROW[:, :])
        W("sync", sSY)

    cc_avb = sCC.n + 1
    W("gpsimd", sSY)
    exchange(cin_avb, avb_allg)
    ar_to(mB2)
    AVRH3 = ar("AVRH3", [128, 4096], BF16)
    WPJ = ar("WPJ", [128, 4096], BF16)
    DMA("gpsimd", WPJ[:, :],
        bass.AP(wproj, 0, [[1024, 128], [128 * 1024, 4], [1, 1024]]))
    W("sync", sCC, cc_avb)
    W("tensor", sG)
    outproj(avb_allg, WPJ, 4, AVRH3)
    ar_to(mB)
    BARRIER()
    TAP(2)

    emit_ffn("f2")

    W("sync", sDV)
    DMA("sync",
        bass.AP(y_out, 0, [[TH, 128], [128 * TH, 8], [1, TH]]),
        bass.AP(X, 0, [[8 * TH, 128], [2048, 8], [1, 2048]]))
    W("sync", sSY)

    import contextlib
    with contextlib.ExitStack() as st:
        for name, c in sems.items():
            c.h = st.enter_context(nc.semaphore(name))
        block = st.enter_context(nc.Block())
        for eng in ENGS:
            getattr(block, eng)(
                (lambda fns: lambda e: [f(e) for f in fns])(prog[eng]))
    return nc


_NC = None


def kernel(
    L0_attn_a_hidden_states, L0_attn_a_Wq, L0_attn_a_Wk, L0_attn_a_Wv,
    L0_attn_a_Wo, L0_attn_a_rel_bias_table, L0_attn_b_w_cq, L0_attn_b_g_qnorm,
    L0_attn_b_w_dq_nope, L0_attn_b_w_dq_rope, L0_attn_b_w_ckv,
    L0_attn_b_g_kvnorm, L0_attn_b_w_dk_nope, L0_attn_b_w_dv,
    L0_attn_b_w_krope, L0_attn_b_w_imp, L0_attn_b_w_selk, L0_attn_b_w_selv,
    L0_attn_b_w_wink, L0_attn_b_w_winv, L0_attn_b_w_gate, L0_attn_b_w_proj,
    L0_ffn_w_gate, L0_ffn_w_up, L0_ffn_w_down,
):
    global _NC
    st = _statics()
    x = np.asarray(L0_attn_a_hidden_states)
    C = np.ascontiguousarray

    wgT_ = C(np.asarray(L0_ffn_w_gate).T)
    wuT_ = C(np.asarray(L0_ffn_w_up).T)
    wdT_ = C(np.asarray(L0_ffn_w_down).T)

    in_maps = []
    for c in range(8):
        b, hb = c // 2, c % 2
        hs = slice(512 * hb, 512 * hb + 512)
        m = {
            "x_ownT": C(x[b, 2048 * hb:2048 * hb + 2048, :].T),
            "wq": C(np.asarray(L0_attn_a_Wq)[:, hs]),
            "wk": C(np.asarray(L0_attn_a_Wk)[:, hs]),
            "wv": C(np.asarray(L0_attn_a_Wv)[:, hs]),
            "wo": C(np.asarray(L0_attn_a_Wo)),
            "rtab": C(np.asarray(L0_attn_a_rel_bias_table)[:, 8 * hb:8 * hb + 8]),
            "wgT": wgT_, "wuT": wuT_, "wdT": wdT_,
            "wcq": C(np.asarray(L0_attn_b_w_cq)),
            "gq": C(np.asarray(L0_attn_b_g_qnorm).reshape(96, 1)),
            "wdqn": C(np.asarray(L0_attn_b_w_dq_nope)[:, 256 * hb:256 * hb + 256]),
            "wdqr": C(np.asarray(L0_attn_b_w_dq_rope)[:, 512 * hb:512 * hb + 512]),
            "wckv": C(np.asarray(L0_attn_b_w_ckv)),
            "gkv": C(np.asarray(L0_attn_b_g_kvnorm).reshape(32, 1)),
            "wdkn": C(np.asarray(L0_attn_b_w_dk_nope)[:, 256 * hb:256 * hb + 256]),
            "wdv": C(np.asarray(L0_attn_b_w_dv)[:, 256 * hb:256 * hb + 256]),
            "wkro": C(np.asarray(L0_attn_b_w_krope)),
            "wimp": C(np.asarray(L0_attn_b_w_imp)),
            "wselk": C(np.asarray(L0_attn_b_w_selk)[:, 768 * hb:768 * hb + 768]),
            "wselv": C(np.asarray(L0_attn_b_w_selv)[:, 256 * hb:256 * hb + 256]),
            "wwink": C(np.asarray(L0_attn_b_w_wink)[:, 768 * hb:768 * hb + 768]),
            "wwinv": C(np.asarray(L0_attn_b_w_winv)[:, 256 * hb:256 * hb + 256]),
            "wgate": C(np.asarray(L0_attn_b_w_gate)),
            "wproj": C(np.asarray(L0_attn_b_w_proj)),
            "s_onehot": st["onehot"], "s_ident": st["ident"],
            "s_identf": st["identf"], "s_trils": st["trils"],
            "s_iota": st["iota"], "s_onesb": st["onesb"],
            "s_onesf": st["onesf"], "s_masks": st["masks"],
            "s_ropec": st["ropec"], "s_ropes": st["ropes"],
            "tok0": np.array([[2048 * hb]], dtype=np.int32),
        }
        in_maps.append(m)

    if _NC is None:
        _NC = build_nc()
    res = run_bass_kernel_spmd(_NC, in_maps, core_ids=list(range(8)))
    out = np.empty((B, T, D), dtype=np.float32)
    for c in range(8):
        b, hb = c // 2, c % 2
        yt = np.asarray(res.results[c]["y_ownT"])
        out[b, 2048 * hb:2048 * hb + 2048, :] = yt.T
    kernel._last_res = res
    return out


# version 09
# speedup vs baseline: 1.0819x; 1.0163x over previous
import math
import numpy as np
import ml_dtypes

import concourse.bass as bass
import concourse.mybir as mybir
from concourse.bass_utils import run_bass_kernel_spmd

F32 = mybir.dt.float32
BF16 = mybir.dt.bfloat16
I32 = mybir.dt.int32
AF = mybir.ActivationFunctionType
ALU = mybir.AluOpType

B, T, D = 4, 4096, 1024
TH = T // 2
NT = T // 128
HPC = 8
KEEP = 256
FFH = 3072
NJ = FFH // 128
SM_SHIFT = -12.0
INV127 = 1.0 / 127.0
ISQ = 1.0 / math.sqrt(96.0)
NBIS = 34
_ALOG = []
_TAPS = False


def _rel_bucket(d):
    nb, maxd = 16, 128
    buckets = (d > 0).astype(np.int64) * nb
    rp = np.abs(d)
    max_exact = nb // 2
    small = rp < max_exact
    rl = max_exact + (
        np.log(np.maximum(rp, 1) / max_exact) / math.log(maxd / max_exact)
        * (nb - max_exact)
    ).astype(np.int64)
    rl = np.minimum(rl, nb - 1)
    return buckets + np.where(small, rp, rl)


_STATICS = None


def _statics():
    global _STATICS
    if _STATICS is not None:
        return _STATICS
    d = np.arange(-4095, 4096)
    buck = _rel_bucket(d)
    onehot = np.zeros((32, 8191), dtype=ml_dtypes.bfloat16)
    onehot[buck, np.arange(8191)] = 1.0
    ident = np.eye(128, dtype=ml_dtypes.bfloat16)
    identf = np.eye(128, dtype=np.float32)
    trils = np.triu(np.ones((128, 128), dtype=np.float32), 1).astype(
        ml_dtypes.bfloat16)
    iota = np.broadcast_to(np.arange(1, 257, dtype=np.float32)[None, :],
                           (128, 256)).copy()
    onesb = np.ones((128, 128), dtype=ml_dtypes.bfloat16)
    onesf = np.ones((128, 128), dtype=np.float32)
    masks = np.zeros((128, 4 * 512), dtype=ml_dtypes.bfloat16)
    for m, c0 in enumerate((0, 128, 256, 384)):
        p = np.arange(128)[:, None]
        j = np.arange(512)[None, :]
        masks[:, 512 * m:512 * (m + 1)] = (p + c0 <= j).astype(
            ml_dtypes.bfloat16)
    f = 1.0 / (10000.0 ** (np.arange(0, 64, 2, dtype=np.float64) / 64.0))
    t = np.arange(T, dtype=np.float64)
    a = np.outer(t, f)
    c2 = np.concatenate([np.cos(a), np.cos(a)], axis=1)
    s2 = np.concatenate([-np.sin(a), np.sin(a)], axis=1)
    ropec = np.zeros((128, NT * 64), dtype=ml_dtypes.bfloat16)
    ropes = np.zeros((128, NT * 64), dtype=ml_dtypes.bfloat16)
    for tt in range(NT):
        ropec[:, 64 * tt:64 * (tt + 1)] = c2[128 * tt:128 * (tt + 1)]
        ropes[:, 64 * tt:64 * (tt + 1)] = s2[128 * tt:128 * (tt + 1)]
    _STATICS = dict(onehot=onehot, ident=ident, identf=identf, trils=trils,
                    iota=iota, onesb=onesb, onesf=onesf, masks=masks,
                    ropec=ropec, ropes=ropes)
    return _STATICS


ENGS = ("sync", "tensor", "vector", "scalar", "gpsimd")


class Ctr:
    def __init__(self, name):
        self.name = name
        self.n = 0
        self.h = None


def build_nc():
    nc = bass.Bass("TRN2", target_bir_lowering=False)

    def din(name, shape, dt=F32):
        return nc.dram_tensor(name, list(shape), dt, kind="ExternalInput")

    x_in = din("x_ownT", (D, TH))
    wq = din("wq", (D, 512))
    wk = din("wk", (D, 512))
    wv = din("wv", (D, 512))
    wo = din("wo", (D, D))
    rtab = din("rtab", (32, 8))
    wgT = din("wgT", (D, FFH), I32)
    wuT = din("wuT", (D, FFH), I32)
    wdT = din("wdT", (FFH, D), I32)
    wcq = din("wcq", (D, 96))
    gq = din("gq", (96, 1))
    wdqn = din("wdqn", (96, 256))
    wdqr = din("wdqr", (96, 512))
    wckv = din("wckv", (D, 32))
    gkv = din("gkv", (32, 1))
    wdkn = din("wdkn", (32, 256))
    wdv = din("wdv", (32, 256))
    wkro = din("wkro", (D, 64))
    wimp = din("wimp", (D, 1))
    wselk = din("wselk", (D, 768))
    wselv = din("wselv", (D, 256))
    wwink = din("wwink", (D, 768))
    wwinv = din("wwinv", (D, 256))
    wgate = din("wgate", (D, 3))
    wproj = din("wproj", (512, D))
    s_onehot = din("s_onehot", (32, 8191), BF16)
    s_ident = din("s_ident", (128, 128), BF16)
    s_identf = din("s_identf", (128, 128))
    s_trils = din("s_trils", (128, 128), BF16)
    s_iota = din("s_iota", (128, 256))
    s_onesb = din("s_onesb", (128, 128), BF16)
    s_onesf = din("s_onesf", (128, 128))
    s_masks = din("s_masks", (128, 2048), BF16)
    s_ropec = din("s_ropec", (128, NT * 64), BF16)
    s_ropes = din("s_ropes", (128, NT * 64), BF16)
    tok0_in = din("tok0", (1, 1), I32)

    y_out = nc.dram_tensor("y_ownT", [D, TH], F32, kind="ExternalOutput")
    dbg = []
    if _TAPS:
        for i in range(7):
            dbg.append(nc.dram_tensor("dbg%d" % i, [D, TH], F32,
                                      kind="ExternalOutput"))
        dbg_xn = nc.dram_tensor("dbg_xn", [2 * D, TH], BF16,
                                kind="ExternalOutput")
        dbg_bias = nc.dram_tensor("dbg_bias", [8, 8191], F32,
                                  kind="ExternalOutput")
        dbg_av = nc.dram_tensor("dbg_av", [512, T], BF16,
                                kind="ExternalOutput")
        dbg_qt = nc.dram_tensor("dbg_qt", [128, 4 * T], BF16,
                                kind="ExternalOutput")
        dbg_kt = nc.dram_tensor("dbg_kt", [128, 4 * T], BF16,
                                kind="ExternalOutput")
        dbg_v = nc.dram_tensor("dbg_v", [128, NT * 520], BF16,
                               kind="ExternalOutput")
        dbg_aev = nc.dram_tensor("dbg_aev", [128, 512], F32,
                                 kind="ExternalOutput")
        dbg_rcp = nc.dram_tensor("dbg_rcp", [128, 8], F32,
                                 kind="ExternalOutput")
        dbg_comb = nc.dram_tensor("dbg_comb", [128, 256], BF16,
                                  kind="ExternalOutput")
        dbg_avb = nc.dram_tensor("dbg_avb", [256, T], BF16,
                                 kind="ExternalOutput")
        dbg_misc = nc.dram_tensor("dbg_misc", [128, 16], F32,
                                  kind="ExternalOutput")
        dbg_grow = nc.dram_tensor("dbg_grow", [1, 16], F32,
                                  kind="ExternalOutput")
        dbg_ab3 = nc.dram_tensor("dbg_ab3", [128, 3 * 512], F32,
                                 kind="ExternalOutput")
        dbg_rc3 = nc.dram_tensor("dbg_rc3", [128, 8], F32,
                                 kind="ExternalOutput")
        dbg_cb3 = nc.dram_tensor("dbg_cb3", [128, 256], BF16,
                                 kind="ExternalOutput")
        dbg_exb = nc.dram_tensor("dbg_exb", [128, 4096], BF16,
                                 kind="ExternalOutput")
        dbg_qth = nc.dram_tensor("dbg_qth", [96, T], BF16,
                                 kind="ExternalOutput")
        dbg_k1h = nc.dram_tensor("dbg_k1h", [96, T], BF16,
                                 kind="ExternalOutput")
        dbg_kwh = nc.dram_tensor("dbg_kwh", [96, T], BF16,
                                 kind="ExternalOutput")
        dbg_cst = nc.dram_tensor("dbg_cst", [128, 64], F32,
                                 kind="ExternalOutput")
        dbg_k1n = nc.dram_tensor("dbg_k1n", [128, NT * 96], BF16,
                                 kind="ExternalOutput")
        dbg_ckv = nc.dram_tensor("dbg_ckv", [32, T], BF16,
                                 kind="ExternalOutput")
        dbg_krn = nc.dram_tensor("dbg_krn", [128, NT * 64], BF16,
                                 kind="ExternalOutput")
        dbg_kst = nc.dram_tensor("dbg_kst", [96, 2048], BF16,
                                 kind="ExternalOutput")
        dbg_vsa = nc.dram_tensor("dbg_vsa", [128, 528], BF16,
                                 kind="ExternalOutput")
        dbg_pts = nc.dram_tensor("dbg_pts", [128, 8192], BF16,
                                 kind="ExternalOutput")
        dbg_junk = nc.dram_tensor("dbg_junk", [128, 32], F32,
                                  kind="ExternalOutput")
        dbg_cum = nc.dram_tensor("dbg_cum", [128, 32], F32,
                                 kind="ExternalOutput")
        dbg_cumf = nc.dram_tensor("dbg_cumf", [128, 32], F32,
                                  kind="ExternalOutput")
        dbg_ssc = nc.dram_tensor("dbg_ssc", [128, 32], F32,
                                 kind="ExternalOutput")

    bias_dram = nc.dram_tensor("bias_dram", [8, 8191], F32)
    cin_half = nc.dram_tensor("cin_half", [D, TH], BF16)
    xn_allg = nc.dram_tensor("xn_allg", [2 * D, TH], BF16)
    cin_av = nc.dram_tensor("cin_av", [512, T], BF16)
    av_allg = nc.dram_tensor("av_allg", [1024, T], BF16)
    cin_avb = nc.dram_tensor("cin_avb", [256, T], BF16)
    avb_allg = nc.dram_tensor("avb_allg", [512, T], BF16)
    kwT_scr = nc.dram_tensor("kwT_scr", [HPC * 96, T], BF16)
    v1_scr = nc.dram_tensor("v1_scr", [HPC * T, 33], BF16)
    vw_scr = nc.dram_tensor("vw_scr", [HPC * T, 33], BF16)

    X = nc.alloc_sbuf_tensor("X", [128, 8 * TH], F32)
    IDENT = nc.alloc_sbuf_tensor("IDENT", [128, 128], BF16)
    IDENTF = nc.alloc_sbuf_tensor("IDENTF", [128, 128], F32)
    TRILS = nc.alloc_sbuf_tensor("TRILS", [128, 128], BF16)
    ONESB = nc.alloc_sbuf_tensor("ONESB", [128, 128], BF16)
    ONESF = nc.alloc_sbuf_tensor("ONESF", [128, 128], F32)
    IOTA = nc.alloc_sbuf_tensor("IOTA", [128, 256], F32)
    CONST = nc.alloc_sbuf_tensor("CONST", [128, 64], F32)
    MISC = nc.alloc_sbuf_tensor("MISC", [128, 16], F32)
    LNROW = nc.alloc_sbuf_tensor("LNROW", [128, 2048], F32)
    XBUF = nc.alloc_sbuf_tensor("XBUF", [128, 2048], BF16)
    XNTMP = nc.alloc_sbuf_tensor("XNTMP", [128, 1024], F32)
    XNST = nc.alloc_sbuf_tensor("XNST", [128, 1024], BF16)
    TOKI = nc.alloc_sbuf_tensor("TOKI", [1, 1], I32)

    SBASE = 229375 - nc.sbuf_bytes_remaining + 64
    SLIMIT = 229376
    arena = {"off": SBASE}

    def ar_mark():
        return arena["off"]

    def ar_to(mark):
        arena["off"] = mark

    def ar(name, shape, dt):
        nbytes = int(np.prod(shape[1:])) * mybir.dt.size(dt)
        off = (arena["off"] + 63) // 64 * 64
        t = nc.alloc_sbuf_tensor_at(name, list(shape), dt, offset=off)
        _ALOG.append((name, off, off + nbytes))
        arena["off"] = off + nbytes
        assert arena["off"] <= SLIMIT, (name, arena["off"])
        return t

    psA = nc.alloc_psum_tensor("psA", [128, 2048], F32)
    ps1 = nc.alloc_psum_tensor("ps1", [128, 512], F32)
    ps2 = nc.alloc_psum_tensor("ps2", [128, 512], F32)
    ps3 = nc.alloc_psum_tensor("ps3", [128, 512], F32)
    psT = nc.alloc_psum_tensor("psT", [128, 512], BF16)

    prog = {e: [] for e in ENGS}
    sems = {}

    def mksem(name):
        c = Ctr(name)
        sems[name] = c
        return c

    sSY = mksem("sSY")
    sG = mksem("sG")
    sCC = mksem("sCC")
    sPE = mksem("sPE")
    sDV = mksem("sDV")
    sAC = mksem("sAC")
    sBar = mksem("sBar")
    env = {}

    def E(eng, fn):
        prog[eng].append(fn)

    def W(eng, ctr, val=None):
        v = ctr.n if val is None else val
        E(eng, lambda e, c=ctr, vv=v: e.wait_ge(c.h, vv))

    def EM(eng, fn, inc=None, inc_n=1):
        if inc is None:
            E(eng, fn)
            return
        inc.n += inc_n
        E(eng, lambda e, f=fn, c=inc, n=inc_n: f(e).then_inc(c.h, n))

    def DMA(eng, out, in_, allow_slow=False):
        sem = sG if eng == "gpsimd" else sSY
        kw = {"allow_slow_non_contiguous": True} if allow_slow else {}
        EM(eng, lambda e, o=out, i=in_, k=dict(kw): e.dma_start(o, i, **k),
           inc=sem, inc_n=16)

    def BARRIER():
        target = sBar.n + len(ENGS)
        for eng in ENGS:
            EM(eng, lambda e: e.nop(), inc=sBar)
        for eng in ENGS:
            W(eng, sBar, target)

    def TAP(i):
        if not _TAPS:
            return
        DMA("sync",
            bass.AP(dbg[i], 0, [[TH, 128], [128 * TH, 8], [1, TH]]),
            bass.AP(X, 0, [[8 * TH, 128], [2048, 8], [1, 2048]]))
        W("sync", sSY)

    DMA("sync", X[:, :],
        bass.AP(x_in, 0, [[TH, 128], [128 * TH, 8], [1, TH]]))
    DMA("sync", IDENT[:, :], s_ident[:, :])
    DMA("sync", IDENTF[:, :], s_identf[:, :])
    DMA("sync", TRILS[:, :], s_trils[:, :])
    DMA("sync", ONESB[:, :], s_onesb[:, :])
    DMA("sync", ONESF[:, :], s_onesf[:, :])
    DMA("sync", IOTA[:, :], s_iota[:, :])
    DMA("sync", TOKI[0:1, 0:1], tok0_in[0:1, 0:1])
    W("sync", sSY)

    def _load_tok(e):
        env["tokr"] = e.to_reg(e.value_load(TOKI[0:1, 0:1]))

    E("sync", _load_tok)

    m0 = ar_mark()
    OH = ar("OH", [32, 8191], BF16)
    RT = ar("RT", [32, 8], BF16)
    BROW = ar("BROW", [8, 8191], F32)
    CROW = ar("CROW", [1, 16], F32)
    DMA("sync", OH[:, :], s_onehot[:, :])
    DMA("gpsimd", RT[:, :], rtab[:, :])
    W("tensor", sSY)
    W("tensor", sG)
    for j in range(16):
        w = 512 if j < 15 else 8191 - 15 * 512
        EM("tensor",
           lambda e, jj=j, ww=w: e.matmul(
               ps1[0:8, 0:ww], RT[0:32, :],
               OH[0:32, 512 * jj:512 * jj + ww], start=True, stop=True),
           inc=sPE)
        W("vector", sPE)
        EM("vector",
           lambda e, jj=j, ww=w: e.tensor_copy(
               BROW[0:8, 512 * jj:512 * jj + ww], ps1[0:8, 0:ww]),
           inc=sDV)
        W("tensor", sDV)
    W("sync", sDV)
    DMA("sync", bias_dram[:, :], BROW[0:8, :])
    W("sync", sSY)
    DMA("sync", CROW[0:1, 0:8],
        bass.AP(bias_dram, 3500, [[1, 1], [8191, 8]]), allow_slow=True)
    DMA("sync", CROW[0:1, 8:16],
        bass.AP(bias_dram, 4500, [[1, 1], [8191, 8]]), allow_slow=True)
    W("tensor", sSY)
    EM("tensor",
       lambda e: e.matmul(ps1[:, 0:16], ONESF[0:1, :], CROW[0:1, :],
                          start=True, stop=True), inc=sPE)
    W("vector", sPE)
    EM("vector",
       lambda e: e.tensor_scalar(
           bass.AP(CONST, 0, [[64, 128], [2, 8]]),
           bass.AP(ps1, 0, [[512, 128], [1, 8]]),
           SM_SHIFT, None, ALU.add), inc=sDV)
    EM("vector",
       lambda e: e.tensor_scalar(
           bass.AP(CONST, 1, [[64, 128], [2, 8]]),
           bass.AP(ps1, 8, [[512, 128], [1, 8]]),
           SM_SHIFT, None, ALU.add), inc=sDV)
    EM("vector", lambda e: e.memset(CONST[:, 63:64], SM_SHIFT), inc=sDV)
    ar_to(m0)
    BARRIER()
    TAP(3)


    def emit_ln(sink_sb=None):
        for tc in range(4):
            pe_marks = {}
            for cc in range(8):
                bo = 1024 * (cc % 2)
                if cc >= 2:
                    W("vector", sPE, pe_marks[cc - 2])
                EM("vector",
                   lambda e, c=cc, t=tc, b=bo: e.tensor_copy(
                       XBUF[:, b:b + 512],
                       X[:, 2048 * c + 512 * t:2048 * c + 512 * t + 512]),
                   inc=sDV)
                W("vector", sDV)
                EM("vector",
                   lambda e, b=bo: e.tensor_mul(
                       XBUF[:, b + 512:b + 1024], XBUF[:, b:b + 512],
                       XBUF[:, b:b + 512]), inc=sDV)
                W("tensor", sDV)
                EM("tensor",
                   lambda e, c=cc, b=bo: e.matmul(
                       psA[0:1, 0:512], ONESB[:, 0:1], XBUF[:, b:b + 512],
                       start=(c == 0), stop=(c == 7)), inc=sPE)
                EM("tensor",
                   lambda e, c=cc, b=bo: e.matmul(
                       psA[0:1, 512:1024], ONESB[:, 0:1],
                       XBUF[:, b + 512:b + 1024],
                       start=(c == 0), stop=(c == 7)), inc=sPE)
                pe_marks[cc] = sPE.n
            W("vector", sPE)
            EM("vector", lambda e: e.tensor_scalar(
                LNROW[0:1, 0:512], psA[0:1, 0:512], 1.0 / D, None, ALU.mult),
               inc=sDV)
            EM("vector", lambda e: e.tensor_scalar(
                LNROW[0:1, 512:1024], psA[0:1, 512:1024], 1.0 / D, None,
                ALU.mult), inc=sDV)
            EM("vector", lambda e: e.tensor_mul(
                LNROW[0:1, 1024:1536], LNROW[0:1, 0:512], LNROW[0:1, 0:512]),
               inc=sDV)
            EM("vector", lambda e: e.tensor_sub(
                LNROW[0:1, 512:1024], LNROW[0:1, 512:1024],
                LNROW[0:1, 1024:1536]), inc=sDV)
            EM("vector", lambda e: e.tensor_scalar(
                LNROW[0:1, 512:1024], LNROW[0:1, 512:1024], 1e-5, None,
                ALU.add), inc=sDV)
            W("scalar", sDV)
            EM("scalar", lambda e: e.activation(
                LNROW[0:1, 1024:1536], LNROW[0:1, 512:1024], AF.Ln,
                bias=0.0, scale=1.0), inc=sAC)
            EM("scalar", lambda e: e.activation(
                LNROW[0:1, 1536:2048], LNROW[0:1, 1024:1536], AF.Exp,
                bias=0.0, scale=-0.5), inc=sAC)
            W("vector", sAC)
            EM("vector", lambda e: e.scalar_tensor_tensor(
                LNROW[0:1, 0:512], LNROW[0:1, 0:512], -1.0,
                LNROW[0:1, 1536:2048], ALU.mult, ALU.mult), inc=sDV)
            W("tensor", sDV)
            EM("tensor", lambda e: e.matmul(
                ps1[:, :], ONESF[0:1, :], LNROW[0:1, 1536:2048],
                start=True, stop=True), inc=sPE)
            EM("tensor", lambda e: e.matmul(
                ps2[:, :], ONESF[0:1, :], LNROW[0:1, 0:512],
                start=True, stop=True), inc=sPE)
            W("vector", sPE)
            sy_marks = {}
            for cc in range(9):
                if cc < 8:
                    EM("vector",
                       lambda e, c=cc, t=tc, o=512 * (cc % 2): e.tensor_mul(
                           XNTMP[:, o:o + 512],
                           X[:, 2048 * c + 512 * t:2048 * c + 512 * t + 512],
                           ps1[:, :]), inc=sDV)
                if cc > 0:
                    c0 = cc - 1
                    o0 = 512 * (c0 % 2)
                    if sink_sb is None:
                        if c0 >= 2:
                            W("vector", sSY, sy_marks[c0 - 2])
                        EM("vector",
                           lambda e, o=o0: e.tensor_add(
                               XNST[:, o:o + 512], XNTMP[:, o:o + 512],
                               ps2[:, :]), inc=sDV)
                        W("sync", sDV)
                        DMA("sync",
                            bass.AP(cin_half, (128 * c0) * TH + 512 * tc,
                                    [[TH, 128], [1, 512]]),
                            XNST[:, 512 * (c0 % 2):512 * (c0 % 2) + 512])
                        sy_marks[c0] = sSY.n
                    else:
                        EM("vector",
                           lambda e, s=sink_sb, c=c0, t=tc, o=o0:
                           e.tensor_add(
                               s[:, 2048 * c + 512 * t:
                                 2048 * c + 512 * t + 512],
                               XNTMP[:, o:o + 512], ps2[:, :]), inc=sDV)
            W("tensor", sDV)

    def exchange(cin, cout):
        EM("gpsimd",
           lambda e, ci=cin, co=cout: e.collective_compute(
               "AllGather", ALU.bypass,
               replica_groups=[[0, 1], [2, 3], [4, 5], [6, 7]],
               ins=[ci.ap().opt()], outs=[co.ap().opt()]),
           inc=sCC)

    def division(br_list, avw, AEVF, RCPC, COMBB, dst):
        nbr = len(br_list)
        W("vector", sSY)
        W("vector", sPE)
        for bi, (bp, _) in enumerate(br_list):
            EM("vector",
               lambda e, b=bp, w=avw, i=bi: e.tensor_copy(
                   AEVF[0:w + 1, 512 * i:512 * (i + 1)], b[0:w + 1, :]),
               inc=sDV)
        W("tensor", sDV)
        for bi in range(nbr):
            for st in range(4):
                EM("tensor",
                   lambda e, i=bi, s=st, w=avw: e.transpose(
                       psA[:, 512 * i + (w + 2) * s:
                           512 * i + (w + 2) * s + (w + 1)],
                       AEVF[0:w + 1, 512 * i + 128 * s:
                            512 * i + 128 * (s + 1)],
                       IDENTF[0:w + 1, 0:w + 1]), inc=sPE)
        W("vector", sPE)
        for bi in range(nbr):
            for st in range(4):
                base = 512 * bi + (avw + 2) * st
                EM("vector",
                   lambda e, bs=base, w=avw, c=4 * bi + st: e.reciprocal(
                       RCPC[:, c:c + 1], psA[:, bs + w:bs + w + 1]),
                   inc=sDV)
        W("vector", sDV)
        for bi, (bp, gcol) in enumerate(br_list):
            if gcol is not None:
                for st in range(4):
                    EM("vector",
                       lambda e, c=4 * bi + st, g=gcol: e.tensor_mul(
                           RCPC[:, c:c + 1], RCPC[:, c:c + 1],
                           CONST[:, g:g + 1]), inc=sDV)
        for st in range(4):
            for bi in range(nbr):
                base = 512 * bi + (avw + 2) * st
                if bi == 0:
                    EM("vector",
                       lambda e, bs=base, w=avw, s=st, c=4 * bi + st:
                       e.tensor_scalar(
                           COMBB[:, w * s:w * (s + 1)], psA[:, bs:bs + w],
                           RCPC[:, c:c + 1], None, ALU.mult), inc=sDV)
                else:
                    EM("vector",
                       lambda e, bs=base, w=avw, s=st, c=4 * bi + st:
                       e.scalar_tensor_tensor(
                           COMBB[:, w * s:w * (s + 1)], psA[:, bs:bs + w],
                           RCPC[:, c:c + 1], COMBB[:, w * s:w * (s + 1)],
                           ALU.mult, ALU.add), inc=sDV)
        W("tensor", sDV)
        for st in range(4):
            EM("tensor",
               lambda e, s=st, w=avw: e.transpose(
                   psT[0:w, 128 * s:128 * (s + 1)],
                   COMBB[:, w * s:w * (s + 1)], IDENT[:, :]), inc=sPE)
        W("vector", sPE)
        EM("vector",
           lambda e, w=avw, d=dst: e.tensor_copy(d[0:w, 0:512],
                                                 psT[0:w, 0:512]),
           inc=sDV)
        W("sync", sDV)

    mA = ar_mark()
    QT = ar("QT", [128, 4 * T], BF16)
    KT = ar("KT", [128, 4 * T], BF16)
    VSB = ar("VSB", [128, NT * 520], BF16)
    mA2 = ar_mark()
    WB1 = ar("WB1", [128, 4096], BF16)
    XNB = ar("XNB", [128, 8192], BF16)

    emit_ln(sink_sb=None)
    TAP(4)
    cc_xn1 = sCC.n + 1
    W("gpsimd", sSY)
    exchange(cin_half, xn_allg)
    W("sync", sCC, cc_xn1)

    def load_xn_chunk(tch, slot, dst=None):
        d = dst if dst is not None else XNB
        hb = tch // 4
        cb = tch % 4
        DMA("sync",
            bass.AP(d, 4096 * slot, [[8192, 128], [512, 8], [1, 512]]),
            bass.AP(xn_allg, (1024 * hb) * TH + 512 * cb,
                    [[TH, 128], [128 * TH, 8], [1, 512]]))

    def load_w_slice(src, width):
        DMA("gpsimd", bass.AP(WB1, 0, [[4096, 128], [width, 8], [1, width]]),
            bass.AP(src, 0, [[width, 128], [128 * width, 8], [1, width]]))

    for which in ("q", "k"):
        src = wq if which == "q" else wk
        OUT = QT if which == "q" else KT
        W("gpsimd", sPE)
        load_w_slice(src, 512)
        W("tensor", sG)
        W("sync", sPE)
        load_xn_chunk(0, 0)
        for tch in range(8):
            if tch < 7:
                W("sync", sPE)
                load_xn_chunk(tch + 1, (tch + 1) % 2)
            W("tensor", sSY, sSY.n - (16 if tch < 7 else 0))
            slot = tch % 2
            for mc in range(4):
                for cc in range(8):
                    EM("tensor",
                       lambda e, m=mc, c=cc, s=slot: e.matmul(
                           psA[:, 512 * m:512 * (m + 1)],
                           WB1[:, 512 * c + 128 * m:512 * c + 128 * (m + 1)],
                           bass.AP(XNB, 4096 * s + 512 * c,
                                   [[8192, 128], [1, 512]]),
                           start=(c == 0), stop=(c == 7)), inc=sPE)
            W("vector", sPE)
            for mc in range(4):
                EM("vector",
                   lambda e, m=mc, t=tch, o=OUT: e.tensor_copy(
                       o[:, 4096 * m + 512 * t:4096 * m + 512 * (t + 1)],
                       psA[:, 512 * m:512 * (m + 1)]), inc=sDV)
            W("tensor", sDV)
    W("gpsimd", sPE)
    load_w_slice(wv, 512)
    W("tensor", sG)
    EM("vector", lambda e: e.memset(VSB[:, :], 1.0), inc=sDV)
    W("tensor", sDV)
    W("sync", sPE)
    load_xn_chunk(0, 0)
    for tch in range(8):
        if tch < 7:
            W("sync", sPE)
            load_xn_chunk(tch + 1, (tch + 1) % 2)
        W("tensor", sSY, sSY.n - (16 if tch < 7 else 0))
        slot = tch % 2
        for tt in range(4):
            tile = 4 * tch + tt
            for cc in range(8):
                EM("tensor",
                   lambda e, s=slot, c=cc, t=tt: e.matmul(
                       ps1[:, :],
                       bass.AP(XNB, 4096 * s + 512 * c + 128 * t,
                               [[8192, 128], [1, 128]]),
                       bass.AP(WB1, 512 * c, [[4096, 128], [1, 512]]),
                       start=(c == 0), stop=(c == 7)), inc=sPE)
            W("vector", sPE)
            EM("vector",
               lambda e, ti=tile: e.tensor_copy(
                   bass.AP(VSB, 520 * ti, [[NT * 520, 128], [65, 8],
                                           [1, 64]]),
                   bass.AP(ps1, 0, [[512, 128], [64, 8], [1, 64]])),
               inc=sDV)
            W("tensor", sDV)

    TAP(5)
    if _TAPS:
        W("sync", sDV)
        DMA("sync", dbg_qt[:, :], QT[:, :])
        DMA("sync", dbg_kt[:, :], KT[:, :])
        DMA("sync", dbg_v[:, :], VSB[:, :])
        DMA("sync", dbg_xn[:, :], xn_allg[:, :])
        DMA("sync", dbg_bias[:, :], bias_dram[:, :])
        W("sync", sSY)

    ar_to(mA2)
    TOEP = ar("TOEP", [128, 6 * 512], F32)
    EXPP = ar("EXPP", [128, 4096], BF16)
    AEVF = ar("AEVF", [128, 512], F32)
    RCPC = ar("RCPC", [128, 16], F32)
    COMBB = ar("COMBB", [128, 256], BF16)
    AVST = ar("AVST", [128, 512], BF16)

    for h in range(HPC):
        mc = h // 2
        roff = 64 * (h % 2)
        W("sync", sDV)
        for mi in range(6):
            c0 = 128 * (mi - 1)
            DMA("sync", TOEP[:, 512 * mi:512 * (mi + 1)],
                bass.AP(bias_dram, 8191 * h + c0 + 3584,
                        [[1, 128], [1, 512]]))
        toep_ready = sSY.n
        for qc in range(8):
            pend = None
            for kb in range(9):
                if kb < 8:
                    W("tensor", sAC)
                    if kb == 0:
                        W("tensor", sDV)
                        if qc == 0:
                            W("tensor", sSY, toep_ready)
                    for i in range(4):
                        kt = 4 * kb + i
                        EM("tensor",
                           lambda e, m=mc, r=roff, k=kt, q=qc, i_=i: e.matmul(
                               psA[:, 512 * i_:512 * (i_ + 1)],
                               KT[r:r + 64, 4096 * m + 128 * k:
                                  4096 * m + 128 * (k + 1)],
                               QT[r:r + 64, 4096 * m + 512 * q:
                                  4096 * m + 512 * (q + 1)],
                               start=True, stop=True), inc=sPE)
                    qk_done = sPE.n
                if pend is not None:
                    pkb = pend
                    pslot = pkb % 2
                    if kb == 8:
                        W("tensor", sAC)
                    for i in range(4):
                        kt = 4 * pkb + i
                        EM("tensor",
                           lambda e, k=kt, hh=h, s=pslot, i_=i,
                           st=(pkb == 0 and i == 0),
                           sp=(pkb == 7 and i == 3): e.matmul(
                               ps1[0:65, :],
                               bass.AP(VSB, 520 * k + 65 * hh,
                                       [[NT * 520, 128], [1, 65]]),
                               EXPP[:, 2048 * s + 512 * i_:
                                    2048 * s + 512 * (i_ + 1)],
                               start=st, stop=sp), inc=sPE)
                if kb == 8:
                    break
                band = [i for i in range(4)
                        if -1 <= (4 * kb + i) - 4 * qc <= 4]
                if band:
                    W("vector", sPE, qk_done)
                    for i in band:
                        mi = (4 * kb + i) - 4 * qc + 1
                        EM("vector",
                           lambda e, i_=i, m=mi: e.tensor_add(
                               psA[:, 512 * i_:512 * (i_ + 1)],
                               psA[:, 512 * i_:512 * (i_ + 1)],
                               bass.AP(TOEP, 512 * m + 511,
                                       [[6 * 512, 128], [-1, 512]])),
                           inc=sDV)
                    W("scalar", sDV)
                else:
                    W("scalar", sPE, qk_done)
                slot = kb % 2
                runs = []
                for i in range(4):
                    mp = (4 * kb + i) - 4 * qc
                    cls = ("band" if -1 <= mp <= 4 else
                           ("left" if mp < -1 else "right"))
                    if runs and runs[-1][0] == cls:
                        runs[-1][2] += 1
                    else:
                        runs = runs + [[cls, i, 1]]
                for cls, i0, ln in runs:
                    if cls == "band":
                        EM("scalar",
                           lambda e, s=slot, i_=i0, l=ln: e.activation(
                               EXPP[:, 2048 * s + 512 * i_:
                                    2048 * s + 512 * (i_ + l)],
                               psA[:, 512 * i_:512 * (i_ + l)], AF.Exp,
                               bias=CONST[:, 63:64], scale=1.0), inc=sAC)
                    else:
                        gcol = 2 * h if cls == "left" else 2 * h + 1
                        EM("scalar",
                           lambda e, s=slot, i_=i0, l=ln, g=gcol:
                           e.activation(
                               EXPP[:, 2048 * s + 512 * i_:
                                    2048 * s + 512 * (i_ + l)],
                               psA[:, 512 * i_:512 * (i_ + l)], AF.Exp,
                               bias=CONST[:, g:g + 1], scale=1.0), inc=sAC)
                pend = kb
            division([(ps1, None)], 64, AEVF, RCPC, COMBB, AVST)
            if _TAPS and h == 0 and qc == 1:
                DMA("sync", dbg_aev[:, :], AEVF[:, :])
                DMA("sync", dbg_rcp[:, :], RCPC[:, :])
                DMA("sync", dbg_comb[:, :], COMBB[:, :])
                W("sync", sSY)
            DMA("sync",
                bass.AP(cin_av, (64 * h) * T + 512 * qc,
                        [[T, 64], [1, 512]]),
                AVST[0:64, 0:512])

    TAP(6)
    if _TAPS:
        W("sync", sDV)
        DMA("sync", dbg_av[:, :], cin_av[:, :])
        W("sync", sSY)

    cc_av = sCC.n + 1
    W("gpsimd", sSY)
    exchange(cin_av, av_allg)
    ar_to(mA2)
    AVRH = ar("AVRH", [128, 4096], BF16)
    WOB = ar("WOB", [128, 8192], BF16)
    DMA("gpsimd", WOB[:, :],
        bass.AP(wo, 0, [[1024, 128], [128 * 1024, 8], [1, 1024]]))
    W("sync", sCC, cc_av)
    W("tensor", sG)

    def outproj(src_allg, WB, ncc, AVBUF):
        for tj in range(4):
            W("sync", sPE)

            def _ld(e, t=tj, sa=src_allg, n_=ncc, ab=AVBUF, ctr=sSY):
                with e.If_cmp(env["tokr"], 0, "IS_EQ"):
                    e.dma_start(
                        bass.AP(ab, 0, [[4096, 128], [512, n_], [1, 512]]),
                        bass.AP(sa, 512 * t,
                                [[T, 128], [128 * T, n_], [1, 512]]),
                    ).then_inc(ctr.h, 16)
                with e.If_cmp(env["tokr"], 0, "IS_NE"):
                    e.dma_start(
                        bass.AP(ab, 0, [[4096, 128], [512, n_], [1, 512]]),
                        bass.AP(sa, 2048 + 512 * t,
                                [[T, 128], [128 * T, n_], [1, 512]]),
                    ).then_inc(ctr.h, 16)

            sSY.n += 16
            E("sync", _ld)
            W("tensor", sSY)
            for dc in range(8):
                for cc in range(ncc):
                    EM("tensor",
                       lambda e, d=dc, c=cc, w=WB, n_=ncc, ab=AVBUF:
                       e.matmul(
                           psA[:, 512 * (d % 4):512 * (d % 4 + 1)],
                           w[:, 1024 * c + 128 * d:1024 * c + 128 * (d + 1)],
                           bass.AP(ab, 512 * c, [[4096, 128], [1, 512]]),
                           start=(c == 0), stop=(c == n_ - 1)), inc=sPE)
                W("vector", sPE)
                EM("vector",
                   lambda e, d=dc, t=tj: e.tensor_add(
                       X[:, 2048 * d + 512 * t:2048 * d + 512 * (t + 1)],
                       X[:, 2048 * d + 512 * t:2048 * d + 512 * (t + 1)],
                       psA[:, 512 * (d % 4):512 * (d % 4 + 1)]), inc=sDV)
                W("tensor", sDV)

    outproj(av_allg, WOB, 8, AVRH)
    ar_to(mA)
    BARRIER()
    TAP(0)

    def emit_ffn(tag):
        mF = ar_mark()
        XN2 = ar("XN2" + tag, [128, 8 * 2048], BF16)
        HB = ar("HB" + tag, [128, NJ * 1024], BF16)
        WDB = ar("WDB" + tag, [128, (NJ // 2) * 1024], BF16)
        WGB = ar("WGB" + tag, [128, 2048], BF16)
        WUB = ar("WUB" + tag, [128, 2048], BF16)
        SILU = ar("SILU" + tag, [128, 512], BF16)

        emit_ln(sink_sb=XN2)

        def load_wd(jh):
            for j4 in range(0, NJ // 2, 4):
                DMA("gpsimd",
                    bass.AP(WDB, 1024 * j4,
                            [[(NJ // 2) * 1024, 128], [1024, 4], [1, 1024]]),
                    bass.AP(wdT, (128 * (j4 + jh * (NJ // 2))) * D,
                            [[D, 128], [128 * D, 4], [1, D]]))

        def load_gu(j, slot):
            DMA("gpsimd",
                bass.AP(WGB, 1024 * slot, [[2048, 128], [128, 8], [1, 128]]),
                bass.AP(wgT, 128 * j, [[FFH, 128], [128 * FFH, 8],
                                       [1, 128]]))
            DMA("gpsimd",
                bass.AP(WUB, 1024 * slot, [[2048, 128], [128, 8], [1, 128]]),
                bass.AP(wuT, 128 * j, [[FFH, 128], [128 * FFH, 8],
                                       [1, 128]]))

        for half in range(2):
            hoff = 1024 * half
            W("gpsimd", sPE)
            load_gu(0, 0)
            for j in range(NJ):
                slot = j % 2
                if j < NJ - 1:
                    W("gpsimd", sPE)
                    load_gu(j + 1, (j + 1) % 2)
                W("tensor", sG, sG.n - (32 if j < NJ - 1 else 0))
                if half == 1 and j == 0:
                    W("vector", sPE)
                for tl in range(2):
                    W("tensor", sAC)
                    W("tensor", sDV)
                    for cc in range(8):
                        EM("tensor",
                           lambda e, c=cc, s=slot, t=tl, ho=hoff: e.matmul(
                               ps1[:, :],
                               bass.AP(WGB, 1024 * s + 128 * c,
                                       [[2048, 128], [1, 128]]),
                               bass.AP(XN2, 2048 * c + ho + 512 * t,
                                       [[8 * 2048, 128], [1, 512]]),
                               start=(c == 0), stop=(c == 7)), inc=sPE)
                        EM("tensor",
                           lambda e, c=cc, s=slot, t=tl, ho=hoff: e.matmul(
                               ps2[:, :],
                               bass.AP(WUB, 1024 * s + 128 * c,
                                       [[2048, 128], [1, 128]]),
                               bass.AP(XN2, 2048 * c + ho + 512 * t,
                                       [[8 * 2048, 128], [1, 512]]),
                               start=(c == 0), stop=(c == 7)), inc=sPE)
                    W("scalar", sPE)
                    EM("scalar",
                       lambda e: e.activation(SILU[:, :], ps1[:, :], AF.Silu,
                                              bias=0.0, scale=INV127),
                       inc=sAC)
                    W("vector", sAC)
                    EM("vector",
                       lambda e, jj=j, t=tl: e.tensor_mul(
                           HB[:, 1024 * jj + 512 * t:
                              1024 * jj + 512 * (t + 1)],
                           SILU[:, :], ps2[:, :]), inc=sDV)
            W("tensor", sDV)
            for jh in range(2):
                W("gpsimd", sPE)
                load_wd(jh)
                W("tensor", sG)
                for tl in range(2):
                    for dc in range(8):
                        bank = ps3 if dc % 2 == 0 else psA
                        for j in range(NJ // 2):
                            EM("tensor",
                               lambda e, d=dc, jj=j, t=tl, b=bank, jh_=jh:
                               e.matmul(
                                   b[:, 0:512],
                                   bass.AP(WDB, 1024 * jj + 128 * d,
                                           [[(NJ // 2) * 1024, 128],
                                            [1, 128]]),
                                   HB[:, 1024 * (jj + jh_ * (NJ // 2))
                                      + 512 * t:
                                      1024 * (jj + jh_ * (NJ // 2))
                                      + 512 * (t + 1)],
                                   start=(jj == 0), stop=(jj == NJ // 2 - 1)),
                               inc=sPE)
                        W("vector", sPE)
                        EM("vector",
                           lambda e, d=dc, t=tl, ho=hoff, b=bank:
                           e.scalar_tensor_tensor(
                               X[:, 2048 * d + ho + 512 * t:
                                 2048 * d + ho + 512 * (t + 1)],
                               b[:, 0:512], INV127 * INV127,
                               X[:, 2048 * d + ho + 512 * t:
                                 2048 * d + ho + 512 * (t + 1)],
                               ALU.mult, ALU.add), inc=sDV)
                        W("tensor", sDV)
        ar_to(mF)
        BARRIER()

    emit_ffn("f1")
    TAP(1)

    mB = ar_mark()
    ROPEC = ar("ROPEC", [128, NT * 64], BF16)
    ROPES = ar("ROPES", [128, NT * 64], BF16)
    NQT = ar("NQT", [96, T], BF16)
    CKVT = ar("CKVT", [32, T], BF16)
    KRN = ar("KRN", [128, NT * 64], BF16)
    KST = ar("KST", [96, HPC * 256], BF16)
    VSA = ar("VSA", [128, 2 * 264], BF16)
    KSN = ar("KSN", [128, 2 * 768], BF16)
    WDQN = ar("WDQN", [96, 256], BF16)
    WDQR = ar("WDQR", [96, 512], BF16)
    WDKN = ar("WDKN", [32, 256], BF16)
    WDVB = ar("WDVB", [32, 256], BF16)
    GQC = ar("GQC", [96, 1], F32)
    GKVC = ar("GKVC", [32, 1], F32)
    T1B = ar("T1B", [128, 512], BF16)
    T2B = ar("T2B", [128, 512], BF16)
    T1Bb = ar("T1Bb", [128, 512], BF16)
    T2Bb = ar("T2Bb", [128, 512], BF16)
    RMSO = ar("RMSO", [128, 128], BF16)
    RMSOb = ar("RMSOb", [128, 128], BF16)
    KR1 = ar("KR1", [128, 64], BF16)
    KR2 = ar("KR2", [128, 128], BF16)
    NATF = ar("NATF", [128, 128], F32)
    SQF = ar("SQF", [128, 128], F32)
    mB2 = ar_mark()
    XNB3 = ar("XNB3", [128, 4096], BF16)
    WCQ = ar("WCQ", [128, 768], BF16)
    WCKV = ar("WCKV", [128, 256], BF16)
    WKRO = ar("WKRO", [128, 512], BF16)
    WWK = ar("WWK", [128, 8 * 768], BF16)
    WWV = ar("WWV", [128, 8 * 256], BF16)
    WIMP = ar("WIMP", [128, 8], BF16)
    WGT3 = ar("WGT3", [128, 24], BF16)
    KWN = ar("KWN", [128, 768], BF16)
    KWNb = ar("KWNb", [128, 768], BF16)
    KWSTG = ar("KWSTG", [96, 1024], BF16)
    VST = ar("VST", [128, 264], BF16)
    XMF = ar("XMF", [128, 16], F32)
    XMB = ar("XMB", [128, 8], BF16)
    GROW = ar("GROW", [1, 16], F32)
    SSC = ar("SSC", [128, 32], F32)
    JUNK = ar("JUNK", [128, 32], F32)
    CUMA = ar("CUMA", [128, 32], F32)
    CUMC = ar("CUMC", [128, 32], F32)
    JUNKB = ar("JUNKB", [128, 32], BF16)
    CP1 = ar("CP1", [1, 32], F32)
    CP2 = ar("CP2", [1, 32], F32)
    PTS = ar("PTS", [128, NT * 256], BF16)
    KFT = ar("KFT", [128, 768], BF16)
    WSK = ar("WSK", [128, 8 * 768], BF16)

    emit_ln(sink_sb=None)
    cc_xn3 = sCC.n + 1
    W("gpsimd", sSY)
    exchange(cin_half, xn_allg)
    DMA("sync", ROPEC[:, :], s_ropec[:, :])
    DMA("sync", ROPES[:, :], s_ropes[:, :])
    DMA("gpsimd", WCQ[:, :],
        bass.AP(wcq, 0, [[96, 128], [128 * 96, 8], [1, 96]]))
    DMA("gpsimd", WCKV[:, :],
        bass.AP(wckv, 0, [[32, 128], [128 * 32, 8], [1, 32]]))
    DMA("gpsimd", WKRO[:, :],
        bass.AP(wkro, 0, [[64, 128], [128 * 64, 8], [1, 64]]))
    DMA("gpsimd", WDQN[:, :], wdqn[:, :])
    DMA("gpsimd", WDQR[:, :], wdqr[:, :])
    DMA("gpsimd", WDKN[:, :], wdkn[:, :])
    DMA("gpsimd", WDVB[:, :], wdv[:, :])
    DMA("gpsimd", WWK[:, :],
        bass.AP(wwink, 0, [[768, 128], [128 * 768, 8], [1, 768]]))
    DMA("gpsimd", WWV[:, :],
        bass.AP(wwinv, 0, [[256, 128], [128 * 256, 8], [1, 256]]))
    DMA("gpsimd", WIMP[:, :],
        bass.AP(wimp, 0, [[1, 128], [128, 8]]), allow_slow=True)
    DMA("gpsimd", WGT3[:, :],
        bass.AP(wgate, 0, [[3, 128], [128 * 3, 8], [1, 3]]))
    DMA("gpsimd", WSK[:, :],
        bass.AP(wselk, 0, [[768, 128], [128 * 768, 8], [1, 768]]))
    DMA("sync", GQC[:, :], gq[:, :])
    DMA("sync", GKVC[:, :], gkv[:, :])
    W("vector", sG)
    W("vector", sSY)
    EM("vector", lambda e: e.tensor_scalar(
        WDQN[:, :], WDQN[:, :], GQC[0:96, 0:1], None, ALU.mult), inc=sDV)
    EM("vector", lambda e: e.tensor_scalar(
        WDQR[:, :], WDQR[:, :], GQC[0:96, 0:1], None, ALU.mult), inc=sDV)
    EM("vector", lambda e: e.tensor_scalar(
        WDKN[:, :], WDKN[:, :], GKVC[0:32, 0:1], None, ALU.mult), inc=sDV)
    EM("vector", lambda e: e.tensor_scalar(
        WDVB[:, :], WDVB[:, :], GKVC[0:32, 0:1], None, ALU.mult), inc=sDV)
    EM("vector", lambda e: e.memset(XMF[:, 0:8], 0.0), inc=sDV)
    W("sync", sCC, cc_xn3)

    def load_xn_tile(tt, slot):
        hb = tt // 16
        ct = tt % 16
        DMA("sync",
            bass.AP(XNB3, 2048 * slot, [[4096, 128], [128, 8], [1, 128]]),
            bass.AP(xn_allg, (1024 * hb) * TH + 128 * ct,
                    [[TH, 128], [128 * TH, 8], [1, 128]]))

    def xnb3(slot, cc):
        return bass.AP(XNB3, 2048 * slot + 128 * cc, [[4096, 128], [1, 128]])

    TRS = [(T1B, T2B, KWN, RMSO), (T1Bb, T2Bb, KWNb, RMSOb)]

    def p1_tr_block(t, gate, kw_spill_prev):
        T1x, T2x, KWx, RMx = TRS[t % 2]
        W("tensor", sDV, gate)
        EM("tensor", lambda e, R=RMx: e.transpose(
            psT[0:96, 0:128], R[:, 32:128], IDENT[:, :]), inc=sPE)
        EM("tensor", lambda e, R=RMx: e.transpose(
            psT[0:32, 128:256], R[:, 0:32], IDENT[:, :]), inc=sPE)
        EM("tensor", lambda e, K=KWx: e.transpose(
            psT[0:96, 256:384], K[:, 0:96], IDENT[:, :]), inc=sPE)
        EM("tensor", lambda e, K=KWx: e.transpose(
            psT[0:96, 384:512], K[:, 96:192], IDENT[:, :]), inc=sPE)
        W("vector", sPE)
        if kw_spill_prev is not None:
            W("vector", sSY, kw_spill_prev)
        EM("vector", lambda e, t=t: e.tensor_copy(
            NQT[0:96, 128 * t:128 * (t + 1)], psT[0:96, 0:128]), inc=sDV)
        EM("vector", lambda e, t=t: e.tensor_copy(
            CKVT[0:32, 128 * t:128 * (t + 1)], psT[0:32, 128:256]), inc=sDV)
        EM("vector", lambda e: e.tensor_copy(
            KWSTG[0:96, 0:128], psT[0:96, 256:384]), inc=sDV)
        EM("vector", lambda e: e.tensor_copy(
            KWSTG[0:96, 128:256], psT[0:96, 384:512]), inc=sDV)
        for rnd, hs in ((1, (2, 3, 4, 5)), (2, (6, 7))):
            W("tensor", sDV)
            for j, hh in enumerate(hs):
                EM("tensor", lambda e, K=KWx, h_=hh, j=j: e.transpose(
                    psT[0:96, 128 * j:128 * (j + 1)],
                    K[:, 96 * h_:96 * (h_ + 1)], IDENT[:, :]), inc=sPE)
            W("vector", sPE)
            for j, hh in enumerate(hs):
                EM("vector", lambda e, h_=hh, j=j: e.tensor_copy(
                    KWSTG[0:96, 128 * h_:128 * (h_ + 1)],
                    psT[0:96, 128 * j:128 * (j + 1)]), inc=sDV)
        W("sync", sDV)
        DMA("sync",
            bass.AP(kwT_scr, 128 * t, [[T, 96], [96 * T, 8], [1, 128]]),
            KWSTG[0:96, :])
        return sSY.n

    readers_done = sDV.n
    pend_tr = None
    kw_spill = None
    vst_spill = None
    W("sync", sPE)
    load_xn_tile(0, 0)
    for tt in range(NT):
        slot = tt % 2
        T1x, T2x, KWx, RMx = TRS[tt % 2]
        if tt < NT - 1:
            W("sync", sPE)
            xn_base = sSY.n
            load_xn_tile(tt + 1, (tt + 1) % 2)
            W("tensor", sSY, xn_base)
        else:
            W("tensor", sSY)
        W("tensor", sDV, readers_done)
        for cc in range(8):
            EM("tensor", lambda e, c=cc, s=slot: e.matmul(
                ps1[:, 0:96], xnb3(s, c),
                bass.AP(WCQ, 96 * c, [[768, 128], [1, 96]]),
                start=(c == 0), stop=(c == 7)), inc=sPE)
            EM("tensor", lambda e, c=cc, s=slot: e.matmul(
                ps2[:, 0:32], xnb3(s, c),
                bass.AP(WCKV, 32 * c, [[256, 128], [1, 32]]),
                start=(c == 0), stop=(c == 7)), inc=sPE)
            EM("tensor", lambda e, c=cc, s=slot: e.matmul(
                psA[:, 1536:1600], xnb3(s, c),
                bass.AP(WKRO, 64 * c, [[512, 128], [1, 64]]),
                start=(c == 0), stop=(c == 7)), inc=sPE)
            EM("tensor", lambda e, c=cc, s=slot: e.matmul(
                psA[:, 0:512], xnb3(s, c),
                bass.AP(WWK, 768 * c, [[8 * 768, 128], [1, 512]]),
                start=(c == 0), stop=(c == 7)), inc=sPE)
            EM("tensor", lambda e, c=cc, s=slot: e.matmul(
                psA[:, 512:768], xnb3(s, c),
                bass.AP(WWK, 768 * c + 512, [[8 * 768, 128], [1, 256]]),
                start=(c == 0), stop=(c == 7)), inc=sPE)
            EM("tensor", lambda e, c=cc, s=slot: e.matmul(
                psA[:, 1024:1280], xnb3(s, c),
                bass.AP(WWV, 256 * c, [[8 * 256, 128], [1, 256]]),
                start=(c == 0), stop=(c == 7)), inc=sPE)
            EM("tensor", lambda e, c=cc, s=slot, t=tt: e.matmul(
                ps3[:, t:t + 1], xnb3(s, c), WIMP[:, c:c + 1],
                start=(c == 0), stop=(c == 7)), inc=sPE)
        if pend_tr is not None:
            kw_spill = p1_tr_block(tt - 1, pend_tr, kw_spill)
        W("vector", sPE)
        for cc in range(8):
            EM("vector", lambda e, c=cc, s=slot: e.reduce_sum(
                XMF[:, 8 + c:9 + c], xnb3(s, c),
                axis=mybir.AxisListType.X), inc=sDV)
        W("vector", sDV)
        EM("vector", lambda e: e.tensor_add(
            XMF[:, 0:8], XMF[:, 0:8], XMF[:, 8:16]), inc=sDV)
        EM("vector", lambda e: e.tensor_copy(NATF[:, 0:96], ps1[:, 0:96]),
           inc=sDV)
        EM("vector", lambda e: e.tensor_copy(NATF[:, 96:128], ps2[:, 0:32]),
           inc=sDV)
        EM("vector", lambda e: e.tensor_scalar(
            KR1[:, :], psA[:, 1536:1600], 1.0 / 16.0, None, ALU.mult),
           inc=sDV)
        EM("vector", lambda e, K=KWx: e.tensor_copy(K[:, :], psA[:, 0:768]),
           inc=sDV)
        if vst_spill is not None:
            W("vector", sSY, vst_spill)
        EM("vector", lambda e: e.memset(VST[:, :], 1.0), inc=sDV)
        W("vector", sDV)
        EM("vector", lambda e: e.tensor_copy(
            bass.AP(VST, 0, [[264, 128], [33, 8], [1, 32]]),
            bass.AP(psA, 1024, [[2048, 128], [32, 8], [1, 32]])), inc=sDV)
        readers_done = sDV.n
        W("sync", sDV)
        DMA("sync",
            bass.AP(vw_scr, (128 * tt) * 33,
                    [[33, 128], [T * 33, 8], [1, 33]]),
            VST[:, :])
        vst_spill = sSY.n
        EM("vector", lambda e: e.tensor_mul(SQF[:, 0:128], NATF[:, 0:128],
                                            NATF[:, 0:128]), inc=sDV)
        W("vector", sDV)
        EM("vector", lambda e: e.reduce_sum(
            MISC[:, 0:1], SQF[:, 0:96], axis=mybir.AxisListType.X), inc=sDV)
        EM("vector", lambda e: e.reduce_sum(
            MISC[:, 4:5], SQF[:, 96:128], axis=mybir.AxisListType.X),
           inc=sDV)
        W("vector", sDV)
        EM("vector", lambda e: e.tensor_scalar(
            MISC[:, 1:2], MISC[:, 0:1], 1.0 / 96, 1e-6, ALU.mult,
            op1=ALU.add), inc=sDV)
        EM("vector", lambda e: e.tensor_scalar(
            MISC[:, 5:6], MISC[:, 4:5], 1.0 / 32, 1e-6, ALU.mult,
            op1=ALU.add), inc=sDV)
        W("scalar", sDV)
        EM("scalar", lambda e: e.activation(
            MISC[:, 2:3], MISC[:, 1:2], AF.Ln, bias=0.0, scale=1.0),
           inc=sAC)
        EM("scalar", lambda e: e.activation(
            MISC[:, 6:7], MISC[:, 5:6], AF.Ln, bias=0.0, scale=1.0),
           inc=sAC)
        EM("scalar", lambda e: e.activation(
            MISC[:, 3:4], MISC[:, 2:3], AF.Exp, bias=0.0, scale=-0.5),
           inc=sAC)
        EM("scalar", lambda e: e.activation(
            MISC[:, 7:8], MISC[:, 6:7], AF.Exp, bias=0.0, scale=-0.5),
           inc=sAC)
        W("vector", sAC)
        EM("vector", lambda e, R=RMx: e.tensor_scalar(
            R[:, 32:128], NATF[:, 0:96], MISC[:, 3:4], None, ALU.mult),
           inc=sDV)
        EM("vector", lambda e, R=RMx: e.tensor_scalar(
            R[:, 0:32], NATF[:, 96:128], MISC[:, 7:8], None, ALU.mult),
           inc=sDV)
        EM("vector", lambda e, t=tt: e.tensor_mul(
            KR2[:, 0:64], KR1[:, :],
            bass.AP(ROPEC, 64 * t, [[NT * 64, 128], [1, 64]])), inc=sDV)
        EM("vector", lambda e, t=tt: e.tensor_mul(
            KR2[:, 64:128],
            bass.AP(KR1, 32, [[64, 128], [-32, 2], [1, 32]]),
            bass.AP(ROPES, 64 * t, [[NT * 64, 128], [1, 64]])), inc=sDV)
        EM("vector", lambda e, t=tt, K=KWx, T1x=T1x: e.tensor_mul(
            bass.AP(T1x, 128, [[512, 128], [64, 6], [1, 64]]),
            bass.AP(K, 32 + 2 * 96, [[768, 128], [96, 6], [1, 64]]),
            bass.AP(ROPEC, 64 * t, [[NT * 64, 128], [0, 6], [1, 64]])),
           inc=sDV)
        EM("vector", lambda e, t=tt: e.tensor_add(
            KRN[:, 64 * t:64 * (t + 1)], KR2[:, 0:64], KR2[:, 64:128]),
           inc=sDV)
        EM("vector", lambda e, t=tt, K=KWx, T2x=T2x: e.tensor_mul(
            bass.AP(T2x, 0, [[512, 128], [64, 8], [1, 64]]),
            bass.AP(K, 64, [[768, 128], [96, 8], [-32, 2], [1, 32]]),
            bass.AP(ROPES, 64 * t, [[NT * 64, 128], [0, 8], [1, 64]])),
           inc=sDV)
        EM("vector", lambda e, t=tt, K=KWx, T1x=T1x: e.tensor_mul(
            bass.AP(T1x, 0, [[512, 128], [64, 2], [1, 64]]),
            bass.AP(K, 32, [[768, 128], [96, 2], [1, 64]]),
            bass.AP(ROPEC, 64 * t, [[NT * 64, 128], [0, 2], [1, 64]])),
           inc=sDV)
        W("vector", sDV)
        EM("vector", lambda e, K=KWx, T1x=T1x, T2x=T2x: e.tensor_add(
            bass.AP(K, 32, [[768, 128], [96, 8], [1, 64]]),
            bass.AP(T1x, 0, [[512, 128], [64, 8], [1, 64]]),
            bass.AP(T2x, 0, [[512, 128], [64, 8], [1, 64]])), inc=sDV)
        pend_tr = sDV.n

    kw_spill = p1_tr_block(NT - 1, pend_tr, kw_spill)
    W("tensor", sDV)

    for tt in range(NT):
        EM("tensor", lambda e, t=tt: e.matmul(
            ps1[:, 0:256], CKVT[0:32, 128 * t:128 * (t + 1)], WDVB[0:32, :],
            start=True, stop=True), inc=sPE)
        W("vector", sPE)
        W("vector", sSY)
        EM("vector", lambda e: e.memset(VST[:, :], 1.0), inc=sDV)
        W("vector", sDV)
        EM("vector", lambda e: e.tensor_copy(
            bass.AP(VST, 0, [[264, 128], [33, 8], [1, 32]]),
            bass.AP(ps1, 0, [[512, 128], [32, 8], [1, 32]])), inc=sDV)
        W("sync", sDV)
        DMA("sync",
            bass.AP(v1_scr, (128 * tt) * 33,
                    [[33, 128], [T * 33, 8], [1, 33]]),
            VST[:, :])
        W("tensor", sSY)

    EM("vector", lambda e: e.tensor_scalar(
        XMB[:, 0:8], XMF[:, 0:8], 1.0 / T, None, ALU.mult), inc=sDV)
    W("tensor", sDV)
    for cc in range(8):
        EM("tensor", lambda e, c=cc: e.matmul(
            ps2[0:1, 0:3], XMB[:, c:c + 1], WGT3[:, 3 * c:3 * (c + 1)],
            start=(c == 0), stop=(c == 7)), inc=sPE)
    W("vector", sPE)
    EM("vector", lambda e: e.tensor_copy(GROW[0:1, 0:3], ps2[0:1, 0:3]),
       inc=sDV)
    W("scalar", sDV)
    EM("scalar", lambda e: e.activation(
        GROW[0:1, 3:6], GROW[0:1, 0:3], AF.Exp, bias=0.0, scale=1.0),
       inc=sAC)
    W("vector", sAC)
    EM("vector", lambda e: e.reduce_sum(
        GROW[0:1, 6:7], GROW[0:1, 3:6], axis=mybir.AxisListType.X), inc=sDV)
    W("vector", sDV)
    EM("vector", lambda e: e.reciprocal(GROW[0:1, 7:8], GROW[0:1, 6:7]),
       inc=sDV)
    W("vector", sDV)
    EM("vector", lambda e: e.tensor_scalar(
        GROW[0:1, 3:6], GROW[0:1, 3:6], GROW[0:1, 7:8], None, ALU.mult),
       inc=sDV)
    W("tensor", sDV)
    EM("tensor", lambda e: e.matmul(
        ps2[:, 8:11], ONESF[0:1, :], GROW[0:1, 3:6], start=True, stop=True),
       inc=sPE)
    W("vector", sPE)
    EM("vector", lambda e: e.tensor_copy(CONST[:, 32:35], ps2[:, 8:11]),
       inc=sDV)

    EM("vector", lambda e: e.tensor_copy(SSC[:, :], ps3[:, 0:32]), inc=sDV)
    EM("vector", lambda e: e.memset(MISC[:, 0:1], -1000.0), inc=sDV)
    EM("vector", lambda e: e.memset(MISC[:, 1:2], 1000.0), inc=sDV)
    EM("vector", lambda e: e.memset(MISC[:, 2:3], 0.0), inc=sDV)
    for it in range(NBIS):
        W("vector", sPE)
        EM("vector", lambda e: e.tensor_scalar(
            JUNK[:, :], SSC[:, :], MISC[:, 2:3], 0.0, ALU.is_gt,
            op1=ALU.add, accum_out=MISC[:, 6:7]), inc=sDV)
        W("tensor", sDV)
        EM("tensor", lambda e: e.matmul(
            ps2[0:1, 16:17], ONESF[:, 0:1], MISC[:, 6:7],
            start=True, stop=True), inc=sPE)
        W("vector", sPE)
        EM("vector", lambda e: e.tensor_copy(GROW[0:1, 8:9],
                                             ps2[0:1, 16:17]), inc=sDV)
        W("tensor", sDV)
        EM("tensor", lambda e: e.matmul(
            ps2[:, 17:18], ONESF[0:1, :], GROW[0:1, 8:9],
            start=True, stop=True), inc=sPE)
        W("vector", sPE)
        EM("vector", lambda e: e.tensor_scalar(
            MISC[:, 3:4], ps2[:, 17:18], float(KEEP) - 0.5, None, ALU.is_gt),
           inc=sDV)
        EM("vector", lambda e: e.tensor_sub(
            MISC[:, 4:5], MISC[:, 2:3], MISC[:, 0:1]), inc=sDV)
        EM("vector", lambda e: e.tensor_sub(
            MISC[:, 8:9], MISC[:, 1:2], MISC[:, 2:3]), inc=sDV)
        EM("vector", lambda e: e.tensor_scalar(
            MISC[:, 5:6], MISC[:, 3:4], -1.0, 1.0, ALU.mult, op1=ALU.add),
           inc=sDV)
        W("vector", sDV)
        EM("vector", lambda e: e.tensor_mul(
            MISC[:, 4:5], MISC[:, 4:5], MISC[:, 3:4]), inc=sDV)
        EM("vector", lambda e: e.tensor_mul(
            MISC[:, 8:9], MISC[:, 8:9], MISC[:, 5:6]), inc=sDV)
        W("vector", sDV)
        EM("vector", lambda e: e.tensor_add(
            MISC[:, 0:1], MISC[:, 0:1], MISC[:, 4:5]), inc=sDV)
        EM("vector", lambda e: e.tensor_sub(
            MISC[:, 1:2], MISC[:, 1:2], MISC[:, 8:9]), inc=sDV)
        W("vector", sDV)
        EM("vector", lambda e: e.tensor_add(
            MISC[:, 2:3], MISC[:, 0:1], MISC[:, 1:2]), inc=sDV)
        W("vector", sDV)
        EM("vector", lambda e: e.tensor_scalar(
            MISC[:, 2:3], MISC[:, 2:3], 0.5, None, ALU.mult), inc=sDV)
        W("vector", sDV)
    W("vector", sDV)
    EM("vector", lambda e: e.tensor_scalar(
        JUNK[:, :], SSC[:, :], MISC[:, 0:1], None, ALU.is_gt), inc=sDV)
    W("vector", sDV)
    EM("vector", lambda e: e.tensor_copy(JUNKB[:, :], JUNK[:, :]), inc=sDV)
    W("tensor", sDV)
    EM("tensor", lambda e: e.matmul(
        ps1[:, 0:32], TRILS[:, :], JUNKB[:, :], start=True, stop=False),
       inc=sPE)
    EM("tensor", lambda e: e.matmul(
        ps1[:, 0:32], IDENT[:, :], JUNKB[:, :], start=False, stop=True),
       inc=sPE)
    EM("tensor", lambda e: e.matmul(
        ps2[0:1, 0:32], ONESB[:, 0:1], JUNKB[:, :], start=True, stop=True),
       inc=sPE)
    W("vector", sPE)
    EM("vector", lambda e, A=CUMA: e.tensor_copy(A[:, :], ps1[:, 0:32]),
       inc=sDV)
    EM("vector", lambda e: e.memset(CP1[0:1, 0:1], 0.0), inc=sDV)
    EM("vector", lambda e: e.tensor_copy(CP1[0:1, 1:32], ps2[0:1, 0:31]),
       inc=sDV)
    cpa, cpc = CP1, CP2
    for k in (1, 2, 4, 8, 16):
        W("vector", sDV)
        EM("vector", lambda e, kk=k, A=cpa, C=cpc: e.tensor_copy(
            C[0:1, 0:kk], A[0:1, 0:kk]), inc=sDV)
        EM("vector", lambda e, kk=k, A=cpa, C=cpc: e.tensor_add(
            C[0:1, kk:32], A[0:1, kk:32], A[0:1, 0:32 - kk]), inc=sDV)
        cpa, cpc = cpc, cpa
    W("tensor", sDV)
    EM("tensor", lambda e, A=cpa: e.matmul(
        ps3[:, 0:32], ONESF[0:1, :], A[0:1, :], start=True, stop=True),
       inc=sPE)
    W("vector", sPE)
    EM("vector", lambda e, A=CUMA, C=CUMC: e.tensor_add(
        C[:, :], A[:, :], ps3[:, 0:32]), inc=sDV)
    W("vector", sDV)
    EM("vector", lambda e, C=CUMC: e.tensor_mul(
        C[:, :], C[:, :], JUNK[:, :]), inc=sDV)
    W("vector", sDV)
    for tt in range(NT):
        EM("vector", lambda e, t=tt, C=CUMC: e.tensor_scalar(
            PTS[:, 256 * t:256 * (t + 1)], IOTA[:, :], C[:, t:t + 1],
            None, ALU.is_equal), inc=sDV)
    W("tensor", sDV)
    if _TAPS:
        W("sync", sDV)
        DMA("sync", dbg_pts[:, :], PTS[:, :])
        DMA("sync", dbg_junk[:, :], JUNK[:, :])
        DMA("sync", dbg_cum[:, :], CUMA[:, :])
        DMA("sync", dbg_cumf[:, :], CUMC[:, :])
        DMA("sync", dbg_ssc[:, :], SSC[:, :])

    W("sync", sPE)
    load_xn_tile(0, 0)
    for tt in range(NT):
        slot = tt % 2
        if tt < NT - 1:
            W("sync", sPE)
            load_xn_tile(tt + 1, (tt + 1) % 2)
        W("tensor", sSY, sSY.n - (16 if tt < NT - 1 else 0))
        for cc in range(8):
            EM("tensor", lambda e, c=cc, s=slot: e.matmul(
                psA[:, 0:512], xnb3(s, c),
                bass.AP(WSK, 768 * c, [[8 * 768, 128], [1, 512]]),
                start=(c == 0), stop=(c == 7)), inc=sPE)
            EM("tensor", lambda e, c=cc, s=slot: e.matmul(
                psA[:, 512:768], xnb3(s, c),
                bass.AP(WSK, 768 * c + 512, [[8 * 768, 128], [1, 256]]),
                start=(c == 0), stop=(c == 7)), inc=sPE)
        W("vector", sPE)
        EM("vector", lambda e: e.tensor_copy(KFT[:, :], psA[:, 0:768]),
           inc=sDV)
        W("tensor", sDV)
        for r in range(2):
            EM("tensor", lambda e, t=tt, rr=r: e.matmul(
                ps1[:, :] if rr == 0 else ps2[:, :],
                PTS[:, 256 * t + 128 * rr:256 * t + 128 * (rr + 1)],
                KFT[:, 0:512], start=(t == 0), stop=(t == NT - 1)), inc=sPE)
            EM("tensor", lambda e, t=tt, rr=r: e.matmul(
                ps3[:, 0:256] if rr == 0 else psA[:, 1536:1792],
                PTS[:, 256 * t + 128 * rr:256 * t + 128 * (rr + 1)],
                KFT[:, 512:768], start=(t == 0), stop=(t == NT - 1)),
               inc=sPE)
    W("vector", sPE)
    for r in range(2):
        EM("vector", lambda e, rr=r: e.tensor_copy(
            KSN[:, 768 * rr:768 * rr + 512],
            ps1[:, :] if rr == 0 else ps2[:, :]), inc=sDV)
        EM("vector", lambda e, rr=r: e.tensor_copy(
            KSN[:, 768 * rr + 512:768 * (rr + 1)],
            ps3[:, 0:256] if rr == 0 else psA[:, 1536:1792]), inc=sDV)
    for r in range(2):
        EM("vector", lambda e, rr=r: e.tensor_mul(
            bass.AP(T1B, 0, [[512, 128], [64, 8], [1, 64]]),
            bass.AP(KSN, 768 * rr + 32, [[2 * 768, 128], [96, 8], [1, 64]]),
            bass.AP(ROPEC, 64 * rr, [[NT * 64, 128], [0, 8], [1, 64]])),
           inc=sDV)
        EM("vector", lambda e, rr=r: e.tensor_mul(
            bass.AP(T2B, 0, [[512, 128], [64, 8], [1, 64]]),
            bass.AP(KSN, 768 * rr + 64, [[2 * 768, 128], [96, 8],
                                         [-32, 2], [1, 32]]),
            bass.AP(ROPES, 64 * rr, [[NT * 64, 128], [0, 8], [1, 64]])),
           inc=sDV)
        W("vector", sDV)
        EM("vector", lambda e, rr=r: e.tensor_add(
            bass.AP(KSN, 768 * rr + 32, [[2 * 768, 128], [96, 8], [1, 64]]),
            bass.AP(T1B, 0, [[512, 128], [64, 8], [1, 64]]),
            bass.AP(T2B, 0, [[512, 128], [64, 8], [1, 64]])), inc=sDV)
    W("tensor", sDV)
    for hh in range(HPC):
        for r in range(2):
            EM("tensor", lambda e, h_=hh, rr=r: e.transpose(
                psT[0:96, 0:128],
                KSN[:, 768 * rr + 96 * h_:768 * rr + 96 * (h_ + 1)],
                IDENT[:, :]), inc=sPE)
            W("vector", sPE)
            EM("vector", lambda e, h_=hh, rr=r: e.tensor_copy(
                KST[0:96, 256 * h_ + 128 * rr:256 * h_ + 128 * (rr + 1)],
                psT[0:96, 0:128]), inc=sDV)
            W("tensor", sDV)

    W("gpsimd", sPE)
    DMA("gpsimd", WWV[:, :],
        bass.AP(wselv, 0, [[256, 128], [128 * 256, 8], [1, 256]]))
    W("tensor", sG)
    W("sync", sPE)
    load_xn_tile(0, 0)
    for tt in range(NT):
        slot = tt % 2
        if tt < NT - 1:
            W("sync", sPE)
            load_xn_tile(tt + 1, (tt + 1) % 2)
        W("tensor", sSY, sSY.n - (16 if tt < NT - 1 else 0))
        for cc in range(8):
            EM("tensor", lambda e, c=cc, s=slot: e.matmul(
                psA[:, 0:256], xnb3(s, c),
                bass.AP(WWV, 256 * c, [[8 * 256, 128], [1, 256]]),
                start=(c == 0), stop=(c == 7)), inc=sPE)
        W("vector", sPE)
        EM("vector", lambda e: e.tensor_copy(KFT[:, 0:256], psA[:, 0:256]),
           inc=sDV)
        W("tensor", sDV)
        for r in range(2):
            EM("tensor", lambda e, t=tt, rr=r: e.matmul(
                ps1[:, 0:256] if rr == 0 else ps2[:, 0:256],
                PTS[:, 256 * t + 128 * rr:256 * t + 128 * (rr + 1)],
                KFT[:, 0:256], start=(t == 0), stop=(t == NT - 1)), inc=sPE)
    W("vector", sPE)
    EM("vector", lambda e: e.memset(VSA[:, :], 1.0), inc=sDV)
    W("vector", sDV)
    for r in range(2):
        EM("vector", lambda e, rr=r: e.tensor_copy(
            bass.AP(VSA, 264 * rr, [[2 * 264, 128], [33, 8], [1, 32]]),
            bass.AP(ps1 if rr == 0 else ps2, 0,
                    [[512, 128], [32, 8], [1, 32]])), inc=sDV)
    W("tensor", sDV)

    ar_to(mB2)
    MASKS = ar("MASKS", [128, 2048], BF16)
    QNH = ar("QNH", [128, NT * 96], BF16)
    K1NH = ar("K1NH", [128, NT * 96], BF16)
    QTH = ar("QTH", [96, T], BF16)
    K1TH = ar("K1TH", [96, T], BF16)
    KWTH = ar("KWTH", [96, T], BF16)
    V1A = ar("V1A", [128, NT * 33], BF16)
    VWA = ar("VWA", [128, NT * 33], BF16)
    EXPB = ar("EXPB", [128, 4096], BF16)
    AEVF3 = ar("AEVF3", [128, 3 * 512], F32)
    RCP3 = ar("RCP3", [128, 16], F32)
    COMB3 = ar("COMB3", [128, 256], BF16)
    AVST3 = ar("AVST3", [128, 512], BF16)
    RB1 = ar("RB1", [128, 512], BF16)
    RB2 = ar("RB2", [128, 512], BF16)
    DMA("sync", MASKS[:, :], s_masks[:, :])
    W("vector", sSY)

    for h in range(HPC):
        W("sync", sPE)
        W("tensor", sDV)
        DMA("sync", KWTH[0:96, :],
            bass.AP(kwT_scr, (96 * h) * T, [[T, 96], [1, T]]))
        DMA("sync", V1A[:, :],
            bass.AP(v1_scr, (T * h) * 33,
                    [[33, 128], [128 * 33, NT], [1, 33]]))
        DMA("sync", VWA[:, :],
            bass.AP(vw_scr, (T * h) * 33,
                    [[33, 128], [128 * 33, NT], [1, 33]]))
        kw_ready = sSY.n
        for t2 in range(0, NT, 2):
            for j, pso in ((0, 0), (1, 256)):
                t = t2 + j
                EM("tensor", lambda e, t=t, h_=h, o=pso: e.matmul(
                    ps1[:, o:o + 32], NQT[0:96, 128 * t:128 * (t + 1)],
                    WDQN[0:96, 32 * h_:32 * (h_ + 1)], start=True,
                    stop=True), inc=sPE)
                EM("tensor", lambda e, t=t, h_=h, o=pso: e.matmul(
                    ps1[:, o + 32:o + 96], NQT[0:96, 128 * t:128 * (t + 1)],
                    WDQR[0:96, 64 * h_:64 * (h_ + 1)], start=True,
                    stop=True), inc=sPE)
                EM("tensor", lambda e, t=t, h_=h, o=pso: e.matmul(
                    ps2[:, o:o + 32], CKVT[0:32, 128 * t:128 * (t + 1)],
                    WDKN[0:32, 32 * h_:32 * (h_ + 1)], start=True,
                    stop=True), inc=sPE)
            W("vector", sPE)
            EM("vector", lambda e: e.tensor_copy(RB1[:, 0:96], ps1[:, 0:96]),
               inc=sDV)
            EM("vector", lambda e: e.tensor_copy(
                RB1[:, 256:352], ps1[:, 256:352]), inc=sDV)
            EM("vector", lambda e, t=t2: e.tensor_copy(
                K1NH[:, 96 * t:96 * t + 32], ps2[:, 0:32]), inc=sDV)
            EM("vector", lambda e, t=t2 + 1: e.tensor_copy(
                K1NH[:, 96 * t:96 * t + 32], ps2[:, 256:288]), inc=sDV)
            EM("vector", lambda e, t=t2: e.tensor_mul(
                T1B[:, 0:64], RB1[:, 32:96],
                bass.AP(ROPEC, 64 * t, [[NT * 64, 128], [1, 64]])), inc=sDV)
            EM("vector", lambda e, t=t2 + 1: e.tensor_mul(
                T1B[:, 128:192], RB1[:, 288:352],
                bass.AP(ROPEC, 64 * t, [[NT * 64, 128], [1, 64]])), inc=sDV)
            EM("vector", lambda e, t=t2: e.tensor_mul(
                T2B[:, 0:64],
                bass.AP(RB1, 64, [[512, 128], [-32, 2], [1, 32]]),
                bass.AP(ROPES, 64 * t, [[NT * 64, 128], [1, 64]])), inc=sDV)
            EM("vector", lambda e, t=t2 + 1: e.tensor_mul(
                T2B[:, 128:192],
                bass.AP(RB1, 256 + 64, [[512, 128], [-32, 2], [1, 32]]),
                bass.AP(ROPES, 64 * t, [[NT * 64, 128], [1, 64]])), inc=sDV)
            EM("vector", lambda e, t=t2: e.tensor_copy(
                K1NH[:, 96 * t + 32:96 * t + 96],
                KRN[:, 64 * t:64 * (t + 1)]), inc=sDV)
            EM("vector", lambda e, t=t2: e.tensor_copy(
                QNH[:, 96 * t:96 * t + 32], RB1[:, 0:32]), inc=sDV)
            EM("vector", lambda e, t=t2: e.tensor_add(
                QNH[:, 96 * t + 32:96 * t + 96], T1B[:, 0:64],
                T2B[:, 0:64]), inc=sDV)
            EM("vector", lambda e, t=t2 + 1: e.tensor_copy(
                K1NH[:, 96 * t + 32:96 * t + 96],
                KRN[:, 64 * t:64 * (t + 1)]), inc=sDV)
            EM("vector", lambda e, t=t2 + 1: e.tensor_copy(
                QNH[:, 96 * t:96 * t + 32], RB1[:, 256:288]), inc=sDV)
            EM("vector", lambda e, t=t2 + 1: e.tensor_add(
                QNH[:, 96 * t + 32:96 * t + 96], T1B[:, 128:192],
                T2B[:, 128:192]), inc=sDV)
            W("tensor", sDV)
            EM("tensor", lambda e, t=t2: e.transpose(
                psT[0:96, 0:128], QNH[:, 96 * t:96 * (t + 1)], IDENT[:, :]),
               inc=sPE)
            EM("tensor", lambda e, t=t2: e.transpose(
                psT[0:96, 128:256], K1NH[:, 96 * t:96 * (t + 1)],
                IDENT[:, :]), inc=sPE)
            EM("tensor", lambda e, t=t2 + 1: e.transpose(
                psT[0:96, 256:384], QNH[:, 96 * t:96 * (t + 1)],
                IDENT[:, :]), inc=sPE)
            EM("tensor", lambda e, t=t2 + 1: e.transpose(
                psT[0:96, 384:512], K1NH[:, 96 * t:96 * (t + 1)],
                IDENT[:, :]), inc=sPE)
            W("vector", sPE)
            EM("vector", lambda e, t=t2: e.tensor_copy(
                QTH[0:96, 128 * t:128 * (t + 1)], psT[0:96, 0:128]),
               inc=sDV)
            EM("vector", lambda e, t=t2: e.tensor_copy(
                K1TH[0:96, 128 * t:128 * (t + 1)], psT[0:96, 128:256]),
               inc=sDV)
            EM("vector", lambda e, t=t2 + 1: e.tensor_copy(
                QTH[0:96, 128 * t:128 * (t + 1)], psT[0:96, 256:384]),
               inc=sDV)
            EM("vector", lambda e, t=t2 + 1: e.tensor_copy(
                K1TH[0:96, 128 * t:128 * (t + 1)], psT[0:96, 384:512]),
               inc=sDV)
            W("tensor", sDV)
        W("tensor", sSY, kw_ready)

        for qc in range(8):
            nkb = qc + 1
            bursts = []
            for br, (KTH_, VA_, bank) in enumerate(
                    ((K1TH, V1A, ps1), (KWTH, VWA, ps3))):
                for kb in range(nkb):
                    bursts.append((KTH_, VA_, bank, kb,
                                   kb == 0, kb == nkb - 1))
            pend = None
            for bi in range(len(bursts) + 1):
                if bi < len(bursts):
                    KTH_, VA_, bank, kb, isfirst, islast = bursts[bi]
                    slot = bi % 2
                    W("tensor", sAC)
                    if bi == 0:
                        W("tensor", sDV)
                    for i in range(4):
                        kt = 4 * kb + i
                        EM("tensor", lambda e, kk=KTH_, kt_=kt, q=qc, i_=i:
                           e.matmul(
                               psA[:, 512 * i_:512 * (i_ + 1)],
                               kk[0:96, 128 * kt_:128 * (kt_ + 1)],
                               QTH[0:96, 512 * q:512 * (q + 1)],
                               start=True, stop=True), inc=sPE)
                    qk_done = sPE.n
                if pend is not None:
                    (pV, pbank, pkb, pslot, pfirst, plast) = pend
                    if plast:
                        W("tensor", sDV)
                    for i in range(4):
                        kt = 4 * pkb + i
                        EM("tensor", lambda e, v=pV, kt_=kt, s=pslot, i_=i,
                           b=pbank, st=(pfirst and i == 0),
                           sp=(plast and i == 3): e.matmul(
                               b[0:33, :],
                               bass.AP(v, 33 * kt_, [[NT * 33, 128],
                                                     [1, 33]]),
                               EXPB[:, 2048 * s + 512 * i_:
                                    2048 * s + 512 * (i_ + 1)],
                               start=st, stop=sp), inc=sPE)
                if bi == len(bursts):
                    break
                W("scalar", sPE, qk_done)
                EM("scalar", lambda e, s=slot: e.activation(
                    EXPB[:, 2048 * s:2048 * (s + 1)], psA[:, :], AF.Exp,
                    bias=CONST[:, 63:64], scale=ISQ), inc=sAC)
                if islast:
                    W("vector", sAC)
                    for i in range(4):
                        EM("vector", lambda e, s=slot, i_=i:
                           e.tensor_mul(
                               EXPB[:, 2048 * s + 512 * i_:
                                    2048 * s + 512 * (i_ + 1)],
                               EXPB[:, 2048 * s + 512 * i_:
                                    2048 * s + 512 * (i_ + 1)],
                               MASKS[:, 512 * i_:512 * (i_ + 1)]),
                           inc=sDV)
                pend = (VA_, bank, kb, slot, isfirst, islast)
            W("tensor", sAC)
            for r in range(2):
                EM("tensor", lambda e, h_=h, rr=r, q=qc: e.matmul(
                    psA[:, 512 * rr:512 * (rr + 1)],
                    KST[0:96, 256 * h_ + 128 * rr:
                        256 * h_ + 128 * (rr + 1)],
                    QTH[0:96, 512 * q:512 * (q + 1)],
                    start=True, stop=True), inc=sPE)
            W("scalar", sPE)
            EM("scalar", lambda e: e.activation(
                EXPB[:, 0:1024], psA[:, 0:1024], AF.Exp,
                bias=CONST[:, 63:64], scale=ISQ), inc=sAC)
            W("tensor", sAC)
            for r in range(2):
                EM("tensor", lambda e, rr=r: e.matmul(
                    ps2[0:33, :],
                    bass.AP(VSA, 264 * rr, [[2 * 264, 128], [1, 33]]),
                    EXPB[:, 512 * rr:512 * (rr + 1)],
                    start=(rr == 0), stop=(rr == 1)), inc=sPE)
            division([(ps1, 32), (ps2, 33), (ps3, 34)], 32,
                     AEVF3, RCP3, COMB3, AVST3)
            if _TAPS and h == 0 and qc == 1:
                DMA("sync", dbg_qth[:, :], QTH[0:96, :])
                DMA("sync", dbg_k1n[:, :], K1NH[:, :])
                DMA("sync", dbg_ckv[:, :], CKVT[0:32, :])
                DMA("sync", dbg_krn[:, :], KRN[:, :])
                DMA("sync", dbg_k1h[:, :], K1TH[0:96, :])
                DMA("sync", dbg_kwh[:, :], KWTH[0:96, :])
                DMA("sync", dbg_cst[:, :], CONST[:, :])
                DMA("sync", dbg_ab3[:, :], AEVF3[:, :])
                DMA("sync", dbg_rc3[:, :], RCP3[:, :])
                DMA("sync", dbg_cb3[:, :], COMB3[:, :])
                DMA("sync", dbg_exb[:, :], EXPB[:, :])
                DMA("sync", dbg_kst[:, :], KST[0:96, :])
                DMA("sync", dbg_vsa[:, :], VSA[:, :])
                W("sync", sSY)
            DMA("sync",
                bass.AP(cin_avb, (32 * h) * T + 512 * qc,
                        [[T, 32], [1, 512]]),
                AVST3[0:32, 0:512])

    if _TAPS:
        W("sync", sDV)
        DMA("sync", dbg_avb[:, :], cin_avb[:, :])
        DMA("sync", dbg_misc[:, :], MISC[:, :])
        DMA("sync", dbg_grow[:, :], GROW[:, :])
        W("sync", sSY)

    cc_avb = sCC.n + 1
    W("gpsimd", sSY)
    exchange(cin_avb, avb_allg)
    ar_to(mB2)
    AVRH3 = ar("AVRH3", [128, 4096], BF16)
    WPJ = ar("WPJ", [128, 4096], BF16)
    DMA("gpsimd", WPJ[:, :],
        bass.AP(wproj, 0, [[1024, 128], [128 * 1024, 4], [1, 1024]]))
    W("sync", sCC, cc_avb)
    W("tensor", sG)
    outproj(avb_allg, WPJ, 4, AVRH3)
    ar_to(mB)
    BARRIER()
    TAP(2)

    emit_ffn("f2")

    W("sync", sDV)
    DMA("sync",
        bass.AP(y_out, 0, [[TH, 128], [128 * TH, 8], [1, TH]]),
        bass.AP(X, 0, [[8 * TH, 128], [2048, 8], [1, 2048]]))
    W("sync", sSY)

    import contextlib
    with contextlib.ExitStack() as st:
        for name, c in sems.items():
            c.h = st.enter_context(nc.semaphore(name))
        block = st.enter_context(nc.Block())
        for eng in ENGS:
            getattr(block, eng)(
                (lambda fns: lambda e: [f(e) for f in fns])(prog[eng]))
    return nc


_NC = None


def kernel(
    L0_attn_a_hidden_states, L0_attn_a_Wq, L0_attn_a_Wk, L0_attn_a_Wv,
    L0_attn_a_Wo, L0_attn_a_rel_bias_table, L0_attn_b_w_cq, L0_attn_b_g_qnorm,
    L0_attn_b_w_dq_nope, L0_attn_b_w_dq_rope, L0_attn_b_w_ckv,
    L0_attn_b_g_kvnorm, L0_attn_b_w_dk_nope, L0_attn_b_w_dv,
    L0_attn_b_w_krope, L0_attn_b_w_imp, L0_attn_b_w_selk, L0_attn_b_w_selv,
    L0_attn_b_w_wink, L0_attn_b_w_winv, L0_attn_b_w_gate, L0_attn_b_w_proj,
    L0_ffn_w_gate, L0_ffn_w_up, L0_ffn_w_down,
):
    global _NC
    st = _statics()
    x = np.asarray(L0_attn_a_hidden_states)
    C = np.ascontiguousarray

    wgT_ = C(np.asarray(L0_ffn_w_gate).T)
    wuT_ = C(np.asarray(L0_ffn_w_up).T)
    wdT_ = C(np.asarray(L0_ffn_w_down).T)

    in_maps = []
    for c in range(8):
        b, hb = c // 2, c % 2
        hs = slice(512 * hb, 512 * hb + 512)
        m = {
            "x_ownT": C(x[b, 2048 * hb:2048 * hb + 2048, :].T),
            "wq": C(np.asarray(L0_attn_a_Wq)[:, hs]),
            "wk": C(np.asarray(L0_attn_a_Wk)[:, hs]),
            "wv": C(np.asarray(L0_attn_a_Wv)[:, hs]),
            "wo": C(np.asarray(L0_attn_a_Wo)),
            "rtab": C(np.asarray(L0_attn_a_rel_bias_table)[:, 8 * hb:8 * hb + 8]),
            "wgT": wgT_, "wuT": wuT_, "wdT": wdT_,
            "wcq": C(np.asarray(L0_attn_b_w_cq)),
            "gq": C(np.asarray(L0_attn_b_g_qnorm).reshape(96, 1)),
            "wdqn": C(np.asarray(L0_attn_b_w_dq_nope)[:, 256 * hb:256 * hb + 256]),
            "wdqr": C(np.asarray(L0_attn_b_w_dq_rope)[:, 512 * hb:512 * hb + 512]),
            "wckv": C(np.asarray(L0_attn_b_w_ckv)),
            "gkv": C(np.asarray(L0_attn_b_g_kvnorm).reshape(32, 1)),
            "wdkn": C(np.asarray(L0_attn_b_w_dk_nope)[:, 256 * hb:256 * hb + 256]),
            "wdv": C(np.asarray(L0_attn_b_w_dv)[:, 256 * hb:256 * hb + 256]),
            "wkro": C(np.asarray(L0_attn_b_w_krope)),
            "wimp": C(np.asarray(L0_attn_b_w_imp)),
            "wselk": C(np.asarray(L0_attn_b_w_selk)[:, 768 * hb:768 * hb + 768]),
            "wselv": C(np.asarray(L0_attn_b_w_selv)[:, 256 * hb:256 * hb + 256]),
            "wwink": C(np.asarray(L0_attn_b_w_wink)[:, 768 * hb:768 * hb + 768]),
            "wwinv": C(np.asarray(L0_attn_b_w_winv)[:, 256 * hb:256 * hb + 256]),
            "wgate": C(np.asarray(L0_attn_b_w_gate)),
            "wproj": C(np.asarray(L0_attn_b_w_proj)),
            "s_onehot": st["onehot"], "s_ident": st["ident"],
            "s_identf": st["identf"], "s_trils": st["trils"],
            "s_iota": st["iota"], "s_onesb": st["onesb"],
            "s_onesf": st["onesf"], "s_masks": st["masks"],
            "s_ropec": st["ropec"], "s_ropes": st["ropes"],
            "tok0": np.array([[2048 * hb]], dtype=np.int32),
        }
        in_maps.append(m)

    if _NC is None:
        _NC = build_nc()
    res = run_bass_kernel_spmd(_NC, in_maps, core_ids=list(range(8)))
    out = np.empty((B, T, D), dtype=np.float32)
    for c in range(8):
        b, hb = c // 2, c % 2
        yt = np.asarray(res.results[c]["y_ownT"])
        out[b, 2048 * hb:2048 * hb + 2048, :] = yt.T
    kernel._last_res = res
    return out
